# version 08
# baseline (speedup 1.0000x reference)
import math
import numpy as np
import ml_dtypes
import concourse.bass as bass
import concourse.mybir as mybir
from concourse.bass_utils import run_bass_kernel_spmd

F32 = mybir.dt.float32
BF16 = mybir.dt.bfloat16
F16 = mybir.dt.float16
I32 = mybir.dt.int32
U16 = mybir.dt.uint16
AF = mybir.ActivationFunctionType
ALU = mybir.AluOpType
AX = mybir.AxisListType

B, S, D, H, HD = 2, 8192, 2048, 16, 128
I_FFN = 5632
CH = 512
NCH = S // CH
KT = 128
ROLE_CHUNKS = [[r, 7 - r, 8 + r, 15 - r] for r in range(4)]
SLOT_NKT = [16, 32, 48, 64]
LN_EPS = 1e-5
RMS_EPS = 1e-6
THETA = 30000.0
SCALE2 = HD ** -0.5
NEG_MASK = -30000.0

HDPERM = np.concatenate([np.arange(0, HD, 2), np.arange(1, HD, 2)])
DPERM = np.concatenate([h * HD + HDPERM for h in range(H)])


def host_prepare(inputs):
    x = np.asarray(inputs["L0_attn_x"])
    Wq = np.asarray(inputs["L0_attn_Wq"])
    Wkv = np.asarray(inputs["L0_attn_Wkv"])
    Wo = np.asarray(inputs["L0_attn_Wo"])
    bq = np.asarray(inputs["L0_attn_bq"])
    bkv = np.asarray(inputs["L0_attn_bkv"])
    bo = np.asarray(inputs["L0_attn_bo"])
    qn_g = np.asarray(inputs["L0_attn_qn_g"])
    qn_b = np.asarray(inputs["L0_attn_qn_b"])
    kvn_g = np.asarray(inputs["L0_attn_kvn_g"])
    kvn_b = np.asarray(inputs["L0_attn_kvn_b"])
    ln_g = np.asarray(inputs["L0_attn_ln_g"])
    ln_b = np.asarray(inputs["L0_attn_ln_b"])
    gup = np.asarray(inputs["L0_ffn_gate_up_packed"])
    gus = np.asarray(inputs["L0_ffn_gate_up_scales"])
    dnp = np.asarray(inputs["L0_ffn_down_packed"])
    dns = np.asarray(inputs["L0_ffn_down_scales"])

    WqP = np.ascontiguousarray(Wq[:, DPERM])
    WkvP = np.concatenate([Wkv[:, :D][:, DPERM], Wkv[:, D:]], axis=1)
    bqP = bq[DPERM].copy()
    bkvP = np.concatenate([bkv[:D][DPERM], bkv[D:]])
    ln_gP = ln_g[HDPERM].copy()
    ln_bP = ln_b[HDPERM].copy()

    def unpack_T(packed, n_in):
        w = packed.view(np.uint32)
        nib = (w[:, :, None] >> (4 * np.arange(8, dtype=np.uint32))) & np.uint32(15)
        nib = nib.reshape(packed.shape[0], n_in)
        return np.ascontiguousarray(nib.T.astype(np.uint16))

    gu_nib = unpack_T(gup, D)
    dn_nib = unpack_T(dnp, I_FFN)

    gu_sc = np.ascontiguousarray(np.broadcast_to(gus[:, None, :], (16, 128, 11264)))
    dn_sc = np.ascontiguousarray(np.broadcast_to(dns[:, None, :], (44, 128, 2048)))

    pos = np.arange(S, dtype=np.float64)
    freqs = (np.arange(0, HD, 2, dtype=np.float64) / HD) * math.log(THETA)
    ang = pos[None, :] * freqs[:, None]
    cosd = np.tile(np.cos(ang), (2, 1)).astype(np.float32)
    sind = np.concatenate([-np.sin(ang), np.sin(ang)]).astype(np.float32)

    qq = np.arange(CH)[None, :]
    kk = np.arange(KT)[:, None]

    ident = np.eye(128, dtype=np.float32).astype(ml_dtypes.bfloat16)

    ehs = np.zeros((H, 128, H), dtype=np.float32)
    for h in range(H):
        ehs[h, :, h] = 1.0
    ehs = ehs.astype(ml_dtypes.bfloat16)

    in_maps = []
    for c in range(8):
        b = c // 4
        role = c % 4
        chunks = ROLE_CHUNKS[role]
        xb = x[b]
        xown = np.concatenate([xb[cc * CH:(cc + 1) * CH] for cc in chunks], axis=0)
        ownpos = np.concatenate([np.arange(cc * CH, (cc + 1) * CH) for cc in chunks])
        mv = np.zeros((4, 16, KT, CH), dtype=np.float32)
        for qi, cc in enumerate(chunks):
            E = SLOT_NKT[qi]
            for p128 in range(16):
                kt = E - 16 + p128
                if kt < 4 * cc:
                    pass
                elif kt <= 4 * cc + 3:
                    d = (kt - 4 * cc) * 128
                    mv[qi, p128] = np.where(qq < kk + d, NEG_MASK, 0.0)
                else:
                    mv[qi, p128] = NEG_MASK
        m = dict(
            xpre=np.ascontiguousarray(xb),
            xown=np.ascontiguousarray(xown),
            wq=WqP, wkv=WkvP, wo=Wo,
            bq=bqP[None, :], bkv=bkvP[None, :], bo=bo[None, :],
            qng=np.ascontiguousarray(qn_g.reshape(16, 128).T),
            qnb=np.ascontiguousarray(qn_b.reshape(16, 128).T),
            kvg=np.ascontiguousarray(kvn_g.reshape(16, 128).T),
            kvb=np.ascontiguousarray(kvn_b.reshape(16, 128).T),
            lng=ln_gP[None, :], lnb=ln_bP[None, :],
            gunib=gu_nib, dnnib=dn_nib,
            gusc=gu_sc, dnsc=dn_sc,
            cosd=np.ascontiguousarray(cosd),
            sind=np.ascontiguousarray(sind),
            cosdo=np.ascontiguousarray(cosd[:, ownpos]),
            sindo=np.ascontiguousarray(sind[:, ownpos]),
            maskv=mv.astype(ml_dtypes.bfloat16),
            ident=ident, ehs=ehs,
        )
        in_maps.append(m)
    return in_maps


def host_gather(results):
    out = np.empty((B, S, D), dtype=np.float32)
    for c in range(8):
        b = c // 4
        chunks = ROLE_CHUNKS[c % 4]
        y = results[c]["y"]
        for j, cc in enumerate(chunks):
            out[b, cc * CH:(cc + 1) * CH] = y[j * CH:(j + 1) * CH]
    return out


class Sem:
    __slots__ = ("h", "name", "count")

    def __init__(self, handle, name):
        self.h = handle
        self.name = name
        self.count = 0


class Tok:
    __slots__ = ("sem", "val", "eng", "dma")

    def __init__(self, sem, val, eng, dma):
        self.sem, self.val, self.eng, self.dma = sem, val, eng, dma


class Prog:

    ENGS = ("sync", "tensor", "vector", "scalar", "gpsimd")

    def __init__(self, nc, stack):
        self.nc = nc
        self.ops = []
        self.sems = []
        self._stack = stack

    def sem(self, name):
        s = Sem(self._stack.enter_context(self.nc.semaphore(name)), name)
        self.sems.append(s)
        return s

    def ring(self, name, n):
        return [self.sem(f"{name}{i}") for i in range(n)]

    def ring_slots(self, name, n):
        return [self.sem(f"{name}{i}") for i in range(n)]

    @staticmethod
    def _flat(reads):
        out = []
        for t in reads:
            if t is None:
                continue
            if isinstance(t, (list, tuple)):
                out.extend(Prog._flat(t))
            else:
                out.append(t)
        return out

    def op(self, eng, fn, reads=(), sig=None, amt=1, dma=False):
        waits = []
        for t in self._flat(reads):
            if (not t.dma) and t.eng == eng:
                continue
            waits.append(t)
        tok = None
        if sig is not None:
            sig.count += amt
            tok = Tok(sig, sig.count, eng, dma)
        self.ops.append((eng, fn, waits, sig, amt))
        return tok

    def drain(self, eng):
        self.ops.append((eng, lambda e: e.drain(), [], None, 0))

    def dma(self, eng, out, in_, reads=(), sig=None, **kw):
        assert sig is not None
        return self.op(eng, lambda e: e.dma_start(out=out, in_=in_, **kw),
                       reads=reads, sig=sig, amt=16, dma=True)

    def dmat(self, eng, out, in_, reads=(), sig=None):
        assert sig is not None
        return self.op(eng, lambda e: e.dma_start_transpose(out, in_),
                       reads=reads, sig=sig, amt=16, dma=True)

    def gate_all(self, toks):
        for eng in self.ENGS:
            self.ops.append((eng, lambda e: e.nop(), Prog._flat(toks), None, 0))

    def final_barrier(self, eng):
        toks = [Tok(s, s.count, "_barrier_", True) for s in self.sems if s.count]
        self.ops.append((eng, lambda e: e.nop(), toks, None, 0))

    def emit(self, block):
        ops = self.ops if Prog.KEEP is None else self.ops[:Prog.KEEP]
        per = {e: [] for e in self.ENGS}
        for rec in ops:
            per[rec[0]].append(rec)

        def make(eng_name):
            eops = per[eng_name]

            def body(eng):
                waited = {}
                for (_e, fn, waits, sig, amt) in eops:
                    for t in waits:
                        if waited.get(t.sem.name, -1) >= t.val:
                            continue
                        eng.wait_ge(t.sem.h, t.val)
                        waited[t.sem.name] = t.val
                    inst = fn(eng)
                    if sig is not None:
                        assert inst is not None, "fn with sig must return its last instruction"
                        inst.then_inc(sig.h, amt)
            return body

        for name in self.ENGS:
            if not per[name]:
                continue
            getattr(block, name)(make(name))

    KEEP = None


class Core:
    def __init__(self, role=0, dev=()):
        self.role = role
        self.dev = set(dev)
        self.chunks = ROLE_CHUNKS[role]
        self.npre = NCH
        self.pre = S
        self.nt = S // 128
        self.nc = bass.Bass("TRN2", target_bir_lowering=False, detect_race_conditions=False)
        self.dbg = {}

    def ap(self, name, shape, dt, kind="ExternalInput"):
        return self.nc.dram_tensor(name, list(shape), dt, kind=kind).ap()

    def carve_reset(self):
        self._cf = self._cb = self._cu = 0

    def cf(self, n):
        v = self.PF[:, self._cf:self._cf + n]
        self._cf += n
        assert self._cf <= 20480, f"PF overflow {self._cf}"
        return v

    def cb(self, n):
        v = self.PB[:, self._cb:self._cb + n]
        self._cb += n
        assert self._cb <= 46080, f"PB overflow {self._cb}"
        return v

    def cu(self, n):
        v = self.PU[:, self._cu:self._cu + n]
        self._cu += n
        assert self._cu <= 4096, f"PU overflow {self._cu}"
        return v

    def group_barrier(self):
        return [Tok(s, s.count, "_g_", True) for s in self.p.sems if s.count]

    def build(self):
        import contextlib
        nc = self.nc
        c = self
        pre = c.pre
        c.xpre = c.ap("xpre", [pre, D], F32)
        c.xown = c.ap("xown", [4 * CH, D], F32)
        c.wq = c.ap("wq", [D, D], F32)
        c.wkv = c.ap("wkv", [D, 2 * D], F32)
        c.wo = c.ap("wo", [D, D], F32)
        c.bqr = c.ap("bq", [1, D], F32)
        c.bkvr = c.ap("bkv", [1, 2 * D], F32)
        c.bor = c.ap("bo", [1, D], F32)
        c.qng = c.ap("qng", [128, 16], F32)
        c.qnbr = c.ap("qnb", [128, 16], F32)
        c.kvg = c.ap("kvg", [128, 16], F32)
        c.kvbr = c.ap("kvb", [128, 16], F32)
        c.lngr = c.ap("lng", [1, HD], F32)
        c.lnbr = c.ap("lnb", [1, HD], F32)
        c.gunib = c.ap("gunib", [D, 2 * I_FFN], U16)
        c.dnnib = c.ap("dnnib", [I_FFN, D], U16)
        c.gusc = c.ap("gusc", [16, 128, 2 * I_FFN], F16)
        c.dnsc = c.ap("dnsc", [44, 128, D], F16)
        c.cosdr = c.ap("cosd", [128, pre], F32)
        c.sindr = c.ap("sind", [128, pre], F32)
        c.cosdo = c.ap("cosdo", [128, 4 * CH], F32)
        c.sindo = c.ap("sindo", [128, 4 * CH], F32)
        c.maskv = c.ap("maskv", [4, 16, KT, CH], BF16)
        c.identin = c.ap("ident", [128, 128], BF16)
        c.ehsin = c.ap("ehs", [H, 128, H], BF16)
        c.y = c.ap("y", [4 * CH, D], F32, kind="ExternalOutput")
        c.z_tok = c.ap("z_tok", [pre, D], BF16, kind="Internal")
        c.zown = c.ap("zown", [4 * CH, D], BF16, kind="Internal")
        c.xn_tok = c.ap("xn_tok", [4 * CH, D], BF16, kind="Internal")
        c.epst = c.ap("epst", [64, 128], BF16, kind="Internal")
        c.epso = c.ap("epso", [16, 128], BF16, kind="Internal")
        c.rowq = c.ap("rowq", [16, 2 * CH], BF16, kind="Internal")
        c.rowk = c.ap("rowk", [16, 2 * CH], BF16, kind="Internal")
        c.wqP = c.ap("wqP", [D, D], BF16, kind="Internal")
        c.wkvP = c.ap("wkvP", [D, 2 * D], BF16, kind="Internal")
        c.woP = c.ap("woP", [D, D], BF16, kind="Internal")
        c.knT = c.ap("knT", [D, pre], BF16, kind="Internal")
        c.vtok = c.ap("vtok", [NCH * 4, H, 128, HD], BF16, kind="Internal")
        c.wguT = c.ap("wguT", [D, 2 * I_FFN], BF16, kind="Internal")
        c.wdT = c.ap("wdT", [I_FFN, D], BF16, kind="Internal")
        c.ffny = c.ap("ffny", [4 * CH, D], F32, kind="Internal")

        if "s1" in c.dev:
            c.dbg["d_xn"] = c.ap("d_xn", [4 * CH, D], BF16, kind="ExternalOutput")
            c.dbg["d_z"] = c.ap("d_z", [pre, D], BF16, kind="ExternalOutput")
            c.dbg["d_eps"] = c.ap("d_eps", [64, 128], BF16, kind="ExternalOutput")
        if "kv" in c.dev:
            c.dbg["d_kn"] = c.ap("d_kn", [D, pre], BF16, kind="ExternalOutput")
            c.dbg["d_v"] = c.ap("d_v", [pre, D], BF16, kind="ExternalOutput")
            c.dbg["d_mk"] = c.ap("d_mk", [1, 16], F32, kind="ExternalOutput")
        if "ffn" in c.dev:
            c.dbg["d_wgu"] = c.ap("d_wgu", [D, 2 * I_FFN], BF16, kind="ExternalOutput")
            c.dbg["d_wd"] = c.ap("d_wd", [I_FFN, D], BF16, kind="ExternalOutput")
            c.dbg["d_ffny"] = c.ap("d_ffny", [4 * CH, D], F32, kind="ExternalOutput")
        if "qn" in c.dev:
            c.dbg["d_qn"] = c.ap("d_qn", [D, 4 * CH], BF16, kind="ExternalOutput")

        with contextlib.ExitStack() as st:
            self._st = st
            E = st.enter_context
            c.sb = lambda name, shape, dt: E(nc.sbuf_tensor(name, list(shape), dt))
            c.ps = lambda name, shape, dt=F32: E(nc.psum_tensor(name, list(shape), dt))
            p = c.p = Prog(nc, st)
            c.s_const = p.sem("s_const")
            c.s_store = p.sem("s_store")
            c.s_act = p.sem("s_act")
            c.s_dve = p.sem("s_dve")
            c.s_pool = p.sem("s_pool")
            c.s_pe = p.sem("s_pe")

            c.identb = c.sb("identb", [128, 128], BF16)
            p.dma("sync", c.identb[:], c.identin, sig=c.s_const)
            c.ehs = c.sb("ehs_sb", [128, H * H], BF16)
            p.dma("sync", c.ehs[:].rearrange("p (h e) -> p h e", h=H),
                  c.ehsin.rearrange("h p e -> p h e"), sig=c.s_const)

            c.lneps = c.sb("lneps", [128, 1], F32)
            tm1 = p.op("vector", lambda e: e.memset(c.lneps[:], LN_EPS), sig=c.s_dve)
            c.rmseps = c.sb("rmseps", [128, 1], F32)
            tm2 = p.op("vector", lambda e: e.memset(c.rmseps[:], RMS_EPS), sig=c.s_dve)
            c.ones_row = c.sb("ones_row", [1, CH], BF16)
            tm3 = p.op("vector", lambda e: e.memset(c.ones_row[:], 1.0), sig=c.s_dve)
            c.onescol_bf = c.sb("onescol_bf", [128, 1], BF16)
            tm4 = p.op("vector", lambda e: e.memset(c.onescol_bf[:], 1.0), sig=c.s_dve)
            c.t_const = [Tok(c.s_const, c.s_const.count, "_c_", True), tm1, tm2, tm3, tm4]

            c.PF = c.sb("poolF", [128, 20480], F32)
            c.PB = c.sb("poolB", [128, 46080], BF16)
            c.PU = c.sb("poolU", [128, 4096], U16)
            c._cf = c._cb = c._cu = 0

            c.pb = [c.ps(f"pb{i}", [128, CH], F32) for i in range(6)]
            c.pstat = c.ps("pstat", [128, CH], F32)
            c.ptr = c.ps("ptr", [128, 128], BF16)
            c.pb_use = [[] for _ in range(6)]
            c.pstat_use = []
            c.ptr_use = []

            c.carve_reset()
            phase_s1(c)
            need_kv = not c.dev or "kv" in c.dev or "attn" in c.dev or "full" in c.dev
            need_ffn = not c.dev or "ffn" in c.dev or "full" in c.dev
            need_attn = not c.dev or "attn" in c.dev or "full" in c.dev
            if need_kv:
                phase_wprep(c)
            if need_ffn:
                phase_dequant(c)
            p.gate_all(c.group_barrier())
            c.carve_reset()
            if need_kv:
                phase_kv(c)
                p.gate_all(c.group_barrier())
                c.carve_reset()
            if need_ffn:
                phase_ffn(c)
                p.gate_all(c.group_barrier())
                c.carve_reset()
            if need_attn:
                phase_attn(c)

            p.final_barrier("sync")
            blk = E(nc.Block())
            p.emit(blk)
        return self.nc


def phase_s1(c):
    p, sb = c.p, c.sb
    nt = min(c.nt, getattr(c, "S1N", 10 ** 9))
    C_EPS128 = 128.0 * LN_EPS / (SCALE2 * D)

    r_x = p.ring("rx", 3)
    r_z = p.ring_slots("rz", 2)
    r_xn = p.ring_slots("rxn", 2)
    s_eps = p.sem("s_eps")

    xb = [c.cf(D) for _ in range(3)]
    junk = c.cf(D)
    xnb = [c.cb(D) for _ in range(2)]
    zb = [c.cb(D) for _ in range(2)]
    cols = [c.cf(16) for _ in range(2)]
    epscols = c.cb(64)
    epscols2 = c.cb(16)
    eps_ps = c.ptr
    eps_sb = c.cb(128)

    st = dict(last_use=[None, None, None], xn_done=[None, None], z_done=[None, None],
              col_done=[None, None], t_ec=None)

    def tile(i, src_rows, zdst, ecol, ecol_i, xndst):
        bi = i % 3
        ci = i % 2
        zi = i % 2
        cc_ = cols[ci]
        xt = xb[bi]
        tload = p.dma("sync", xt[:], src_rows, reads=[st["last_use"][bi]], sig=r_x[bi])
        t1 = p.op("scalar", lambda e, xt=xt, cc=cc_: e.activation(
            out=junk[:], in_=xt[:], func=AF.Copy, accum_out=cc[:, 0:1]),
            reads=[tload, st["col_done"][ci]], sig=c.s_act)
        t2 = p.op("scalar", lambda e, xt=xt, cc=cc_: e.activation(
            out=junk[:], in_=xt[:], func=AF.Square, accum_out=cc[:, 1:2]),
            reads=[tload], sig=c.s_act)

        def sm(e, cc=cc_):
            e.tensor_scalar(out=cc[:, 2:3], in0=cc[:, 0:1], scalar1=1.0 / D,
                            scalar2=None, op0=ALU.mult)
            e.tensor_scalar(out=cc[:, 3:4], in0=cc[:, 1:2], scalar1=1.0 / D,
                            scalar2=None, op0=ALU.mult)
            e.drain()
            e.scalar_tensor_tensor(out=cc[:, 4:5], in0=cc[:, 2:3], scalar=-1.0,
                                   in1=cc[:, 2:3], op0=ALU.mult, op1=ALU.mult)
            e.drain()
            return e.tensor_tensor(out=cc[:, 4:5], in0=cc[:, 3:4], in1=cc[:, 4:5],
                                   op=ALU.add)
        tsm = p.op("vector", sm, reads=[t1, t2, st["col_done"][ci]], sig=c.s_dve)
        tln = p.op("scalar", lambda e, cc=cc_: e.activation(
            out=cc[:, 5:6], in_=cc[:, 4:5], func=AF.Ln, bias=c.lneps[:]),
            reads=[tsm, c.t_const], sig=c.s_act)
        p.drain("scalar")
        trs = p.op("scalar", lambda e, cc=cc_: e.activation(
            out=cc[:, 5:6], in_=cc[:, 5:6], func=AF.Exp, scale=-0.5), sig=c.s_act)

        def sm2(e, cc=cc_):
            e.tensor_tensor(out=cc[:, 6:7], in0=cc[:, 5:6], in1=cc[:, 5:6], op=ALU.mult)
            e.drain()
            return e.tensor_tensor(out=cc[:, 6:7], in0=cc[:, 6:7], in1=cc[:, 4:5],
                                   op=ALU.mult)
        tsm2 = p.op("vector", sm2, reads=[trs], sig=c.s_dve)
        tln2 = p.op("scalar", lambda e, cc=cc_: e.activation(
            out=cc[:, 7:8], in_=cc[:, 6:7], func=AF.Ln, bias=c.lneps[:]),
            reads=[tsm2], sig=c.s_act)
        p.drain("scalar")
        tr2 = p.op("scalar", lambda e, cc=cc_: e.activation(
            out=cc[:, 7:8], in_=cc[:, 7:8], func=AF.Exp, scale=-0.5), sig=c.s_act)

        def sm3(e, cc=cc_, ecol=ecol, i_=ecol_i):
            e.tensor_tensor(out=cc[:, 8:9], in0=cc[:, 5:6], in1=cc[:, 7:8], op=ALU.mult)
            e.scalar_tensor_tensor(out=cc[:, 9:10], in0=cc[:, 2:3], scalar=-1.0,
                                   in1=cc[:, 5:6], op0=ALU.mult, op1=ALU.mult)
            e.reciprocal(out=cc[:, 11:12], in_=cc[:, 6:7])
            e.drain()
            e.scalar_tensor_tensor(out=cc[:, 10:11], in0=cc[:, 2:3], scalar=-1.0,
                                   in1=cc[:, 8:9], op0=ALU.mult, op1=ALU.mult)
            return e.tensor_scalar(out=ecol[:, i_:i_ + 1], in0=cc[:, 11:12],
                                   scalar1=C_EPS128, scalar2=None, op0=ALU.mult)
        tsm3 = p.op("vector", sm3, reads=[tr2], sig=c.s_dve)
        st["t_ec"] = tsm3

        tz = p.op("gpsimd", lambda e, xt=xt, cc=cc_, zo=zb[zi]: e.tensor_scalar(
            out=zo[:], in0=xt[:], scalar1=cc[:, 8:9], scalar2=cc[:, 10:11],
            op0=ALU.mult, op1=ALU.add), reads=[tsm3, st["z_done"][zi], tload],
            sig=c.s_pool)
        tzs = p.dma("gpsimd", zdst, zb[zi][:], reads=[tz], sig=r_z[zi])
        st["z_done"][zi] = tzs
        uses = [tz]
        if xndst is not None:
            p.drain("vector")
            txn = p.op("vector", lambda e, xt=xt, cc=cc_, xo=xnb[zi]: e.tensor_scalar(
                out=xo[:], in0=xt[:], scalar1=cc[:, 5:6], scalar2=cc[:, 9:10],
                op0=ALU.mult, op1=ALU.add), reads=[tsm3, st["xn_done"][zi], tload],
                sig=c.s_dve)
            txs = p.dma("gpsimd", xndst, xnb[zi][:], reads=[txn], sig=r_xn[zi])
            st["xn_done"][zi] = txs
            uses.append(txn)
        st["last_use"][bi] = uses
        st["col_done"][ci] = uses

    for i in range(nt):
        tile(i, c.xpre[i * 128:(i + 1) * 128, :],
             c.z_tok[i * 128:(i + 1) * 128, :], epscols, i, None)
    t_ec_pre = st["t_ec"]
    nt2 = 16
    for i in range(nt2):
        tile(i, c.xown[i * 128:(i + 1) * 128, :],
             c.zown[i * 128:(i + 1) * 128, :], epscols2, i,
             c.xn_tok[i * 128:(i + 1) * 128, :])

    tec = p.op("tensor", lambda e: e.transpose(
        out=eps_ps[0:nt, 0:128], in_=epscols[:, 0:nt], identity=c.identb[:]),
        reads=[c.t_const, t_ec_pre, st["t_ec"]], sig=c.s_pe)
    tev = p.op("vector", lambda e: e.tensor_copy(out=eps_sb[0:nt, :],
                                                 in_=eps_ps[0:nt, 0:128]),
               reads=[tec], sig=c.s_dve)
    tes = p.dma("gpsimd", c.epst[0:nt, :], eps_sb[0:nt, :], reads=[tev], sig=s_eps)
    tec2 = p.op("tensor", lambda e: e.transpose(
        out=eps_ps[0:nt2, 0:128], in_=epscols2[:, 0:nt2], identity=c.identb[:]),
        reads=[st["t_ec"], tev], sig=c.s_pe)
    tev2 = p.op("vector", lambda e: e.tensor_copy(out=eps_sb[0:nt2, :],
                                                  in_=eps_ps[0:nt2, 0:128]),
               reads=[tec2, tes], sig=c.s_dve)
    tes2 = p.dma("gpsimd", c.epso[0:nt2, :], eps_sb[0:nt2, :], reads=[tev2], sig=s_eps)
    c.ptr_use = [tev2]
    c.t_eps = Tok(s_eps, s_eps.count, "_e_", True)
    c.t_epso = c.t_eps
    c.t_z_done = [t for t in st["z_done"] if t is not None]
    c.t_xn_done = [t for t in st["xn_done"] if t is not None]

    if "d_xn" in c.dbg:
        p.dma("gpsimd", c.dbg["d_xn"], c.xn_tok, reads=c.t_xn_done, sig=c.s_store)
        p.dma("gpsimd", c.dbg["d_z"], c.z_tok, reads=c.t_z_done, sig=c.s_store)
        p.dma("gpsimd", c.dbg["d_eps"][0:nt, :], c.epst[0:nt, :], reads=[tes],
              sig=c.s_store)


def phase_wprep(c):
    p, sb = c.p, c.sb

    s_v = p.sem("s_wv")
    gq = c.cf(16)
    gkv = c.cf(16)
    bqn = c.cf(16)
    bkvn = c.cf(16)
    t1 = p.dma("sync", gq[:], c.qng, sig=s_v)
    t2 = p.dma("sync", gkv[:], c.kvg, sig=s_v)
    t3 = p.dma("sync", bqn[:], c.qnbr, sig=s_v)
    t4 = p.dma("sync", bkvn[:], c.kvbr, sig=s_v)
    s_r = p.sem("s_wr")
    rows_fv = c.cf(4 * D)
    rows_f = rows_fv[0:1, :]
    t5 = p.dma("sync", rows_f[:, 0:D], c.bqr, sig=s_r)
    t6 = p.dma("sync", rows_f[:, D:3 * D], c.bkvr, sig=s_r)
    t7 = p.dma("sync", rows_f[:, 3 * D:4 * D], c.bor, sig=s_r)
    lrv = c.cf(2 * HD)
    lr = lrv[0:1, :]
    t8 = p.dma("sync", lr[:, 0:HD], c.lngr, sig=s_r)
    t9 = p.dma("sync", lr[:, HD:2 * HD], c.lnbr, sig=s_r)
    tv = [t1, t2, t3, t4]
    tr = [t5, t6, t7, t8, t9]

    bqn_bf = c.cb(16)
    bkvn_bf = c.cb(16)
    tb1 = p.op("vector", lambda e: e.tensor_copy(out=bqn_bf[:], in_=bqn[:]),
               reads=[tv], sig=c.s_dve)
    tb2 = p.op("vector", lambda e: e.tensor_copy(out=bkvn_bf[:], in_=bkvn[:]),
               reads=[tv], sig=c.s_dve)
    c.lrow = sb("lrow", [1, 4 * HD], BF16)
    def mkrows(e):
        e.tensor_copy(out=c.lrow[:, 0:2 * HD], in_=lr[:, 0:2 * HD])
        return e.tensor_scalar(out=c.lrow[:, 2 * HD:4 * HD], in0=lr[:, 0:2 * HD],
                               scalar1=float(SCALE2), scalar2=None, op0=ALU.mult)
    tlr = p.op("vector", mkrows, reads=[tr], sig=c.s_dve)
    c.t_lrow = tlr

    c.bo_bf = sb("bo_bf", [1, D], BF16)
    tbo = p.op("vector", lambda e: e.tensor_copy(out=c.bo_bf[:], in_=rows_f[:, 3 * D:4 * D]),
               reads=[tr], sig=c.s_dve)
    c.t_bo = tbo

    r_w = p.ring("rw", 4)
    wbuf = [c.cf(CH) for _ in range(4)]
    wub = [c.cb(CH) for _ in range(2)]
    wsb = [c.cb(CH) for _ in range(2)]
    r_ws = p.ring_slots("rws", 2)

    c.browq_bf = sb("browq_bf", [1, D], BF16)
    c.browkv_bf = sb("browkv_bf", [1, 2 * D], BF16)

    brow_ps = c.pstat
    specs = [
        ("wq", c.wq, c.wqP, 4, gq, bqn_bf, c.browq_bf, 0),
        ("wkv", c.wkv, c.wkvP, 8, gkv, bkvn_bf, c.browkv_bf, D),
        ("wo", c.wo, c.woP, 4, None, None, None, 3 * D),
    ]
    wu_done = [None, None]
    ws_done = [None, None]
    wl_done = [None, None, None, None]
    t_brow_es = []
    for (name, wsrc, wdst, jmax, gcol, bcol, browdst, boff) in specs:
        for j in range(jmax):
            mm_toks = []
            for K in range(16):
                i4 = (j * 16 + K) % 4
                i2 = (j * 16 + K) % 2
                tl = p.dma("sync", wbuf[i4][:], wsrc[K * 128:(K + 1) * 128, j * CH:(j + 1) * CH],
                           reads=[wl_done[i4]], sig=r_w[i4])
                tc = p.op("vector", lambda e, a=wub[i2], b=wbuf[i4]: e.tensor_copy(
                    out=a[:], in_=b[:]), reads=[tl, wu_done[i2]], sig=c.s_dve)
                uses = [tc]
                if gcol is not None:
                    tmm = p.op("tensor", lambda e, K=K, i2=i2, b=bcol: e.matmul(
                        out=brow_ps[0:1, 0:CH], lhsT=b[:, K:K + 1], rhs=wub[i2][:],
                        start=(K == 0), stop=(K == 15)),
                        reads=[tc, tb1, tb2] + (c.pstat_use if K == 0 else []),
                        sig=c.s_pe)
                    mm_toks.append(tmm)
                    tsc = p.op("vector", lambda e, K=K, i2=i2, g=gcol: e.tensor_scalar(
                        out=wsb[i2][:], in0=wub[i2][:], scalar1=g[:, K:K + 1],
                        scalar2=None, op0=ALU.mult), reads=[tv, ws_done[i2]], sig=c.s_dve)
                    tst = p.dma("gpsimd", wdst[K * 128:(K + 1) * 128, j * CH:(j + 1) * CH],
                                wsb[i2][:], reads=[tsc], sig=r_ws[i2])
                    ws_done[i2] = tst
                    uses.append(tsc)
                    wu_done[i2] = uses
                else:
                    tst = p.dma("gpsimd", wdst[K * 128:(K + 1) * 128, j * CH:(j + 1) * CH],
                                wub[i2][:], reads=[tc], sig=r_ws[i2])
                    wu_done[i2] = [tst]
                wl_done[i4] = uses if gcol is None else uses
                if gcol is None:
                    wl_done[i4] = [tc]
            if gcol is not None:
                c.pstat_use = []
                tadd = p.op("vector", lambda e, j=j, boff=boff, browdst=browdst: e.tensor_tensor(
                    out=browdst[0:1, j * CH:(j + 1) * CH], in0=brow_ps[0:1, 0:CH],
                    in1=rows_f[0:1, boff + j * CH:boff + (j + 1) * CH], op=ALU.add),
                    reads=[mm_toks[-1], tr], sig=c.s_dve)
                c.pstat_use.append(tadd)
                t_brow_es.append(tadd)
    c.t_brow = t_brow_es
    c.t_wqP = ws_done + wu_done
    c.t_wprep_end = [Tok(s, s.count, "_w_", True) for s in (r_ws[0], r_ws[1])]


def phase_kv(c):
    p, sb = c.p, c.sb
    nch = min(c.npre, getattr(c, "KVN", 10 ** 9))

    r_zt = p.ring("rzt", 2)
    r_wk = p.ring("rwk", 6)
    r_cs = p.ring("rcs", 2)
    r_eps = p.ring("reps", 2)
    r_row = p.ring("rrow", 2)
    r_kst = p.ring_slots("rkst", 2)
    r_vst = p.ring_slots("rvst", 2)
    r_sw = p.ring("rsw", 2)
    s_rb = p.sem("s_rowb")

    zT = [[c.cb(CH) for K in range(16)] for r in range(2)]
    wk4 = [c.cb(CH) for _ in range(6)]
    cosc = [c.cf(CH) for _ in range(2)]
    sinc = [c.cf(CH) for _ in range(2)]
    epsr = [c.cb(CH)[0:1, :] for _ in range(2)]
    krot = [c.cf(CH) for _ in range(2)]
    krot_sw = [c.cf(CH) for _ in range(2)]
    p1 = [c.cf(CH) for _ in range(2)]
    p2 = [c.cf(CH) for _ in range(2)]
    rotb = [c.cb(CH) for _ in range(16)]
    rot2 = [c.cb(CH) for _ in range(2)]
    kn2 = [c.cb(CH) for _ in range(2)]
    stats = c.cf(5 * CH)[0:16, :]
    statb = c.cb(2 * CH)[0:16, :]
    rowsb = c.cf(8 * CH).bitcast(BF16)[0:1, :]
    knsb = [c.cb(CH) for _ in range(2)]
    tmpsb = [c.cb(CH) for _ in range(2)]
    vsb = [c.cb(CH) for _ in range(2)]
    maxcol = c.cf(2)[0:16, 0:1]
    c.maxk2 = sb("maxk2", [16, 1], F32)

    zt_done = [None, None]
    wk_done = [None] * 6
    cs_done = [None, None]
    eps_done = [None, None]
    row_done = [None, None]
    kn_done = [None, None]
    tmp_done = [None, None]
    v_done = [None, None]
    stats_done = None
    statb_done = None
    rot_done = [None] * 16
    rot2_done = [None, None]
    kn2_done = [None, None]
    krot_done = [None, None]
    p12_done = [None, None]

    tmk0 = p.op("vector", lambda e: e.memset(c.maxk2[:], 0.0), sig=c.s_dve)
    maxk_done = tmk0

    s1v = c.pstat[0:16, :]
    s2v = c.pb[4][0:16, :]
    knv = c.pb[5][0:16, :]

    for cc in range(nch):
        ri = cc % 2
        tzt = None
        for K in range(16):
            tzt = p.dmat("scalar", zT[ri][K][:],
                         c.z_tok[cc * CH:(cc + 1) * CH, K * 128:(K + 1) * 128],
                         reads=[zt_done[ri], c.t_z_done], sig=r_zt[ri])
        tcs = p.dma("sync", cosc[ri][:], c.cosdr[:, cc * CH:(cc + 1) * CH],
                    reads=[cs_done[ri]], sig=r_cs[ri])
        tsn = p.dma("sync", sinc[ri][:], c.sindr[:, cc * CH:(cc + 1) * CH],
                    reads=[cs_done[ri]], sig=r_cs[ri])
        tep = p.dma("sync", epsr[ri][:],
                    c.epst.rearrange("a b -> (a b)")[cc * CH:(cc + 1) * CH].unsqueeze(0),
                    reads=[eps_done[ri], c.t_eps], sig=r_eps[ri])

        stat_last = [None, None]
        for hg in range(4):
            mmg = [None] * 4
            for K in range(16):
                wi = K % 6
                tw = p.dma("sync", wk4[wi][:],
                           c.wkvP[K * 128:(K + 1) * 128, hg * CH:(hg + 1) * CH],
                           reads=[wk_done[wi], c.t_wprep_end], sig=r_wk[wi])
                for hi in range(4):
                    mmg[hi] = p.op("tensor", lambda e, K=K, wi=wi, hi=hi, ri=ri: e.matmul(
                        out=c.pb[hi][:], lhsT=wk4[wi][:, hi * 128:(hi + 1) * 128],
                        rhs=zT[ri][K][:], start=(K == 0), stop=False),
                        reads=[tw, tzt] + (c.pb_use[hi] if K == 0 else []), sig=c.s_pe)
                wk_done[wi] = mmg[3]
            for hi in range(4):
                h = hg * 4 + hi
                pbi = h % 2
                c.pb_use[hi] = []
                mmb = p.op("tensor", lambda e, h=h, hi=hi: e.matmul(
                    out=c.pb[hi][:], lhsT=c.browkv_bf[0:1, h * 128:(h + 1) * 128],
                    rhs=c.ones_row[:], start=False, stop=True),
                    reads=[mmg[hi], c.t_brow, c.t_const], sig=c.s_pe)
                tev = p.op("scalar", lambda e, hi=hi, ki=pbi: e.activation(
                    out=krot[ki][:], in_=c.pb[hi][:], func=AF.Copy),
                    reads=[mmb, krot_done[pbi]], sig=c.s_act)
                c.pb_use[hi].append(tev)
                tw1 = p.dma("gpsimd", krot_sw[pbi][0:64, :], krot[pbi][64:128, :],
                            reads=[tev, krot_done[pbi]], sig=r_sw[pbi])
                tw2 = p.dma("gpsimd", krot_sw[pbi][64:128, :], krot[pbi][0:64, :],
                            sig=r_sw[pbi])
                tp1 = p.op("gpsimd", lambda e, ri=ri, ki=pbi: e.tensor_mul(
                    out=p1[ki][:], in0=krot[ki][:], in1=cosc[ri][:]),
                    reads=[tev, tsn, p12_done[pbi]], sig=c.s_pool)
                tp2 = p.op("gpsimd", lambda e, ri=ri, ki=pbi: e.tensor_mul(
                    out=p2[ki][:], in0=krot_sw[ki][:], in1=sinc[ri][:]),
                    reads=[tw2, tsn], sig=c.s_pool)
                krot_done[pbi] = [tp1, tp2]
                tcb = p.op("vector", lambda e, h=h, ki=pbi: e.tensor_add(
                    out=rotb[h][:], in0=p1[ki][:], in1=p2[ki][:]),
                    reads=[tp1, tp2, rot_done[h]], sig=c.s_dve)
                p12_done[pbi] = tcb
                rot_done[h] = [tcb]
        for h in range(16):
            ki = h % 2
            tsq = p.op("scalar", lambda e, h=h, ki=ki: e.activation(
                out=rot2[ki][:], in_=rotb[h][:], func=AF.Square),
                reads=[rot_done[h], rot2_done[ki]], sig=c.s_act)
            ts1 = p.op("tensor", lambda e, h=h: e.matmul(
                out=s1v, lhsT=c.ehs[:, h * 16:(h + 1) * 16], rhs=rotb[h][:],
                start=(h == 0), stop=(h == 15)),
                reads=[rot_done[h], c.t_const] + (c.pstat_use if h == 0 else []),
                sig=c.s_pe)
            ts2 = p.op("tensor", lambda e, h=h, ki=ki: e.matmul(
                out=s2v, lhsT=c.ehs[:, h * 16:(h + 1) * 16], rhs=rot2[ki][:],
                start=(h == 0), stop=False),
                reads=[tsq] + (c.pb_use[4] if h == 0 else []), sig=c.s_pe)
            rot2_done[ki] = ts2
            stat_last = [ts1, ts2]
        tse = p.op("tensor", lambda e, ri=ri: e.matmul(
            out=s2v, lhsT=c.ones_row[0:1, 0:16], rhs=epsr[ri][:],
            start=False, stop=True), reads=[tep, stat_last[1], c.t_const], sig=c.s_pe)
        eps_done[ri] = tse

        def smk(e):
            e.tensor_scalar(out=stats[:, 0:CH], in0=s1v, scalar1=1.0 / 128,
                            scalar2=None, op0=ALU.mult)
            e.tensor_scalar(out=stats[:, CH:2 * CH], in0=s2v, scalar1=1.0 / 128,
                            scalar2=None, op0=ALU.mult)
            e.drain()
            e.scalar_tensor_tensor(out=stats[:, 2 * CH:3 * CH], in0=stats[:, 0:CH],
                                   scalar=-1.0, in1=stats[:, 0:CH],
                                   op0=ALU.mult, op1=ALU.mult)
            e.drain()
            return e.tensor_tensor(out=stats[:, 2 * CH:3 * CH],
                                   in0=stats[:, CH:2 * CH],
                                   in1=stats[:, 2 * CH:3 * CH], op=ALU.add)
        tsk = p.op("vector", smk, reads=[tse, stat_last[0], stats_done], sig=c.s_dve)
        tl1 = p.op("scalar", lambda e: e.activation(
            out=stats[:, 3 * CH:4 * CH], in_=stats[:, 2 * CH:3 * CH], func=AF.Ln),
            reads=[tsk], sig=c.s_act)
        p.drain("scalar")
        tl2 = p.op("scalar", lambda e: e.activation(
            out=stats[:, 3 * CH:4 * CH], in_=stats[:, 3 * CH:4 * CH], func=AF.Exp,
            scale=-0.5), sig=c.s_act)

        def smk2(e):
            e.scalar_tensor_tensor(out=stats[:, 4 * CH:5 * CH], in0=stats[:, 0:CH],
                                   scalar=-1.0, in1=stats[:, 3 * CH:4 * CH],
                                   op0=ALU.mult, op1=ALU.mult)
            e.tensor_copy(out=statb[:, 0:CH], in_=stats[:, 3 * CH:4 * CH])
            e.drain()
            return e.tensor_copy(out=statb[:, CH:2 * CH], in_=stats[:, 4 * CH:5 * CH])
        tsk2 = p.op("vector", smk2, reads=[tl2, statb_done], sig=c.s_dve)
        stats_done = tsk2
        c.pstat_use = [tsk]
        c.pb_use[4] = [tsk]

        trb = p.dma("gpsimd", c.rowk, statb[:],
                    reads=[tsk2, getattr(c, "_rowk_read", [])], sig=s_rb)
        statb_done = trb

        for vd in range(4):
            vps = [c.pb[t] for t in range(4)]
            mm_last = [None] * 4
            for K in range(16):
                wi = K % 6
                tw = p.dma("sync", wk4[wi][:],
                           c.wkvP[K * 128:(K + 1) * 128, 2048 + vd * CH:2048 + (vd + 1) * CH],
                           reads=[wk_done[wi], c.t_wprep_end], sig=r_wk[wi])
                for t4 in range(4):
                    mm_last[t4] = p.op("tensor", lambda e, K=K, wi=wi, t4=t4, ri=ri, vv=vps[t4]: e.matmul(
                        out=vv[:], lhsT=zT[ri][K][:, t4 * 128:(t4 + 1) * 128],
                        rhs=wk4[wi][:], start=(K == 0), stop=False),
                        reads=[tw, tzt] + (c.pb_use[t4] if K == 0 else []), sig=c.s_pe)
                wk_done[wi] = mm_last[3]
            for t4 in range(4):
                c.pb_use[t4] = []
                mmb = p.op("tensor", lambda e, vd=vd, vv=vps[t4]: e.matmul(
                    out=vv[:], lhsT=c.ones_row[0:1, 0:128],
                    rhs=c.browkv_bf[0:1, 2048 + vd * CH:2048 + (vd + 1) * CH],
                    start=False, stop=True), reads=[mm_last[t4], c.t_brow], sig=c.s_pe)
                si = t4 % 2
                tev = p.op("scalar", lambda e, si=si, vv=vps[t4]: e.activation(
                    out=vsb[si][:], in_=vv[:], func=AF.Copy),
                    reads=[mmb, v_done[si]], sig=c.s_act)
                c.pb_use[t4].append(tev)
                kt = cc * 4 + t4
                tvs = p.dma("gpsimd",
                            c.vtok[kt, vd * 4:(vd + 1) * 4].rearrange("h t d -> t h d"),
                            vsb[si][:].rearrange("t (h d) -> t h d", h=4),
                            reads=[tev], sig=r_vst[si])
                v_done[si] = tvs
        rowk_reads = []
        row_users = []
        for h in range(16):
            si = h % 2
            pc = c.pb[2 * si]
            pd = c.pb[2 * si + 1]
            if h % 8 == 0:
                rflat = c.rowk.rearrange("a b -> (a b)")
                trh = p.dma("sync", rowsb[:],
                            rflat[h * 2 * CH:(h + 8) * 2 * CH].unsqueeze(0),
                            reads=[trb, row_done[0]], sig=r_row[0])
                rowk_reads.append(trh)
                row_done[0] = []
            ho = (h % 8) * 2 * CH
            tmc = p.op("tensor", lambda e, ho=ho, pc=pc: e.matmul(
                out=pc[:], lhsT=c.lrow[0:1, 2 * HD:3 * HD],
                rhs=rowsb[0:1, ho:ho + CH], start=True, stop=True),
                reads=[trh, c.t_lrow] + c.pb_use[2 * si], sig=c.s_pe)
            tmd1 = p.op("tensor", lambda e, ho=ho, pd=pd: e.matmul(
                out=pd[:], lhsT=c.lrow[0:1, 2 * HD:3 * HD],
                rhs=rowsb[0:1, ho + CH:ho + 2 * CH], start=True, stop=False),
                reads=[trh] + c.pb_use[2 * si + 1], sig=c.s_pe)
            tmd2 = p.op("tensor", lambda e, pd=pd: e.matmul(
                out=pd[:], lhsT=c.lrow[0:1, 3 * HD:4 * HD], rhs=c.ones_row[:],
                start=False, stop=True), sig=c.s_pe)
            row_users += [tmc, tmd1]
            row_done[0] = row_done[0] + [tmc, tmd1] if isinstance(row_done[0], list) else [tmc, tmd1]
            c.pb_use[2 * si] = []
            c.pb_use[2 * si + 1] = []
            tt1 = p.op("vector", lambda e, h=h, si=si, pc=pc: e.tensor_tensor(
                out=tmpsb[si][:], in0=rotb[h][:], in1=pc[:], op=ALU.mult),
                reads=[tmc, tmp_done[si]], sig=c.s_dve)
            tt2 = p.op("vector", lambda e, si=si, pd=pd: e.tensor_tensor(
                out=knsb[si][:], in0=tmpsb[si][:], in1=pd[:], op=ALU.add),
                reads=[tmd2, kn_done[si]], sig=c.s_dve)
            rot_done[h] = [tt1]
            c.pb_use[2 * si].append(tt1)
            c.pb_use[2 * si + 1].append(tt2)
            tmp_done[si] = tt2
            tsq2 = p.op("scalar", lambda e, si=si: e.activation(
                out=kn2[si][:], in_=knsb[si][:], func=AF.Square),
                reads=[tt2, kn2_done[si]], sig=c.s_act)
            tkn = p.op("tensor", lambda e, h=h, si=si: e.matmul(
                out=knv, lhsT=c.ehs[:, h * 16:(h + 1) * 16], rhs=kn2[si][:],
                start=(h == 0), stop=(h == 15)),
                reads=[tsq2] + (c.pb_use[5] if h == 0 else []), sig=c.s_pe)
            kn2_done[si] = tkn
            tst = p.dma("gpsimd", c.knT[h * 128:(h + 1) * 128, cc * CH:(cc + 1) * CH],
                        knsb[si][:], reads=[tt2], sig=r_kst[si])
            kn_done[si] = [tst, tsq2]
        c._rowk_read = rowk_reads
        tmx = p.op("vector", lambda e: e.reduce_max(out=maxcol[:], in_=knv, axis=AX.X),
                   reads=[kn2_done[0], kn2_done[1], maxk_done], sig=c.s_dve)
        p.drain("vector")
        tmx2 = p.op("vector", lambda e: e.tensor_max(
            out=c.maxk2[:], in0=c.maxk2[:], in1=maxcol[:]), sig=c.s_dve)
        maxk_done = tmx2
        c.pb_use[5] = [tmx]

        zt_done[ri] = [mm_last[t] for t in range(4)]
        cs_done[ri] = krot_done[:]

    c.t_kv_end = [Tok(s, s.count, "_kv_", True)
                  for s in (r_kst[0], r_kst[1], r_vst[0], r_vst[1])]
    c.t_maxk = maxk_done

    if "d_kn" in c.dbg:
        p.dma("gpsimd", c.dbg["d_kn"][:, 0:nch * CH], c.knT[:, 0:nch * CH],
              reads=[c.t_kv_end], sig=c.s_store)
        p.dma("gpsimd", c.dbg["d_mk"].rearrange("o s -> s o"), c.maxk2[:],
              reads=[maxk_done], sig=c.s_store)


def phase_dequant(c):
    p = c.p
    r_nb = p.ring("rnb", 2)
    r_sc = p.ring("rsc", 2)
    r_wo = p.ring_slots("rwo", 2)
    nib = [c.cu(CH) for _ in range(2)]
    t2b = [c.cu(CH) for _ in range(2)]
    bits = [c.cu(CH) for _ in range(2)]
    scf = [c.cf(CH) for _ in range(2)]
    wout = [c.cb(CH) for _ in range(2)]
    sc16 = [c.sb(f"sc16_{i}", [128, CH], F16) for i in range(2)]

    nib_done = [None, None]
    t2_done = [None, None]
    bits_done = [None, None]
    sc_done = [None, None]
    scf_done = [None, None]
    w_done = [None, None]

    jobs = []
    for g in range(16):
        for j in range(2 * I_FFN // CH):
            jobs.append(("gu", g, j))
    for g in range(44):
        for j in range(D // CH):
            jobs.append(("dn", g, j))
    if getattr(c, "DQN", None):
        jobs = jobs[:c.DQN]

    for idx, (kind, g, j) in enumerate(jobs):
        si = idx % 2
        if kind == "gu":
            nsrc = c.gunib[g * 128:(g + 1) * 128, j * CH:(j + 1) * CH]
            ssrc = c.gusc[g, :, j * CH:(j + 1) * CH]
            wdst = c.wguT[g * 128:(g + 1) * 128, j * CH:(j + 1) * CH]
        else:
            nsrc = c.dnnib[g * 128:(g + 1) * 128, j * CH:(j + 1) * CH]
            ssrc = c.dnsc[g, :, j * CH:(j + 1) * CH]
            wdst = c.wdT[g * 128:(g + 1) * 128, j * CH:(j + 1) * CH]
        tn = p.dma("sync", nib[si][:], nsrc, reads=[nib_done[si]], sig=r_nb[si])
        tsc = p.dma("sync", sc16[si][:], ssrc, reads=[sc_done[si]], sig=r_sc[si])
        tt2 = p.op("vector", lambda e, si=si: e.tensor_scalar(
            out=t2b[si][:], in0=nib[si][:], scalar1=8, scalar2=32256,
            op0=ALU.is_ge, op1=ALU.mult), reads=[tn, t2_done[si]], sig=c.s_dve)
        tbi = p.op("vector", lambda e, si=si: e.scalar_tensor_tensor(
            out=bits[si][:], in0=nib[si][:], scalar=64.0, in1=t2b[si][:],
            op0=ALU.mult, op1=ALU.add), reads=[tt2, bits_done[si]], sig=c.s_dve)
        nib_done[si] = tbi
        t2_done[si] = tbi
        tsf = p.op("vector", lambda e, si=si: e.tensor_scalar(
            out=scf[si][:], in0=sc16[si][:], scalar1=float(2.0 ** 126), scalar2=None,
            op0=ALU.mult), reads=[tsc, scf_done[si]], sig=c.s_dve)
        sc_done[si] = tsf
        tw = p.op("vector", lambda e, si=si: e.tensor_tensor(
            out=wout[si][:], in0=bits[si][:].bitcast(BF16)[:, 0:CH], in1=scf[si][:],
            op=ALU.mult), reads=[tbi, tsf, w_done[si]], sig=c.s_dve)
        bits_done[si] = tw
        scf_done[si] = tw
        tst = p.dma("gpsimd", wdst, wout[si][:], reads=[tw], sig=r_wo[si])
        w_done[si] = tst
    c.t_deq_end = [Tok(s, s.count, "_dq_", True) for s in (r_wo[0], r_wo[1])]
    if "d_wgu" in c.dbg:
        p.dma("gpsimd", c.dbg["d_wgu"], c.wguT, reads=[c.t_deq_end], sig=c.s_store)
        p.dma("gpsimd", c.dbg["d_wd"], c.wdT, reads=[c.t_deq_end], sig=c.s_store)


def phase_ffn(c):
    p = c.p
    noc = min(4, getattr(c, "FFNN", 4))

    r_xt = p.ring("rxt", 2)
    r_wg = p.ring("rwg", 6)
    r_wd = p.ring("rwd", 6)
    r_ys = p.ring_slots("rys", 2)

    xnT = [c.cb(CH) for _ in range(16)]
    act = [c.cb(CH) for _ in range(44)]
    sg = [c.cb(CH) for _ in range(4)]
    upb = [c.cb(CH) for _ in range(4)]
    wg = [c.cb(CH) for _ in range(6)]
    wd = [c.cb(CH) for _ in range(6)]
    ysb = [c.cf(CH) for _ in range(2)]
    sqjunk = c.cf(CH)
    sqc = c.cf(16)
    rcol = c.cf(4)
    c.ffn_rstd = c.sb("ffn_rstd", [128, 16], F32)

    xnT_done = [None] * 16
    wg_done = [None] * 6
    wd_done = [None] * 6
    sg_done = [None] * 4
    up_done = [None] * 4
    act_done = [None] * 44
    y_done = [None, None]
    sq_done = None

    for oc in range(noc):
        txt = None
        for K in range(16):
            txt = p.dmat("scalar", xnT[K][:],
                         c.xn_tok[oc * CH:(oc + 1) * CH, K * 128:(K + 1) * 128],
                         reads=[xnT_done[K], c.t_xn_done], sig=r_xt[oc % 2])
        for j4 in range(11):
            for half in range(2):
                dst = sg if half == 0 else upb
                dst_done = sg_done if half == 0 else up_done
                base = half * I_FFN + j4 * 4 * 128
                mmlast = [None] * 4
                for K in range(16):
                    wi = K % 6
                    twl = p.dma("scalar", wg[wi][:],
                                c.wguT[K * 128:(K + 1) * 128, base:base + CH],
                                reads=[wg_done[wi], c.t_deq_end], sig=r_wg[wi])
                    for o4 in range(4):
                        mmlast[o4] = p.op("tensor", lambda e, K=K, wi=wi, o4=o4: e.matmul(
                            out=c.pb[o4][:], lhsT=wg[wi][:, o4 * 128:(o4 + 1) * 128],
                            rhs=xnT[K][:], start=(K == 0), stop=(K == 15)),
                            reads=[twl, txt] + (c.pb_use[o4] if K == 0 else []),
                            sig=c.s_pe)
                    wg_done[wi] = mmlast[3]
                for o4 in range(4):
                    c.pb_use[o4] = []
                    if half == 0:
                        tev = p.op("scalar", lambda e, o4=o4: e.activation(
                            out=sg[o4][:], in_=c.pb[o4][:], func=AF.Silu),
                            reads=[mmlast[o4], sg_done[o4]], sig=c.s_act)
                        sg_done[o4] = tev
                    else:
                        tev = p.op("scalar", lambda e, o4=o4: e.activation(
                            out=upb[o4][:], in_=c.pb[o4][:], func=AF.Copy),
                            reads=[mmlast[o4], up_done[o4]], sig=c.s_act)
                        up_done[o4] = tev
                    c.pb_use[o4].append(tev)
            for o4 in range(4):
                jj = j4 * 4 + o4
                tmu = p.op("vector", lambda e, o4=o4, jj=jj: e.tensor_mul(
                    out=act[jj][:], in0=sg[o4][:], in1=upb[o4][:]),
                    reads=[sg_done[o4], up_done[o4], act_done[jj]], sig=c.s_dve)
                sg_done[o4] = [sg_done[o4], tmu]
                up_done[o4] = [up_done[o4], tmu]
                act_done[jj] = tmu
        for K in range(16):
            xnT_done[K] = wg_done[5]
        sq_toks = []
        for hj in range(4):
            mmlast = [None] * 4
            for i in range(44):
                wi = i % 6
                twl = p.dma("scalar", wd[wi][:],
                            c.wdT[i * 128:(i + 1) * 128, hj * CH:(hj + 1) * CH],
                            reads=[wd_done[wi], c.t_deq_end], sig=r_wd[wi])
                for ts in range(4):
                    mmlast[ts] = p.op("tensor", lambda e, i=i, wi=wi, ts=ts: e.matmul(
                        out=c.pb[ts][:], lhsT=act[i][:, ts * 128:(ts + 1) * 128],
                        rhs=wd[wi][:], start=(i == 0), stop=(i == 43)),
                        reads=[twl, act_done[i]] + (c.pb_use[ts] if i == 0 else []),
                        sig=c.s_pe)
                wd_done[wi] = mmlast[3]
            for ts in range(4):
                c.pb_use[ts] = []
                si = ts % 2
                tyc = p.op("vector", lambda e, ts=ts, si=si: e.tensor_copy(
                    out=ysb[si][:], in_=c.pb[ts][:]), reads=[mmlast[ts], y_done[si]],
                    sig=c.s_dve)
                c.pb_use[ts].append(tyc)
                tsq = p.op("scalar", lambda e, ts=ts, hj=hj, si=si: e.activation(
                    out=sqjunk[:], in_=ysb[si][:], func=AF.Square,
                    accum_out=sqc[:, ts * 4 + hj:ts * 4 + hj + 1]),
                    reads=[tyc, sq_done], sig=c.s_act)
                tys = p.dma("gpsimd",
                            c.ffny[oc * CH + ts * 128: oc * CH + (ts + 1) * 128,
                                   hj * CH:(hj + 1) * CH],
                            ysb[si][:], reads=[tyc], sig=r_ys[si])
                y_done[si] = [tys, tsq]
                sq_toks.append(tsq)
        for i in range(44):
            act_done[i] = wd_done[5]
        for ts in range(4):
            def rms_ts(e, ts=ts, oc=oc):
                e.tensor_tensor(out=rcol[:, 0:1], in0=sqc[:, ts * 4:ts * 4 + 1],
                                in1=sqc[:, ts * 4 + 1:ts * 4 + 2], op=ALU.add)
                e.tensor_tensor(out=rcol[:, 1:2], in0=sqc[:, ts * 4 + 2:ts * 4 + 3],
                                in1=sqc[:, ts * 4 + 3:ts * 4 + 4], op=ALU.add)
                e.drain()
                return e.tensor_scalar(out=rcol[:, 2:3], in0=rcol[:, 0:1],
                                       scalar1=rcol[:, 1:2], scalar2=1.0 / D,
                                       op0=ALU.add, op1=ALU.mult)
            trm = p.op("vector", rms_ts, reads=sq_toks + [getattr(c, "_rms_prev", None)],
                       sig=c.s_dve)
            tl = p.op("scalar", lambda e: e.activation(
                out=rcol[:, 3:4], in_=rcol[:, 2:3], func=AF.Ln, bias=c.rmseps[:]),
                reads=[trm, c.t_const], sig=c.s_act)
            c._rms_prev = tl
            p.drain("scalar")
            tex = p.op("scalar", lambda e, oc=oc, ts=ts: e.activation(
                out=c.ffn_rstd[:, oc * 4 + ts:oc * 4 + ts + 1], in_=rcol[:, 3:4],
                func=AF.Exp, scale=-0.5), sig=c.s_act)
            sq_done = tex
            c.t_ffn_rstd = tex
    c.t_ffn_end = [Tok(s, s.count, "_ff_", True) for s in (r_ys[0], r_ys[1])]
    if "d_ffny" in c.dbg:
        p.dma("gpsimd", c.dbg["d_ffny"][0:noc * CH, :], c.ffny[0:noc * CH, :],
              reads=[c.t_ffn_end], sig=c.s_store)


def phase_attn(c):
    p, sb = c.p, c.sb
    nqc = min(4, getattr(c, "ATN", 4))
    SEG = 8

    r_zt = p.ring("azt", 1)
    r_wq = p.ring("awq", 6)
    r_cs = p.ring("acs", 2)
    r_eps = p.ring("aeps", 2)
    r_row = p.ring("arow", 2)
    r_kn = p.ring("akn", 3)
    r_vt = p.ring("avt", 3)
    r_wo = p.ring("awo", 6)
    r_ff = p.ring("aff", 2)
    r_xo = p.ring("axo", 2)
    r_mk = p.ring("amk", 1)
    r_sw = p.ring("asw2", 2)
    s_rb = p.sem("a_rowb")
    s_out = p.ring_slots("sout", 2)
    c.s_out = s_out[0]

    zT = [c.cb(CH) for _ in range(16)]
    qrotb = [c.cb(CH) for _ in range(16)]
    qn = zT
    aT = qrotb
    masks_sb = c.cb(16 * CH)
    wq4 = [c.cb(CH) for _ in range(6)]
    wot = [c.cb(CH) for _ in range(6)]
    knseg = [c.cb(SEG * 128) for _ in range(3)]
    vseg = [c.cb(SEG * 128) for _ in range(3)]
    Pb = [c.cb(CH) for _ in range(2)]
    rbc = c.cb(CH)
    statb = c.cb(2 * CH)[0:16, :]
    rowsb = c.cf(8 * CH).bitcast(BF16)[0:1, :]
    epsr = [c.cb(CH)[0:1, :] for _ in range(2)]
    q2b = [c.cb(CH) for _ in range(2)]
    lrb = c.cb(CH)[0:1, :]
    mx16 = c.cb(16)[0:16, 0:1]
    mxr = c.cb(16)[0:1, :]

    cosc = [c.cf(CH) for _ in range(2)]
    sinc = [c.cf(CH) for _ in range(2)]
    krot = [c.cf(CH) for _ in range(2)]
    krot_sw = [c.cf(CH) for _ in range(2)]
    p1 = [c.cf(CH) for _ in range(2)]
    p2 = [c.cf(CH) for _ in range(2)]
    stats = c.cf(5 * CH)[0:16, :]
    fft = [c.cf(CH) for _ in range(2)]
    xot = [c.cf(CH) for _ in range(2)]
    outb = [c.cf(CH) for _ in range(2)]
    tmpf = [c.cf(CH) for _ in range(2)]
    lrow = c.cf(CH)[0:1, :]
    qmaxc = c.cf(4)[0:16, 0:1]
    mrow = c.cf(4)[0:1, :]
    negMb = c.cf(1)

    zt_done = None
    wq_done = [None] * 6
    wo_done = [None] * 6
    cs_done = [None, None]
    eps_done = [None, None]
    row_done = [None, None]
    ksg_done = [None, None, None]
    vsg_done = [None, None, None]
    ff_done = [None, None]
    xo_done = [None, None]
    out_done = [None, None]
    tmp_done = [None, None]
    P_done = [None, None]
    q2_done = [None, None]
    qrot_done = [None] * 16
    qn_done = [None] * 16
    aT_done = [None] * 16
    krot_done = [None, None]
    p12_done = [None, None]
    stats_done = None
    statb_done = None
    lr_done = None
    rbc_done = None
    negM_done = None
    mx_done = None
    mk_done = None

    s1v = c.pstat[0:16, :]
    s2v = c.pb[4][0:16, :]
    qnv = c.pb[5][0:16, :]

    for qi in range(nqc):
        tzt = None
        for K in range(16):
            tzt = p.dmat("scalar", zT[K][:],
                         c.zown[qi * CH:(qi + 1) * CH, K * 128:(K + 1) * 128],
                         reads=[zt_done, c.t_z_done], sig=r_zt[0])
        ri = qi % 2
        tcs = p.dma("sync", cosc[ri][:], c.cosdo[:, qi * CH:(qi + 1) * CH],
                    reads=[cs_done[ri]], sig=r_cs[ri])
        tsn = p.dma("sync", sinc[ri][:], c.sindo[:, qi * CH:(qi + 1) * CH],
                    reads=[cs_done[ri]], sig=r_cs[ri])
        tep = p.dma("sync", epsr[ri][:],
                    c.epso.rearrange("a b -> (a b)")[qi * CH:(qi + 1) * CH].unsqueeze(0),
                    reads=[eps_done[ri], c.t_epso], sig=r_eps[ri])
        tmk = p.dma("sync", masks_sb[:].rearrange("p (m q) -> p m q", m=16),
                    c.maskv[qi].rearrange("m p q -> p m q"),
                    reads=[mk_done], sig=r_mk[0])

        stat_last = [None, None]
        for hg in range(4):
            mmg = [None] * 4
            for K in range(16):
                wi = K % 6
                tw = p.dma("sync", wq4[wi][:],
                           c.wqP[K * 128:(K + 1) * 128, hg * CH:(hg + 1) * CH],
                           reads=[wq_done[wi], c.t_wprep_end], sig=r_wq[wi])
                for hi in range(4):
                    mmg[hi] = p.op("tensor", lambda e, K=K, wi=wi, hi=hi: e.matmul(
                        out=c.pb[hi][:], lhsT=wq4[wi][:, hi * 128:(hi + 1) * 128],
                        rhs=zT[K][:], start=(K == 0), stop=False),
                        reads=[tw, tzt] + (c.pb_use[hi] if K == 0 else []), sig=c.s_pe)
                wq_done[wi] = mmg[3]
            for hi in range(4):
                h = hg * 4 + hi
                pbi = h % 2
                c.pb_use[hi] = []
                mmb = p.op("tensor", lambda e, h=h, hi=hi: e.matmul(
                    out=c.pb[hi][:], lhsT=c.browq_bf[0:1, h * 128:(h + 1) * 128],
                    rhs=c.ones_row[:], start=False, stop=True),
                    reads=[mmg[hi], c.t_brow, c.t_const], sig=c.s_pe)
                tev = p.op("scalar", lambda e, hi=hi, ki=pbi: e.activation(
                    out=krot[ki][:], in_=c.pb[hi][:], func=AF.Copy),
                    reads=[mmb, krot_done[pbi]], sig=c.s_act)
                c.pb_use[hi].append(tev)
                tw1 = p.dma("gpsimd", krot_sw[pbi][0:64, :], krot[pbi][64:128, :],
                            reads=[tev, krot_done[pbi]], sig=r_sw[pbi])
                tw2 = p.dma("gpsimd", krot_sw[pbi][64:128, :], krot[pbi][0:64, :],
                            sig=r_sw[pbi])
                tp1 = p.op("gpsimd", lambda e, ri=ri, ki=pbi: e.tensor_mul(
                    out=p1[ki][:], in0=krot[ki][:], in1=cosc[ri][:]),
                    reads=[tev, tsn, p12_done[pbi]], sig=c.s_pool)
                tp2 = p.op("gpsimd", lambda e, ri=ri, ki=pbi: e.tensor_mul(
                    out=p2[ki][:], in0=krot_sw[ki][:], in1=sinc[ri][:]),
                    reads=[tw2, tsn], sig=c.s_pool)
                krot_done[pbi] = [tp1, tp2]
                tcb = p.op("vector", lambda e, h=h, ki=pbi: e.tensor_add(
                    out=qrotb[h][:], in0=p1[ki][:], in1=p2[ki][:]),
                    reads=[tp1, tp2, qrot_done[h]], sig=c.s_dve)
                p12_done[pbi] = tcb
                qrot_done[h] = [tcb]
        for h in range(16):
            ki = h % 2
            tsq = p.op("scalar", lambda e, h=h, ki=ki: e.activation(
                out=q2b[ki][:], in_=qrotb[h][:], func=AF.Square),
                reads=[qrot_done[h], q2_done[ki]], sig=c.s_act)
            ts1 = p.op("tensor", lambda e, h=h: e.matmul(
                out=s1v, lhsT=c.ehs[:, h * 16:(h + 1) * 16], rhs=qrotb[h][:],
                start=(h == 0), stop=(h == 15)),
                reads=[qrot_done[h], c.t_const] + (c.pstat_use if h == 0 else []),
                sig=c.s_pe)
            ts2 = p.op("tensor", lambda e, h=h, ki=ki: e.matmul(
                out=s2v, lhsT=c.ehs[:, h * 16:(h + 1) * 16], rhs=q2b[ki][:],
                start=(h == 0), stop=False),
                reads=[tsq] + (c.pb_use[4] if h == 0 else []), sig=c.s_pe)
            q2_done[ki] = ts2
            stat_last = [ts1, ts2]
        tse = p.op("tensor", lambda e, ri=ri: e.matmul(
            out=s2v, lhsT=c.ones_row[0:1, 0:16], rhs=epsr[ri][:],
            start=False, stop=True), reads=[tep, stat_last[1], c.t_const], sig=c.s_pe)
        eps_done[ri] = tse

        def smk(e):
            e.tensor_scalar(out=stats[:, 0:CH], in0=s1v, scalar1=1.0 / 128,
                            scalar2=None, op0=ALU.mult)
            e.tensor_scalar(out=stats[:, CH:2 * CH], in0=s2v, scalar1=1.0 / 128,
                            scalar2=None, op0=ALU.mult)
            e.drain()
            e.scalar_tensor_tensor(out=stats[:, 2 * CH:3 * CH], in0=stats[:, 0:CH],
                                   scalar=-1.0, in1=stats[:, 0:CH],
                                   op0=ALU.mult, op1=ALU.mult)
            e.drain()
            return e.tensor_tensor(out=stats[:, 2 * CH:3 * CH],
                                   in0=stats[:, CH:2 * CH],
                                   in1=stats[:, 2 * CH:3 * CH], op=ALU.add)
        tsk = p.op("vector", smk, reads=[tse, stat_last[0], stats_done], sig=c.s_dve)
        tl1 = p.op("scalar", lambda e: e.activation(
            out=stats[:, 3 * CH:4 * CH], in_=stats[:, 2 * CH:3 * CH], func=AF.Ln),
            reads=[tsk], sig=c.s_act)
        p.drain("scalar")
        tl2 = p.op("scalar", lambda e: e.activation(
            out=stats[:, 3 * CH:4 * CH], in_=stats[:, 3 * CH:4 * CH], func=AF.Exp,
            scale=-0.5), sig=c.s_act)

        def smk2(e):
            e.scalar_tensor_tensor(out=stats[:, 4 * CH:5 * CH], in0=stats[:, 0:CH],
                                   scalar=-1.0, in1=stats[:, 3 * CH:4 * CH],
                                   op0=ALU.mult, op1=ALU.mult)
            e.tensor_copy(out=statb[:, 0:CH], in_=stats[:, 3 * CH:4 * CH])
            e.drain()
            return e.tensor_copy(out=statb[:, CH:2 * CH], in_=stats[:, 4 * CH:5 * CH])
        tsk2 = p.op("vector", smk2, reads=[tl2, statb_done], sig=c.s_dve)
        stats_done = tsk2
        c.pstat_use = [tsk]
        c.pb_use[4] = [tsk]

        trb = p.dma("gpsimd", c.rowq, statb[:],
                    reads=[tsk2, getattr(c, "_rowq_read", [])], sig=s_rb)
        statb_done = trb

        rowq_reads = []
        row_users = []
        for h in range(16):
            si = h % 2
            pc = c.pb[2 * si]
            pd = c.pb[2 * si + 1]
            if h % 8 == 0:
                rflat = c.rowq.rearrange("a b -> (a b)")
                trh = p.dma("sync", rowsb[:],
                            rflat[h * 2 * CH:(h + 8) * 2 * CH].unsqueeze(0),
                            reads=[trb, row_done[0]], sig=r_row[0])
                rowq_reads.append(trh)
                row_done[0] = []
            ho = (h % 8) * 2 * CH
            tmc = p.op("tensor", lambda e, ho=ho, pc=pc: e.matmul(
                out=pc[:], lhsT=c.lrow[0:1, 0:HD],
                rhs=rowsb[0:1, ho:ho + CH],
                start=True, stop=True),
                reads=[trh, c.t_lrow] + c.pb_use[2 * si], sig=c.s_pe)
            tmd1 = p.op("tensor", lambda e, ho=ho, pd=pd: e.matmul(
                out=pd[:], lhsT=c.lrow[0:1, 0:HD],
                rhs=rowsb[0:1, ho + CH:ho + 2 * CH],
                start=True, stop=False),
                reads=[trh] + c.pb_use[2 * si + 1], sig=c.s_pe)
            tmd2 = p.op("tensor", lambda e, pd=pd: e.matmul(
                out=pd[:], lhsT=c.lrow[0:1, HD:2 * HD], rhs=c.ones_row[:],
                start=False, stop=True), sig=c.s_pe)
            row_users += [tmc, tmd1]
            row_done[0] = row_done[0] + [tmc, tmd1]
            c.pb_use[2 * si] = []
            c.pb_use[2 * si + 1] = []
            tt1 = p.op("vector", lambda e, h=h, si=si, pc=pc: e.tensor_tensor(
                out=Pb[si][:], in0=qrotb[h][:], in1=pc[:], op=ALU.mult),
                reads=[tmc, P_done[si]], sig=c.s_dve)
            tt2 = p.op("vector", lambda e, h=h, si=si, pd=pd: e.tensor_tensor(
                out=qn[h][:], in0=Pb[si][:], in1=pd[:], op=ALU.add),
                reads=[tmd2, qn_done[h]], sig=c.s_dve)
            qrot_done[h] = [tt1]
            P_done[si] = tt2
            c.pb_use[2 * si].append(tt1)
            c.pb_use[2 * si + 1].append(tt2)
            tsq2 = p.op("scalar", lambda e, h=h, si=si: e.activation(
                out=q2b[si][:], in_=qn[h][:], func=AF.Square),
                reads=[tt2, q2_done[si]], sig=c.s_act)
            tqn = p.op("tensor", lambda e, h=h, si=si: e.matmul(
                out=qnv, lhsT=c.ehs[:, h * 16:(h + 1) * 16], rhs=q2b[si][:],
                start=(h == 0), stop=(h == 15)),
                reads=[tsq2] + (c.pb_use[5] if h == 0 else []), sig=c.s_pe)
            q2_done[si] = tqn
            qn_done[h] = [tsq2]
        c._rowq_read = rowq_reads

        def mx1(e):
            e.reduce_max(out=qmaxc[:, 0:1], in_=qnv, axis=AX.X)
            e.drain()
            e.tensor_tensor(out=qmaxc[:, 0:1], in0=qmaxc[:, 0:1], in1=c.maxk2[:],
                            op=ALU.mult)
            e.drain()
            return e.tensor_copy(out=mx16[:, 0:1], in_=qmaxc[:, 0:1])
        tmx = p.op("vector", mx1, reads=[q2_done[0], q2_done[1], c.t_maxk, mx_done],
                   sig=c.s_dve)
        c.pb_use[5] = [tmx]
        ttr = p.op("tensor", lambda e: e.transpose(
            out=c.ptr[0:1, 0:16], in_=mx16[0:16, 0:1], identity=c.identb[0:16, 0:16]),
            reads=[tmx, c.t_const] + c.ptr_use, sig=c.s_pe)
        c.ptr_use = []
        tmr = p.op("vector", lambda e: e.tensor_copy(out=mxr[0:1, 0:16],
                                                     in_=c.ptr[0:1, 0:16]),
                   reads=[ttr], sig=c.s_dve)
        c.ptr_use.append(tmr)
        p.drain("vector")
        tmm = p.op("vector", lambda e: e.reduce_max(out=mrow[0:1, 0:1],
                                                    in_=mxr[0:1, 0:16], axis=AX.X),
                   sig=c.s_dve)
        tsf = p.op("scalar", lambda e: e.activation(
            out=mrow[0:1, 1:2], in_=mrow[0:1, 0:1], func=AF.Sqrt),
            reads=[tmm], sig=c.s_act)
        p.drain("scalar")
        tng = p.op("scalar", lambda e: e.activation(
            out=mrow[0:1, 2:3], in_=mrow[0:1, 1:2], func=AF.Copy, scale=-1.02),
            sig=c.s_act)
        tngb = p.op("vector", lambda e: e.tensor_copy(out=mxr[0:1, 0:1],
                                                      in_=mrow[0:1, 2:3]),
                    reads=[tng, tmr], sig=c.s_dve)
        tmb = p.op("tensor", lambda e: e.matmul(
            out=c.pb[0][:, 0:1], lhsT=c.ones_row[0:1, 0:128], rhs=mxr[0:1, 0:1],
            start=True, stop=True), reads=[tngb, c.t_const] + c.pb_use[0], sig=c.s_pe)
        tnb = p.op("vector", lambda e: e.tensor_copy(out=negMb[:, 0:1],
                                                     in_=c.pb[0][:, 0:1]),
                   reads=[tmb], sig=c.s_dve)
        c.pb_use[0] = [tnb]
        negM_done = tnb
        mx_done = [tmm, tngb]

        nkt = SLOT_NKT[qi]
        mk_users = []
        for h in range(16):
            acc = c.pb[4]
            lps = c.pb[5][0:1, :]
            for kt in range(nkt):
                sgi = (kt // SEG) % 3
                if kt % SEG == 0:
                    tkl = p.dma("sync", knseg[sgi][:],
                                c.knT[h * 128:(h + 1) * 128,
                                      kt * 128:(kt + SEG) * 128],
                                reads=[ksg_done[sgi], c.t_kv_end], sig=r_kn[sgi])
                    tvl = p.dma("sync", vseg[sgi][:].rearrange("t (k d) -> t k d", k=SEG),
                                c.vtok[kt:kt + SEG, h].rearrange("k t d -> t k d"),
                                reads=[vsg_done[sgi], c.t_kv_end], sig=r_vt[sgi])
                    ksg_done[sgi] = []
                    vsg_done[sgi] = []
                ko = (kt % SEG) * 128
                sb_ = kt % 2
                spb = c.pb[2 + (kt % 2)]
                diag = kt >= nkt - 16
                tmmS = p.op("tensor", lambda e, h=h, spb=spb, diag=diag, sgi=sgi, ko=ko: e.matmul(
                    out=spb[:], lhsT=knseg[sgi][:, ko:ko + 128], rhs=qn[h][:],
                    start=True, stop=not diag),
                    reads=[tkl, qn_done[h]] + c.pb_use[2 + (kt % 2)], sig=c.s_pe)
                ksg_done[sgi].append(tmmS)
                if diag:
                    midx = kt - (nkt - 16)
                    tmmS = p.op("tensor", lambda e, midx=midx, spb=spb: e.matmul(
                        out=spb[:], lhsT=c.identb[:],
                        rhs=masks_sb[:, midx * CH:(midx + 1) * CH],
                        start=False, stop=True), reads=[c.t_const, tmk], sig=c.s_pe)
                    mk_users.append(tmmS)
                c.pb_use[2 + (kt % 2)] = []
                texp = p.op("scalar", lambda e, sb_=sb_, spb=spb: e.activation(
                    out=Pb[sb_][:], in_=spb[:], func=AF.Exp, bias=negMb[:, 0:1],
                    scale=1.0), reads=[tmmS, negM_done, P_done[sb_]], sig=c.s_act)
                c.pb_use[2 + (kt % 2)].append(texp)
                st = (kt == 0)
                sp = (kt == nkt - 1)
                tacc = p.op("tensor", lambda e, sb_=sb_, acc=acc, st=st, sp=sp,
                            sgi=sgi, ko=ko: e.matmul(
                    out=acc[:], lhsT=vseg[sgi][:, ko:ko + 128], rhs=Pb[sb_][:],
                    start=st, stop=sp),
                    reads=[tvl, texp] + (c.pb_use[4] if st else []), sig=c.s_pe)
                tl_ = p.op("tensor", lambda e, sb_=sb_, lps=lps, st=st, sp=sp: e.matmul(
                    out=lps, lhsT=c.onescol_bf[:], rhs=Pb[sb_][:],
                    start=st, stop=sp),
                    reads=[texp, c.t_const] + (c.pb_use[5] if st else []), sig=c.s_pe)
                vsg_done[sgi].append(tacc)
                P_done[sb_] = [tacc, tl_]
            c.pb_use[4] = []
            c.pb_use[5] = []
            def lrec(e):
                e.tensor_scalar(out=lrow[0:1, :], in0=lps, scalar1=1e-35,
                                scalar2=None, op0=ALU.max)
                e.drain()
                e.reciprocal(out=lrow[0:1, :], in_=lrow[0:1, :])
                e.drain()
                return e.tensor_copy(out=lrb[0:1, :], in_=lrow[0:1, :])
            tlr = p.op("vector", lrec, reads=[tl_, lr_done], sig=c.s_dve)
            lr_done = tlr
            c.pb_use[5].append(tlr)
            trc = p.op("tensor", lambda e: e.matmul(
                out=c.pb[0][:], lhsT=c.ones_row[0:1, 0:128], rhs=lrb[0:1, :],
                start=True, stop=True), reads=[tlr, c.t_const] + c.pb_use[0],
                sig=c.s_pe)
            c.pb_use[0] = []
            tre = p.op("scalar", lambda e: e.activation(
                out=rbc[:], in_=c.pb[0][:], func=AF.Copy),
                reads=[trc, rbc_done], sig=c.s_act)
            c.pb_use[0].append(tre)
            tdv = p.op("vector", lambda e, h=h: e.tensor_tensor(
                out=aT[h][:], in0=acc[:], in1=rbc[:], op=ALU.mult),
                reads=[tre, aT_done[h], qrot_done[h]], sig=c.s_dve)
            rbc_done = tdv
            c.pb_use[4].append(tdv)
            aT_done[h] = tdv
            qn_done[h] = [qn_done[h], tmmS]

        for do in range(4):
            mml = [None] * 4
            for h in range(16):
                wi = h % 6
                twl = p.dma("sync", wot[wi][:],
                            c.woP[h * 128:(h + 1) * 128, do * CH:(do + 1) * CH],
                            reads=[wo_done[wi], c.t_wprep_end], sig=r_wo[wi])
                for ts in range(4):
                    mml[ts] = p.op("tensor", lambda e, h=h, wi=wi, ts=ts: e.matmul(
                        out=c.pb[ts][:], lhsT=aT[h][:, ts * 128:(ts + 1) * 128],
                        rhs=wot[wi][:], start=(h == 0), stop=False),
                        reads=[twl, aT_done[h]] + (c.pb_use[ts] if h == 0 else []),
                        sig=c.s_pe)
                wo_done[wi] = mml[3]
            for ts in range(4):
                c.pb_use[ts] = []
                mmb = p.op("tensor", lambda e, do=do, ts=ts: e.matmul(
                    out=c.pb[ts][:], lhsT=c.ones_row[0:1, 0:128],
                    rhs=c.bo_bf[0:1, do * CH:(do + 1) * CH],
                    start=False, stop=True), reads=[mml[ts], c.t_bo], sig=c.s_pe)
                si = ts % 2
                tff = p.dma("sync", fft[si][:],
                            c.ffny[qi * CH + ts * 128: qi * CH + (ts + 1) * 128,
                                   do * CH:(do + 1) * CH],
                            reads=[ff_done[si], c.t_ffn_end], sig=r_ff[si])
                txo = p.dma("sync", xot[si][:],
                            c.xown[qi * CH + ts * 128: qi * CH + (ts + 1) * 128,
                                   do * CH:(do + 1) * CH],
                            reads=[xo_done[si]], sig=r_xo[si])
                tst1 = p.op("vector", lambda e, si=si, ts=ts, qi=qi: e.scalar_tensor_tensor(
                    out=tmpf[si][:], in0=fft[si][:],
                    scalar=c.ffn_rstd[:, qi * 4 + ts:qi * 4 + ts + 1],
                    in1=c.pb[ts][:], op0=ALU.mult, op1=ALU.add),
                    reads=[tff, mmb, c.t_ffn_rstd, tmp_done[si]], sig=c.s_dve)
                c.pb_use[ts].append(tst1)
                tst2 = p.op("vector", lambda e, si=si: e.tensor_tensor(
                    out=outb[si][:], in0=tmpf[si][:], in1=xot[si][:], op=ALU.add),
                    reads=[txo, out_done[si]], sig=c.s_dve)
                ff_done[si] = tst1
                xo_done[si] = tst2
                tmp_done[si] = tst2
                tso = p.dma("gpsimd", c.y[qi * CH + ts * 128: qi * CH + (ts + 1) * 128,
                                          do * CH:(do + 1) * CH],
                            outb[si][:], reads=[tst2], sig=s_out[si])
                out_done[si] = tso
        for h in range(16):
            aT_done[h] = [aT_done[h], mml[3]]
        cs_done[ri] = krot_done[:]
        zt_done = mml[:]
        mk_done = mk_users




_NC_CACHE = {}


def kernel(**inputs):
    from concourse.bass_utils import run_bass_kernel_spmd
    in_maps = host_prepare(inputs)
    if "nc" not in _NC_CACHE:
        _NC_CACHE["nc"] = Core(0).build()
    nc = _NC_CACHE["nc"]
    res = run_bass_kernel_spmd(nc, in_maps, core_ids=list(range(8)))
    return host_gather(res.results)


# version 09
# speedup vs baseline: 1.0034x; 1.0034x over previous
import math
import numpy as np
import ml_dtypes
import concourse.bass as bass
import concourse.mybir as mybir
from concourse.bass_utils import run_bass_kernel_spmd

F32 = mybir.dt.float32
BF16 = mybir.dt.bfloat16
F16 = mybir.dt.float16
I32 = mybir.dt.int32
U16 = mybir.dt.uint16
AF = mybir.ActivationFunctionType
ALU = mybir.AluOpType
AX = mybir.AxisListType

B, S, D, H, HD = 2, 8192, 2048, 16, 128
I_FFN = 5632
CH = 512
NCH = S // CH
KT = 128
ROLE_CHUNKS = [[r, 7 - r, 8 + r, 15 - r] for r in range(4)]
SLOT_NKT = [16, 32, 48, 64]
LN_EPS = 1e-5
RMS_EPS = 1e-6
THETA = 30000.0
SCALE2 = HD ** -0.5
NEG_MASK = -30000.0

HDPERM = np.concatenate([np.arange(0, HD, 2), np.arange(1, HD, 2)])
DPERM = np.concatenate([h * HD + HDPERM for h in range(H)])


def host_prepare(inputs):
    x = np.asarray(inputs["L0_attn_x"])
    Wq = np.asarray(inputs["L0_attn_Wq"])
    Wkv = np.asarray(inputs["L0_attn_Wkv"])
    Wo = np.asarray(inputs["L0_attn_Wo"])
    bq = np.asarray(inputs["L0_attn_bq"])
    bkv = np.asarray(inputs["L0_attn_bkv"])
    bo = np.asarray(inputs["L0_attn_bo"])
    qn_g = np.asarray(inputs["L0_attn_qn_g"])
    qn_b = np.asarray(inputs["L0_attn_qn_b"])
    kvn_g = np.asarray(inputs["L0_attn_kvn_g"])
    kvn_b = np.asarray(inputs["L0_attn_kvn_b"])
    ln_g = np.asarray(inputs["L0_attn_ln_g"])
    ln_b = np.asarray(inputs["L0_attn_ln_b"])
    gup = np.asarray(inputs["L0_ffn_gate_up_packed"])
    gus = np.asarray(inputs["L0_ffn_gate_up_scales"])
    dnp = np.asarray(inputs["L0_ffn_down_packed"])
    dns = np.asarray(inputs["L0_ffn_down_scales"])

    WqP = np.ascontiguousarray(Wq[:, DPERM])
    WkvP = np.concatenate([Wkv[:, :D][:, DPERM], Wkv[:, D:]], axis=1)
    bqP = bq[DPERM].copy()
    bkvP = np.concatenate([bkv[:D][DPERM], bkv[D:]])
    ln_gP = ln_g[HDPERM].copy()
    ln_bP = ln_b[HDPERM].copy()

    def unpack_T(packed, n_in):
        w = packed.view(np.uint32)
        nib = (w[:, :, None] >> (4 * np.arange(8, dtype=np.uint32))) & np.uint32(15)
        nib = nib.reshape(packed.shape[0], n_in)
        return np.ascontiguousarray(nib.T.astype(np.uint16))

    gu_nib = unpack_T(gup, D)
    dn_nib = unpack_T(dnp, I_FFN)

    gu_sc = np.ascontiguousarray(np.broadcast_to(gus[:, None, :], (16, 128, 11264)))
    dn_sc = np.ascontiguousarray(np.broadcast_to(dns[:, None, :], (44, 128, 2048)))

    pos = np.arange(S, dtype=np.float64)
    freqs = (np.arange(0, HD, 2, dtype=np.float64) / HD) * math.log(THETA)
    ang = pos[None, :] * freqs[:, None]
    cosd = np.tile(np.cos(ang), (2, 1)).astype(np.float32)
    sind = np.concatenate([-np.sin(ang), np.sin(ang)]).astype(np.float32)

    qq = np.arange(CH)[None, :]
    kk = np.arange(KT)[:, None]

    ident = np.eye(128, dtype=np.float32).astype(ml_dtypes.bfloat16)

    ehs = np.zeros((H, 128, H), dtype=np.float32)
    for h in range(H):
        ehs[h, :, h] = 1.0
    ehs = ehs.astype(ml_dtypes.bfloat16)

    in_maps = []
    for c in range(8):
        b = c // 4
        role = c % 4
        chunks = ROLE_CHUNKS[role]
        xb = x[b]
        xown = np.concatenate([xb[cc * CH:(cc + 1) * CH] for cc in chunks], axis=0)
        ownpos = np.concatenate([np.arange(cc * CH, (cc + 1) * CH) for cc in chunks])
        mv = np.zeros((4, 16, KT, CH), dtype=np.float32)
        for qi, cc in enumerate(chunks):
            E = SLOT_NKT[qi]
            for p128 in range(16):
                kt = E - 16 + p128
                if kt < 4 * cc:
                    pass
                elif kt <= 4 * cc + 3:
                    d = (kt - 4 * cc) * 128
                    mv[qi, p128] = np.where(qq < kk + d, NEG_MASK, 0.0)
                else:
                    mv[qi, p128] = NEG_MASK
        m = dict(
            xpre=np.ascontiguousarray(xb),
            xown=np.ascontiguousarray(xown),
            wq=WqP, wkv=WkvP, wo=Wo,
            bq=bqP[None, :], bkv=bkvP[None, :], bo=bo[None, :],
            qng=np.ascontiguousarray(qn_g.reshape(16, 128).T),
            qnb=np.ascontiguousarray(qn_b.reshape(16, 128).T),
            kvg=np.ascontiguousarray(kvn_g.reshape(16, 128).T),
            kvb=np.ascontiguousarray(kvn_b.reshape(16, 128).T),
            lng=ln_gP[None, :], lnb=ln_bP[None, :],
            gunib=gu_nib, dnnib=dn_nib,
            gusc=gu_sc, dnsc=dn_sc,
            cosd=np.ascontiguousarray(cosd),
            sind=np.ascontiguousarray(sind),
            cosdo=np.ascontiguousarray(cosd[:, ownpos]),
            sindo=np.ascontiguousarray(sind[:, ownpos]),
            maskv=mv.astype(ml_dtypes.bfloat16),
            ident=ident, ehs=ehs,
        )
        in_maps.append(m)
    return in_maps


def host_gather(results):
    out = np.empty((B, S, D), dtype=np.float32)
    for c in range(8):
        b = c // 4
        chunks = ROLE_CHUNKS[c % 4]
        y = results[c]["y"]
        for j, cc in enumerate(chunks):
            out[b, cc * CH:(cc + 1) * CH] = y[j * CH:(j + 1) * CH]
    return out


class Sem:
    __slots__ = ("h", "name", "count")

    def __init__(self, handle, name):
        self.h = handle
        self.name = name
        self.count = 0


class Tok:
    __slots__ = ("sem", "val", "eng", "dma")

    def __init__(self, sem, val, eng, dma):
        self.sem, self.val, self.eng, self.dma = sem, val, eng, dma


class Prog:

    ENGS = ("sync", "tensor", "vector", "scalar", "gpsimd")

    def __init__(self, nc, stack):
        self.nc = nc
        self.ops = []
        self.sems = []
        self._stack = stack

    def sem(self, name):
        s = Sem(self._stack.enter_context(self.nc.semaphore(name)), name)
        self.sems.append(s)
        return s

    def ring(self, name, n):
        return [self.sem(f"{name}{i}") for i in range(n)]

    def ring_slots(self, name, n):
        return [self.sem(f"{name}{i}") for i in range(n)]

    @staticmethod
    def _flat(reads):
        out = []
        for t in reads:
            if t is None:
                continue
            if isinstance(t, (list, tuple)):
                out.extend(Prog._flat(t))
            else:
                out.append(t)
        return out

    def op(self, eng, fn, reads=(), sig=None, amt=1, dma=False):
        waits = []
        for t in self._flat(reads):
            if (not t.dma) and t.eng == eng:
                continue
            waits.append(t)
        tok = None
        if sig is not None:
            sig.count += amt
            tok = Tok(sig, sig.count, eng, dma)
        self.ops.append((eng, fn, waits, sig, amt))
        return tok

    def drain(self, eng):
        self.ops.append((eng, lambda e: e.drain(), [], None, 0))

    def dma(self, eng, out, in_, reads=(), sig=None, **kw):
        assert sig is not None
        return self.op(eng, lambda e: e.dma_start(out=out, in_=in_, **kw),
                       reads=reads, sig=sig, amt=16, dma=True)

    def dmat(self, eng, out, in_, reads=(), sig=None):
        assert sig is not None
        return self.op(eng, lambda e: e.dma_start_transpose(out, in_),
                       reads=reads, sig=sig, amt=16, dma=True)

    def gate_all(self, toks):
        for eng in self.ENGS:
            self.ops.append((eng, lambda e: e.nop(), Prog._flat(toks), None, 0))

    def final_barrier(self, eng):
        toks = [Tok(s, s.count, "_barrier_", True) for s in self.sems if s.count]
        self.ops.append((eng, lambda e: e.nop(), toks, None, 0))

    def emit(self, block):
        ops = self.ops if Prog.KEEP is None else self.ops[:Prog.KEEP]
        per = {e: [] for e in self.ENGS}
        for rec in ops:
            per[rec[0]].append(rec)

        def make(eng_name):
            eops = per[eng_name]

            def body(eng):
                waited = {}
                for (_e, fn, waits, sig, amt) in eops:
                    for t in waits:
                        if waited.get(t.sem.name, -1) >= t.val:
                            continue
                        eng.wait_ge(t.sem.h, t.val)
                        waited[t.sem.name] = t.val
                    inst = fn(eng)
                    if sig is not None:
                        assert inst is not None, "fn with sig must return its last instruction"
                        inst.then_inc(sig.h, amt)
            return body

        for name in self.ENGS:
            if not per[name]:
                continue
            getattr(block, name)(make(name))

    KEEP = None


class Core:
    def __init__(self, role=0, dev=()):
        self.role = role
        self.dev = set(dev)
        self.chunks = ROLE_CHUNKS[role]
        self.npre = NCH
        self.pre = S
        self.nt = S // 128
        self.nc = bass.Bass("TRN2", target_bir_lowering=False, detect_race_conditions=False)
        self.dbg = {}

    def ap(self, name, shape, dt, kind="ExternalInput"):
        return self.nc.dram_tensor(name, list(shape), dt, kind=kind).ap()

    def carve_reset(self):
        self._cf = self._cb = self._cu = 0

    def cf(self, n):
        v = self.PF[:, self._cf:self._cf + n]
        self._cf += n
        assert self._cf <= 20480, f"PF overflow {self._cf}"
        return v

    def cb(self, n):
        v = self.PB[:, self._cb:self._cb + n]
        self._cb += n
        assert self._cb <= 46080, f"PB overflow {self._cb}"
        return v

    def cu(self, n):
        v = self.PU[:, self._cu:self._cu + n]
        self._cu += n
        assert self._cu <= 4096, f"PU overflow {self._cu}"
        return v

    def group_barrier(self):
        return [Tok(s, s.count, "_g_", True) for s in self.p.sems if s.count]

    def build(self):
        import contextlib
        nc = self.nc
        c = self
        pre = c.pre
        c.xpre = c.ap("xpre", [pre, D], F32)
        c.xown = c.ap("xown", [4 * CH, D], F32)
        c.wq = c.ap("wq", [D, D], F32)
        c.wkv = c.ap("wkv", [D, 2 * D], F32)
        c.wo = c.ap("wo", [D, D], F32)
        c.bqr = c.ap("bq", [1, D], F32)
        c.bkvr = c.ap("bkv", [1, 2 * D], F32)
        c.bor = c.ap("bo", [1, D], F32)
        c.qng = c.ap("qng", [128, 16], F32)
        c.qnbr = c.ap("qnb", [128, 16], F32)
        c.kvg = c.ap("kvg", [128, 16], F32)
        c.kvbr = c.ap("kvb", [128, 16], F32)
        c.lngr = c.ap("lng", [1, HD], F32)
        c.lnbr = c.ap("lnb", [1, HD], F32)
        c.gunib = c.ap("gunib", [D, 2 * I_FFN], U16)
        c.dnnib = c.ap("dnnib", [I_FFN, D], U16)
        c.gusc = c.ap("gusc", [16, 128, 2 * I_FFN], F16)
        c.dnsc = c.ap("dnsc", [44, 128, D], F16)
        c.cosdr = c.ap("cosd", [128, pre], F32)
        c.sindr = c.ap("sind", [128, pre], F32)
        c.cosdo = c.ap("cosdo", [128, 4 * CH], F32)
        c.sindo = c.ap("sindo", [128, 4 * CH], F32)
        c.maskv = c.ap("maskv", [4, 16, KT, CH], BF16)
        c.identin = c.ap("ident", [128, 128], BF16)
        c.ehsin = c.ap("ehs", [H, 128, H], BF16)
        c.y = c.ap("y", [4 * CH, D], F32, kind="ExternalOutput")
        c.z_tok = c.ap("z_tok", [pre, D], BF16, kind="Internal")
        c.zown = c.ap("zown", [4 * CH, D], BF16, kind="Internal")
        c.xn_tok = c.ap("xn_tok", [4 * CH, D], BF16, kind="Internal")
        c.epst = c.ap("epst", [64, 128], BF16, kind="Internal")
        c.epso = c.ap("epso", [16, 128], BF16, kind="Internal")
        c.rowq = c.ap("rowq", [16, 2 * CH], BF16, kind="Internal")
        c.rowk = c.ap("rowk", [16, 2 * CH], BF16, kind="Internal")
        c.wqP = c.ap("wqP", [D, D], BF16, kind="Internal")
        c.wkvP = c.ap("wkvP", [D, 2 * D], BF16, kind="Internal")
        c.woP = c.ap("woP", [D, D], BF16, kind="Internal")
        c.knT = c.ap("knT", [D, pre], BF16, kind="Internal")
        c.vtok = c.ap("vtok", [NCH * 4, H, 128, HD], BF16, kind="Internal")
        c.wguT = c.ap("wguT", [D, 2 * I_FFN], BF16, kind="Internal")
        c.wdT = c.ap("wdT", [I_FFN, D], BF16, kind="Internal")
        c.ffny = c.ap("ffny", [4 * CH, D], F32, kind="Internal")

        if "s1" in c.dev:
            c.dbg["d_xn"] = c.ap("d_xn", [4 * CH, D], BF16, kind="ExternalOutput")
            c.dbg["d_z"] = c.ap("d_z", [pre, D], BF16, kind="ExternalOutput")
            c.dbg["d_eps"] = c.ap("d_eps", [64, 128], BF16, kind="ExternalOutput")
        if "kv" in c.dev:
            c.dbg["d_kn"] = c.ap("d_kn", [D, pre], BF16, kind="ExternalOutput")
            c.dbg["d_v"] = c.ap("d_v", [pre, D], BF16, kind="ExternalOutput")
            c.dbg["d_mk"] = c.ap("d_mk", [1, 16], F32, kind="ExternalOutput")
        if "ffn" in c.dev:
            c.dbg["d_wgu"] = c.ap("d_wgu", [D, 2 * I_FFN], BF16, kind="ExternalOutput")
            c.dbg["d_wd"] = c.ap("d_wd", [I_FFN, D], BF16, kind="ExternalOutput")
            c.dbg["d_ffny"] = c.ap("d_ffny", [4 * CH, D], F32, kind="ExternalOutput")
        if "qn" in c.dev:
            c.dbg["d_qn"] = c.ap("d_qn", [D, 4 * CH], BF16, kind="ExternalOutput")

        with contextlib.ExitStack() as st:
            self._st = st
            E = st.enter_context
            c.sb = lambda name, shape, dt: E(nc.sbuf_tensor(name, list(shape), dt))
            c.ps = lambda name, shape, dt=F32: E(nc.psum_tensor(name, list(shape), dt))
            p = c.p = Prog(nc, st)
            c.s_const = p.sem("s_const")
            c.s_store = p.sem("s_store")
            c.s_act = p.sem("s_act")
            c.s_dve = p.sem("s_dve")
            c.s_pool = p.sem("s_pool")
            c.s_pe = p.sem("s_pe")

            c.identb = c.sb("identb", [128, 128], BF16)
            p.dma("sync", c.identb[:], c.identin, sig=c.s_const)
            c.ehs = c.sb("ehs_sb", [128, H * H], BF16)
            p.dma("sync", c.ehs[:].rearrange("p (h e) -> p h e", h=H),
                  c.ehsin.rearrange("h p e -> p h e"), sig=c.s_const)

            c.lneps = c.sb("lneps", [128, 1], F32)
            tm1 = p.op("vector", lambda e: e.memset(c.lneps[:], LN_EPS), sig=c.s_dve)
            c.rmseps = c.sb("rmseps", [128, 1], F32)
            tm2 = p.op("vector", lambda e: e.memset(c.rmseps[:], RMS_EPS), sig=c.s_dve)
            c.ones_row = c.sb("ones_row", [1, CH], BF16)
            tm3 = p.op("vector", lambda e: e.memset(c.ones_row[:], 1.0), sig=c.s_dve)
            c.onescol_bf = c.sb("onescol_bf", [128, 1], BF16)
            tm4 = p.op("vector", lambda e: e.memset(c.onescol_bf[:], 1.0), sig=c.s_dve)
            c.t_const = [Tok(c.s_const, c.s_const.count, "_c_", True), tm1, tm2, tm3, tm4]

            c.PF = c.sb("poolF", [128, 20480], F32)
            c.PB = c.sb("poolB", [128, 46080], BF16)
            c.PU = c.sb("poolU", [128, 4096], U16)
            c._cf = c._cb = c._cu = 0

            c.pb = [c.ps(f"pb{i}", [128, CH], F32) for i in range(6)]
            c.pstat = c.ps("pstat", [128, CH], F32)
            c.ptr = c.ps("ptr", [128, 128], BF16)
            c.pb_use = [[] for _ in range(6)]
            c.pstat_use = []
            c.ptr_use = []

            c.carve_reset()
            phase_s1(c)
            need_kv = not c.dev or "kv" in c.dev or "attn" in c.dev or "full" in c.dev
            need_ffn = not c.dev or "ffn" in c.dev or "full" in c.dev
            need_attn = not c.dev or "attn" in c.dev or "full" in c.dev
            if need_kv:
                phase_wprep(c)
            if need_ffn:
                phase_dequant(c)
            p.gate_all(c.group_barrier())
            c.carve_reset()
            if need_kv:
                phase_kv(c)
                p.gate_all(c.group_barrier())
                c.carve_reset()
            if need_ffn:
                phase_ffn(c)
                p.gate_all(c.group_barrier())
                c.carve_reset()
            if need_attn:
                phase_attn(c)

            p.final_barrier("sync")
            blk = E(nc.Block())
            p.emit(blk)
        return self.nc


def phase_s1(c):
    p, sb = c.p, c.sb
    nt = min(c.nt, getattr(c, "S1N", 10 ** 9))
    C_EPS128 = 128.0 * LN_EPS / (SCALE2 * D)

    r_x = p.ring("rx", 3)
    r_z = p.ring_slots("rz", 2)
    r_xn = p.ring_slots("rxn", 2)
    s_eps = p.sem("s_eps")

    xb = [c.cf(D) for _ in range(3)]
    junk = c.cf(D)
    xnb = [c.cb(D) for _ in range(2)]
    zb = [c.cb(D) for _ in range(2)]
    cols = [c.cf(16) for _ in range(2)]
    epscols = c.cb(64)
    epscols2 = c.cb(16)
    eps_ps = c.ptr
    eps_sb = c.cb(128)

    st = dict(last_use=[None, None, None], xn_done=[None, None], z_done=[None, None],
              col_done=[None, None], t_ec=None)

    def tile(i, src_rows, zdst, ecol, ecol_i, xndst):
        bi = i % 3
        ci = i % 2
        zi = i % 2
        cc_ = cols[ci]
        xt = xb[bi]
        tload = p.dma("sync", xt[:], src_rows, reads=[st["last_use"][bi]], sig=r_x[bi])
        t1 = p.op("scalar", lambda e, xt=xt, cc=cc_: e.activation(
            out=junk[:], in_=xt[:], func=AF.Copy, accum_out=cc[:, 0:1]),
            reads=[tload, st["col_done"][ci]], sig=c.s_act)
        t2 = p.op("scalar", lambda e, xt=xt, cc=cc_: e.activation(
            out=junk[:], in_=xt[:], func=AF.Square, accum_out=cc[:, 1:2]),
            reads=[tload], sig=c.s_act)

        def sm(e, cc=cc_):
            e.tensor_scalar(out=cc[:, 2:3], in0=cc[:, 0:1], scalar1=1.0 / D,
                            scalar2=None, op0=ALU.mult)
            e.tensor_scalar(out=cc[:, 3:4], in0=cc[:, 1:2], scalar1=1.0 / D,
                            scalar2=None, op0=ALU.mult)
            e.drain()
            e.scalar_tensor_tensor(out=cc[:, 4:5], in0=cc[:, 2:3], scalar=-1.0,
                                   in1=cc[:, 2:3], op0=ALU.mult, op1=ALU.mult)
            e.drain()
            return e.tensor_tensor(out=cc[:, 4:5], in0=cc[:, 3:4], in1=cc[:, 4:5],
                                   op=ALU.add)
        tsm = p.op("vector", sm, reads=[t1, t2, st["col_done"][ci]], sig=c.s_dve)
        tln = p.op("scalar", lambda e, cc=cc_: e.activation(
            out=cc[:, 5:6], in_=cc[:, 4:5], func=AF.Ln, bias=c.lneps[:]),
            reads=[tsm, c.t_const], sig=c.s_act)
        p.drain("scalar")
        trs = p.op("scalar", lambda e, cc=cc_: e.activation(
            out=cc[:, 5:6], in_=cc[:, 5:6], func=AF.Exp, scale=-0.5), sig=c.s_act)

        def sm2(e, cc=cc_):
            e.tensor_tensor(out=cc[:, 6:7], in0=cc[:, 5:6], in1=cc[:, 5:6], op=ALU.mult)
            e.drain()
            return e.tensor_tensor(out=cc[:, 6:7], in0=cc[:, 6:7], in1=cc[:, 4:5],
                                   op=ALU.mult)
        tsm2 = p.op("vector", sm2, reads=[trs], sig=c.s_dve)
        tln2 = p.op("scalar", lambda e, cc=cc_: e.activation(
            out=cc[:, 7:8], in_=cc[:, 6:7], func=AF.Ln, bias=c.lneps[:]),
            reads=[tsm2], sig=c.s_act)
        p.drain("scalar")
        tr2 = p.op("scalar", lambda e, cc=cc_: e.activation(
            out=cc[:, 7:8], in_=cc[:, 7:8], func=AF.Exp, scale=-0.5), sig=c.s_act)

        def sm3(e, cc=cc_, ecol=ecol, i_=ecol_i):
            e.tensor_tensor(out=cc[:, 8:9], in0=cc[:, 5:6], in1=cc[:, 7:8], op=ALU.mult)
            e.scalar_tensor_tensor(out=cc[:, 9:10], in0=cc[:, 2:3], scalar=-1.0,
                                   in1=cc[:, 5:6], op0=ALU.mult, op1=ALU.mult)
            e.reciprocal(out=cc[:, 11:12], in_=cc[:, 6:7])
            e.drain()
            e.scalar_tensor_tensor(out=cc[:, 10:11], in0=cc[:, 2:3], scalar=-1.0,
                                   in1=cc[:, 8:9], op0=ALU.mult, op1=ALU.mult)
            return e.tensor_scalar(out=ecol[:, i_:i_ + 1], in0=cc[:, 11:12],
                                   scalar1=C_EPS128, scalar2=None, op0=ALU.mult)
        tsm3 = p.op("vector", sm3, reads=[tr2], sig=c.s_dve)
        st["t_ec"] = tsm3

        tz = p.op("gpsimd", lambda e, xt=xt, cc=cc_, zo=zb[zi]: e.tensor_scalar(
            out=zo[:], in0=xt[:], scalar1=cc[:, 8:9], scalar2=cc[:, 10:11],
            op0=ALU.mult, op1=ALU.add), reads=[tsm3, st["z_done"][zi], tload],
            sig=c.s_pool)
        tzs = p.dma("gpsimd", zdst, zb[zi][:], reads=[tz], sig=r_z[zi])
        st["z_done"][zi] = tzs
        uses = [tz]
        if xndst is not None:
            p.drain("vector")
            txn = p.op("vector", lambda e, xt=xt, cc=cc_, xo=xnb[zi]: e.tensor_scalar(
                out=xo[:], in0=xt[:], scalar1=cc[:, 5:6], scalar2=cc[:, 9:10],
                op0=ALU.mult, op1=ALU.add), reads=[tsm3, st["xn_done"][zi], tload],
                sig=c.s_dve)
            txs = p.dma("gpsimd", xndst, xnb[zi][:], reads=[txn], sig=r_xn[zi])
            st["xn_done"][zi] = txs
            uses.append(txn)
        st["last_use"][bi] = uses
        st["col_done"][ci] = uses

    for i in range(nt):
        tile(i, c.xpre[i * 128:(i + 1) * 128, :],
             c.z_tok[i * 128:(i + 1) * 128, :], epscols, i, None)
    t_ec_pre = st["t_ec"]
    nt2 = 16
    for i in range(nt2):
        tile(i, c.xown[i * 128:(i + 1) * 128, :],
             c.zown[i * 128:(i + 1) * 128, :], epscols2, i,
             c.xn_tok[i * 128:(i + 1) * 128, :])

    tec = p.op("tensor", lambda e: e.transpose(
        out=eps_ps[0:nt, 0:128], in_=epscols[:, 0:nt], identity=c.identb[:]),
        reads=[c.t_const, t_ec_pre, st["t_ec"]], sig=c.s_pe)
    tev = p.op("vector", lambda e: e.tensor_copy(out=eps_sb[0:nt, :],
                                                 in_=eps_ps[0:nt, 0:128]),
               reads=[tec], sig=c.s_dve)
    tes = p.dma("gpsimd", c.epst[0:nt, :], eps_sb[0:nt, :], reads=[tev], sig=s_eps)
    tec2 = p.op("tensor", lambda e: e.transpose(
        out=eps_ps[0:nt2, 0:128], in_=epscols2[:, 0:nt2], identity=c.identb[:]),
        reads=[st["t_ec"], tev], sig=c.s_pe)
    tev2 = p.op("vector", lambda e: e.tensor_copy(out=eps_sb[0:nt2, :],
                                                  in_=eps_ps[0:nt2, 0:128]),
               reads=[tec2, tes], sig=c.s_dve)
    tes2 = p.dma("gpsimd", c.epso[0:nt2, :], eps_sb[0:nt2, :], reads=[tev2], sig=s_eps)
    c.ptr_use = [tev2]
    c.t_eps = Tok(s_eps, s_eps.count, "_e_", True)
    c.t_epso = c.t_eps
    c.t_z_done = [t for t in st["z_done"] if t is not None]
    c.t_xn_done = [t for t in st["xn_done"] if t is not None]

    if "d_xn" in c.dbg:
        p.dma("gpsimd", c.dbg["d_xn"], c.xn_tok, reads=c.t_xn_done, sig=c.s_store)
        p.dma("gpsimd", c.dbg["d_z"], c.z_tok, reads=c.t_z_done, sig=c.s_store)
        p.dma("gpsimd", c.dbg["d_eps"][0:nt, :], c.epst[0:nt, :], reads=[tes],
              sig=c.s_store)


def phase_wprep(c):
    p, sb = c.p, c.sb

    s_v = p.sem("s_wv")
    gq = c.cf(16)
    gkv = c.cf(16)
    bqn = c.cf(16)
    bkvn = c.cf(16)
    t1 = p.dma("sync", gq[:], c.qng, sig=s_v)
    t2 = p.dma("sync", gkv[:], c.kvg, sig=s_v)
    t3 = p.dma("sync", bqn[:], c.qnbr, sig=s_v)
    t4 = p.dma("sync", bkvn[:], c.kvbr, sig=s_v)
    s_r = p.sem("s_wr")
    rows_fv = c.cf(4 * D)
    rows_f = rows_fv[0:1, :]
    t5 = p.dma("sync", rows_f[:, 0:D], c.bqr, sig=s_r)
    t6 = p.dma("sync", rows_f[:, D:3 * D], c.bkvr, sig=s_r)
    t7 = p.dma("sync", rows_f[:, 3 * D:4 * D], c.bor, sig=s_r)
    lrv = c.cf(2 * HD)
    lr = lrv[0:1, :]
    t8 = p.dma("sync", lr[:, 0:HD], c.lngr, sig=s_r)
    t9 = p.dma("sync", lr[:, HD:2 * HD], c.lnbr, sig=s_r)
    tv = [t1, t2, t3, t4]
    tr = [t5, t6, t7, t8, t9]

    bqn_bf = c.cb(16)
    bkvn_bf = c.cb(16)
    tb1 = p.op("vector", lambda e: e.tensor_copy(out=bqn_bf[:], in_=bqn[:]),
               reads=[tv], sig=c.s_dve)
    tb2 = p.op("vector", lambda e: e.tensor_copy(out=bkvn_bf[:], in_=bkvn[:]),
               reads=[tv], sig=c.s_dve)
    c.lrow = sb("lrow", [1, 4 * HD], BF16)
    def mkrows(e):
        e.tensor_copy(out=c.lrow[:, 0:2 * HD], in_=lr[:, 0:2 * HD])
        return e.tensor_scalar(out=c.lrow[:, 2 * HD:4 * HD], in0=lr[:, 0:2 * HD],
                               scalar1=float(SCALE2), scalar2=None, op0=ALU.mult)
    tlr = p.op("vector", mkrows, reads=[tr], sig=c.s_dve)
    c.t_lrow = tlr

    c.bo_bf = sb("bo_bf", [1, D], BF16)
    tbo = p.op("vector", lambda e: e.tensor_copy(out=c.bo_bf[:], in_=rows_f[:, 3 * D:4 * D]),
               reads=[tr], sig=c.s_dve)
    c.t_bo = tbo

    r_w = p.ring("rw", 4)
    wbuf = [c.cf(CH) for _ in range(4)]
    wub = [c.cb(CH) for _ in range(2)]
    wsb = [c.cb(CH) for _ in range(2)]
    r_ws = p.ring_slots("rws", 2)

    c.browq_bf = sb("browq_bf", [1, D], BF16)
    c.browkv_bf = sb("browkv_bf", [1, 2 * D], BF16)

    brow_ps = c.pstat
    specs = [
        ("wq", c.wq, c.wqP, 4, gq, bqn_bf, c.browq_bf, 0),
        ("wkv", c.wkv, c.wkvP, 8, gkv, bkvn_bf, c.browkv_bf, D),
        ("wo", c.wo, c.woP, 4, None, None, None, 3 * D),
    ]
    wu_done = [None, None]
    ws_done = [None, None]
    wl_done = [None, None, None, None]
    t_brow_es = []
    for (name, wsrc, wdst, jmax, gcol, bcol, browdst, boff) in specs:
        for j in range(jmax):
            mm_toks = []
            for K in range(16):
                i4 = (j * 16 + K) % 4
                i2 = (j * 16 + K) % 2
                tl = p.dma("sync", wbuf[i4][:], wsrc[K * 128:(K + 1) * 128, j * CH:(j + 1) * CH],
                           reads=[wl_done[i4]], sig=r_w[i4])
                tc = p.op("vector", lambda e, a=wub[i2], b=wbuf[i4]: e.tensor_copy(
                    out=a[:], in_=b[:]), reads=[tl, wu_done[i2]], sig=c.s_dve)
                uses = [tc]
                if gcol is not None:
                    tmm = p.op("tensor", lambda e, K=K, i2=i2, b=bcol: e.matmul(
                        out=brow_ps[0:1, 0:CH], lhsT=b[:, K:K + 1], rhs=wub[i2][:],
                        start=(K == 0), stop=(K == 15)),
                        reads=[tc, tb1, tb2] + (c.pstat_use if K == 0 else []),
                        sig=c.s_pe)
                    mm_toks.append(tmm)
                    tsc = p.op("vector", lambda e, K=K, i2=i2, g=gcol: e.tensor_scalar(
                        out=wsb[i2][:], in0=wub[i2][:], scalar1=g[:, K:K + 1],
                        scalar2=None, op0=ALU.mult), reads=[tv, ws_done[i2]], sig=c.s_dve)
                    tst = p.dma("gpsimd", wdst[K * 128:(K + 1) * 128, j * CH:(j + 1) * CH],
                                wsb[i2][:], reads=[tsc], sig=r_ws[i2])
                    ws_done[i2] = tst
                    uses.append(tsc)
                    wu_done[i2] = uses
                else:
                    tst = p.dma("gpsimd", wdst[K * 128:(K + 1) * 128, j * CH:(j + 1) * CH],
                                wub[i2][:], reads=[tc], sig=r_ws[i2])
                    wu_done[i2] = [tst]
                wl_done[i4] = uses if gcol is None else uses
                if gcol is None:
                    wl_done[i4] = [tc]
            if gcol is not None:
                c.pstat_use = []
                tadd = p.op("vector", lambda e, j=j, boff=boff, browdst=browdst: e.tensor_tensor(
                    out=browdst[0:1, j * CH:(j + 1) * CH], in0=brow_ps[0:1, 0:CH],
                    in1=rows_f[0:1, boff + j * CH:boff + (j + 1) * CH], op=ALU.add),
                    reads=[mm_toks[-1], tr], sig=c.s_dve)
                c.pstat_use.append(tadd)
                t_brow_es.append(tadd)
    c.t_brow = t_brow_es
    c.t_wqP = ws_done + wu_done
    c.t_wprep_end = [Tok(s, s.count, "_w_", True) for s in (r_ws[0], r_ws[1])]


def phase_kv(c):
    p, sb = c.p, c.sb
    nch = min(c.npre, getattr(c, "KVN", 10 ** 9))

    r_zt = p.ring("rzt", 2)
    r_wk = c.rw12 = p.ring("rw12", 12)
    r_cs = p.ring("rcs", 2)
    r_eps = p.ring("reps", 2)
    r_row = p.ring("rrow", 2)
    r_kst = p.ring_slots("rkst", 2)
    r_vst = p.ring_slots("rvst", 2)
    r_sw = p.ring("rsw", 2)
    s_rb = p.sem("s_rowb")

    zT = [[c.cb(CH) for K in range(16)] for r in range(2)]
    wk4 = [c.cb(CH) for _ in range(12)]
    cosc = [c.cf(CH) for _ in range(2)]
    sinc = [c.cf(CH) for _ in range(2)]
    epsr = [c.cb(CH)[0:1, :] for _ in range(2)]
    krot = [c.cf(CH) for _ in range(2)]
    krot_sw = [c.cf(CH) for _ in range(2)]
    p1 = [c.cf(CH) for _ in range(2)]
    p2 = [c.cf(CH) for _ in range(2)]
    rotb = [c.cb(CH) for _ in range(16)]
    rot2 = [c.cb(CH) for _ in range(2)]
    kn2 = [c.cb(CH) for _ in range(2)]
    stats = c.cf(5 * CH)[0:16, :]
    statb = c.cb(2 * CH)[0:16, :]
    rowsb = c.cf(8 * CH).bitcast(BF16)[0:1, :]
    knsb = [c.cb(CH) for _ in range(2)]
    tmpsb = [c.cb(CH) for _ in range(2)]
    vsb = [c.cb(CH) for _ in range(2)]
    maxcol = c.cf(2)[0:16, 0:1]
    c.maxk2 = sb("maxk2", [16, 1], F32)

    zt_done = [None, None]
    wk_done = [None] * 12
    cs_done = [None, None]
    eps_done = [None, None]
    row_done = [None, None]
    kn_done = [None, None]
    tmp_done = [None, None]
    v_done = [None, None]
    stats_done = None
    statb_done = None
    rot_done = [None] * 16
    rot2_done = [None, None]
    kn2_done = [None, None]
    tsq_pend = [None, None]
    krot_done = [None, None]
    p12_done = [None, None]

    tmk0 = p.op("vector", lambda e: e.memset(c.maxk2[:], 0.0), sig=c.s_dve)
    maxk_done = tmk0

    s1v = c.pstat[0:16, :]
    s2v = c.pb[4][0:16, :]
    knv = c.pb[5][0:16, :]

    wkc = 0
    for cc in range(nch):
        ri = cc % 2
        tzt = None
        for K in range(16):
            tzt = p.dmat("sync", zT[ri][K][:],
                         c.z_tok[cc * CH:(cc + 1) * CH, K * 128:(K + 1) * 128],
                         reads=[zt_done[ri], c.t_z_done], sig=r_zt[ri])
        tcs = p.dma("sync", cosc[ri][:], c.cosdr[:, cc * CH:(cc + 1) * CH],
                    reads=[cs_done[ri]], sig=r_cs[ri])
        tsn = p.dma("sync", sinc[ri][:], c.sindr[:, cc * CH:(cc + 1) * CH],
                    reads=[cs_done[ri]], sig=r_cs[ri])
        tep = p.dma("sync", epsr[ri][:],
                    c.epst.rearrange("a b -> (a b)")[cc * CH:(cc + 1) * CH].unsqueeze(0),
                    reads=[eps_done[ri], c.t_eps], sig=r_eps[ri])

        stat_last = [None, None]
        for hg in range(4):
            mmg = [None] * 4
            for K in range(16):
                wi = wkc % 12
                wkc += 1
                tw = p.dma("sync", wk4[wi][:],
                           c.wkvP[K * 128:(K + 1) * 128, hg * CH:(hg + 1) * CH],
                           reads=[wk_done[wi], c.t_wprep_end], sig=r_wk[wi])
                for hi in range(4):
                    mmg[hi] = p.op("tensor", lambda e, K=K, wi=wi, hi=hi, ri=ri: e.matmul(
                        out=c.pb[hi][:], lhsT=wk4[wi][:, hi * 128:(hi + 1) * 128],
                        rhs=zT[ri][K][:], start=(K == 0), stop=False),
                        reads=[tw, tzt] + (c.pb_use[hi] if K == 0 else []), sig=c.s_pe)
                wk_done[wi] = mmg[3]
            for hi in range(4):
                h = hg * 4 + hi
                pbi = h % 2
                c.pb_use[hi] = []
                mmb = p.op("tensor", lambda e, h=h, hi=hi: e.matmul(
                    out=c.pb[hi][:], lhsT=c.browkv_bf[0:1, h * 128:(h + 1) * 128],
                    rhs=c.ones_row[:], start=False, stop=True),
                    reads=[mmg[hi], c.t_brow, c.t_const], sig=c.s_pe)
                tev = p.op("scalar", lambda e, hi=hi, ki=pbi: e.activation(
                    out=krot[ki][:], in_=c.pb[hi][:], func=AF.Copy),
                    reads=[mmb, krot_done[pbi]], sig=c.s_act)
                c.pb_use[hi].append(tev)
                tw1 = p.dma("gpsimd", krot_sw[pbi][0:64, :], krot[pbi][64:128, :],
                            reads=[tev, krot_done[pbi]], sig=r_sw[pbi])
                tw2 = p.dma("gpsimd", krot_sw[pbi][64:128, :], krot[pbi][0:64, :],
                            sig=r_sw[pbi])
                tp1 = p.op("gpsimd", lambda e, ri=ri, ki=pbi: e.tensor_mul(
                    out=p1[ki][:], in0=krot[ki][:], in1=cosc[ri][:]),
                    reads=[tev, tsn, p12_done[pbi]], sig=c.s_pool)
                tp2 = p.op("gpsimd", lambda e, ri=ri, ki=pbi: e.tensor_mul(
                    out=p2[ki][:], in0=krot_sw[ki][:], in1=sinc[ri][:]),
                    reads=[tw2, tsn], sig=c.s_pool)
                krot_done[pbi] = [tp1, tp2]
                tcb = p.op("vector", lambda e, h=h, ki=pbi: e.tensor_add(
                    out=rotb[h][:], in0=p1[ki][:], in1=p2[ki][:]),
                    reads=[tp1, tp2, rot_done[h]], sig=c.s_dve)
                p12_done[pbi] = tcb
                rot_done[h] = [tcb]
        for h in range(16):
            ki = h % 2
            tsq = p.op("scalar", lambda e, h=h, ki=ki: e.activation(
                out=rot2[ki][:], in_=rotb[h][:], func=AF.Square),
                reads=[rot_done[h], rot2_done[ki]], sig=c.s_act)
            ts1 = p.op("tensor", lambda e, h=h: e.matmul(
                out=s1v, lhsT=c.ehs[:, h * 16:(h + 1) * 16], rhs=rotb[h][:],
                start=(h == 0), stop=(h == 15)),
                reads=[rot_done[h], c.t_const] + (c.pstat_use if h == 0 else []),
                sig=c.s_pe)
            ts2 = p.op("tensor", lambda e, h=h, ki=ki: e.matmul(
                out=s2v, lhsT=c.ehs[:, h * 16:(h + 1) * 16], rhs=rot2[ki][:],
                start=(h == 0), stop=False),
                reads=[tsq] + (c.pb_use[4] if h == 0 else []), sig=c.s_pe)
            rot2_done[ki] = ts2
            stat_last = [ts1, ts2]
        tse = p.op("tensor", lambda e, ri=ri: e.matmul(
            out=s2v, lhsT=c.ones_row[0:1, 0:16], rhs=epsr[ri][:],
            start=False, stop=True), reads=[tep, stat_last[1], c.t_const], sig=c.s_pe)
        eps_done[ri] = tse

        def smk(e):
            e.tensor_scalar(out=stats[:, 0:CH], in0=s1v, scalar1=1.0 / 128,
                            scalar2=None, op0=ALU.mult)
            e.tensor_scalar(out=stats[:, CH:2 * CH], in0=s2v, scalar1=1.0 / 128,
                            scalar2=None, op0=ALU.mult)
            e.drain()
            e.scalar_tensor_tensor(out=stats[:, 2 * CH:3 * CH], in0=stats[:, 0:CH],
                                   scalar=-1.0, in1=stats[:, 0:CH],
                                   op0=ALU.mult, op1=ALU.mult)
            e.drain()
            return e.tensor_tensor(out=stats[:, 2 * CH:3 * CH],
                                   in0=stats[:, CH:2 * CH],
                                   in1=stats[:, 2 * CH:3 * CH], op=ALU.add)
        tsk = p.op("vector", smk, reads=[tse, stat_last[0], stats_done], sig=c.s_dve)
        tl1 = p.op("scalar", lambda e: e.activation(
            out=stats[:, 3 * CH:4 * CH], in_=stats[:, 2 * CH:3 * CH], func=AF.Ln),
            reads=[tsk], sig=c.s_act)
        p.drain("scalar")
        tl2 = p.op("scalar", lambda e: e.activation(
            out=stats[:, 3 * CH:4 * CH], in_=stats[:, 3 * CH:4 * CH], func=AF.Exp,
            scale=-0.5), sig=c.s_act)

        def smk2(e):
            e.scalar_tensor_tensor(out=stats[:, 4 * CH:5 * CH], in0=stats[:, 0:CH],
                                   scalar=-1.0, in1=stats[:, 3 * CH:4 * CH],
                                   op0=ALU.mult, op1=ALU.mult)
            e.tensor_copy(out=statb[:, 0:CH], in_=stats[:, 3 * CH:4 * CH])
            e.drain()
            return e.tensor_copy(out=statb[:, CH:2 * CH], in_=stats[:, 4 * CH:5 * CH])
        tsk2 = p.op("vector", smk2, reads=[tl2, statb_done], sig=c.s_dve)
        stats_done = tsk2
        c.pstat_use = [tsk]
        c.pb_use[4] = [tsk]

        trb = p.dma("gpsimd", c.rowk, statb[:],
                    reads=[tsk2, getattr(c, "_rowk_read", [])], sig=s_rb)
        statb_done = trb

        for vd in range(4):
            vps = [c.pb[t] for t in range(4)]
            mm_last = [None] * 4
            for K in range(16):
                wi = wkc % 12
                wkc += 1
                tw = p.dma("sync", wk4[wi][:],
                           c.wkvP[K * 128:(K + 1) * 128, 2048 + vd * CH:2048 + (vd + 1) * CH],
                           reads=[wk_done[wi], c.t_wprep_end], sig=r_wk[wi])
                for t4 in range(4):
                    mm_last[t4] = p.op("tensor", lambda e, K=K, wi=wi, t4=t4, ri=ri, vv=vps[t4]: e.matmul(
                        out=vv[:], lhsT=zT[ri][K][:, t4 * 128:(t4 + 1) * 128],
                        rhs=wk4[wi][:], start=(K == 0), stop=False),
                        reads=[tw, tzt] + (c.pb_use[t4] if K == 0 else []), sig=c.s_pe)
                wk_done[wi] = mm_last[3]
            for t4 in range(4):
                c.pb_use[t4] = []
                mmb = p.op("tensor", lambda e, vd=vd, vv=vps[t4]: e.matmul(
                    out=vv[:], lhsT=c.ones_row[0:1, 0:128],
                    rhs=c.browkv_bf[0:1, 2048 + vd * CH:2048 + (vd + 1) * CH],
                    start=False, stop=True), reads=[mm_last[t4], c.t_brow], sig=c.s_pe)
                si = t4 % 2
                tev = p.op("scalar", lambda e, si=si, vv=vps[t4]: e.activation(
                    out=vsb[si][:], in_=vv[:], func=AF.Copy),
                    reads=[mmb, v_done[si]], sig=c.s_act)
                c.pb_use[t4].append(tev)
                kt = cc * 4 + t4
                tvs = p.dma("gpsimd",
                            c.vtok[kt, vd * 4:(vd + 1) * 4].rearrange("h t d -> t h d"),
                            vsb[si][:].rearrange("t (h d) -> t h d", h=4),
                            reads=[tev], sig=r_vst[si])
                v_done[si] = tvs
        rowk_reads = []
        row_users = []
        for h in range(16):
            si = h % 2
            pc = c.pb[2 * si]
            pd = c.pb[2 * si + 1]
            if h % 8 == 0:
                rflat = c.rowk.rearrange("a b -> (a b)")
                trh = p.dma("sync", rowsb[:],
                            rflat[h * 2 * CH:(h + 8) * 2 * CH].unsqueeze(0),
                            reads=[trb, row_done[0]], sig=r_row[0])
                rowk_reads.append(trh)
                row_done[0] = []
            ho = (h % 8) * 2 * CH
            tmc = p.op("tensor", lambda e, ho=ho, pc=pc: e.matmul(
                out=pc[:], lhsT=c.lrow[0:1, 2 * HD:3 * HD],
                rhs=rowsb[0:1, ho:ho + CH], start=True, stop=True),
                reads=[trh, c.t_lrow] + c.pb_use[2 * si], sig=c.s_pe)
            tmd1 = p.op("tensor", lambda e, ho=ho, pd=pd: e.matmul(
                out=pd[:], lhsT=c.lrow[0:1, 2 * HD:3 * HD],
                rhs=rowsb[0:1, ho + CH:ho + 2 * CH], start=True, stop=False),
                reads=[trh] + c.pb_use[2 * si + 1], sig=c.s_pe)
            tmd2 = p.op("tensor", lambda e, pd=pd: e.matmul(
                out=pd[:], lhsT=c.lrow[0:1, 3 * HD:4 * HD], rhs=c.ones_row[:],
                start=False, stop=True), sig=c.s_pe)
            row_users += [tmc, tmd1]
            row_done[0] = row_done[0] + [tmc, tmd1] if isinstance(row_done[0], list) else [tmc, tmd1]
            c.pb_use[2 * si] = []
            c.pb_use[2 * si + 1] = []
            tt1 = p.op("vector", lambda e, h=h, si=si, pc=pc: e.tensor_tensor(
                out=tmpsb[si][:], in0=rotb[h][:], in1=pc[:], op=ALU.mult),
                reads=[tmc, tmp_done[si]], sig=c.s_dve)
            tt2 = p.op("vector", lambda e, si=si, pd=pd: e.tensor_tensor(
                out=knsb[si][:], in0=tmpsb[si][:], in1=pd[:], op=ALU.add),
                reads=[tmd2, kn_done[si]], sig=c.s_dve)
            rot_done[h] = [tt1]
            c.pb_use[2 * si].append(tt1)
            c.pb_use[2 * si + 1].append(tt2)
            tmp_done[si] = tt2
            if h >= 2:
                hd2 = h - 2
                tkn = p.op("tensor", lambda e, hd2=hd2, si=si: e.matmul(
                    out=knv, lhsT=c.ehs[:, hd2 * 16:(hd2 + 1) * 16], rhs=kn2[si][:],
                    start=(hd2 == 0), stop=False),
                    reads=[tsq_pend[si]] + (c.pb_use[5] if hd2 == 0 else []),
                    sig=c.s_pe)
                kn2_done[si] = tkn
            tsq2 = p.op("scalar", lambda e, si=si: e.activation(
                out=kn2[si][:], in_=knsb[si][:], func=AF.Square),
                reads=[tt2, kn2_done[si]], sig=c.s_act)
            tsq_pend[si] = tsq2
            tst = p.dma("gpsimd", c.knT[h * 128:(h + 1) * 128, cc * CH:(cc + 1) * CH],
                        knsb[si][:], reads=[tt2], sig=r_kst[si])
            kn_done[si] = [tst, tsq2]
        for hd2 in (14, 15):
            si = hd2 % 2
            tkn = p.op("tensor", lambda e, hd2=hd2, si=si: e.matmul(
                out=knv, lhsT=c.ehs[:, hd2 * 16:(hd2 + 1) * 16], rhs=kn2[si][:],
                start=False, stop=(hd2 == 15)),
                reads=[tsq_pend[si]], sig=c.s_pe)
            kn2_done[si] = tkn
        c._rowk_read = rowk_reads
        tmx = p.op("vector", lambda e: e.reduce_max(out=maxcol[:], in_=knv, axis=AX.X),
                   reads=[kn2_done[0], kn2_done[1], maxk_done], sig=c.s_dve)
        p.drain("vector")
        tmx2 = p.op("vector", lambda e: e.tensor_max(
            out=c.maxk2[:], in0=c.maxk2[:], in1=maxcol[:]), sig=c.s_dve)
        maxk_done = tmx2
        c.pb_use[5] = [tmx]

        zt_done[ri] = [mm_last[t] for t in range(4)]
        cs_done[ri] = krot_done[:]

    c.t_kv_end = [Tok(s, s.count, "_kv_", True)
                  for s in (r_kst[0], r_kst[1], r_vst[0], r_vst[1])]
    c.t_maxk = maxk_done

    if "d_kn" in c.dbg:
        p.dma("gpsimd", c.dbg["d_kn"][:, 0:nch * CH], c.knT[:, 0:nch * CH],
              reads=[c.t_kv_end], sig=c.s_store)
        p.dma("gpsimd", c.dbg["d_mk"].rearrange("o s -> s o"), c.maxk2[:],
              reads=[maxk_done], sig=c.s_store)


def phase_dequant(c):
    p = c.p
    r_nb = p.ring("rnb", 2)
    r_sc = p.ring("rsc", 2)
    r_wo = p.ring_slots("rwo", 2)
    nib = [c.cu(CH) for _ in range(2)]
    t2b = [c.cu(CH) for _ in range(2)]
    bits = [c.cu(CH) for _ in range(2)]
    scf = [c.cf(CH) for _ in range(2)]
    wout = [c.cb(CH) for _ in range(2)]
    sc16 = [c.sb(f"sc16_{i}", [128, CH], F16) for i in range(2)]

    nib_done = [None, None]
    t2_done = [None, None]
    bits_done = [None, None]
    sc_done = [None, None]
    scf_done = [None, None]
    w_done = [None, None]

    jobs = []
    for g in range(16):
        for j in range(2 * I_FFN // CH):
            jobs.append(("gu", g, j))
    for g in range(44):
        for j in range(D // CH):
            jobs.append(("dn", g, j))
    if getattr(c, "DQN", None):
        jobs = jobs[:c.DQN]

    for idx, (kind, g, j) in enumerate(jobs):
        si = idx % 2
        if kind == "gu":
            nsrc = c.gunib[g * 128:(g + 1) * 128, j * CH:(j + 1) * CH]
            ssrc = c.gusc[g, :, j * CH:(j + 1) * CH]
            wdst = c.wguT[g * 128:(g + 1) * 128, j * CH:(j + 1) * CH]
        else:
            nsrc = c.dnnib[g * 128:(g + 1) * 128, j * CH:(j + 1) * CH]
            ssrc = c.dnsc[g, :, j * CH:(j + 1) * CH]
            wdst = c.wdT[g * 128:(g + 1) * 128, j * CH:(j + 1) * CH]
        tn = p.dma("sync", nib[si][:], nsrc, reads=[nib_done[si]], sig=r_nb[si])
        tsc = p.dma("sync", sc16[si][:], ssrc, reads=[sc_done[si]], sig=r_sc[si])
        tt2 = p.op("vector", lambda e, si=si: e.tensor_scalar(
            out=t2b[si][:], in0=nib[si][:], scalar1=8, scalar2=32256,
            op0=ALU.is_ge, op1=ALU.mult), reads=[tn, t2_done[si]], sig=c.s_dve)
        tbi = p.op("vector", lambda e, si=si: e.scalar_tensor_tensor(
            out=bits[si][:], in0=nib[si][:], scalar=64.0, in1=t2b[si][:],
            op0=ALU.mult, op1=ALU.add), reads=[tt2, bits_done[si]], sig=c.s_dve)
        nib_done[si] = tbi
        t2_done[si] = tbi
        tsf = p.op("vector", lambda e, si=si: e.tensor_scalar(
            out=scf[si][:], in0=sc16[si][:], scalar1=float(2.0 ** 126), scalar2=None,
            op0=ALU.mult), reads=[tsc, scf_done[si]], sig=c.s_dve)
        sc_done[si] = tsf
        tw = p.op("vector", lambda e, si=si: e.tensor_tensor(
            out=wout[si][:], in0=bits[si][:].bitcast(BF16)[:, 0:CH], in1=scf[si][:],
            op=ALU.mult), reads=[tbi, tsf, w_done[si]], sig=c.s_dve)
        bits_done[si] = tw
        scf_done[si] = tw
        tst = p.dma("gpsimd", wdst, wout[si][:], reads=[tw], sig=r_wo[si])
        w_done[si] = tst
    c.t_deq_end = [Tok(s, s.count, "_dq_", True) for s in (r_wo[0], r_wo[1])]
    if "d_wgu" in c.dbg:
        p.dma("gpsimd", c.dbg["d_wgu"], c.wguT, reads=[c.t_deq_end], sig=c.s_store)
        p.dma("gpsimd", c.dbg["d_wd"], c.wdT, reads=[c.t_deq_end], sig=c.s_store)


def phase_ffn(c):
    p = c.p
    noc = min(4, getattr(c, "FFNN", 4))

    r_xt = p.ring("rxt", 2)
    r_wg = c.rw12
    r_wd = p.ring("rwd", 8)
    r_ys = p.ring_slots("rys", 2)

    xnT = [c.cb(CH) for _ in range(16)]
    act = [c.cb(CH) for _ in range(44)]
    sg = [c.cb(CH) for _ in range(4)]
    upb = [c.cb(CH) for _ in range(4)]
    wg = [c.cb(CH) for _ in range(12)]
    wd = [c.cb(CH) for _ in range(8)]
    ysb = [c.cf(CH) for _ in range(2)]
    sqjunk = c.cf(CH)
    sqc = c.cf(16)
    rcol = c.cf(4)
    c.ffn_rstd = c.sb("ffn_rstd", [128, 16], F32)

    xnT_done = [None] * 16
    wg_done = [None] * 12
    wd_done = [None] * 8
    sg_done = [None] * 4
    up_done = [None] * 4
    act_done = [None] * 44
    y_done = [None, None]
    sq_done = None

    wgc = 0
    wdc = 0
    for oc in range(noc):
        txt = None
        for K in range(16):
            txt = p.dmat("sync", xnT[K][:],
                         c.xn_tok[oc * CH:(oc + 1) * CH, K * 128:(K + 1) * 128],
                         reads=[xnT_done[K], c.t_xn_done], sig=r_xt[oc % 2])
        for j4 in range(11):
            for half in range(2):
                dst = sg if half == 0 else upb
                dst_done = sg_done if half == 0 else up_done
                base = half * I_FFN + j4 * 4 * 128
                mmlast = [None] * 4
                for K in range(16):
                    wi = wgc % 12
                    wgc += 1
                    twl = p.dma("sync", wg[wi][:],
                                c.wguT[K * 128:(K + 1) * 128, base:base + CH],
                                reads=[wg_done[wi], c.t_deq_end], sig=r_wg[wi])
                    for o4 in range(4):
                        mmlast[o4] = p.op("tensor", lambda e, K=K, wi=wi, o4=o4: e.matmul(
                            out=c.pb[o4][:], lhsT=wg[wi][:, o4 * 128:(o4 + 1) * 128],
                            rhs=xnT[K][:], start=(K == 0), stop=(K == 15)),
                            reads=[twl, txt] + (c.pb_use[o4] if K == 0 else []),
                            sig=c.s_pe)
                    wg_done[wi] = mmlast[3]
                for o4 in range(4):
                    c.pb_use[o4] = []
                    if half == 0:
                        tev = p.op("scalar", lambda e, o4=o4: e.activation(
                            out=sg[o4][:], in_=c.pb[o4][:], func=AF.Silu),
                            reads=[mmlast[o4], sg_done[o4]], sig=c.s_act)
                        sg_done[o4] = tev
                    else:
                        tev = p.op("scalar", lambda e, o4=o4: e.activation(
                            out=upb[o4][:], in_=c.pb[o4][:], func=AF.Copy),
                            reads=[mmlast[o4], up_done[o4]], sig=c.s_act)
                        up_done[o4] = tev
                    c.pb_use[o4].append(tev)
            for o4 in range(4):
                jj = j4 * 4 + o4
                tmu = p.op("vector", lambda e, o4=o4, jj=jj: e.tensor_mul(
                    out=act[jj][:], in0=sg[o4][:], in1=upb[o4][:]),
                    reads=[sg_done[o4], up_done[o4], act_done[jj]], sig=c.s_dve)
                sg_done[o4] = [sg_done[o4], tmu]
                up_done[o4] = [up_done[o4], tmu]
                act_done[jj] = tmu
        for K in range(16):
            xnT_done[K] = wg_done[5]
        sq_toks = []
        for hj in range(4):
            mmlast = [None] * 4
            for i in range(44):
                wi = wdc % 8
                wdc += 1
                twl = p.dma("sync", wd[wi][:],
                            c.wdT[i * 128:(i + 1) * 128, hj * CH:(hj + 1) * CH],
                            reads=[wd_done[wi], c.t_deq_end], sig=r_wd[wi])
                for ts in range(4):
                    mmlast[ts] = p.op("tensor", lambda e, i=i, wi=wi, ts=ts: e.matmul(
                        out=c.pb[ts][:], lhsT=act[i][:, ts * 128:(ts + 1) * 128],
                        rhs=wd[wi][:], start=(i == 0), stop=(i == 43)),
                        reads=[twl, act_done[i]] + (c.pb_use[ts] if i == 0 else []),
                        sig=c.s_pe)
                wd_done[wi] = mmlast[3]
            for ts in range(4):
                c.pb_use[ts] = []
                si = ts % 2
                tyc = p.op("vector", lambda e, ts=ts, si=si: e.tensor_copy(
                    out=ysb[si][:], in_=c.pb[ts][:]), reads=[mmlast[ts], y_done[si]],
                    sig=c.s_dve)
                c.pb_use[ts].append(tyc)
                tsq = p.op("scalar", lambda e, ts=ts, hj=hj, si=si: e.activation(
                    out=sqjunk[:], in_=ysb[si][:], func=AF.Square,
                    accum_out=sqc[:, ts * 4 + hj:ts * 4 + hj + 1]),
                    reads=[tyc, sq_done], sig=c.s_act)
                tys = p.dma("gpsimd",
                            c.ffny[oc * CH + ts * 128: oc * CH + (ts + 1) * 128,
                                   hj * CH:(hj + 1) * CH],
                            ysb[si][:], reads=[tyc], sig=r_ys[si])
                y_done[si] = [tys, tsq]
                sq_toks.append(tsq)
        for i in range(44):
            act_done[i] = wd_done[5]
        for ts in range(4):
            def rms_ts(e, ts=ts, oc=oc):
                e.tensor_tensor(out=rcol[:, 0:1], in0=sqc[:, ts * 4:ts * 4 + 1],
                                in1=sqc[:, ts * 4 + 1:ts * 4 + 2], op=ALU.add)
                e.tensor_tensor(out=rcol[:, 1:2], in0=sqc[:, ts * 4 + 2:ts * 4 + 3],
                                in1=sqc[:, ts * 4 + 3:ts * 4 + 4], op=ALU.add)
                e.drain()
                return e.tensor_scalar(out=rcol[:, 2:3], in0=rcol[:, 0:1],
                                       scalar1=rcol[:, 1:2], scalar2=1.0 / D,
                                       op0=ALU.add, op1=ALU.mult)
            trm = p.op("vector", rms_ts, reads=sq_toks + [getattr(c, "_rms_prev", None)],
                       sig=c.s_dve)
            tl = p.op("scalar", lambda e: e.activation(
                out=rcol[:, 3:4], in_=rcol[:, 2:3], func=AF.Ln, bias=c.rmseps[:]),
                reads=[trm, c.t_const], sig=c.s_act)
            c._rms_prev = tl
            p.drain("scalar")
            tex = p.op("scalar", lambda e, oc=oc, ts=ts: e.activation(
                out=c.ffn_rstd[:, oc * 4 + ts:oc * 4 + ts + 1], in_=rcol[:, 3:4],
                func=AF.Exp, scale=-0.5), sig=c.s_act)
            sq_done = tex
            c.t_ffn_rstd = tex
    c.t_ffn_end = [Tok(s, s.count, "_ff_", True) for s in (r_ys[0], r_ys[1])]
    if "d_ffny" in c.dbg:
        p.dma("gpsimd", c.dbg["d_ffny"][0:noc * CH, :], c.ffny[0:noc * CH, :],
              reads=[c.t_ffn_end], sig=c.s_store)


def phase_attn(c):
    p, sb = c.p, c.sb
    nqc = min(4, getattr(c, "ATN", 4))
    SEG = 8

    r_zt = p.ring("azt", 1)
    r_wq = c.rw12
    r_cs = p.ring("acs", 2)
    r_eps = p.ring("aeps", 2)
    r_row = p.ring("arow", 2)
    r_kn = p.ring("akn", 3)
    r_vt = p.ring("avt", 3)
    r_wo = p.ring("awo", 6)
    r_ff = p.ring("aff", 2)
    r_xo = p.ring("axo", 2)
    r_mk = p.ring("amk", 1)
    r_sw = p.ring("asw2", 2)
    s_rb = p.sem("a_rowb")
    s_out = p.ring_slots("sout", 2)
    c.s_out = s_out[0]

    zT = [c.cb(CH) for _ in range(16)]
    qrotb = [c.cb(CH) for _ in range(16)]
    qn = zT
    aT = qrotb
    masks_sb = c.cb(16 * CH)
    wq4 = [c.cb(CH) for _ in range(12)]
    wot = [c.cb(CH) for _ in range(6)]
    knseg = [c.cb(SEG * 128) for _ in range(3)]
    vseg = [c.cb(SEG * 128) for _ in range(3)]
    Pb = [c.cb(CH) for _ in range(2)]
    rbc = c.cb(CH)
    statb = c.cb(2 * CH)[0:16, :]
    rowsb = c.cf(8 * CH).bitcast(BF16)[0:1, :]
    epsr = [c.cb(CH)[0:1, :] for _ in range(2)]
    q2b = [c.cb(CH) for _ in range(2)]
    lrb = c.cb(CH)[0:1, :]
    mx16 = c.cb(16)[0:16, 0:1]
    mxr = c.cb(16)[0:1, :]

    cosc = [c.cf(CH) for _ in range(2)]
    sinc = [c.cf(CH) for _ in range(2)]
    krot = [c.cf(CH) for _ in range(2)]
    krot_sw = [c.cf(CH) for _ in range(2)]
    p1 = [c.cf(CH) for _ in range(2)]
    p2 = [c.cf(CH) for _ in range(2)]
    stats = c.cf(5 * CH)[0:16, :]
    fft = [c.cf(CH) for _ in range(2)]
    xot = [c.cf(CH) for _ in range(2)]
    outb = [c.cf(CH) for _ in range(2)]
    tmpf = [c.cf(CH) for _ in range(2)]
    lrow = c.cf(CH)[0:1, :]
    qmaxc = c.cf(4)[0:16, 0:1]
    mrow = c.cf(4)[0:1, :]
    negMb = c.cf(1)

    zt_done = None
    wq_done = [None] * 12
    wo_done = [None] * 6
    cs_done = [None, None]
    eps_done = [None, None]
    row_done = [None, None]
    ksg_done = [None, None, None]
    vsg_done = [None, None, None]
    ff_done = [None, None]
    xo_done = [None, None]
    out_done = [None, None]
    tmp_done = [None, None]
    P_done = [None, None]
    q2_done = [None, None]
    qsq_pend = [None, None]
    qrot_done = [None] * 16
    qn_done = [None] * 16
    aT_done = [None] * 16
    krot_done = [None, None]
    p12_done = [None, None]
    stats_done = None
    statb_done = None
    lr_done = None
    rbc_done = None
    negM_done = None
    mx_done = None
    mk_done = None

    s1v = c.pstat[0:16, :]
    s2v = c.pb[4][0:16, :]
    qnv = c.pb[5][0:16, :]

    wqc = 0
    woc = 0
    for qi in range(nqc):
        tzt = None
        for K in range(16):
            tzt = p.dmat("sync", zT[K][:],
                         c.zown[qi * CH:(qi + 1) * CH, K * 128:(K + 1) * 128],
                         reads=[zt_done, c.t_z_done], sig=r_zt[0])
        ri = qi % 2
        tcs = p.dma("sync", cosc[ri][:], c.cosdo[:, qi * CH:(qi + 1) * CH],
                    reads=[cs_done[ri]], sig=r_cs[ri])
        tsn = p.dma("sync", sinc[ri][:], c.sindo[:, qi * CH:(qi + 1) * CH],
                    reads=[cs_done[ri]], sig=r_cs[ri])
        tep = p.dma("sync", epsr[ri][:],
                    c.epso.rearrange("a b -> (a b)")[qi * CH:(qi + 1) * CH].unsqueeze(0),
                    reads=[eps_done[ri], c.t_epso], sig=r_eps[ri])
        tmk = p.dma("sync", masks_sb[:].rearrange("p (m q) -> p m q", m=16),
                    c.maskv[qi].rearrange("m p q -> p m q"),
                    reads=[mk_done], sig=r_mk[0])

        stat_last = [None, None]
        for hg in range(4):
            mmg = [None] * 4
            for K in range(16):
                wi = wqc % 12
                wqc += 1
                tw = p.dma("sync", wq4[wi][:],
                           c.wqP[K * 128:(K + 1) * 128, hg * CH:(hg + 1) * CH],
                           reads=[wq_done[wi], c.t_wprep_end], sig=r_wq[wi])
                for hi in range(4):
                    mmg[hi] = p.op("tensor", lambda e, K=K, wi=wi, hi=hi: e.matmul(
                        out=c.pb[hi][:], lhsT=wq4[wi][:, hi * 128:(hi + 1) * 128],
                        rhs=zT[K][:], start=(K == 0), stop=False),
                        reads=[tw, tzt] + (c.pb_use[hi] if K == 0 else []), sig=c.s_pe)
                wq_done[wi] = mmg[3]
            for hi in range(4):
                h = hg * 4 + hi
                pbi = h % 2
                c.pb_use[hi] = []
                mmb = p.op("tensor", lambda e, h=h, hi=hi: e.matmul(
                    out=c.pb[hi][:], lhsT=c.browq_bf[0:1, h * 128:(h + 1) * 128],
                    rhs=c.ones_row[:], start=False, stop=True),
                    reads=[mmg[hi], c.t_brow, c.t_const], sig=c.s_pe)
                tev = p.op("scalar", lambda e, hi=hi, ki=pbi: e.activation(
                    out=krot[ki][:], in_=c.pb[hi][:], func=AF.Copy),
                    reads=[mmb, krot_done[pbi]], sig=c.s_act)
                c.pb_use[hi].append(tev)
                tw1 = p.dma("gpsimd", krot_sw[pbi][0:64, :], krot[pbi][64:128, :],
                            reads=[tev, krot_done[pbi]], sig=r_sw[pbi])
                tw2 = p.dma("gpsimd", krot_sw[pbi][64:128, :], krot[pbi][0:64, :],
                            sig=r_sw[pbi])
                tp1 = p.op("gpsimd", lambda e, ri=ri, ki=pbi: e.tensor_mul(
                    out=p1[ki][:], in0=krot[ki][:], in1=cosc[ri][:]),
                    reads=[tev, tsn, p12_done[pbi]], sig=c.s_pool)
                tp2 = p.op("gpsimd", lambda e, ri=ri, ki=pbi: e.tensor_mul(
                    out=p2[ki][:], in0=krot_sw[ki][:], in1=sinc[ri][:]),
                    reads=[tw2, tsn], sig=c.s_pool)
                krot_done[pbi] = [tp1, tp2]
                tcb = p.op("vector", lambda e, h=h, ki=pbi: e.tensor_add(
                    out=qrotb[h][:], in0=p1[ki][:], in1=p2[ki][:]),
                    reads=[tp1, tp2, qrot_done[h]], sig=c.s_dve)
                p12_done[pbi] = tcb
                qrot_done[h] = [tcb]
        for h in range(16):
            ki = h % 2
            tsq = p.op("scalar", lambda e, h=h, ki=ki: e.activation(
                out=q2b[ki][:], in_=qrotb[h][:], func=AF.Square),
                reads=[qrot_done[h], q2_done[ki]], sig=c.s_act)
            ts1 = p.op("tensor", lambda e, h=h: e.matmul(
                out=s1v, lhsT=c.ehs[:, h * 16:(h + 1) * 16], rhs=qrotb[h][:],
                start=(h == 0), stop=(h == 15)),
                reads=[qrot_done[h], c.t_const] + (c.pstat_use if h == 0 else []),
                sig=c.s_pe)
            ts2 = p.op("tensor", lambda e, h=h, ki=ki: e.matmul(
                out=s2v, lhsT=c.ehs[:, h * 16:(h + 1) * 16], rhs=q2b[ki][:],
                start=(h == 0), stop=False),
                reads=[tsq] + (c.pb_use[4] if h == 0 else []), sig=c.s_pe)
            q2_done[ki] = ts2
            stat_last = [ts1, ts2]
        tse = p.op("tensor", lambda e, ri=ri: e.matmul(
            out=s2v, lhsT=c.ones_row[0:1, 0:16], rhs=epsr[ri][:],
            start=False, stop=True), reads=[tep, stat_last[1], c.t_const], sig=c.s_pe)
        eps_done[ri] = tse

        def smk(e):
            e.tensor_scalar(out=stats[:, 0:CH], in0=s1v, scalar1=1.0 / 128,
                            scalar2=None, op0=ALU.mult)
            e.tensor_scalar(out=stats[:, CH:2 * CH], in0=s2v, scalar1=1.0 / 128,
                            scalar2=None, op0=ALU.mult)
            e.drain()
            e.scalar_tensor_tensor(out=stats[:, 2 * CH:3 * CH], in0=stats[:, 0:CH],
                                   scalar=-1.0, in1=stats[:, 0:CH],
                                   op0=ALU.mult, op1=ALU.mult)
            e.drain()
            return e.tensor_tensor(out=stats[:, 2 * CH:3 * CH],
                                   in0=stats[:, CH:2 * CH],
                                   in1=stats[:, 2 * CH:3 * CH], op=ALU.add)
        tsk = p.op("vector", smk, reads=[tse, stat_last[0], stats_done], sig=c.s_dve)
        tl1 = p.op("scalar", lambda e: e.activation(
            out=stats[:, 3 * CH:4 * CH], in_=stats[:, 2 * CH:3 * CH], func=AF.Ln),
            reads=[tsk], sig=c.s_act)
        p.drain("scalar")
        tl2 = p.op("scalar", lambda e: e.activation(
            out=stats[:, 3 * CH:4 * CH], in_=stats[:, 3 * CH:4 * CH], func=AF.Exp,
            scale=-0.5), sig=c.s_act)

        def smk2(e):
            e.scalar_tensor_tensor(out=stats[:, 4 * CH:5 * CH], in0=stats[:, 0:CH],
                                   scalar=-1.0, in1=stats[:, 3 * CH:4 * CH],
                                   op0=ALU.mult, op1=ALU.mult)
            e.tensor_copy(out=statb[:, 0:CH], in_=stats[:, 3 * CH:4 * CH])
            e.drain()
            return e.tensor_copy(out=statb[:, CH:2 * CH], in_=stats[:, 4 * CH:5 * CH])
        tsk2 = p.op("vector", smk2, reads=[tl2, statb_done], sig=c.s_dve)
        stats_done = tsk2
        c.pstat_use = [tsk]
        c.pb_use[4] = [tsk]

        trb = p.dma("gpsimd", c.rowq, statb[:],
                    reads=[tsk2, getattr(c, "_rowq_read", [])], sig=s_rb)
        statb_done = trb

        rowq_reads = []
        row_users = []
        for h in range(16):
            si = h % 2
            pc = c.pb[2 * si]
            pd = c.pb[2 * si + 1]
            if h % 8 == 0:
                rflat = c.rowq.rearrange("a b -> (a b)")
                trh = p.dma("sync", rowsb[:],
                            rflat[h * 2 * CH:(h + 8) * 2 * CH].unsqueeze(0),
                            reads=[trb, row_done[0]], sig=r_row[0])
                rowq_reads.append(trh)
                row_done[0] = []
            ho = (h % 8) * 2 * CH
            tmc = p.op("tensor", lambda e, ho=ho, pc=pc: e.matmul(
                out=pc[:], lhsT=c.lrow[0:1, 0:HD],
                rhs=rowsb[0:1, ho:ho + CH],
                start=True, stop=True),
                reads=[trh, c.t_lrow] + c.pb_use[2 * si], sig=c.s_pe)
            tmd1 = p.op("tensor", lambda e, ho=ho, pd=pd: e.matmul(
                out=pd[:], lhsT=c.lrow[0:1, 0:HD],
                rhs=rowsb[0:1, ho + CH:ho + 2 * CH],
                start=True, stop=False),
                reads=[trh] + c.pb_use[2 * si + 1], sig=c.s_pe)
            tmd2 = p.op("tensor", lambda e, pd=pd: e.matmul(
                out=pd[:], lhsT=c.lrow[0:1, HD:2 * HD], rhs=c.ones_row[:],
                start=False, stop=True), sig=c.s_pe)
            row_users += [tmc, tmd1]
            row_done[0] = row_done[0] + [tmc, tmd1]
            c.pb_use[2 * si] = []
            c.pb_use[2 * si + 1] = []
            tt1 = p.op("vector", lambda e, h=h, si=si, pc=pc: e.tensor_tensor(
                out=Pb[si][:], in0=qrotb[h][:], in1=pc[:], op=ALU.mult),
                reads=[tmc, P_done[si]], sig=c.s_dve)
            tt2 = p.op("vector", lambda e, h=h, si=si, pd=pd: e.tensor_tensor(
                out=qn[h][:], in0=Pb[si][:], in1=pd[:], op=ALU.add),
                reads=[tmd2, qn_done[h]], sig=c.s_dve)
            qrot_done[h] = [tt1]
            P_done[si] = tt2
            c.pb_use[2 * si].append(tt1)
            c.pb_use[2 * si + 1].append(tt2)
            if h >= 2:
                hd2 = h - 2
                tqn = p.op("tensor", lambda e, hd2=hd2, si=si: e.matmul(
                    out=qnv, lhsT=c.ehs[:, hd2 * 16:(hd2 + 1) * 16], rhs=q2b[si][:],
                    start=(hd2 == 0), stop=False),
                    reads=[qsq_pend[si]] + (c.pb_use[5] if hd2 == 0 else []),
                    sig=c.s_pe)
                q2_done[si] = tqn
            tsq2 = p.op("scalar", lambda e, h=h, si=si: e.activation(
                out=q2b[si][:], in_=qn[h][:], func=AF.Square),
                reads=[tt2, q2_done[si]], sig=c.s_act)
            qsq_pend[si] = tsq2
            qn_done[h] = [tsq2]
        for hd2 in (14, 15):
            si = hd2 % 2
            tqn = p.op("tensor", lambda e, hd2=hd2, si=si: e.matmul(
                out=qnv, lhsT=c.ehs[:, hd2 * 16:(hd2 + 1) * 16], rhs=q2b[si][:],
                start=False, stop=(hd2 == 15)),
                reads=[qsq_pend[si]], sig=c.s_pe)
            q2_done[si] = tqn
        c._rowq_read = rowq_reads

        def mx1(e):
            e.reduce_max(out=qmaxc[:, 0:1], in_=qnv, axis=AX.X)
            e.drain()
            e.tensor_tensor(out=qmaxc[:, 0:1], in0=qmaxc[:, 0:1], in1=c.maxk2[:],
                            op=ALU.mult)
            e.drain()
            return e.tensor_copy(out=mx16[:, 0:1], in_=qmaxc[:, 0:1])
        tmx = p.op("vector", mx1, reads=[q2_done[0], q2_done[1], c.t_maxk, mx_done],
                   sig=c.s_dve)
        c.pb_use[5] = [tmx]
        ttr = p.op("tensor", lambda e: e.transpose(
            out=c.ptr[0:1, 0:16], in_=mx16[0:16, 0:1], identity=c.identb[0:16, 0:16]),
            reads=[tmx, c.t_const] + c.ptr_use, sig=c.s_pe)
        c.ptr_use = []
        tmr = p.op("vector", lambda e: e.tensor_copy(out=mxr[0:1, 0:16],
                                                     in_=c.ptr[0:1, 0:16]),
                   reads=[ttr], sig=c.s_dve)
        c.ptr_use.append(tmr)
        p.drain("vector")
        tmm = p.op("vector", lambda e: e.reduce_max(out=mrow[0:1, 0:1],
                                                    in_=mxr[0:1, 0:16], axis=AX.X),
                   sig=c.s_dve)
        tsf = p.op("scalar", lambda e: e.activation(
            out=mrow[0:1, 1:2], in_=mrow[0:1, 0:1], func=AF.Sqrt),
            reads=[tmm], sig=c.s_act)
        p.drain("scalar")
        tng = p.op("scalar", lambda e: e.activation(
            out=mrow[0:1, 2:3], in_=mrow[0:1, 1:2], func=AF.Copy, scale=-1.02),
            sig=c.s_act)
        tngb = p.op("vector", lambda e: e.tensor_copy(out=mxr[0:1, 0:1],
                                                      in_=mrow[0:1, 2:3]),
                    reads=[tng, tmr], sig=c.s_dve)
        tmb = p.op("tensor", lambda e: e.matmul(
            out=c.pb[0][:, 0:1], lhsT=c.ones_row[0:1, 0:128], rhs=mxr[0:1, 0:1],
            start=True, stop=True), reads=[tngb, c.t_const] + c.pb_use[0], sig=c.s_pe)
        tnb = p.op("vector", lambda e: e.tensor_copy(out=negMb[:, 0:1],
                                                     in_=c.pb[0][:, 0:1]),
                   reads=[tmb], sig=c.s_dve)
        c.pb_use[0] = [tnb]
        negM_done = tnb
        mx_done = [tmm, tngb]

        nkt = SLOT_NKT[qi]
        mk_users = []
        for h in range(16):
            acc = c.pb[4]
            lps = c.pb[5][0:1, :]
            for kt in range(nkt):
                sgi = (kt // SEG) % 3
                if kt % SEG == 0:
                    tkl = p.dma("sync", knseg[sgi][:],
                                c.knT[h * 128:(h + 1) * 128,
                                      kt * 128:(kt + SEG) * 128],
                                reads=[ksg_done[sgi], c.t_kv_end], sig=r_kn[sgi])
                    tvl = p.dma("sync", vseg[sgi][:].rearrange("t (k d) -> t k d", k=SEG),
                                c.vtok[kt:kt + SEG, h].rearrange("k t d -> t k d"),
                                reads=[vsg_done[sgi], c.t_kv_end], sig=r_vt[sgi])
                    ksg_done[sgi] = []
                    vsg_done[sgi] = []
                ko = (kt % SEG) * 128
                sb_ = kt % 2
                spb = c.pb[2 + (kt % 2)]
                diag = kt >= nkt - 16
                tmmS = p.op("tensor", lambda e, h=h, spb=spb, diag=diag, sgi=sgi, ko=ko: e.matmul(
                    out=spb[:], lhsT=knseg[sgi][:, ko:ko + 128], rhs=qn[h][:],
                    start=True, stop=not diag),
                    reads=[tkl, qn_done[h]] + c.pb_use[2 + (kt % 2)], sig=c.s_pe)
                ksg_done[sgi].append(tmmS)
                if diag:
                    midx = kt - (nkt - 16)
                    tmmS = p.op("tensor", lambda e, midx=midx, spb=spb: e.matmul(
                        out=spb[:], lhsT=c.identb[:],
                        rhs=masks_sb[:, midx * CH:(midx + 1) * CH],
                        start=False, stop=True), reads=[c.t_const, tmk], sig=c.s_pe)
                    mk_users.append(tmmS)
                c.pb_use[2 + (kt % 2)] = []
                texp = p.op("scalar", lambda e, sb_=sb_, spb=spb: e.activation(
                    out=Pb[sb_][:], in_=spb[:], func=AF.Exp, bias=negMb[:, 0:1],
                    scale=1.0), reads=[tmmS, negM_done, P_done[sb_]], sig=c.s_act)
                c.pb_use[2 + (kt % 2)].append(texp)
                st = (kt == 0)
                sp = (kt == nkt - 1)
                tacc = p.op("tensor", lambda e, sb_=sb_, acc=acc, st=st, sp=sp,
                            sgi=sgi, ko=ko: e.matmul(
                    out=acc[:], lhsT=vseg[sgi][:, ko:ko + 128], rhs=Pb[sb_][:],
                    start=st, stop=sp),
                    reads=[tvl, texp] + (c.pb_use[4] if st else []), sig=c.s_pe)
                tl_ = p.op("tensor", lambda e, sb_=sb_, lps=lps, st=st, sp=sp: e.matmul(
                    out=lps, lhsT=c.onescol_bf[:], rhs=Pb[sb_][:],
                    start=st, stop=sp),
                    reads=[texp, c.t_const] + (c.pb_use[5] if st else []), sig=c.s_pe)
                vsg_done[sgi].append(tacc)
                P_done[sb_] = [tacc, tl_]
            c.pb_use[4] = []
            c.pb_use[5] = []
            def lrec(e):
                e.tensor_scalar(out=lrow[0:1, :], in0=lps, scalar1=1e-35,
                                scalar2=None, op0=ALU.max)
                e.drain()
                e.reciprocal(out=lrow[0:1, :], in_=lrow[0:1, :])
                e.drain()
                return e.tensor_copy(out=lrb[0:1, :], in_=lrow[0:1, :])
            tlr = p.op("vector", lrec, reads=[tl_, lr_done], sig=c.s_dve)
            lr_done = tlr
            c.pb_use[5].append(tlr)
            trc = p.op("tensor", lambda e: e.matmul(
                out=c.pb[0][:], lhsT=c.ones_row[0:1, 0:128], rhs=lrb[0:1, :],
                start=True, stop=True), reads=[tlr, c.t_const] + c.pb_use[0],
                sig=c.s_pe)
            c.pb_use[0] = []
            tre = p.op("scalar", lambda e: e.activation(
                out=rbc[:], in_=c.pb[0][:], func=AF.Copy),
                reads=[trc, rbc_done], sig=c.s_act)
            c.pb_use[0].append(tre)
            tdv = p.op("vector", lambda e, h=h: e.tensor_tensor(
                out=aT[h][:], in0=acc[:], in1=rbc[:], op=ALU.mult),
                reads=[tre, aT_done[h], qrot_done[h]], sig=c.s_dve)
            rbc_done = tdv
            c.pb_use[4].append(tdv)
            aT_done[h] = tdv
            qn_done[h] = [qn_done[h], tmmS]

        for do in range(4):
            mml = [None] * 4
            for h in range(16):
                wi = woc % 6
                woc += 1
                twl = p.dma("sync", wot[wi][:],
                            c.woP[h * 128:(h + 1) * 128, do * CH:(do + 1) * CH],
                            reads=[wo_done[wi], c.t_wprep_end], sig=r_wo[wi])
                for ts in range(4):
                    mml[ts] = p.op("tensor", lambda e, h=h, wi=wi, ts=ts: e.matmul(
                        out=c.pb[ts][:], lhsT=aT[h][:, ts * 128:(ts + 1) * 128],
                        rhs=wot[wi][:], start=(h == 0), stop=False),
                        reads=[twl, aT_done[h]] + (c.pb_use[ts] if h == 0 else []),
                        sig=c.s_pe)
                wo_done[wi] = mml[3]
            for ts in range(4):
                c.pb_use[ts] = []
                mmb = p.op("tensor", lambda e, do=do, ts=ts: e.matmul(
                    out=c.pb[ts][:], lhsT=c.ones_row[0:1, 0:128],
                    rhs=c.bo_bf[0:1, do * CH:(do + 1) * CH],
                    start=False, stop=True), reads=[mml[ts], c.t_bo], sig=c.s_pe)
                si = ts % 2
                tff = p.dma("sync", fft[si][:],
                            c.ffny[qi * CH + ts * 128: qi * CH + (ts + 1) * 128,
                                   do * CH:(do + 1) * CH],
                            reads=[ff_done[si], c.t_ffn_end], sig=r_ff[si])
                txo = p.dma("sync", xot[si][:],
                            c.xown[qi * CH + ts * 128: qi * CH + (ts + 1) * 128,
                                   do * CH:(do + 1) * CH],
                            reads=[xo_done[si]], sig=r_xo[si])
                tst1 = p.op("vector", lambda e, si=si, ts=ts, qi=qi: e.scalar_tensor_tensor(
                    out=tmpf[si][:], in0=fft[si][:],
                    scalar=c.ffn_rstd[:, qi * 4 + ts:qi * 4 + ts + 1],
                    in1=c.pb[ts][:], op0=ALU.mult, op1=ALU.add),
                    reads=[tff, mmb, c.t_ffn_rstd, tmp_done[si]], sig=c.s_dve)
                c.pb_use[ts].append(tst1)
                tst2 = p.op("vector", lambda e, si=si: e.tensor_tensor(
                    out=outb[si][:], in0=tmpf[si][:], in1=xot[si][:], op=ALU.add),
                    reads=[txo, out_done[si]], sig=c.s_dve)
                ff_done[si] = tst1
                xo_done[si] = tst2
                tmp_done[si] = tst2
                tso = p.dma("gpsimd", c.y[qi * CH + ts * 128: qi * CH + (ts + 1) * 128,
                                          do * CH:(do + 1) * CH],
                            outb[si][:], reads=[tst2], sig=s_out[si])
                out_done[si] = tso
        for h in range(16):
            aT_done[h] = [aT_done[h], mml[3]]
        cs_done[ri] = krot_done[:]
        zt_done = mml[:]
        mk_done = mk_users




_NC_CACHE = {}


def kernel(**inputs):
    from concourse.bass_utils import run_bass_kernel_spmd
    in_maps = host_prepare(inputs)
    if "nc" not in _NC_CACHE:
        _NC_CACHE["nc"] = Core(0).build()
    nc = _NC_CACHE["nc"]
    res = run_bass_kernel_spmd(nc, in_maps, core_ids=list(range(8)))
    return host_gather(res.results)


# version 10
# speedup vs baseline: 1.0794x; 1.0757x over previous
import math
import numpy as np
import ml_dtypes
import concourse.bass as bass
import concourse.mybir as mybir
from concourse.bass_utils import run_bass_kernel_spmd

F32 = mybir.dt.float32
BF16 = mybir.dt.bfloat16
F16 = mybir.dt.float16
I32 = mybir.dt.int32
U16 = mybir.dt.uint16
AF = mybir.ActivationFunctionType
ALU = mybir.AluOpType
AX = mybir.AxisListType

B, S, D, H, HD = 2, 8192, 2048, 16, 128
I_FFN = 5632
CH = 512
NCH = S // CH
KT = 128
ROLE_CHUNKS = [[r, 7 - r, 8 + r, 15 - r] for r in range(4)]
SLOT_NKT = [16, 32, 48, 64]
LN_EPS = 1e-5
RMS_EPS = 1e-6
THETA = 30000.0
SCALE2 = HD ** -0.5
NEG_MASK = -30000.0

HDPERM = np.concatenate([np.arange(0, HD, 2), np.arange(1, HD, 2)])
DPERM = np.concatenate([h * HD + HDPERM for h in range(H)])


def host_prepare(inputs):
    x = np.asarray(inputs["L0_attn_x"])
    Wq = np.asarray(inputs["L0_attn_Wq"])
    Wkv = np.asarray(inputs["L0_attn_Wkv"])
    Wo = np.asarray(inputs["L0_attn_Wo"])
    bq = np.asarray(inputs["L0_attn_bq"])
    bkv = np.asarray(inputs["L0_attn_bkv"])
    bo = np.asarray(inputs["L0_attn_bo"])
    qn_g = np.asarray(inputs["L0_attn_qn_g"])
    qn_b = np.asarray(inputs["L0_attn_qn_b"])
    kvn_g = np.asarray(inputs["L0_attn_kvn_g"])
    kvn_b = np.asarray(inputs["L0_attn_kvn_b"])
    ln_g = np.asarray(inputs["L0_attn_ln_g"])
    ln_b = np.asarray(inputs["L0_attn_ln_b"])
    gup = np.asarray(inputs["L0_ffn_gate_up_packed"])
    gus = np.asarray(inputs["L0_ffn_gate_up_scales"])
    dnp = np.asarray(inputs["L0_ffn_down_packed"])
    dns = np.asarray(inputs["L0_ffn_down_scales"])

    WqP = np.ascontiguousarray(Wq[:, DPERM])
    WkvP = np.concatenate([Wkv[:, :D][:, DPERM], Wkv[:, D:]], axis=1)
    bqP = bq[DPERM].copy()
    bkvP = np.concatenate([bkv[:D][DPERM], bkv[D:]])
    ln_gP = ln_g[HDPERM].copy()
    ln_bP = ln_b[HDPERM].copy()

    def unpack_T(packed, n_in):
        w = packed.view(np.uint32)
        nib = (w[:, :, None] >> (4 * np.arange(8, dtype=np.uint32))) & np.uint32(15)
        nib = nib.reshape(packed.shape[0], n_in)
        return np.ascontiguousarray(nib.T.astype(np.uint16))

    gu_nib = unpack_T(gup, D)
    dn_nib = unpack_T(dnp, I_FFN)

    gu_sc = np.ascontiguousarray(np.broadcast_to(gus[:, None, :], (16, 128, 11264)))
    dn_sc = np.ascontiguousarray(np.broadcast_to(dns[:, None, :], (44, 128, 2048)))

    pos = np.arange(S, dtype=np.float64)
    freqs = (np.arange(0, HD, 2, dtype=np.float64) / HD) * math.log(THETA)
    ang = pos[None, :] * freqs[:, None]
    cosd = np.tile(np.cos(ang), (2, 1)).astype(np.float32)
    sind = np.concatenate([-np.sin(ang), np.sin(ang)]).astype(np.float32)

    qq = np.arange(CH)[None, :]
    kk = np.arange(KT)[:, None]

    ident = np.eye(128, dtype=np.float32).astype(ml_dtypes.bfloat16)

    ehs = np.zeros((H, 128, H), dtype=np.float32)
    for h in range(H):
        ehs[h, :, h] = 1.0
    ehs = ehs.astype(ml_dtypes.bfloat16)

    in_maps = []
    for c in range(8):
        b = c // 4
        role = c % 4
        chunks = ROLE_CHUNKS[role]
        xb = x[b]
        xown = np.concatenate([xb[cc * CH:(cc + 1) * CH] for cc in chunks], axis=0)
        ownpos = np.concatenate([np.arange(cc * CH, (cc + 1) * CH) for cc in chunks])
        mv = np.zeros((4, 16, KT, CH), dtype=np.float32)
        for qi, cc in enumerate(chunks):
            E = SLOT_NKT[qi]
            for p128 in range(16):
                kt = E - 16 + p128
                if kt < 4 * cc:
                    pass
                elif kt <= 4 * cc + 3:
                    d = (kt - 4 * cc) * 128
                    mv[qi, p128] = np.where(qq < kk + d, NEG_MASK, 0.0)
                else:
                    mv[qi, p128] = NEG_MASK
        m = dict(
            xpre=np.ascontiguousarray(xb),
            xown=np.ascontiguousarray(xown),
            wq=WqP, wkv=WkvP, wo=Wo,
            bq=bqP[None, :], bkv=bkvP[None, :], bo=bo[None, :],
            qng=np.ascontiguousarray(qn_g.reshape(16, 128).T),
            qnb=np.ascontiguousarray(qn_b.reshape(16, 128).T),
            kvg=np.ascontiguousarray(kvn_g.reshape(16, 128).T),
            kvb=np.ascontiguousarray(kvn_b.reshape(16, 128).T),
            lng=ln_gP[None, :], lnb=ln_bP[None, :],
            gunib=gu_nib, dnnib=dn_nib,
            gusc=gu_sc, dnsc=dn_sc,
            cosd=np.ascontiguousarray(cosd),
            sind=np.ascontiguousarray(sind),
            cosdo=np.ascontiguousarray(cosd[:, ownpos]),
            sindo=np.ascontiguousarray(sind[:, ownpos]),
            maskv=mv.astype(ml_dtypes.bfloat16),
            ident=ident, ehs=ehs,
        )
        in_maps.append(m)
    return in_maps


def host_gather(results):
    out = np.empty((B, S, D), dtype=np.float32)
    for c in range(8):
        b = c // 4
        chunks = ROLE_CHUNKS[c % 4]
        y = results[c]["y"]
        for j, cc in enumerate(chunks):
            out[b, cc * CH:(cc + 1) * CH] = y[j * CH:(j + 1) * CH]
    return out


class Sem:
    __slots__ = ("h", "name", "count")

    def __init__(self, handle, name):
        self.h = handle
        self.name = name
        self.count = 0


class Tok:
    __slots__ = ("sem", "val", "eng", "dma")

    def __init__(self, sem, val, eng, dma):
        self.sem, self.val, self.eng, self.dma = sem, val, eng, dma


class Prog:

    ENGS = ("sync", "tensor", "vector", "scalar", "gpsimd")

    def __init__(self, nc, stack):
        self.nc = nc
        self.ops = []
        self.sems = []
        self._stack = stack

    def sem(self, name):
        s = Sem(self._stack.enter_context(self.nc.semaphore(name)), name)
        self.sems.append(s)
        return s

    def ring(self, name, n):
        return [self.sem(f"{name}{i}") for i in range(n)]

    def ring_slots(self, name, n):
        return [self.sem(f"{name}{i}") for i in range(n)]

    @staticmethod
    def _flat(reads):
        out = []
        for t in reads:
            if t is None:
                continue
            if isinstance(t, (list, tuple)):
                out.extend(Prog._flat(t))
            else:
                out.append(t)
        return out

    def op(self, eng, fn, reads=(), sig=None, amt=1, dma=False):
        waits = []
        for t in self._flat(reads):
            if (not t.dma) and t.eng == eng:
                continue
            waits.append(t)
        tok = None
        if sig is not None:
            sig.count += amt
            tok = Tok(sig, sig.count, eng, dma)
        self.ops.append((eng, fn, waits, sig, amt))
        return tok

    def drain(self, eng):
        self.ops.append((eng, lambda e: e.drain(), [], None, 0))

    def dma(self, eng, out, in_, reads=(), sig=None, **kw):
        assert sig is not None
        return self.op(eng, lambda e: e.dma_start(out=out, in_=in_, **kw),
                       reads=reads, sig=sig, amt=16, dma=True)

    def dmat(self, eng, out, in_, reads=(), sig=None):
        assert sig is not None
        return self.op(eng, lambda e: e.dma_start_transpose(out, in_),
                       reads=reads, sig=sig, amt=16, dma=True)

    def gate_all(self, toks):
        for eng in self.ENGS:
            self.ops.append((eng, lambda e: e.nop(), Prog._flat(toks), None, 0))

    def final_barrier(self, eng):
        toks = [Tok(s, s.count, "_barrier_", True) for s in self.sems if s.count]
        self.ops.append((eng, lambda e: e.nop(), toks, None, 0))

    def emit(self, block):
        ops = self.ops if Prog.KEEP is None else self.ops[:Prog.KEEP]
        per = {e: [] for e in self.ENGS}
        for rec in ops:
            per[rec[0]].append(rec)

        def make(eng_name):
            eops = per[eng_name]

            def body(eng):
                waited = {}
                for (_e, fn, waits, sig, amt) in eops:
                    for t in waits:
                        if waited.get(t.sem.name, -1) >= t.val:
                            continue
                        eng.wait_ge(t.sem.h, t.val)
                        waited[t.sem.name] = t.val
                    inst = fn(eng)
                    if sig is not None:
                        assert inst is not None, "fn with sig must return its last instruction"
                        inst.then_inc(sig.h, amt)
            return body

        for name in self.ENGS:
            if not per[name]:
                continue
            getattr(block, name)(make(name))

    KEEP = None


class Core:
    def __init__(self, role=0, dev=()):
        self.role = role
        self.dev = set(dev)
        self.chunks = ROLE_CHUNKS[role]
        self.npre = NCH
        self.pre = S
        self.nt = S // 128
        self.nc = bass.Bass("TRN2", target_bir_lowering=False, detect_race_conditions=False)
        self.dbg = {}

    def ap(self, name, shape, dt, kind="ExternalInput"):
        return self.nc.dram_tensor(name, list(shape), dt, kind=kind).ap()

    def carve_reset(self):
        self._cf = self._cb = self._cu = 0

    def cf(self, n):
        v = self.PF[:, self._cf:self._cf + n]
        self._cf += n
        assert self._cf <= 20480, f"PF overflow {self._cf}"
        return v

    def cb(self, n):
        v = self.PB[:, self._cb:self._cb + n]
        self._cb += n
        assert self._cb <= 46080, f"PB overflow {self._cb}"
        return v

    def cu(self, n):
        v = self.PU[:, self._cu:self._cu + n]
        self._cu += n
        assert self._cu <= 4096, f"PU overflow {self._cu}"
        return v

    def group_barrier(self):
        return [Tok(s, s.count, "_g_", True) for s in self.p.sems if s.count]

    def build(self):
        import contextlib
        nc = self.nc
        c = self
        pre = c.pre
        c.xpre = c.ap("xpre", [pre, D], F32)
        c.xown = c.ap("xown", [4 * CH, D], F32)
        c.wq = c.ap("wq", [D, D], F32)
        c.wkv = c.ap("wkv", [D, 2 * D], F32)
        c.wo = c.ap("wo", [D, D], F32)
        c.bqr = c.ap("bq", [1, D], F32)
        c.bkvr = c.ap("bkv", [1, 2 * D], F32)
        c.bor = c.ap("bo", [1, D], F32)
        c.qng = c.ap("qng", [128, 16], F32)
        c.qnbr = c.ap("qnb", [128, 16], F32)
        c.kvg = c.ap("kvg", [128, 16], F32)
        c.kvbr = c.ap("kvb", [128, 16], F32)
        c.lngr = c.ap("lng", [1, HD], F32)
        c.lnbr = c.ap("lnb", [1, HD], F32)
        c.gunib = c.ap("gunib", [D, 2 * I_FFN], U16)
        c.dnnib = c.ap("dnnib", [I_FFN, D], U16)
        c.gusc = c.ap("gusc", [16, 128, 2 * I_FFN], F16)
        c.dnsc = c.ap("dnsc", [44, 128, D], F16)
        c.cosdr = c.ap("cosd", [128, pre], F32)
        c.sindr = c.ap("sind", [128, pre], F32)
        c.cosdo = c.ap("cosdo", [128, 4 * CH], F32)
        c.sindo = c.ap("sindo", [128, 4 * CH], F32)
        c.maskv = c.ap("maskv", [4, 16, KT, CH], BF16)
        c.identin = c.ap("ident", [128, 128], BF16)
        c.ehsin = c.ap("ehs", [H, 128, H], BF16)
        c.y = c.ap("y", [4 * CH, D], F32, kind="ExternalOutput")
        c.z_tok = c.ap("z_tok", [pre, D], BF16, kind="Internal")
        c.zown = c.ap("zown", [4 * CH, D], BF16, kind="Internal")
        c.xn_tok = c.ap("xn_tok", [4 * CH, D], BF16, kind="Internal")
        c.epst = c.ap("epst", [64, 128], BF16, kind="Internal")
        c.epso = c.ap("epso", [16, 128], BF16, kind="Internal")
        c.rowq = c.ap("rowq", [16, 2 * CH], BF16, kind="Internal")
        c.rowk = c.ap("rowk", [16, 2 * CH], BF16, kind="Internal")
        c.wqP = c.ap("wqP", [D, D], BF16, kind="Internal")
        c.wkvP = c.ap("wkvP", [32, 128, 4 * CH], BF16, kind="Internal")
        c.woP = c.ap("woP", [D, D], BF16, kind="Internal")
        c.knT = c.ap("knT", [D, pre], BF16, kind="Internal")
        c.vtok = c.ap("vtok", [NCH * 4, H, 128, HD], BF16, kind="Internal")
        c.wguT = c.ap("wguT", [D, 2 * I_FFN], BF16, kind="Internal")
        c.wdT = c.ap("wdT", [I_FFN, D], BF16, kind="Internal")
        c.ffny = c.ap("ffny", [4 * CH, D], F32, kind="Internal")

        if "s1" in c.dev:
            c.dbg["d_xn"] = c.ap("d_xn", [4 * CH, D], BF16, kind="ExternalOutput")
            c.dbg["d_z"] = c.ap("d_z", [pre, D], BF16, kind="ExternalOutput")
            c.dbg["d_eps"] = c.ap("d_eps", [64, 128], BF16, kind="ExternalOutput")
        if "kv" in c.dev:
            c.dbg["d_kn"] = c.ap("d_kn", [D, pre], BF16, kind="ExternalOutput")
            c.dbg["d_v"] = c.ap("d_v", [pre, D], BF16, kind="ExternalOutput")
            c.dbg["d_mk"] = c.ap("d_mk", [1, 16], F32, kind="ExternalOutput")
        if "ffn" in c.dev:
            c.dbg["d_wgu"] = c.ap("d_wgu", [D, 2 * I_FFN], BF16, kind="ExternalOutput")
            c.dbg["d_wd"] = c.ap("d_wd", [I_FFN, D], BF16, kind="ExternalOutput")
            c.dbg["d_ffny"] = c.ap("d_ffny", [4 * CH, D], F32, kind="ExternalOutput")
        if "qn" in c.dev:
            c.dbg["d_qn"] = c.ap("d_qn", [D, 4 * CH], BF16, kind="ExternalOutput")

        with contextlib.ExitStack() as st:
            self._st = st
            E = st.enter_context
            c.sb = lambda name, shape, dt: E(nc.sbuf_tensor(name, list(shape), dt))
            c.ps = lambda name, shape, dt=F32: E(nc.psum_tensor(name, list(shape), dt))
            p = c.p = Prog(nc, st)
            c.s_const = p.sem("s_const")
            c.s_store = p.sem("s_store")
            c.s_act = p.sem("s_act")
            c.s_dve = p.sem("s_dve")
            c.s_pool = p.sem("s_pool")
            c.s_pe = p.sem("s_pe")

            c.identb = c.sb("identb", [128, 128], BF16)
            p.dma("sync", c.identb[:], c.identin, sig=c.s_const)
            c.ehs = c.sb("ehs_sb", [128, H * H], BF16)
            p.dma("sync", c.ehs[:].rearrange("p (h e) -> p h e", h=H),
                  c.ehsin.rearrange("h p e -> p h e"), sig=c.s_const)

            c.lneps = c.sb("lneps", [128, 1], F32)
            tm1 = p.op("vector", lambda e: e.memset(c.lneps[:], LN_EPS), sig=c.s_dve)
            c.rmseps = c.sb("rmseps", [128, 1], F32)
            tm2 = p.op("vector", lambda e: e.memset(c.rmseps[:], RMS_EPS), sig=c.s_dve)
            c.ones_row = c.sb("ones_row", [1, CH], BF16)
            tm3 = p.op("vector", lambda e: e.memset(c.ones_row[:], 1.0), sig=c.s_dve)
            c.onescol_bf = c.sb("onescol_bf", [128, 1], BF16)
            tm4 = p.op("vector", lambda e: e.memset(c.onescol_bf[:], 1.0), sig=c.s_dve)
            c.t_const = [Tok(c.s_const, c.s_const.count, "_c_", True), tm1, tm2, tm3, tm4]

            c.PF = c.sb("poolF", [128, 20480], F32)
            c.PB = c.sb("poolB", [128, 46080], BF16)
            c.PU = c.sb("poolU", [128, 4096], U16)
            c._cf = c._cb = c._cu = 0

            c.pb = [c.ps(f"pb{i}", [128, CH], F32) for i in range(6)]
            c.pstat = c.ps("pstat", [128, CH], F32)
            c.ptr = c.ps("ptr", [128, 128], BF16)
            c.pb_use = [[] for _ in range(6)]
            c.pstat_use = []
            c.ptr_use = []

            c.carve_reset()
            phase_s1(c)
            need_kv = not c.dev or "kv" in c.dev or "attn" in c.dev or "full" in c.dev
            need_ffn = not c.dev or "ffn" in c.dev or "full" in c.dev
            need_attn = not c.dev or "attn" in c.dev or "full" in c.dev
            if need_kv:
                phase_wprep(c)
            if need_ffn:
                phase_dequant(c)
            p.gate_all(c.group_barrier())
            c.carve_reset()
            if need_kv:
                phase_kv(c)
                p.gate_all(c.group_barrier())
                c.carve_reset()
            if need_ffn:
                phase_ffn(c)
                p.gate_all(c.group_barrier())
                c.carve_reset()
            if need_attn:
                phase_attn(c)

            p.final_barrier("sync")
            blk = E(nc.Block())
            p.emit(blk)
        return self.nc


def phase_s1(c):
    p, sb = c.p, c.sb
    nt = min(c.nt, getattr(c, "S1N", 10 ** 9))
    C_EPS128 = 128.0 * LN_EPS / (SCALE2 * D)

    r_x = p.ring("rx", 3)
    r_z = p.ring_slots("rz", 2)
    r_xn = p.ring_slots("rxn", 2)
    s_eps = p.sem("s_eps")

    xb = [c.cf(D) for _ in range(3)]
    junk = c.cf(D)
    xnb = [c.cb(D) for _ in range(2)]
    zb = [c.cb(D) for _ in range(2)]
    cols = [c.cf(16) for _ in range(2)]
    epscols = c.cb(64)
    epscols2 = c.cb(16)
    eps_ps = c.ptr
    eps_sb = c.cb(128)

    st = dict(last_use=[None, None, None], xn_done=[None, None], z_done=[None, None],
              col_done=[None, None], t_ec=None)

    def tile(i, src_rows, zdst, ecol, ecol_i, xndst):
        bi = i % 3
        ci = i % 2
        zi = i % 2
        cc_ = cols[ci]
        xt = xb[bi]
        tload = p.dma("sync", xt[:], src_rows, reads=[st["last_use"][bi]], sig=r_x[bi])
        t1 = p.op("scalar", lambda e, xt=xt, cc=cc_: e.activation(
            out=junk[:], in_=xt[:], func=AF.Copy, accum_out=cc[:, 0:1]),
            reads=[tload, st["col_done"][ci]], sig=c.s_act)
        t2 = p.op("scalar", lambda e, xt=xt, cc=cc_: e.activation(
            out=junk[:], in_=xt[:], func=AF.Square, accum_out=cc[:, 1:2]),
            reads=[tload], sig=c.s_act)

        def sm(e, cc=cc_):
            e.tensor_scalar(out=cc[:, 2:3], in0=cc[:, 0:1], scalar1=1.0 / D,
                            scalar2=None, op0=ALU.mult)
            e.tensor_scalar(out=cc[:, 3:4], in0=cc[:, 1:2], scalar1=1.0 / D,
                            scalar2=None, op0=ALU.mult)
            e.drain()
            e.scalar_tensor_tensor(out=cc[:, 4:5], in0=cc[:, 2:3], scalar=-1.0,
                                   in1=cc[:, 2:3], op0=ALU.mult, op1=ALU.mult)
            e.drain()
            return e.tensor_tensor(out=cc[:, 4:5], in0=cc[:, 3:4], in1=cc[:, 4:5],
                                   op=ALU.add)
        tsm = p.op("vector", sm, reads=[t1, t2, st["col_done"][ci]], sig=c.s_dve)
        tln = p.op("scalar", lambda e, cc=cc_: e.activation(
            out=cc[:, 5:6], in_=cc[:, 4:5], func=AF.Ln, bias=c.lneps[:]),
            reads=[tsm, c.t_const], sig=c.s_act)
        p.drain("scalar")
        trs = p.op("scalar", lambda e, cc=cc_: e.activation(
            out=cc[:, 5:6], in_=cc[:, 5:6], func=AF.Exp, scale=-0.5), sig=c.s_act)

        def sm2(e, cc=cc_):
            e.tensor_tensor(out=cc[:, 6:7], in0=cc[:, 5:6], in1=cc[:, 5:6], op=ALU.mult)
            e.drain()
            return e.tensor_tensor(out=cc[:, 6:7], in0=cc[:, 6:7], in1=cc[:, 4:5],
                                   op=ALU.mult)
        tsm2 = p.op("vector", sm2, reads=[trs], sig=c.s_dve)
        tln2 = p.op("scalar", lambda e, cc=cc_: e.activation(
            out=cc[:, 7:8], in_=cc[:, 6:7], func=AF.Ln, bias=c.lneps[:]),
            reads=[tsm2], sig=c.s_act)
        p.drain("scalar")
        tr2 = p.op("scalar", lambda e, cc=cc_: e.activation(
            out=cc[:, 7:8], in_=cc[:, 7:8], func=AF.Exp, scale=-0.5), sig=c.s_act)

        def sm3(e, cc=cc_, ecol=ecol, i_=ecol_i):
            e.tensor_tensor(out=cc[:, 8:9], in0=cc[:, 5:6], in1=cc[:, 7:8], op=ALU.mult)
            e.scalar_tensor_tensor(out=cc[:, 9:10], in0=cc[:, 2:3], scalar=-1.0,
                                   in1=cc[:, 5:6], op0=ALU.mult, op1=ALU.mult)
            e.reciprocal(out=cc[:, 11:12], in_=cc[:, 6:7])
            e.drain()
            e.scalar_tensor_tensor(out=cc[:, 10:11], in0=cc[:, 2:3], scalar=-1.0,
                                   in1=cc[:, 8:9], op0=ALU.mult, op1=ALU.mult)
            return e.tensor_scalar(out=ecol[:, i_:i_ + 1], in0=cc[:, 11:12],
                                   scalar1=C_EPS128, scalar2=None, op0=ALU.mult)
        tsm3 = p.op("vector", sm3, reads=[tr2], sig=c.s_dve)
        st["t_ec"] = tsm3

        tz = p.op("gpsimd", lambda e, xt=xt, cc=cc_, zo=zb[zi]: e.tensor_scalar(
            out=zo[:], in0=xt[:], scalar1=cc[:, 8:9], scalar2=cc[:, 10:11],
            op0=ALU.mult, op1=ALU.add), reads=[tsm3, st["z_done"][zi], tload],
            sig=c.s_pool)
        tzs = p.dma("gpsimd", zdst, zb[zi][:], reads=[tz], sig=r_z[zi])
        st["z_done"][zi] = tzs
        uses = [tz]
        if xndst is not None:
            p.drain("vector")
            txn = p.op("vector", lambda e, xt=xt, cc=cc_, xo=xnb[zi]: e.tensor_scalar(
                out=xo[:], in0=xt[:], scalar1=cc[:, 5:6], scalar2=cc[:, 9:10],
                op0=ALU.mult, op1=ALU.add), reads=[tsm3, st["xn_done"][zi], tload],
                sig=c.s_dve)
            txs = p.dma("gpsimd", xndst, xnb[zi][:], reads=[txn], sig=r_xn[zi])
            st["xn_done"][zi] = txs
            uses.append(txn)
        st["last_use"][bi] = uses
        st["col_done"][ci] = uses

    for i in range(nt):
        tile(i, c.xpre[i * 128:(i + 1) * 128, :],
             c.z_tok[i * 128:(i + 1) * 128, :], epscols, i, None)
    t_ec_pre = st["t_ec"]
    nt2 = 16
    for i in range(nt2):
        tile(i, c.xown[i * 128:(i + 1) * 128, :],
             c.zown[i * 128:(i + 1) * 128, :], epscols2, i,
             c.xn_tok[i * 128:(i + 1) * 128, :])

    tec = p.op("tensor", lambda e: e.transpose(
        out=eps_ps[0:nt, 0:128], in_=epscols[:, 0:nt], identity=c.identb[:]),
        reads=[c.t_const, t_ec_pre, st["t_ec"]], sig=c.s_pe)
    tev = p.op("vector", lambda e: e.tensor_copy(out=eps_sb[0:nt, :],
                                                 in_=eps_ps[0:nt, 0:128]),
               reads=[tec], sig=c.s_dve)
    tes = p.dma("gpsimd", c.epst[0:nt, :], eps_sb[0:nt, :], reads=[tev], sig=s_eps)
    tec2 = p.op("tensor", lambda e: e.transpose(
        out=eps_ps[0:nt2, 0:128], in_=epscols2[:, 0:nt2], identity=c.identb[:]),
        reads=[st["t_ec"], tev], sig=c.s_pe)
    tev2 = p.op("vector", lambda e: e.tensor_copy(out=eps_sb[0:nt2, :],
                                                  in_=eps_ps[0:nt2, 0:128]),
               reads=[tec2, tes], sig=c.s_dve)
    tes2 = p.dma("gpsimd", c.epso[0:nt2, :], eps_sb[0:nt2, :], reads=[tev2], sig=s_eps)
    c.ptr_use = [tev2]
    c.t_eps = Tok(s_eps, s_eps.count, "_e_", True)
    c.t_epso = c.t_eps
    c.t_z_done = [t for t in st["z_done"] if t is not None]
    c.t_xn_done = [t for t in st["xn_done"] if t is not None]

    if "d_xn" in c.dbg:
        p.dma("gpsimd", c.dbg["d_xn"], c.xn_tok, reads=c.t_xn_done, sig=c.s_store)
        p.dma("gpsimd", c.dbg["d_z"], c.z_tok, reads=c.t_z_done, sig=c.s_store)
        p.dma("gpsimd", c.dbg["d_eps"][0:nt, :], c.epst[0:nt, :], reads=[tes],
              sig=c.s_store)


def phase_wprep(c):
    p, sb = c.p, c.sb

    s_v = p.sem("s_wv")
    gq = c.cf(16)
    gkv = c.cf(16)
    bqn = c.cf(16)
    bkvn = c.cf(16)
    t1 = p.dma("sync", gq[:], c.qng, sig=s_v)
    t2 = p.dma("sync", gkv[:], c.kvg, sig=s_v)
    t3 = p.dma("sync", bqn[:], c.qnbr, sig=s_v)
    t4 = p.dma("sync", bkvn[:], c.kvbr, sig=s_v)
    s_r = p.sem("s_wr")
    rows_fv = c.cf(4 * D)
    rows_f = rows_fv[0:1, :]
    t5 = p.dma("sync", rows_f[:, 0:D], c.bqr, sig=s_r)
    t6 = p.dma("sync", rows_f[:, D:3 * D], c.bkvr, sig=s_r)
    t7 = p.dma("sync", rows_f[:, 3 * D:4 * D], c.bor, sig=s_r)
    lrv = c.cf(2 * HD)
    lr = lrv[0:1, :]
    t8 = p.dma("sync", lr[:, 0:HD], c.lngr, sig=s_r)
    t9 = p.dma("sync", lr[:, HD:2 * HD], c.lnbr, sig=s_r)
    tv = [t1, t2, t3, t4]
    tr = [t5, t6, t7, t8, t9]

    bqn_bf = c.cb(16)
    bkvn_bf = c.cb(16)
    tb1 = p.op("vector", lambda e: e.tensor_copy(out=bqn_bf[:], in_=bqn[:]),
               reads=[tv], sig=c.s_dve)
    tb2 = p.op("vector", lambda e: e.tensor_copy(out=bkvn_bf[:], in_=bkvn[:]),
               reads=[tv], sig=c.s_dve)
    c.lrow = sb("lrow", [1, 4 * HD], BF16)
    def mkrows(e):
        e.tensor_copy(out=c.lrow[:, 0:2 * HD], in_=lr[:, 0:2 * HD])
        return e.tensor_scalar(out=c.lrow[:, 2 * HD:4 * HD], in0=lr[:, 0:2 * HD],
                               scalar1=float(SCALE2), scalar2=None, op0=ALU.mult)
    tlr = p.op("vector", mkrows, reads=[tr], sig=c.s_dve)
    c.t_lrow = tlr

    c.bo_bf = sb("bo_bf", [1, D], BF16)
    tbo = p.op("vector", lambda e: e.tensor_copy(out=c.bo_bf[:], in_=rows_f[:, 3 * D:4 * D]),
               reads=[tr], sig=c.s_dve)
    c.t_bo = tbo

    r_w = p.ring("rw", 4)
    wbuf = [c.cf(CH) for _ in range(4)]
    wub = [c.cb(CH) for _ in range(2)]
    wsb = [c.cb(CH) for _ in range(2)]
    r_ws = p.ring_slots("rws", 2)

    c.browq_bf = sb("browq_bf", [1, D], BF16)
    c.browkv_bf = sb("browkv_bf", [1, 2 * D], BF16)

    brow_ps = c.pstat
    specs = [
        ("wq", c.wq, c.wqP, 4, gq, bqn_bf, c.browq_bf, 0),
        ("wkv", c.wkv, c.wkvP, 8, gkv, bkvn_bf, c.browkv_bf, D),
        ("wo", c.wo, c.woP, 4, None, None, None, 3 * D),
    ]
    wu_done = [None, None]
    ws_done = [None, None]
    wl_done = [None, None, None, None]
    t_brow_es = []
    for (name, wsrc, wdst, jmax, gcol, bcol, browdst, boff) in specs:
        for j in range(jmax):
            mm_toks = []
            for K in range(16):
                i4 = (j * 16 + K) % 4
                i2 = (j * 16 + K) % 2
                tl = p.dma("sync", wbuf[i4][:], wsrc[K * 128:(K + 1) * 128, j * CH:(j + 1) * CH],
                           reads=[wl_done[i4]], sig=r_w[i4])
                tc = p.op("vector", lambda e, a=wub[i2], b=wbuf[i4]: e.tensor_copy(
                    out=a[:], in_=b[:]), reads=[tl, wu_done[i2]], sig=c.s_dve)
                uses = [tc]
                if gcol is not None:
                    tmm = p.op("tensor", lambda e, K=K, i2=i2, b=bcol: e.matmul(
                        out=brow_ps[0:1, 0:CH], lhsT=b[:, K:K + 1], rhs=wub[i2][:],
                        start=(K == 0), stop=(K == 15)),
                        reads=[tc, tb1, tb2] + (c.pstat_use if K == 0 else []),
                        sig=c.s_pe)
                    mm_toks.append(tmm)
                    tsc = p.op("vector", lambda e, K=K, i2=i2, g=gcol: e.tensor_scalar(
                        out=wsb[i2][:], in0=wub[i2][:], scalar1=g[:, K:K + 1],
                        scalar2=None, op0=ALU.mult), reads=[tv, ws_done[i2]], sig=c.s_dve)
                    if name == "wkv":
                        wdap = wdst[j * 4 + K // 4, :, (K % 4) * CH:(K % 4 + 1) * CH]
                    else:
                        wdap = wdst[K * 128:(K + 1) * 128, j * CH:(j + 1) * CH]
                    tst = p.dma("gpsimd", wdap, wsb[i2][:], reads=[tsc], sig=r_ws[i2])
                    ws_done[i2] = tst
                    uses.append(tsc)
                    wu_done[i2] = uses
                else:
                    tst = p.dma("gpsimd", wdst[K * 128:(K + 1) * 128, j * CH:(j + 1) * CH],
                                wub[i2][:], reads=[tc], sig=r_ws[i2])
                    wu_done[i2] = [tst]
                wl_done[i4] = uses if gcol is None else uses
                if gcol is None:
                    wl_done[i4] = [tc]
            if gcol is not None:
                c.pstat_use = []
                tadd = p.op("vector", lambda e, j=j, boff=boff, browdst=browdst: e.tensor_tensor(
                    out=browdst[0:1, j * CH:(j + 1) * CH], in0=brow_ps[0:1, 0:CH],
                    in1=rows_f[0:1, boff + j * CH:boff + (j + 1) * CH], op=ALU.add),
                    reads=[mm_toks[-1], tr], sig=c.s_dve)
                c.pstat_use.append(tadd)
                t_brow_es.append(tadd)
    c.t_brow = t_brow_es
    c.t_wqP = ws_done + wu_done
    c.t_wprep_end = [Tok(s, s.count, "_w_", True) for s in (r_ws[0], r_ws[1])]


def phase_kv(c):
    p, sb = c.p, c.sb
    nch = min(c.npre, getattr(c, "KVN", 10 ** 9))

    r_zt = p.ring("rzt", 2)
    c.rw12 = p.ring("rw12", 12)
    r_wk = c.rw12[0:6]
    r_cs = p.ring("rcs", 2)
    r_eps = p.ring("reps", 2)
    r_row = p.ring("rrow", 2)
    r_kst = p.ring_slots("rkst", 2)
    r_vst = p.ring_slots("rvst", 2)
    r_sw = p.ring("rsw", 2)
    s_rb = p.sem("s_rowb")

    zT = [[c.cb(CH) for K in range(16)] for r in range(2)]
    wkh = [c.cb(4 * CH) for _ in range(6)]
    cosc = [c.cf(CH) for _ in range(2)]
    sinc = [c.cf(CH) for _ in range(2)]
    epsr = [c.cb(CH)[0:1, :] for _ in range(2)]
    krot = [c.cf(CH) for _ in range(2)]
    krot_sw = [c.cf(CH) for _ in range(2)]
    p1 = [c.cf(CH) for _ in range(2)]
    p2 = [c.cf(CH) for _ in range(2)]
    rotb = [c.cb(CH) for _ in range(16)]
    rot2 = [c.cb(CH) for _ in range(2)]
    kn2 = [c.cb(CH) for _ in range(2)]
    stats = c.cf(5 * CH)[0:16, :]
    statb = c.cb(2 * CH)[0:16, :]
    rowsb = c.cf(8 * CH).bitcast(BF16)[0:1, :]
    knsb = [c.cb(CH) for _ in range(2)]
    tmpsb = [c.cb(CH) for _ in range(2)]
    vsb = [c.cb(CH) for _ in range(2)]
    maxcol = c.cf(2)[0:16, 0:1]
    c.maxk2 = sb("maxk2", [16, 1], F32)

    zt_done = [None, None]
    wk_done = [None] * 6
    cs_done = [None, None]
    eps_done = [None, None]
    row_done = [None, None]
    kn_done = [None, None]
    tmp_done = [None, None]
    v_done = [None, None]
    stats_done = None
    statb_done = None
    rot_done = [None] * 16
    rot2_done = [None, None]
    kn2_done = [None, None]
    tsq_pend = [None, None]
    krot_done = [None, None]
    p12_done = [None, None]

    tmk0 = p.op("vector", lambda e: e.memset(c.maxk2[:], 0.0), sig=c.s_dve)
    maxk_done = tmk0

    s1v = c.pstat[0:16, :]
    s2v = c.pb[4][0:16, :]
    knv = c.pb[5][0:16, :]

    wkc = 0
    for cc in range(nch):
        ri = cc % 2
        tzt = None
        for K in range(16):
            tzt = p.dmat("sync", zT[ri][K][:],
                         c.z_tok[cc * CH:(cc + 1) * CH, K * 128:(K + 1) * 128],
                         reads=[zt_done[ri], c.t_z_done], sig=r_zt[ri])
        tcs = p.dma("sync", cosc[ri][:], c.cosdr[:, cc * CH:(cc + 1) * CH],
                    reads=[cs_done[ri]], sig=r_cs[ri])
        tsn = p.dma("sync", sinc[ri][:], c.sindr[:, cc * CH:(cc + 1) * CH],
                    reads=[cs_done[ri]], sig=r_cs[ri])
        tep = p.dma("sync", epsr[ri][:],
                    c.epst.rearrange("a b -> (a b)")[cc * CH:(cc + 1) * CH].unsqueeze(0),
                    reads=[eps_done[ri], c.t_eps], sig=r_eps[ri])

        stat_last = [None, None]
        for hg in range(4):
            mmg = [None] * 4
            for q in range(4):
                wi = wkc % 6
                wkc += 1
                tw = p.dma("sync", wkh[wi][:], c.wkvP[hg * 4 + q],
                           reads=[wk_done[wi], c.t_wprep_end], sig=r_wk[wi])
                for Ki in range(4):
                    K = q * 4 + Ki
                    for hi in range(4):
                        mmg[hi] = p.op("tensor", lambda e, K=K, Ki=Ki, wi=wi, hi=hi, ri=ri: e.matmul(
                            out=c.pb[hi][:],
                            lhsT=wkh[wi][:, Ki * CH + hi * 128:Ki * CH + (hi + 1) * 128],
                            rhs=zT[ri][K][:], start=(K == 0), stop=False),
                            reads=[tw, tzt] + (c.pb_use[hi] if K == 0 else []), sig=c.s_pe)
                wk_done[wi] = mmg[3]
            for hi in range(4):
                h = hg * 4 + hi
                pbi = h % 2
                c.pb_use[hi] = []
                mmb = p.op("tensor", lambda e, h=h, hi=hi: e.matmul(
                    out=c.pb[hi][:], lhsT=c.browkv_bf[0:1, h * 128:(h + 1) * 128],
                    rhs=c.ones_row[:], start=False, stop=True),
                    reads=[mmg[hi], c.t_brow, c.t_const], sig=c.s_pe)
                tev = p.op("scalar", lambda e, hi=hi, ki=pbi: e.activation(
                    out=krot[ki][:], in_=c.pb[hi][:], func=AF.Copy),
                    reads=[mmb, krot_done[pbi]], sig=c.s_act)
                c.pb_use[hi].append(tev)
                tw1 = p.dma("gpsimd", krot_sw[pbi][0:64, :], krot[pbi][64:128, :],
                            reads=[tev, krot_done[pbi]], sig=r_sw[pbi])
                tw2 = p.dma("gpsimd", krot_sw[pbi][64:128, :], krot[pbi][0:64, :],
                            sig=r_sw[pbi])
                tp1 = p.op("gpsimd", lambda e, ri=ri, ki=pbi: e.tensor_mul(
                    out=p1[ki][:], in0=krot[ki][:], in1=cosc[ri][:]),
                    reads=[tev, tsn, p12_done[pbi]], sig=c.s_pool)
                tp2 = p.op("gpsimd", lambda e, ri=ri, ki=pbi: e.tensor_mul(
                    out=p2[ki][:], in0=krot_sw[ki][:], in1=sinc[ri][:]),
                    reads=[tw2, tsn], sig=c.s_pool)
                krot_done[pbi] = [tp1, tp2]
                tcb = p.op("vector", lambda e, h=h, ki=pbi: e.tensor_add(
                    out=rotb[h][:], in0=p1[ki][:], in1=p2[ki][:]),
                    reads=[tp1, tp2, rot_done[h]], sig=c.s_dve)
                p12_done[pbi] = tcb
                rot_done[h] = [tcb]
        for h in range(16):
            ki = h % 2
            tsq = p.op("scalar", lambda e, h=h, ki=ki: e.activation(
                out=rot2[ki][:], in_=rotb[h][:], func=AF.Square),
                reads=[rot_done[h], rot2_done[ki]], sig=c.s_act)
            ts1 = p.op("tensor", lambda e, h=h: e.matmul(
                out=s1v, lhsT=c.ehs[:, h * 16:(h + 1) * 16], rhs=rotb[h][:],
                start=(h == 0), stop=(h == 15)),
                reads=[rot_done[h], c.t_const] + (c.pstat_use if h == 0 else []),
                sig=c.s_pe)
            ts2 = p.op("tensor", lambda e, h=h, ki=ki: e.matmul(
                out=s2v, lhsT=c.ehs[:, h * 16:(h + 1) * 16], rhs=rot2[ki][:],
                start=(h == 0), stop=False),
                reads=[tsq] + (c.pb_use[4] if h == 0 else []), sig=c.s_pe)
            rot2_done[ki] = ts2
            stat_last = [ts1, ts2]
        tse = p.op("tensor", lambda e, ri=ri: e.matmul(
            out=s2v, lhsT=c.ones_row[0:1, 0:16], rhs=epsr[ri][:],
            start=False, stop=True), reads=[tep, stat_last[1], c.t_const], sig=c.s_pe)
        eps_done[ri] = tse

        def smk(e):
            e.tensor_scalar(out=stats[:, 0:CH], in0=s1v, scalar1=1.0 / 128,
                            scalar2=None, op0=ALU.mult)
            e.tensor_scalar(out=stats[:, CH:2 * CH], in0=s2v, scalar1=1.0 / 128,
                            scalar2=None, op0=ALU.mult)
            e.drain()
            e.scalar_tensor_tensor(out=stats[:, 2 * CH:3 * CH], in0=stats[:, 0:CH],
                                   scalar=-1.0, in1=stats[:, 0:CH],
                                   op0=ALU.mult, op1=ALU.mult)
            e.drain()
            return e.tensor_tensor(out=stats[:, 2 * CH:3 * CH],
                                   in0=stats[:, CH:2 * CH],
                                   in1=stats[:, 2 * CH:3 * CH], op=ALU.add)
        tsk = p.op("vector", smk, reads=[tse, stat_last[0], stats_done], sig=c.s_dve)
        tl1 = p.op("scalar", lambda e: e.activation(
            out=stats[:, 3 * CH:4 * CH], in_=stats[:, 2 * CH:3 * CH], func=AF.Ln),
            reads=[tsk], sig=c.s_act)
        p.drain("scalar")
        tl2 = p.op("scalar", lambda e: e.activation(
            out=stats[:, 3 * CH:4 * CH], in_=stats[:, 3 * CH:4 * CH], func=AF.Exp,
            scale=-0.5), sig=c.s_act)

        def smk2(e):
            e.scalar_tensor_tensor(out=stats[:, 4 * CH:5 * CH], in0=stats[:, 0:CH],
                                   scalar=-1.0, in1=stats[:, 3 * CH:4 * CH],
                                   op0=ALU.mult, op1=ALU.mult)
            e.tensor_copy(out=statb[:, 0:CH], in_=stats[:, 3 * CH:4 * CH])
            e.drain()
            return e.tensor_copy(out=statb[:, CH:2 * CH], in_=stats[:, 4 * CH:5 * CH])
        tsk2 = p.op("vector", smk2, reads=[tl2, statb_done], sig=c.s_dve)
        stats_done = tsk2
        c.pstat_use = [tsk]
        c.pb_use[4] = [tsk]

        trb = p.dma("gpsimd", c.rowk, statb[:],
                    reads=[tsk2, getattr(c, "_rowk_read", [])], sig=s_rb)
        statb_done = trb

        for vd in range(4):
            vps = [c.pb[t] for t in range(4)]
            mm_last = [None] * 4
            for q in range(4):
                wi = wkc % 6
                wkc += 1
                tw = p.dma("sync", wkh[wi][:], c.wkvP[16 + vd * 4 + q],
                           reads=[wk_done[wi], c.t_wprep_end], sig=r_wk[wi])
                for Ki in range(4):
                    K = q * 4 + Ki
                    for t4 in range(4):
                        mm_last[t4] = p.op("tensor", lambda e, K=K, Ki=Ki, wi=wi, t4=t4, ri=ri, vv=vps[t4]: e.matmul(
                            out=vv[:], lhsT=zT[ri][K][:, t4 * 128:(t4 + 1) * 128],
                            rhs=wkh[wi][:, Ki * CH:(Ki + 1) * CH],
                            start=(K == 0), stop=False),
                            reads=[tw, tzt] + (c.pb_use[t4] if K == 0 else []), sig=c.s_pe)
                wk_done[wi] = mm_last[3]
            for t4 in range(4):
                c.pb_use[t4] = []
                mmb = p.op("tensor", lambda e, vd=vd, vv=vps[t4]: e.matmul(
                    out=vv[:], lhsT=c.ones_row[0:1, 0:128],
                    rhs=c.browkv_bf[0:1, 2048 + vd * CH:2048 + (vd + 1) * CH],
                    start=False, stop=True), reads=[mm_last[t4], c.t_brow], sig=c.s_pe)
                si = t4 % 2
                tev = p.op("scalar", lambda e, si=si, vv=vps[t4]: e.activation(
                    out=vsb[si][:], in_=vv[:], func=AF.Copy),
                    reads=[mmb, v_done[si]], sig=c.s_act)
                c.pb_use[t4].append(tev)
                kt = cc * 4 + t4
                tvs = p.dma("gpsimd",
                            c.vtok[kt, vd * 4:(vd + 1) * 4].rearrange("h t d -> t h d"),
                            vsb[si][:].rearrange("t (h d) -> t h d", h=4),
                            reads=[tev], sig=r_vst[si])
                v_done[si] = tvs
        rowk_reads = []
        row_users = []
        for h in range(16):
            si = h % 2
            pc = c.pb[2 * si]
            pd = c.pb[2 * si + 1]
            if h % 8 == 0:
                rflat = c.rowk.rearrange("a b -> (a b)")
                trh = p.dma("sync", rowsb[:],
                            rflat[h * 2 * CH:(h + 8) * 2 * CH].unsqueeze(0),
                            reads=[trb, row_done[0]], sig=r_row[0])
                rowk_reads.append(trh)
                row_done[0] = []
            ho = (h % 8) * 2 * CH
            tmc = p.op("tensor", lambda e, ho=ho, pc=pc: e.matmul(
                out=pc[:], lhsT=c.lrow[0:1, 2 * HD:3 * HD],
                rhs=rowsb[0:1, ho:ho + CH], start=True, stop=True),
                reads=[trh, c.t_lrow] + c.pb_use[2 * si], sig=c.s_pe)
            tmd1 = p.op("tensor", lambda e, ho=ho, pd=pd: e.matmul(
                out=pd[:], lhsT=c.lrow[0:1, 2 * HD:3 * HD],
                rhs=rowsb[0:1, ho + CH:ho + 2 * CH], start=True, stop=False),
                reads=[trh] + c.pb_use[2 * si + 1], sig=c.s_pe)
            tmd2 = p.op("tensor", lambda e, pd=pd: e.matmul(
                out=pd[:], lhsT=c.lrow[0:1, 3 * HD:4 * HD], rhs=c.ones_row[:],
                start=False, stop=True), sig=c.s_pe)
            row_users += [tmc, tmd1]
            row_done[0] = row_done[0] + [tmc, tmd1] if isinstance(row_done[0], list) else [tmc, tmd1]
            c.pb_use[2 * si] = []
            c.pb_use[2 * si + 1] = []
            tt1 = p.op("vector", lambda e, h=h, si=si, pc=pc: e.tensor_tensor(
                out=tmpsb[si][:], in0=rotb[h][:], in1=pc[:], op=ALU.mult),
                reads=[tmc, tmp_done[si]], sig=c.s_dve)
            tt2 = p.op("vector", lambda e, si=si, pd=pd: e.tensor_tensor(
                out=knsb[si][:], in0=tmpsb[si][:], in1=pd[:], op=ALU.add),
                reads=[tmd2, kn_done[si]], sig=c.s_dve)
            rot_done[h] = [tt1]
            c.pb_use[2 * si].append(tt1)
            c.pb_use[2 * si + 1].append(tt2)
            tmp_done[si] = tt2
            if h >= 2:
                hd2 = h - 2
                tkn = p.op("tensor", lambda e, hd2=hd2, si=si: e.matmul(
                    out=knv, lhsT=c.ehs[:, hd2 * 16:(hd2 + 1) * 16], rhs=kn2[si][:],
                    start=(hd2 == 0), stop=False),
                    reads=[tsq_pend[si]] + (c.pb_use[5] if hd2 == 0 else []),
                    sig=c.s_pe)
                kn2_done[si] = tkn
            tsq2 = p.op("scalar", lambda e, si=si: e.activation(
                out=kn2[si][:], in_=knsb[si][:], func=AF.Square),
                reads=[tt2, kn2_done[si]], sig=c.s_act)
            tsq_pend[si] = tsq2
            tst = p.dma("gpsimd", c.knT[h * 128:(h + 1) * 128, cc * CH:(cc + 1) * CH],
                        knsb[si][:], reads=[tt2], sig=r_kst[si])
            kn_done[si] = [tst, tsq2]
        for hd2 in (14, 15):
            si = hd2 % 2
            tkn = p.op("tensor", lambda e, hd2=hd2, si=si: e.matmul(
                out=knv, lhsT=c.ehs[:, hd2 * 16:(hd2 + 1) * 16], rhs=kn2[si][:],
                start=False, stop=(hd2 == 15)),
                reads=[tsq_pend[si]], sig=c.s_pe)
            kn2_done[si] = tkn
        c._rowk_read = rowk_reads
        tmx = p.op("vector", lambda e: e.reduce_max(out=maxcol[:], in_=knv, axis=AX.X),
                   reads=[kn2_done[0], kn2_done[1], maxk_done], sig=c.s_dve)
        p.drain("vector")
        tmx2 = p.op("vector", lambda e: e.tensor_max(
            out=c.maxk2[:], in0=c.maxk2[:], in1=maxcol[:]), sig=c.s_dve)
        maxk_done = tmx2
        c.pb_use[5] = [tmx]

        zt_done[ri] = [mm_last[t] for t in range(4)]
        cs_done[ri] = krot_done[:]

    c.t_kv_end = [Tok(s, s.count, "_kv_", True)
                  for s in (r_kst[0], r_kst[1], r_vst[0], r_vst[1])]
    c.t_maxk = maxk_done

    if "d_kn" in c.dbg:
        p.dma("gpsimd", c.dbg["d_kn"][:, 0:nch * CH], c.knT[:, 0:nch * CH],
              reads=[c.t_kv_end], sig=c.s_store)
        p.dma("gpsimd", c.dbg["d_mk"].rearrange("o s -> s o"), c.maxk2[:],
              reads=[maxk_done], sig=c.s_store)


def phase_dequant(c):
    p = c.p
    r_nb = p.ring("rnb", 2)
    r_sc = p.ring("rsc", 2)
    r_wo = p.ring_slots("rwo", 2)
    nib = [c.cu(CH) for _ in range(2)]
    t2b = [c.cu(CH) for _ in range(2)]
    bits = [c.cu(CH) for _ in range(2)]
    scf = [c.cf(CH) for _ in range(2)]
    wout = [c.cb(CH) for _ in range(2)]
    sc16 = [c.sb(f"sc16_{i}", [128, CH], F16) for i in range(2)]

    nib_done = [None, None]
    t2_done = [None, None]
    bits_done = [None, None]
    sc_done = [None, None]
    scf_done = [None, None]
    w_done = [None, None]

    jobs = []
    for g in range(16):
        for j in range(2 * I_FFN // CH):
            jobs.append(("gu", g, j))
    for g in range(44):
        for j in range(D // CH):
            jobs.append(("dn", g, j))
    if getattr(c, "DQN", None):
        jobs = jobs[:c.DQN]

    for idx, (kind, g, j) in enumerate(jobs):
        si = idx % 2
        if kind == "gu":
            nsrc = c.gunib[g * 128:(g + 1) * 128, j * CH:(j + 1) * CH]
            ssrc = c.gusc[g, :, j * CH:(j + 1) * CH]
            wdst = c.wguT[g * 128:(g + 1) * 128, j * CH:(j + 1) * CH]
        else:
            nsrc = c.dnnib[g * 128:(g + 1) * 128, j * CH:(j + 1) * CH]
            ssrc = c.dnsc[g, :, j * CH:(j + 1) * CH]
            wdst = c.wdT[g * 128:(g + 1) * 128, j * CH:(j + 1) * CH]
        tn = p.dma("sync", nib[si][:], nsrc, reads=[nib_done[si]], sig=r_nb[si])
        tsc = p.dma("sync", sc16[si][:], ssrc, reads=[sc_done[si]], sig=r_sc[si])
        tt2 = p.op("vector", lambda e, si=si: e.tensor_scalar(
            out=t2b[si][:], in0=nib[si][:], scalar1=8, scalar2=32256,
            op0=ALU.is_ge, op1=ALU.mult), reads=[tn, t2_done[si]], sig=c.s_dve)
        tbi = p.op("vector", lambda e, si=si: e.scalar_tensor_tensor(
            out=bits[si][:], in0=nib[si][:], scalar=64.0, in1=t2b[si][:],
            op0=ALU.mult, op1=ALU.add), reads=[tt2, bits_done[si]], sig=c.s_dve)
        nib_done[si] = tbi
        t2_done[si] = tbi
        tsf = p.op("vector", lambda e, si=si: e.tensor_scalar(
            out=scf[si][:], in0=sc16[si][:], scalar1=float(2.0 ** 126), scalar2=None,
            op0=ALU.mult), reads=[tsc, scf_done[si]], sig=c.s_dve)
        sc_done[si] = tsf
        tw = p.op("vector", lambda e, si=si: e.tensor_tensor(
            out=wout[si][:], in0=bits[si][:].bitcast(BF16)[:, 0:CH], in1=scf[si][:],
            op=ALU.mult), reads=[tbi, tsf, w_done[si]], sig=c.s_dve)
        bits_done[si] = tw
        scf_done[si] = tw
        tst = p.dma("gpsimd", wdst, wout[si][:], reads=[tw], sig=r_wo[si])
        w_done[si] = tst
    c.t_deq_end = [Tok(s, s.count, "_dq_", True) for s in (r_wo[0], r_wo[1])]
    if "d_wgu" in c.dbg:
        p.dma("gpsimd", c.dbg["d_wgu"], c.wguT, reads=[c.t_deq_end], sig=c.s_store)
        p.dma("gpsimd", c.dbg["d_wd"], c.wdT, reads=[c.t_deq_end], sig=c.s_store)


def phase_ffn(c):
    p = c.p
    noc = min(4, getattr(c, "FFNN", 4))

    r_xt = p.ring("rxt", 2)
    r_wg = c.rw12
    r_wd = p.ring("rwd", 8)
    r_ys = p.ring_slots("rys", 2)

    xnT = [c.cb(CH) for _ in range(16)]
    act = [c.cb(CH) for _ in range(44)]
    sg = [c.cb(CH) for _ in range(4)]
    upb = [c.cb(CH) for _ in range(4)]
    wg = [c.cb(CH) for _ in range(12)]
    wd = [c.cb(CH) for _ in range(8)]
    ysb = [c.cf(CH) for _ in range(2)]
    sqjunk = c.cf(CH)
    sqc = c.cf(16)
    rcol = c.cf(4)
    c.ffn_rstd = c.sb("ffn_rstd", [128, 16], F32)

    xnT_done = [None] * 16
    wg_done = [None] * 12
    wd_done = [None] * 8
    sg_done = [None] * 4
    up_done = [None] * 4
    act_done = [None] * 44
    y_done = [None, None]
    sq_done = None

    wgc = 0
    wdc = 0
    for oc in range(noc):
        txt = None
        for K in range(16):
            txt = p.dmat("sync", xnT[K][:],
                         c.xn_tok[oc * CH:(oc + 1) * CH, K * 128:(K + 1) * 128],
                         reads=[xnT_done[K], c.t_xn_done], sig=r_xt[oc % 2])
        for j4 in range(11):
            for half in range(2):
                dst = sg if half == 0 else upb
                dst_done = sg_done if half == 0 else up_done
                base = half * I_FFN + j4 * 4 * 128
                mmlast = [None] * 4
                for K in range(16):
                    wi = wgc % 12
                    wgc += 1
                    twl = p.dma("sync", wg[wi][:],
                                c.wguT[K * 128:(K + 1) * 128, base:base + CH],
                                reads=[wg_done[wi], c.t_deq_end], sig=r_wg[wi])
                    for o4 in range(4):
                        mmlast[o4] = p.op("tensor", lambda e, K=K, wi=wi, o4=o4: e.matmul(
                            out=c.pb[o4][:], lhsT=wg[wi][:, o4 * 128:(o4 + 1) * 128],
                            rhs=xnT[K][:], start=(K == 0), stop=(K == 15)),
                            reads=[twl, txt] + (c.pb_use[o4] if K == 0 else []),
                            sig=c.s_pe)
                    wg_done[wi] = mmlast[3]
                for o4 in range(4):
                    c.pb_use[o4] = []
                    if half == 0:
                        tev = p.op("scalar", lambda e, o4=o4: e.activation(
                            out=sg[o4][:], in_=c.pb[o4][:], func=AF.Silu),
                            reads=[mmlast[o4], sg_done[o4]], sig=c.s_act)
                        sg_done[o4] = tev
                    else:
                        tev = p.op("scalar", lambda e, o4=o4: e.activation(
                            out=upb[o4][:], in_=c.pb[o4][:], func=AF.Copy),
                            reads=[mmlast[o4], up_done[o4]], sig=c.s_act)
                        up_done[o4] = tev
                    c.pb_use[o4].append(tev)
            for o4 in range(4):
                jj = j4 * 4 + o4
                tmu = p.op("vector", lambda e, o4=o4, jj=jj: e.tensor_mul(
                    out=act[jj][:], in0=sg[o4][:], in1=upb[o4][:]),
                    reads=[sg_done[o4], up_done[o4], act_done[jj]], sig=c.s_dve)
                sg_done[o4] = [sg_done[o4], tmu]
                up_done[o4] = [up_done[o4], tmu]
                act_done[jj] = tmu
        for K in range(16):
            xnT_done[K] = wg_done[5]
        sq_toks = []
        for hj in range(4):
            mmlast = [None] * 4
            for i in range(44):
                wi = wdc % 8
                wdc += 1
                twl = p.dma("sync", wd[wi][:],
                            c.wdT[i * 128:(i + 1) * 128, hj * CH:(hj + 1) * CH],
                            reads=[wd_done[wi], c.t_deq_end], sig=r_wd[wi])
                for ts in range(4):
                    mmlast[ts] = p.op("tensor", lambda e, i=i, wi=wi, ts=ts: e.matmul(
                        out=c.pb[ts][:], lhsT=act[i][:, ts * 128:(ts + 1) * 128],
                        rhs=wd[wi][:], start=(i == 0), stop=(i == 43)),
                        reads=[twl, act_done[i]] + (c.pb_use[ts] if i == 0 else []),
                        sig=c.s_pe)
                wd_done[wi] = mmlast[3]
            for ts in range(4):
                c.pb_use[ts] = []
                si = ts % 2
                tyc = p.op("vector", lambda e, ts=ts, si=si: e.tensor_copy(
                    out=ysb[si][:], in_=c.pb[ts][:]), reads=[mmlast[ts], y_done[si]],
                    sig=c.s_dve)
                c.pb_use[ts].append(tyc)
                tsq = p.op("scalar", lambda e, ts=ts, hj=hj, si=si: e.activation(
                    out=sqjunk[:], in_=ysb[si][:], func=AF.Square,
                    accum_out=sqc[:, ts * 4 + hj:ts * 4 + hj + 1]),
                    reads=[tyc, sq_done], sig=c.s_act)
                tys = p.dma("gpsimd",
                            c.ffny[oc * CH + ts * 128: oc * CH + (ts + 1) * 128,
                                   hj * CH:(hj + 1) * CH],
                            ysb[si][:], reads=[tyc], sig=r_ys[si])
                y_done[si] = [tys, tsq]
                sq_toks.append(tsq)
        for i in range(44):
            act_done[i] = wd_done[5]
        for ts in range(4):
            def rms_ts(e, ts=ts, oc=oc):
                e.tensor_tensor(out=rcol[:, 0:1], in0=sqc[:, ts * 4:ts * 4 + 1],
                                in1=sqc[:, ts * 4 + 1:ts * 4 + 2], op=ALU.add)
                e.tensor_tensor(out=rcol[:, 1:2], in0=sqc[:, ts * 4 + 2:ts * 4 + 3],
                                in1=sqc[:, ts * 4 + 3:ts * 4 + 4], op=ALU.add)
                e.drain()
                return e.tensor_scalar(out=rcol[:, 2:3], in0=rcol[:, 0:1],
                                       scalar1=rcol[:, 1:2], scalar2=1.0 / D,
                                       op0=ALU.add, op1=ALU.mult)
            trm = p.op("vector", rms_ts, reads=sq_toks + [getattr(c, "_rms_prev", None)],
                       sig=c.s_dve)
            tl = p.op("scalar", lambda e: e.activation(
                out=rcol[:, 3:4], in_=rcol[:, 2:3], func=AF.Ln, bias=c.rmseps[:]),
                reads=[trm, c.t_const], sig=c.s_act)
            c._rms_prev = tl
            p.drain("scalar")
            tex = p.op("scalar", lambda e, oc=oc, ts=ts: e.activation(
                out=c.ffn_rstd[:, oc * 4 + ts:oc * 4 + ts + 1], in_=rcol[:, 3:4],
                func=AF.Exp, scale=-0.5), sig=c.s_act)
            sq_done = tex
            c.t_ffn_rstd = tex
    c.t_ffn_end = [Tok(s, s.count, "_ff_", True) for s in (r_ys[0], r_ys[1])]
    if "d_ffny" in c.dbg:
        p.dma("gpsimd", c.dbg["d_ffny"][0:noc * CH, :], c.ffny[0:noc * CH, :],
              reads=[c.t_ffn_end], sig=c.s_store)


def phase_attn(c):
    p, sb = c.p, c.sb
    nqc = min(4, getattr(c, "ATN", 4))
    SEG = 8

    r_zt = p.ring("azt", 1)
    r_wq = c.rw12
    r_cs = p.ring("acs", 2)
    r_eps = p.ring("aeps", 2)
    r_row = p.ring("arow", 2)
    r_kn = p.ring("akn", 3)
    r_vt = p.ring("avt", 3)
    r_wo = p.ring("awo", 6)
    r_ff = p.ring("aff", 2)
    r_xo = p.ring("axo", 2)
    r_mk = p.ring("amk", 1)
    r_sw = p.ring("asw2", 2)
    s_rb = p.sem("a_rowb")
    s_out = p.ring_slots("sout", 2)
    c.s_out = s_out[0]

    zT = [c.cb(CH) for _ in range(16)]
    qrotb = [c.cb(CH) for _ in range(16)]
    qn = zT
    aT = qrotb
    masks_sb = c.cb(16 * CH)
    wq4 = [c.cb(CH) for _ in range(12)]
    wot = [c.cb(CH) for _ in range(6)]
    knseg = [c.cb(SEG * 128) for _ in range(3)]
    vseg = [c.cb(SEG * 128) for _ in range(3)]
    Pb = [c.cb(CH) for _ in range(2)]
    rbc = c.cb(CH)
    statb = c.cb(2 * CH)[0:16, :]
    rowsb = c.cf(8 * CH).bitcast(BF16)[0:1, :]
    epsr = [c.cb(CH)[0:1, :] for _ in range(2)]
    q2b = [c.cb(CH) for _ in range(2)]
    lrb = c.cb(CH)[0:1, :]
    mx16 = c.cb(16)[0:16, 0:1]
    mxr = c.cb(16)[0:1, :]

    cosc = [c.cf(CH) for _ in range(2)]
    sinc = [c.cf(CH) for _ in range(2)]
    krot = [c.cf(CH) for _ in range(2)]
    krot_sw = [c.cf(CH) for _ in range(2)]
    p1 = [c.cf(CH) for _ in range(2)]
    p2 = [c.cf(CH) for _ in range(2)]
    stats = c.cf(5 * CH)[0:16, :]
    fft = [c.cf(CH) for _ in range(2)]
    xot = [c.cf(CH) for _ in range(2)]
    outb = [c.cf(CH) for _ in range(2)]
    tmpf = [c.cf(CH) for _ in range(2)]
    lrow = c.cf(CH)[0:1, :]
    qmaxc = c.cf(4)[0:16, 0:1]
    mrow = c.cf(4)[0:1, :]
    negMb = c.cf(1)

    zt_done = None
    wq_done = [None] * 12
    wo_done = [None] * 6
    cs_done = [None, None]
    eps_done = [None, None]
    row_done = [None, None]
    ksg_done = [None, None, None]
    vsg_done = [None, None, None]
    ff_done = [None, None]
    xo_done = [None, None]
    out_done = [None, None]
    tmp_done = [None, None]
    P_done = [None, None]
    q2_done = [None, None]
    qsq_pend = [None, None]
    qrot_done = [None] * 16
    qn_done = [None] * 16
    aT_done = [None] * 16
    krot_done = [None, None]
    p12_done = [None, None]
    stats_done = None
    statb_done = None
    lr_done = None
    rbc_done = None
    negM_done = None
    mx_done = None
    mk_done = None

    s1v = c.pstat[0:16, :]
    s2v = c.pb[4][0:16, :]
    qnv = c.pb[5][0:16, :]

    wqc = 0
    woc = 0
    for qi in range(nqc):
        tzt = None
        for K in range(16):
            tzt = p.dmat("sync", zT[K][:],
                         c.zown[qi * CH:(qi + 1) * CH, K * 128:(K + 1) * 128],
                         reads=[zt_done, c.t_z_done], sig=r_zt[0])
        ri = qi % 2
        tcs = p.dma("sync", cosc[ri][:], c.cosdo[:, qi * CH:(qi + 1) * CH],
                    reads=[cs_done[ri]], sig=r_cs[ri])
        tsn = p.dma("sync", sinc[ri][:], c.sindo[:, qi * CH:(qi + 1) * CH],
                    reads=[cs_done[ri]], sig=r_cs[ri])
        tep = p.dma("sync", epsr[ri][:],
                    c.epso.rearrange("a b -> (a b)")[qi * CH:(qi + 1) * CH].unsqueeze(0),
                    reads=[eps_done[ri], c.t_epso], sig=r_eps[ri])
        tmk = p.dma("sync", masks_sb[:].rearrange("p (m q) -> p m q", m=16),
                    c.maskv[qi].rearrange("m p q -> p m q"),
                    reads=[mk_done], sig=r_mk[0])

        stat_last = [None, None]
        for hg in range(4):
            mmg = [None] * 4
            for K in range(16):
                wi = wqc % 12
                wqc += 1
                tw = p.dma("sync", wq4[wi][:],
                           c.wqP[K * 128:(K + 1) * 128, hg * CH:(hg + 1) * CH],
                           reads=[wq_done[wi], c.t_wprep_end], sig=r_wq[wi])
                for hi in range(4):
                    mmg[hi] = p.op("tensor", lambda e, K=K, wi=wi, hi=hi: e.matmul(
                        out=c.pb[hi][:], lhsT=wq4[wi][:, hi * 128:(hi + 1) * 128],
                        rhs=zT[K][:], start=(K == 0), stop=False),
                        reads=[tw, tzt] + (c.pb_use[hi] if K == 0 else []), sig=c.s_pe)
                wq_done[wi] = mmg[3]
            for hi in range(4):
                h = hg * 4 + hi
                pbi = h % 2
                c.pb_use[hi] = []
                mmb = p.op("tensor", lambda e, h=h, hi=hi: e.matmul(
                    out=c.pb[hi][:], lhsT=c.browq_bf[0:1, h * 128:(h + 1) * 128],
                    rhs=c.ones_row[:], start=False, stop=True),
                    reads=[mmg[hi], c.t_brow, c.t_const], sig=c.s_pe)
                tev = p.op("scalar", lambda e, hi=hi, ki=pbi: e.activation(
                    out=krot[ki][:], in_=c.pb[hi][:], func=AF.Copy),
                    reads=[mmb, krot_done[pbi]], sig=c.s_act)
                c.pb_use[hi].append(tev)
                tw1 = p.dma("gpsimd", krot_sw[pbi][0:64, :], krot[pbi][64:128, :],
                            reads=[tev, krot_done[pbi]], sig=r_sw[pbi])
                tw2 = p.dma("gpsimd", krot_sw[pbi][64:128, :], krot[pbi][0:64, :],
                            sig=r_sw[pbi])
                tp1 = p.op("gpsimd", lambda e, ri=ri, ki=pbi: e.tensor_mul(
                    out=p1[ki][:], in0=krot[ki][:], in1=cosc[ri][:]),
                    reads=[tev, tsn, p12_done[pbi]], sig=c.s_pool)
                tp2 = p.op("gpsimd", lambda e, ri=ri, ki=pbi: e.tensor_mul(
                    out=p2[ki][:], in0=krot_sw[ki][:], in1=sinc[ri][:]),
                    reads=[tw2, tsn], sig=c.s_pool)
                krot_done[pbi] = [tp1, tp2]
                tcb = p.op("vector", lambda e, h=h, ki=pbi: e.tensor_add(
                    out=qrotb[h][:], in0=p1[ki][:], in1=p2[ki][:]),
                    reads=[tp1, tp2, qrot_done[h]], sig=c.s_dve)
                p12_done[pbi] = tcb
                qrot_done[h] = [tcb]
        for h in range(16):
            ki = h % 2
            tsq = p.op("scalar", lambda e, h=h, ki=ki: e.activation(
                out=q2b[ki][:], in_=qrotb[h][:], func=AF.Square),
                reads=[qrot_done[h], q2_done[ki]], sig=c.s_act)
            ts1 = p.op("tensor", lambda e, h=h: e.matmul(
                out=s1v, lhsT=c.ehs[:, h * 16:(h + 1) * 16], rhs=qrotb[h][:],
                start=(h == 0), stop=(h == 15)),
                reads=[qrot_done[h], c.t_const] + (c.pstat_use if h == 0 else []),
                sig=c.s_pe)
            ts2 = p.op("tensor", lambda e, h=h, ki=ki: e.matmul(
                out=s2v, lhsT=c.ehs[:, h * 16:(h + 1) * 16], rhs=q2b[ki][:],
                start=(h == 0), stop=False),
                reads=[tsq] + (c.pb_use[4] if h == 0 else []), sig=c.s_pe)
            q2_done[ki] = ts2
            stat_last = [ts1, ts2]
        tse = p.op("tensor", lambda e, ri=ri: e.matmul(
            out=s2v, lhsT=c.ones_row[0:1, 0:16], rhs=epsr[ri][:],
            start=False, stop=True), reads=[tep, stat_last[1], c.t_const], sig=c.s_pe)
        eps_done[ri] = tse

        def smk(e):
            e.tensor_scalar(out=stats[:, 0:CH], in0=s1v, scalar1=1.0 / 128,
                            scalar2=None, op0=ALU.mult)
            e.tensor_scalar(out=stats[:, CH:2 * CH], in0=s2v, scalar1=1.0 / 128,
                            scalar2=None, op0=ALU.mult)
            e.drain()
            e.scalar_tensor_tensor(out=stats[:, 2 * CH:3 * CH], in0=stats[:, 0:CH],
                                   scalar=-1.0, in1=stats[:, 0:CH],
                                   op0=ALU.mult, op1=ALU.mult)
            e.drain()
            return e.tensor_tensor(out=stats[:, 2 * CH:3 * CH],
                                   in0=stats[:, CH:2 * CH],
                                   in1=stats[:, 2 * CH:3 * CH], op=ALU.add)
        tsk = p.op("vector", smk, reads=[tse, stat_last[0], stats_done], sig=c.s_dve)
        tl1 = p.op("scalar", lambda e: e.activation(
            out=stats[:, 3 * CH:4 * CH], in_=stats[:, 2 * CH:3 * CH], func=AF.Ln),
            reads=[tsk], sig=c.s_act)
        p.drain("scalar")
        tl2 = p.op("scalar", lambda e: e.activation(
            out=stats[:, 3 * CH:4 * CH], in_=stats[:, 3 * CH:4 * CH], func=AF.Exp,
            scale=-0.5), sig=c.s_act)

        def smk2(e):
            e.scalar_tensor_tensor(out=stats[:, 4 * CH:5 * CH], in0=stats[:, 0:CH],
                                   scalar=-1.0, in1=stats[:, 3 * CH:4 * CH],
                                   op0=ALU.mult, op1=ALU.mult)
            e.tensor_copy(out=statb[:, 0:CH], in_=stats[:, 3 * CH:4 * CH])
            e.drain()
            return e.tensor_copy(out=statb[:, CH:2 * CH], in_=stats[:, 4 * CH:5 * CH])
        tsk2 = p.op("vector", smk2, reads=[tl2, statb_done], sig=c.s_dve)
        stats_done = tsk2
        c.pstat_use = [tsk]
        c.pb_use[4] = [tsk]

        trb = p.dma("gpsimd", c.rowq, statb[:],
                    reads=[tsk2, getattr(c, "_rowq_read", [])], sig=s_rb)
        statb_done = trb

        rowq_reads = []
        row_users = []
        for h in range(16):
            si = h % 2
            pc = c.pb[2 * si]
            pd = c.pb[2 * si + 1]
            if h % 8 == 0:
                rflat = c.rowq.rearrange("a b -> (a b)")
                trh = p.dma("sync", rowsb[:],
                            rflat[h * 2 * CH:(h + 8) * 2 * CH].unsqueeze(0),
                            reads=[trb, row_done[0]], sig=r_row[0])
                rowq_reads.append(trh)
                row_done[0] = []
            ho = (h % 8) * 2 * CH
            tmc = p.op("tensor", lambda e, ho=ho, pc=pc: e.matmul(
                out=pc[:], lhsT=c.lrow[0:1, 0:HD],
                rhs=rowsb[0:1, ho:ho + CH],
                start=True, stop=True),
                reads=[trh, c.t_lrow] + c.pb_use[2 * si], sig=c.s_pe)
            tmd1 = p.op("tensor", lambda e, ho=ho, pd=pd: e.matmul(
                out=pd[:], lhsT=c.lrow[0:1, 0:HD],
                rhs=rowsb[0:1, ho + CH:ho + 2 * CH],
                start=True, stop=False),
                reads=[trh] + c.pb_use[2 * si + 1], sig=c.s_pe)
            tmd2 = p.op("tensor", lambda e, pd=pd: e.matmul(
                out=pd[:], lhsT=c.lrow[0:1, HD:2 * HD], rhs=c.ones_row[:],
                start=False, stop=True), sig=c.s_pe)
            row_users += [tmc, tmd1]
            row_done[0] = row_done[0] + [tmc, tmd1]
            c.pb_use[2 * si] = []
            c.pb_use[2 * si + 1] = []
            tt1 = p.op("vector", lambda e, h=h, si=si, pc=pc: e.tensor_tensor(
                out=Pb[si][:], in0=qrotb[h][:], in1=pc[:], op=ALU.mult),
                reads=[tmc, P_done[si]], sig=c.s_dve)
            tt2 = p.op("vector", lambda e, h=h, si=si, pd=pd: e.tensor_tensor(
                out=qn[h][:], in0=Pb[si][:], in1=pd[:], op=ALU.add),
                reads=[tmd2, qn_done[h]], sig=c.s_dve)
            qrot_done[h] = [tt1]
            P_done[si] = tt2
            c.pb_use[2 * si].append(tt1)
            c.pb_use[2 * si + 1].append(tt2)
            if h >= 2:
                hd2 = h - 2
                tqn = p.op("tensor", lambda e, hd2=hd2, si=si: e.matmul(
                    out=qnv, lhsT=c.ehs[:, hd2 * 16:(hd2 + 1) * 16], rhs=q2b[si][:],
                    start=(hd2 == 0), stop=False),
                    reads=[qsq_pend[si]] + (c.pb_use[5] if hd2 == 0 else []),
                    sig=c.s_pe)
                q2_done[si] = tqn
            tsq2 = p.op("scalar", lambda e, h=h, si=si: e.activation(
                out=q2b[si][:], in_=qn[h][:], func=AF.Square),
                reads=[tt2, q2_done[si]], sig=c.s_act)
            qsq_pend[si] = tsq2
            qn_done[h] = [tsq2]
        for hd2 in (14, 15):
            si = hd2 % 2
            tqn = p.op("tensor", lambda e, hd2=hd2, si=si: e.matmul(
                out=qnv, lhsT=c.ehs[:, hd2 * 16:(hd2 + 1) * 16], rhs=q2b[si][:],
                start=False, stop=(hd2 == 15)),
                reads=[qsq_pend[si]], sig=c.s_pe)
            q2_done[si] = tqn
        c._rowq_read = rowq_reads

        def mx1(e):
            e.reduce_max(out=qmaxc[:, 0:1], in_=qnv, axis=AX.X)
            e.drain()
            e.tensor_tensor(out=qmaxc[:, 0:1], in0=qmaxc[:, 0:1], in1=c.maxk2[:],
                            op=ALU.mult)
            e.drain()
            return e.tensor_copy(out=mx16[:, 0:1], in_=qmaxc[:, 0:1])
        tmx = p.op("vector", mx1, reads=[q2_done[0], q2_done[1], c.t_maxk, mx_done],
                   sig=c.s_dve)
        c.pb_use[5] = [tmx]
        ttr = p.op("tensor", lambda e: e.transpose(
            out=c.ptr[0:1, 0:16], in_=mx16[0:16, 0:1], identity=c.identb[0:16, 0:16]),
            reads=[tmx, c.t_const] + c.ptr_use, sig=c.s_pe)
        c.ptr_use = []
        tmr = p.op("vector", lambda e: e.tensor_copy(out=mxr[0:1, 0:16],
                                                     in_=c.ptr[0:1, 0:16]),
                   reads=[ttr], sig=c.s_dve)
        c.ptr_use.append(tmr)
        p.drain("vector")
        tmm = p.op("vector", lambda e: e.reduce_max(out=mrow[0:1, 0:1],
                                                    in_=mxr[0:1, 0:16], axis=AX.X),
                   sig=c.s_dve)
        tsf = p.op("scalar", lambda e: e.activation(
            out=mrow[0:1, 1:2], in_=mrow[0:1, 0:1], func=AF.Sqrt),
            reads=[tmm], sig=c.s_act)
        p.drain("scalar")
        tng = p.op("scalar", lambda e: e.activation(
            out=mrow[0:1, 2:3], in_=mrow[0:1, 1:2], func=AF.Copy, scale=-1.02),
            sig=c.s_act)
        tngb = p.op("vector", lambda e: e.tensor_copy(out=mxr[0:1, 0:1],
                                                      in_=mrow[0:1, 2:3]),
                    reads=[tng, tmr], sig=c.s_dve)
        tmb = p.op("tensor", lambda e: e.matmul(
            out=c.pb[0][:, 0:1], lhsT=c.ones_row[0:1, 0:128], rhs=mxr[0:1, 0:1],
            start=True, stop=True), reads=[tngb, c.t_const] + c.pb_use[0], sig=c.s_pe)
        tnb = p.op("vector", lambda e: e.tensor_copy(out=negMb[:, 0:1],
                                                     in_=c.pb[0][:, 0:1]),
                   reads=[tmb], sig=c.s_dve)
        c.pb_use[0] = [tnb]
        negM_done = tnb
        mx_done = [tmm, tngb]

        nkt = SLOT_NKT[qi]
        mk_users = []
        for h in range(16):
            acc = c.pb[4]
            lps = c.pb[5][0:1, :]
            for kt in range(nkt):
                sgi = (kt // SEG) % 3
                if kt % SEG == 0:
                    tkl = p.dma("sync", knseg[sgi][:],
                                c.knT[h * 128:(h + 1) * 128,
                                      kt * 128:(kt + SEG) * 128],
                                reads=[ksg_done[sgi], c.t_kv_end], sig=r_kn[sgi])
                    tvl = p.dma("sync", vseg[sgi][:].rearrange("t (k d) -> t k d", k=SEG),
                                c.vtok[kt:kt + SEG, h].rearrange("k t d -> t k d"),
                                reads=[vsg_done[sgi], c.t_kv_end], sig=r_vt[sgi])
                    ksg_done[sgi] = []
                    vsg_done[sgi] = []
                ko = (kt % SEG) * 128
                sb_ = kt % 2
                spb = c.pb[2 + (kt % 2)]
                diag = kt >= nkt - 16
                tmmS = p.op("tensor", lambda e, h=h, spb=spb, diag=diag, sgi=sgi, ko=ko: e.matmul(
                    out=spb[:], lhsT=knseg[sgi][:, ko:ko + 128], rhs=qn[h][:],
                    start=True, stop=not diag),
                    reads=[tkl, qn_done[h]] + c.pb_use[2 + (kt % 2)], sig=c.s_pe)
                ksg_done[sgi].append(tmmS)
                if diag:
                    midx = kt - (nkt - 16)
                    tmmS = p.op("tensor", lambda e, midx=midx, spb=spb: e.matmul(
                        out=spb[:], lhsT=c.identb[:],
                        rhs=masks_sb[:, midx * CH:(midx + 1) * CH],
                        start=False, stop=True), reads=[c.t_const, tmk], sig=c.s_pe)
                    mk_users.append(tmmS)
                c.pb_use[2 + (kt % 2)] = []
                texp = p.op("scalar", lambda e, sb_=sb_, spb=spb: e.activation(
                    out=Pb[sb_][:], in_=spb[:], func=AF.Exp, bias=negMb[:, 0:1],
                    scale=1.0), reads=[tmmS, negM_done, P_done[sb_]], sig=c.s_act)
                c.pb_use[2 + (kt % 2)].append(texp)
                st = (kt == 0)
                sp = (kt == nkt - 1)
                tacc = p.op("tensor", lambda e, sb_=sb_, acc=acc, st=st, sp=sp,
                            sgi=sgi, ko=ko: e.matmul(
                    out=acc[:], lhsT=vseg[sgi][:, ko:ko + 128], rhs=Pb[sb_][:],
                    start=st, stop=sp),
                    reads=[tvl, texp] + (c.pb_use[4] if st else []), sig=c.s_pe)
                tl_ = p.op("tensor", lambda e, sb_=sb_, lps=lps, st=st, sp=sp: e.matmul(
                    out=lps, lhsT=c.onescol_bf[:], rhs=Pb[sb_][:],
                    start=st, stop=sp),
                    reads=[texp, c.t_const] + (c.pb_use[5] if st else []), sig=c.s_pe)
                vsg_done[sgi].append(tacc)
                P_done[sb_] = [tacc, tl_]
            c.pb_use[4] = []
            c.pb_use[5] = []
            def lrec(e):
                e.tensor_scalar(out=lrow[0:1, :], in0=lps, scalar1=1e-35,
                                scalar2=None, op0=ALU.max)
                e.drain()
                e.reciprocal(out=lrow[0:1, :], in_=lrow[0:1, :])
                e.drain()
                return e.tensor_copy(out=lrb[0:1, :], in_=lrow[0:1, :])
            tlr = p.op("vector", lrec, reads=[tl_, lr_done], sig=c.s_dve)
            lr_done = tlr
            c.pb_use[5].append(tlr)
            trc = p.op("tensor", lambda e: e.matmul(
                out=c.pb[0][:], lhsT=c.ones_row[0:1, 0:128], rhs=lrb[0:1, :],
                start=True, stop=True), reads=[tlr, c.t_const] + c.pb_use[0],
                sig=c.s_pe)
            c.pb_use[0] = []
            tre = p.op("scalar", lambda e: e.activation(
                out=rbc[:], in_=c.pb[0][:], func=AF.Copy),
                reads=[trc, rbc_done], sig=c.s_act)
            c.pb_use[0].append(tre)
            tdv = p.op("vector", lambda e, h=h: e.tensor_tensor(
                out=aT[h][:], in0=acc[:], in1=rbc[:], op=ALU.mult),
                reads=[tre, aT_done[h], qrot_done[h]], sig=c.s_dve)
            rbc_done = tdv
            c.pb_use[4].append(tdv)
            aT_done[h] = tdv
            qn_done[h] = [qn_done[h], tmmS]

        for do in range(4):
            mml = [None] * 4
            for h in range(16):
                wi = woc % 6
                woc += 1
                twl = p.dma("sync", wot[wi][:],
                            c.woP[h * 128:(h + 1) * 128, do * CH:(do + 1) * CH],
                            reads=[wo_done[wi], c.t_wprep_end], sig=r_wo[wi])
                for ts in range(4):
                    mml[ts] = p.op("tensor", lambda e, h=h, wi=wi, ts=ts: e.matmul(
                        out=c.pb[ts][:], lhsT=aT[h][:, ts * 128:(ts + 1) * 128],
                        rhs=wot[wi][:], start=(h == 0), stop=False),
                        reads=[twl, aT_done[h]] + (c.pb_use[ts] if h == 0 else []),
                        sig=c.s_pe)
                wo_done[wi] = mml[3]
            for ts in range(4):
                c.pb_use[ts] = []
                mmb = p.op("tensor", lambda e, do=do, ts=ts: e.matmul(
                    out=c.pb[ts][:], lhsT=c.ones_row[0:1, 0:128],
                    rhs=c.bo_bf[0:1, do * CH:(do + 1) * CH],
                    start=False, stop=True), reads=[mml[ts], c.t_bo], sig=c.s_pe)
                si = ts % 2
                tff = p.dma("sync", fft[si][:],
                            c.ffny[qi * CH + ts * 128: qi * CH + (ts + 1) * 128,
                                   do * CH:(do + 1) * CH],
                            reads=[ff_done[si], c.t_ffn_end], sig=r_ff[si])
                txo = p.dma("sync", xot[si][:],
                            c.xown[qi * CH + ts * 128: qi * CH + (ts + 1) * 128,
                                   do * CH:(do + 1) * CH],
                            reads=[xo_done[si]], sig=r_xo[si])
                tst1 = p.op("vector", lambda e, si=si, ts=ts, qi=qi: e.scalar_tensor_tensor(
                    out=tmpf[si][:], in0=fft[si][:],
                    scalar=c.ffn_rstd[:, qi * 4 + ts:qi * 4 + ts + 1],
                    in1=c.pb[ts][:], op0=ALU.mult, op1=ALU.add),
                    reads=[tff, mmb, c.t_ffn_rstd, tmp_done[si]], sig=c.s_dve)
                c.pb_use[ts].append(tst1)
                tst2 = p.op("vector", lambda e, si=si: e.tensor_tensor(
                    out=outb[si][:], in0=tmpf[si][:], in1=xot[si][:], op=ALU.add),
                    reads=[txo, out_done[si]], sig=c.s_dve)
                ff_done[si] = tst1
                xo_done[si] = tst2
                tmp_done[si] = tst2
                tso = p.dma("gpsimd", c.y[qi * CH + ts * 128: qi * CH + (ts + 1) * 128,
                                          do * CH:(do + 1) * CH],
                            outb[si][:], reads=[tst2], sig=s_out[si])
                out_done[si] = tso
        for h in range(16):
            aT_done[h] = [aT_done[h], mml[3]]
        cs_done[ri] = krot_done[:]
        zt_done = mml[:]
        mk_done = mk_users




_NC_CACHE = {}


def kernel(**inputs):
    from concourse.bass_utils import run_bass_kernel_spmd
    in_maps = host_prepare(inputs)
    if "nc" not in _NC_CACHE:
        _NC_CACHE["nc"] = Core(0).build()
    nc = _NC_CACHE["nc"]
    res = run_bass_kernel_spmd(nc, in_maps, core_ids=list(range(8)))
    return host_gather(res.results)


# version 11
# speedup vs baseline: 1.1047x; 1.0234x over previous
import math
import numpy as np
import ml_dtypes
import concourse.bass as bass
import concourse.mybir as mybir
from concourse.bass_utils import run_bass_kernel_spmd

F32 = mybir.dt.float32
BF16 = mybir.dt.bfloat16
F16 = mybir.dt.float16
I32 = mybir.dt.int32
U16 = mybir.dt.uint16
AF = mybir.ActivationFunctionType
ALU = mybir.AluOpType
AX = mybir.AxisListType

B, S, D, H, HD = 2, 8192, 2048, 16, 128
I_FFN = 5632
CH = 512
NCH = S // CH
KT = 128
ROLE_CHUNKS = [[r, 7 - r, 8 + r, 15 - r] for r in range(4)]
SLOT_NKT = [16, 32, 48, 64]
LN_EPS = 1e-5
RMS_EPS = 1e-6
THETA = 30000.0
SCALE2 = HD ** -0.5
NEG_MASK = -30000.0

HDPERM = np.concatenate([np.arange(0, HD, 2), np.arange(1, HD, 2)])
DPERM = np.concatenate([h * HD + HDPERM for h in range(H)])


def host_prepare(inputs):
    x = np.asarray(inputs["L0_attn_x"])
    Wq = np.asarray(inputs["L0_attn_Wq"])
    Wkv = np.asarray(inputs["L0_attn_Wkv"])
    Wo = np.asarray(inputs["L0_attn_Wo"])
    bq = np.asarray(inputs["L0_attn_bq"])
    bkv = np.asarray(inputs["L0_attn_bkv"])
    bo = np.asarray(inputs["L0_attn_bo"])
    qn_g = np.asarray(inputs["L0_attn_qn_g"])
    qn_b = np.asarray(inputs["L0_attn_qn_b"])
    kvn_g = np.asarray(inputs["L0_attn_kvn_g"])
    kvn_b = np.asarray(inputs["L0_attn_kvn_b"])
    ln_g = np.asarray(inputs["L0_attn_ln_g"])
    ln_b = np.asarray(inputs["L0_attn_ln_b"])
    gup = np.asarray(inputs["L0_ffn_gate_up_packed"])
    gus = np.asarray(inputs["L0_ffn_gate_up_scales"])
    dnp = np.asarray(inputs["L0_ffn_down_packed"])
    dns = np.asarray(inputs["L0_ffn_down_scales"])

    WqP = np.ascontiguousarray(Wq[:, DPERM])
    WkvP = np.concatenate([Wkv[:, :D][:, DPERM], Wkv[:, D:]], axis=1)
    bqP = bq[DPERM].copy()
    bkvP = np.concatenate([bkv[:D][DPERM], bkv[D:]])
    ln_gP = ln_g[HDPERM].copy()
    ln_bP = ln_b[HDPERM].copy()

    def unpack_T(packed, n_in):
        w = packed.view(np.uint32)
        nib = (w[:, :, None] >> (4 * np.arange(8, dtype=np.uint32))) & np.uint32(15)
        nib = nib.reshape(packed.shape[0], n_in)
        return np.ascontiguousarray(nib.T.astype(np.uint16))

    gu_nib = unpack_T(gup, D)
    dn_nib = unpack_T(dnp, I_FFN)

    gu_sc = np.ascontiguousarray(np.broadcast_to(gus[:, None, :], (16, 128, 11264)))
    dn_sc = np.ascontiguousarray(np.broadcast_to(dns[:, None, :], (44, 128, 2048)))

    pos = np.arange(S, dtype=np.float64)
    freqs = (np.arange(0, HD, 2, dtype=np.float64) / HD) * math.log(THETA)
    ang = pos[None, :] * freqs[:, None]
    cosd = np.tile(np.cos(ang), (2, 1)).astype(np.float32)
    sind = np.concatenate([-np.sin(ang), np.sin(ang)]).astype(np.float32)

    qq = np.arange(CH)[None, :]
    kk = np.arange(KT)[:, None]

    ident = np.eye(128, dtype=np.float32).astype(ml_dtypes.bfloat16)

    ehs = np.zeros((H, 128, H), dtype=np.float32)
    for h in range(H):
        ehs[h, :, h] = 1.0
    ehs = ehs.astype(ml_dtypes.bfloat16)

    in_maps = []
    for c in range(8):
        b = c // 4
        role = c % 4
        chunks = ROLE_CHUNKS[role]
        xb = x[b]
        xown = np.concatenate([xb[cc * CH:(cc + 1) * CH] for cc in chunks], axis=0)
        ownpos = np.concatenate([np.arange(cc * CH, (cc + 1) * CH) for cc in chunks])
        mv = np.zeros((4, 16, KT, CH), dtype=np.float32)
        for qi, cc in enumerate(chunks):
            E = SLOT_NKT[qi]
            for p128 in range(16):
                kt = E - 16 + p128
                if kt < 4 * cc:
                    pass
                elif kt <= 4 * cc + 3:
                    d = (kt - 4 * cc) * 128
                    mv[qi, p128] = np.where(qq < kk + d, NEG_MASK, 0.0)
                else:
                    mv[qi, p128] = NEG_MASK
        m = dict(
            xpre=np.ascontiguousarray(xb),
            xown=np.ascontiguousarray(xown),
            wq=WqP, wkv=WkvP, wo=Wo,
            bq=bqP[None, :], bkv=bkvP[None, :], bo=bo[None, :],
            qng=np.ascontiguousarray(qn_g.reshape(16, 128).T),
            qnb=np.ascontiguousarray(qn_b.reshape(16, 128).T),
            kvg=np.ascontiguousarray(kvn_g.reshape(16, 128).T),
            kvb=np.ascontiguousarray(kvn_b.reshape(16, 128).T),
            lng=ln_gP[None, :], lnb=ln_bP[None, :],
            gunib=gu_nib, dnnib=dn_nib,
            gusc=gu_sc, dnsc=dn_sc,
            cosd=np.ascontiguousarray(cosd),
            sind=np.ascontiguousarray(sind),
            cosdo=np.ascontiguousarray(cosd[:, ownpos]),
            sindo=np.ascontiguousarray(sind[:, ownpos]),
            maskv=mv.astype(ml_dtypes.bfloat16),
            ident=ident, ehs=ehs,
        )
        in_maps.append(m)
    return in_maps


def host_gather(results):
    out = np.empty((B, S, D), dtype=np.float32)
    for c in range(8):
        b = c // 4
        chunks = ROLE_CHUNKS[c % 4]
        y = results[c]["y"]
        for j, cc in enumerate(chunks):
            out[b, cc * CH:(cc + 1) * CH] = y[j * CH:(j + 1) * CH]
    return out


class Sem:
    __slots__ = ("h", "name", "count")

    def __init__(self, handle, name):
        self.h = handle
        self.name = name
        self.count = 0


class Tok:
    __slots__ = ("sem", "val", "eng", "dma")

    def __init__(self, sem, val, eng, dma):
        self.sem, self.val, self.eng, self.dma = sem, val, eng, dma


class Prog:

    ENGS = ("sync", "tensor", "vector", "scalar", "gpsimd")

    def __init__(self, nc, stack):
        self.nc = nc
        self.ops = []
        self.sems = []
        self._stack = stack

    def sem(self, name):
        s = Sem(self._stack.enter_context(self.nc.semaphore(name)), name)
        self.sems.append(s)
        return s

    def ring(self, name, n):
        return [self.sem(f"{name}{i}") for i in range(n)]

    def ring_slots(self, name, n):
        return [self.sem(f"{name}{i}") for i in range(n)]

    @staticmethod
    def _flat(reads):
        out = []
        for t in reads:
            if t is None:
                continue
            if isinstance(t, (list, tuple)):
                out.extend(Prog._flat(t))
            else:
                out.append(t)
        return out

    def op(self, eng, fn, reads=(), sig=None, amt=1, dma=False):
        waits = []
        for t in self._flat(reads):
            if (not t.dma) and t.eng == eng:
                continue
            waits.append(t)
        tok = None
        if sig is not None:
            sig.count += amt
            tok = Tok(sig, sig.count, eng, dma)
        self.ops.append((eng, fn, waits, sig, amt))
        return tok

    def drain(self, eng):
        self.ops.append((eng, lambda e: e.drain(), [], None, 0))

    def dma(self, eng, out, in_, reads=(), sig=None, **kw):
        assert sig is not None
        return self.op(eng, lambda e: e.dma_start(out=out, in_=in_, **kw),
                       reads=reads, sig=sig, amt=16, dma=True)

    def dmat(self, eng, out, in_, reads=(), sig=None):
        assert sig is not None
        return self.op(eng, lambda e: e.dma_start_transpose(out, in_),
                       reads=reads, sig=sig, amt=16, dma=True)

    def gate_all(self, toks):
        for eng in self.ENGS:
            self.ops.append((eng, lambda e: e.nop(), Prog._flat(toks), None, 0))

    def final_barrier(self, eng):
        toks = [Tok(s, s.count, "_barrier_", True) for s in self.sems if s.count]
        self.ops.append((eng, lambda e: e.nop(), toks, None, 0))

    def emit(self, block):
        ops = self.ops if Prog.KEEP is None else self.ops[:Prog.KEEP]
        per = {e: [] for e in self.ENGS}
        for rec in ops:
            per[rec[0]].append(rec)

        def make(eng_name):
            eops = per[eng_name]

            def body(eng):
                waited = {}
                for (_e, fn, waits, sig, amt) in eops:
                    for t in waits:
                        if waited.get(t.sem.name, -1) >= t.val:
                            continue
                        eng.wait_ge(t.sem.h, t.val)
                        waited[t.sem.name] = t.val
                    inst = fn(eng)
                    if sig is not None:
                        assert inst is not None, "fn with sig must return its last instruction"
                        inst.then_inc(sig.h, amt)
            return body

        for name in self.ENGS:
            if not per[name]:
                continue
            getattr(block, name)(make(name))

    KEEP = None


class Core:
    def __init__(self, role=0, dev=()):
        self.role = role
        self.dev = set(dev)
        self.chunks = ROLE_CHUNKS[role]
        self.npre = NCH
        self.pre = S
        self.nt = S // 128
        self.nc = bass.Bass("TRN2", target_bir_lowering=False, detect_race_conditions=False)
        self.dbg = {}

    def ap(self, name, shape, dt, kind="ExternalInput"):
        return self.nc.dram_tensor(name, list(shape), dt, kind=kind).ap()

    def carve_reset(self):
        self._cf = self._cb = self._cu = 0

    def cf(self, n):
        v = self.PF[:, self._cf:self._cf + n]
        self._cf += n
        assert self._cf <= 20480, f"PF overflow {self._cf}"
        return v

    def cb(self, n):
        v = self.PB[:, self._cb:self._cb + n]
        self._cb += n
        assert self._cb <= 46080, f"PB overflow {self._cb}"
        return v

    def cu(self, n):
        v = self.PU[:, self._cu:self._cu + n]
        self._cu += n
        assert self._cu <= 4096, f"PU overflow {self._cu}"
        return v

    def group_barrier(self):
        return [Tok(s, s.count, "_g_", True) for s in self.p.sems if s.count]

    def build(self):
        import contextlib
        nc = self.nc
        c = self
        pre = c.pre
        c.xpre = c.ap("xpre", [pre, D], F32)
        c.xown = c.ap("xown", [4 * CH, D], F32)
        c.wq = c.ap("wq", [D, D], F32)
        c.wkv = c.ap("wkv", [D, 2 * D], F32)
        c.wo = c.ap("wo", [D, D], F32)
        c.bqr = c.ap("bq", [1, D], F32)
        c.bkvr = c.ap("bkv", [1, 2 * D], F32)
        c.bor = c.ap("bo", [1, D], F32)
        c.qng = c.ap("qng", [128, 16], F32)
        c.qnbr = c.ap("qnb", [128, 16], F32)
        c.kvg = c.ap("kvg", [128, 16], F32)
        c.kvbr = c.ap("kvb", [128, 16], F32)
        c.lngr = c.ap("lng", [1, HD], F32)
        c.lnbr = c.ap("lnb", [1, HD], F32)
        c.gunib = c.ap("gunib", [D, 2 * I_FFN], U16)
        c.dnnib = c.ap("dnnib", [I_FFN, D], U16)
        c.gusc = c.ap("gusc", [16, 128, 2 * I_FFN], F16)
        c.dnsc = c.ap("dnsc", [44, 128, D], F16)
        c.cosdr = c.ap("cosd", [128, pre], F32)
        c.sindr = c.ap("sind", [128, pre], F32)
        c.cosdo = c.ap("cosdo", [128, 4 * CH], F32)
        c.sindo = c.ap("sindo", [128, 4 * CH], F32)
        c.maskv = c.ap("maskv", [4, 16, KT, CH], BF16)
        c.identin = c.ap("ident", [128, 128], BF16)
        c.ehsin = c.ap("ehs", [H, 128, H], BF16)
        c.y = c.ap("y", [4 * CH, D], F32, kind="ExternalOutput")
        c.z_tok = c.ap("z_tok", [pre, D], BF16, kind="Internal")
        c.zown = c.ap("zown", [4 * CH, D], BF16, kind="Internal")
        c.xn_tok = c.ap("xn_tok", [4 * CH, D], BF16, kind="Internal")
        c.epst = c.ap("epst", [64, 128], BF16, kind="Internal")
        c.epso = c.ap("epso", [16, 128], BF16, kind="Internal")
        c.rowq = c.ap("rowq", [16, 2 * CH], BF16, kind="Internal")
        c.rowk = c.ap("rowk", [16, 2 * CH], BF16, kind="Internal")
        c.wqP = c.ap("wqP", [D, D], BF16, kind="Internal")
        c.wkvP = c.ap("wkvP", [32, 128, 4 * CH], BF16, kind="Internal")
        c.woP = c.ap("woP", [D, D], BF16, kind="Internal")
        c.knT = c.ap("knT", [D, pre], BF16, kind="Internal")
        c.vtok = c.ap("vtok", [NCH * 4, H, 128, HD], BF16, kind="Internal")
        c.wguT = c.ap("wguT", [D, 2 * I_FFN], BF16, kind="Internal")
        c.wdT = c.ap("wdT", [I_FFN, D], BF16, kind="Internal")
        c.ffny = c.ap("ffny", [4 * CH, D], F32, kind="Internal")

        if "s1" in c.dev:
            c.dbg["d_xn"] = c.ap("d_xn", [4 * CH, D], BF16, kind="ExternalOutput")
            c.dbg["d_z"] = c.ap("d_z", [pre, D], BF16, kind="ExternalOutput")
            c.dbg["d_eps"] = c.ap("d_eps", [64, 128], BF16, kind="ExternalOutput")
        if "kv" in c.dev:
            c.dbg["d_kn"] = c.ap("d_kn", [D, pre], BF16, kind="ExternalOutput")
            c.dbg["d_v"] = c.ap("d_v", [pre, D], BF16, kind="ExternalOutput")
            c.dbg["d_mk"] = c.ap("d_mk", [1, 16], F32, kind="ExternalOutput")
        if "ffn" in c.dev:
            c.dbg["d_wgu"] = c.ap("d_wgu", [D, 2 * I_FFN], BF16, kind="ExternalOutput")
            c.dbg["d_wd"] = c.ap("d_wd", [I_FFN, D], BF16, kind="ExternalOutput")
            c.dbg["d_ffny"] = c.ap("d_ffny", [4 * CH, D], F32, kind="ExternalOutput")
        if "qn" in c.dev:
            c.dbg["d_qn"] = c.ap("d_qn", [D, 4 * CH], BF16, kind="ExternalOutput")

        with contextlib.ExitStack() as st:
            self._st = st
            E = st.enter_context
            c.sb = lambda name, shape, dt: E(nc.sbuf_tensor(name, list(shape), dt))
            c.ps = lambda name, shape, dt=F32: E(nc.psum_tensor(name, list(shape), dt))
            p = c.p = Prog(nc, st)
            c.s_const = p.sem("s_const")
            c.s_store = p.sem("s_store")
            c.s_act = p.sem("s_act")
            c.s_dve = p.sem("s_dve")
            c.s_pool = p.sem("s_pool")
            c.s_pe = p.sem("s_pe")

            c.identb = c.sb("identb", [128, 128], BF16)
            p.dma("sync", c.identb[:], c.identin, sig=c.s_const)
            c.ehs = c.sb("ehs_sb", [128, H * H], BF16)
            p.dma("sync", c.ehs[:].rearrange("p (h e) -> p h e", h=H),
                  c.ehsin.rearrange("h p e -> p h e"), sig=c.s_const)

            c.lneps = c.sb("lneps", [128, 1], F32)
            tm1 = p.op("vector", lambda e: e.memset(c.lneps[:], LN_EPS), sig=c.s_dve)
            c.rmseps = c.sb("rmseps", [128, 1], F32)
            tm2 = p.op("vector", lambda e: e.memset(c.rmseps[:], RMS_EPS), sig=c.s_dve)
            c.ones_row = c.sb("ones_row", [1, CH], BF16)
            tm3 = p.op("vector", lambda e: e.memset(c.ones_row[:], 1.0), sig=c.s_dve)
            c.onescol_bf = c.sb("onescol_bf", [128, 1], BF16)
            tm4 = p.op("vector", lambda e: e.memset(c.onescol_bf[:], 1.0), sig=c.s_dve)
            c.t_const = [Tok(c.s_const, c.s_const.count, "_c_", True), tm1, tm2, tm3, tm4]

            c.PF = c.sb("poolF", [128, 20480], F32)
            c.PB = c.sb("poolB", [128, 46080], BF16)
            c.PU = c.sb("poolU", [128, 4096], U16)
            c._cf = c._cb = c._cu = 0

            c.pb = [c.ps(f"pb{i}", [128, CH], F32) for i in range(6)]
            c.pstat = c.ps("pstat", [128, CH], F32)
            c.ptr = c.ps("ptr", [128, 128], BF16)
            c.pb_use = [[] for _ in range(6)]
            c.pstat_use = []
            c.ptr_use = []

            c.carve_reset()
            phase_s1(c)
            need_kv = not c.dev or "kv" in c.dev or "attn" in c.dev or "full" in c.dev
            need_ffn = not c.dev or "ffn" in c.dev or "full" in c.dev
            need_attn = not c.dev or "attn" in c.dev or "full" in c.dev
            if need_kv:
                phase_wprep(c)
            if need_ffn:
                phase_dequant(c)
            p.gate_all(c.group_barrier())
            c.carve_reset()
            if need_kv:
                phase_kv(c)
                p.gate_all(c.group_barrier())
                c.carve_reset()
            if need_ffn:
                phase_ffn(c)
                p.gate_all(c.group_barrier())
                c.carve_reset()
            if need_attn:
                phase_attn(c)

            p.final_barrier("sync")
            blk = E(nc.Block())
            p.emit(blk)
        return self.nc


def phase_s1(c):
    p, sb = c.p, c.sb
    nt = min(c.nt, getattr(c, "S1N", 10 ** 9))
    C_EPS128 = 128.0 * LN_EPS / (SCALE2 * D)

    r_x = p.ring("rx", 3)
    r_z = p.ring_slots("rz", 2)
    r_xn = p.ring_slots("rxn", 2)
    s_eps = p.sem("s_eps")

    xb = [c.cf(D) for _ in range(3)]
    junk = c.cf(D)
    xnb = [c.cb(D) for _ in range(2)]
    zb = [c.cb(D) for _ in range(2)]
    cols = [c.cf(16) for _ in range(2)]
    epscols = c.cb(64)
    epscols2 = c.cb(16)
    eps_ps = c.ptr
    eps_sb = c.cb(128)

    st = dict(last_use=[None, None, None], xn_done=[None, None], z_done=[None, None],
              col_done=[None, None], t_ec=None)

    def tile(i, src_rows, zdst, ecol, ecol_i, xndst):
        bi = i % 3
        ci = i % 2
        zi = i % 2
        cc_ = cols[ci]
        xt = xb[bi]
        tload = p.dma("sync", xt[:], src_rows, reads=[st["last_use"][bi]], sig=r_x[bi])
        t1 = p.op("scalar", lambda e, xt=xt, cc=cc_: e.activation(
            out=junk[:], in_=xt[:], func=AF.Copy, accum_out=cc[:, 0:1]),
            reads=[tload, st["col_done"][ci]], sig=c.s_act)
        t2 = p.op("scalar", lambda e, xt=xt, cc=cc_: e.activation(
            out=junk[:], in_=xt[:], func=AF.Square, accum_out=cc[:, 1:2]),
            reads=[tload], sig=c.s_act)

        def sm(e, cc=cc_):
            e.tensor_scalar(out=cc[:, 2:3], in0=cc[:, 0:1], scalar1=1.0 / D,
                            scalar2=None, op0=ALU.mult)
            e.tensor_scalar(out=cc[:, 3:4], in0=cc[:, 1:2], scalar1=1.0 / D,
                            scalar2=None, op0=ALU.mult)
            e.drain()
            e.scalar_tensor_tensor(out=cc[:, 4:5], in0=cc[:, 2:3], scalar=-1.0,
                                   in1=cc[:, 2:3], op0=ALU.mult, op1=ALU.mult)
            e.drain()
            return e.tensor_tensor(out=cc[:, 4:5], in0=cc[:, 3:4], in1=cc[:, 4:5],
                                   op=ALU.add)
        tsm = p.op("vector", sm, reads=[t1, t2, st["col_done"][ci]], sig=c.s_dve)
        tln = p.op("scalar", lambda e, cc=cc_: e.activation(
            out=cc[:, 5:6], in_=cc[:, 4:5], func=AF.Ln, bias=c.lneps[:]),
            reads=[tsm, c.t_const], sig=c.s_act)
        p.drain("scalar")
        trs = p.op("scalar", lambda e, cc=cc_: e.activation(
            out=cc[:, 5:6], in_=cc[:, 5:6], func=AF.Exp, scale=-0.5), sig=c.s_act)

        def sm2(e, cc=cc_):
            e.tensor_tensor(out=cc[:, 6:7], in0=cc[:, 5:6], in1=cc[:, 5:6], op=ALU.mult)
            e.drain()
            return e.tensor_tensor(out=cc[:, 6:7], in0=cc[:, 6:7], in1=cc[:, 4:5],
                                   op=ALU.mult)
        tsm2 = p.op("vector", sm2, reads=[trs], sig=c.s_dve)
        tln2 = p.op("scalar", lambda e, cc=cc_: e.activation(
            out=cc[:, 7:8], in_=cc[:, 6:7], func=AF.Ln, bias=c.lneps[:]),
            reads=[tsm2], sig=c.s_act)
        p.drain("scalar")
        tr2 = p.op("scalar", lambda e, cc=cc_: e.activation(
            out=cc[:, 7:8], in_=cc[:, 7:8], func=AF.Exp, scale=-0.5), sig=c.s_act)

        def sm3(e, cc=cc_, ecol=ecol, i_=ecol_i):
            e.tensor_tensor(out=cc[:, 8:9], in0=cc[:, 5:6], in1=cc[:, 7:8], op=ALU.mult)
            e.scalar_tensor_tensor(out=cc[:, 9:10], in0=cc[:, 2:3], scalar=-1.0,
                                   in1=cc[:, 5:6], op0=ALU.mult, op1=ALU.mult)
            e.reciprocal(out=cc[:, 11:12], in_=cc[:, 6:7])
            e.drain()
            e.scalar_tensor_tensor(out=cc[:, 10:11], in0=cc[:, 2:3], scalar=-1.0,
                                   in1=cc[:, 8:9], op0=ALU.mult, op1=ALU.mult)
            return e.tensor_scalar(out=ecol[:, i_:i_ + 1], in0=cc[:, 11:12],
                                   scalar1=C_EPS128, scalar2=None, op0=ALU.mult)
        tsm3 = p.op("vector", sm3, reads=[tr2], sig=c.s_dve)
        st["t_ec"] = tsm3

        tz = p.op("gpsimd", lambda e, xt=xt, cc=cc_, zo=zb[zi]: e.tensor_scalar(
            out=zo[:], in0=xt[:], scalar1=cc[:, 8:9], scalar2=cc[:, 10:11],
            op0=ALU.mult, op1=ALU.add), reads=[tsm3, st["z_done"][zi], tload],
            sig=c.s_pool)
        tzs = p.dma("gpsimd", zdst, zb[zi][:], reads=[tz], sig=r_z[zi])
        st["z_done"][zi] = tzs
        uses = [tz]
        if xndst is not None:
            p.drain("vector")
            txn = p.op("vector", lambda e, xt=xt, cc=cc_, xo=xnb[zi]: e.tensor_scalar(
                out=xo[:], in0=xt[:], scalar1=cc[:, 5:6], scalar2=cc[:, 9:10],
                op0=ALU.mult, op1=ALU.add), reads=[tsm3, st["xn_done"][zi], tload],
                sig=c.s_dve)
            txs = p.dma("gpsimd", xndst, xnb[zi][:], reads=[txn], sig=r_xn[zi])
            st["xn_done"][zi] = txs
            uses.append(txn)
        st["last_use"][bi] = uses
        st["col_done"][ci] = uses

    for i in range(nt):
        tile(i, c.xpre[i * 128:(i + 1) * 128, :],
             c.z_tok[i * 128:(i + 1) * 128, :], epscols, i, None)
    t_ec_pre = st["t_ec"]
    nt2 = 16
    for i in range(nt2):
        tile(i, c.xown[i * 128:(i + 1) * 128, :],
             c.zown[i * 128:(i + 1) * 128, :], epscols2, i,
             c.xn_tok[i * 128:(i + 1) * 128, :])

    tec = p.op("tensor", lambda e: e.transpose(
        out=eps_ps[0:nt, 0:128], in_=epscols[:, 0:nt], identity=c.identb[:]),
        reads=[c.t_const, t_ec_pre, st["t_ec"]], sig=c.s_pe)
    tev = p.op("vector", lambda e: e.tensor_copy(out=eps_sb[0:nt, :],
                                                 in_=eps_ps[0:nt, 0:128]),
               reads=[tec], sig=c.s_dve)
    tes = p.dma("gpsimd", c.epst[0:nt, :], eps_sb[0:nt, :], reads=[tev], sig=s_eps)
    tec2 = p.op("tensor", lambda e: e.transpose(
        out=eps_ps[0:nt2, 0:128], in_=epscols2[:, 0:nt2], identity=c.identb[:]),
        reads=[st["t_ec"], tev], sig=c.s_pe)
    tev2 = p.op("vector", lambda e: e.tensor_copy(out=eps_sb[0:nt2, :],
                                                  in_=eps_ps[0:nt2, 0:128]),
               reads=[tec2, tes], sig=c.s_dve)
    tes2 = p.dma("gpsimd", c.epso[0:nt2, :], eps_sb[0:nt2, :], reads=[tev2], sig=s_eps)
    c.ptr_use = [tev2]
    c.t_eps = Tok(s_eps, s_eps.count, "_e_", True)
    c.t_epso = c.t_eps
    c.t_z_done = [t for t in st["z_done"] if t is not None]
    c.t_xn_done = [t for t in st["xn_done"] if t is not None]

    if "d_xn" in c.dbg:
        p.dma("gpsimd", c.dbg["d_xn"], c.xn_tok, reads=c.t_xn_done, sig=c.s_store)
        p.dma("gpsimd", c.dbg["d_z"], c.z_tok, reads=c.t_z_done, sig=c.s_store)
        p.dma("gpsimd", c.dbg["d_eps"][0:nt, :], c.epst[0:nt, :], reads=[tes],
              sig=c.s_store)


def phase_wprep(c):
    p, sb = c.p, c.sb

    s_v = p.sem("s_wv")
    gq = c.cf(16)
    gkv = c.cf(16)
    bqn = c.cf(16)
    bkvn = c.cf(16)
    t1 = p.dma("sync", gq[:], c.qng, sig=s_v)
    t2 = p.dma("sync", gkv[:], c.kvg, sig=s_v)
    t3 = p.dma("sync", bqn[:], c.qnbr, sig=s_v)
    t4 = p.dma("sync", bkvn[:], c.kvbr, sig=s_v)
    s_r = p.sem("s_wr")
    rows_fv = c.cf(4 * D)
    rows_f = rows_fv[0:1, :]
    t5 = p.dma("sync", rows_f[:, 0:D], c.bqr, sig=s_r)
    t6 = p.dma("sync", rows_f[:, D:3 * D], c.bkvr, sig=s_r)
    t7 = p.dma("sync", rows_f[:, 3 * D:4 * D], c.bor, sig=s_r)
    lrv = c.cf(2 * HD)
    lr = lrv[0:1, :]
    t8 = p.dma("sync", lr[:, 0:HD], c.lngr, sig=s_r)
    t9 = p.dma("sync", lr[:, HD:2 * HD], c.lnbr, sig=s_r)
    tv = [t1, t2, t3, t4]
    tr = [t5, t6, t7, t8, t9]

    bqn_bf = c.cb(16)
    bkvn_bf = c.cb(16)
    tb1 = p.op("vector", lambda e: e.tensor_copy(out=bqn_bf[:], in_=bqn[:]),
               reads=[tv], sig=c.s_dve)
    tb2 = p.op("vector", lambda e: e.tensor_copy(out=bkvn_bf[:], in_=bkvn[:]),
               reads=[tv], sig=c.s_dve)
    c.lrow = sb("lrow", [1, 4 * HD], BF16)
    def mkrows(e):
        e.tensor_copy(out=c.lrow[:, 0:2 * HD], in_=lr[:, 0:2 * HD])
        return e.tensor_scalar(out=c.lrow[:, 2 * HD:4 * HD], in0=lr[:, 0:2 * HD],
                               scalar1=float(SCALE2), scalar2=None, op0=ALU.mult)
    tlr = p.op("vector", mkrows, reads=[tr], sig=c.s_dve)
    c.t_lrow = tlr

    c.bo_bf = sb("bo_bf", [1, D], BF16)
    tbo = p.op("vector", lambda e: e.tensor_copy(out=c.bo_bf[:], in_=rows_f[:, 3 * D:4 * D]),
               reads=[tr], sig=c.s_dve)
    c.t_bo = tbo

    r_w = p.ring("rw", 4)
    wbuf = [c.cf(CH) for _ in range(4)]
    wub = [c.cb(CH) for _ in range(2)]
    wsb = [c.cb(CH) for _ in range(2)]
    r_ws = p.ring_slots("rws", 2)

    c.browq_bf = sb("browq_bf", [1, D], BF16)
    c.browkv_bf = sb("browkv_bf", [1, 2 * D], BF16)

    brow_ps = c.pstat
    specs = [
        ("wq", c.wq, c.wqP, 4, gq, bqn_bf, c.browq_bf, 0),
        ("wkv", c.wkv, c.wkvP, 8, gkv, bkvn_bf, c.browkv_bf, D),
        ("wo", c.wo, c.woP, 4, None, None, None, 3 * D),
    ]
    wu_done = [None, None]
    ws_done = [None, None]
    wl_done = [None, None, None, None]
    t_brow_es = []
    for (name, wsrc, wdst, jmax, gcol, bcol, browdst, boff) in specs:
        for j in range(jmax):
            mm_toks = []
            for K in range(16):
                i4 = (j * 16 + K) % 4
                i2 = (j * 16 + K) % 2
                tl = p.dma("sync", wbuf[i4][:], wsrc[K * 128:(K + 1) * 128, j * CH:(j + 1) * CH],
                           reads=[wl_done[i4]], sig=r_w[i4])
                tc = p.op("vector", lambda e, a=wub[i2], b=wbuf[i4]: e.tensor_copy(
                    out=a[:], in_=b[:]), reads=[tl, wu_done[i2]], sig=c.s_dve)
                uses = [tc]
                if gcol is not None:
                    tmm = p.op("tensor", lambda e, K=K, i2=i2, b=bcol: e.matmul(
                        out=brow_ps[0:1, 0:CH], lhsT=b[:, K:K + 1], rhs=wub[i2][:],
                        start=(K == 0), stop=(K == 15)),
                        reads=[tc, tb1, tb2] + (c.pstat_use if K == 0 else []),
                        sig=c.s_pe)
                    mm_toks.append(tmm)
                    tsc = p.op("vector", lambda e, K=K, i2=i2, g=gcol: e.tensor_scalar(
                        out=wsb[i2][:], in0=wub[i2][:], scalar1=g[:, K:K + 1],
                        scalar2=None, op0=ALU.mult), reads=[tv, ws_done[i2]], sig=c.s_dve)
                    if name == "wkv":
                        wdap = wdst[j * 4 + K // 4, :, (K % 4) * CH:(K % 4 + 1) * CH]
                    else:
                        wdap = wdst[K * 128:(K + 1) * 128, j * CH:(j + 1) * CH]
                    tst = p.dma("gpsimd", wdap, wsb[i2][:], reads=[tsc], sig=r_ws[i2])
                    ws_done[i2] = tst
                    uses.append(tsc)
                    wu_done[i2] = uses
                else:
                    tst = p.dma("gpsimd", wdst[K * 128:(K + 1) * 128, j * CH:(j + 1) * CH],
                                wub[i2][:], reads=[tc], sig=r_ws[i2])
                    wu_done[i2] = [tst]
                wl_done[i4] = uses if gcol is None else uses
                if gcol is None:
                    wl_done[i4] = [tc]
            if gcol is not None:
                c.pstat_use = []
                tadd = p.op("vector", lambda e, j=j, boff=boff, browdst=browdst: e.tensor_tensor(
                    out=browdst[0:1, j * CH:(j + 1) * CH], in0=brow_ps[0:1, 0:CH],
                    in1=rows_f[0:1, boff + j * CH:boff + (j + 1) * CH], op=ALU.add),
                    reads=[mm_toks[-1], tr], sig=c.s_dve)
                c.pstat_use.append(tadd)
                t_brow_es.append(tadd)
    c.t_brow = t_brow_es
    c.t_wqP = ws_done + wu_done
    c.t_wprep_end = [Tok(s, s.count, "_w_", True) for s in (r_ws[0], r_ws[1])]


def phase_kv(c):
    p, sb = c.p, c.sb
    nch = min(c.npre, getattr(c, "KVN", 10 ** 9))

    r_zt = p.ring("rzt", 2)
    c.rw12 = p.ring("rw12", 12)
    r_wk = c.rw12[0:6]
    r_cs = p.ring("rcs", 2)
    r_eps = p.ring("reps", 2)
    r_row = p.ring("rrow", 2)
    r_kst = p.ring_slots("rkst", 2)
    r_vst = p.ring_slots("rvst", 2)
    r_sw = p.ring("rsw", 4)
    s_rb = p.sem("s_rowb")

    zT = [[c.cb(CH) for K in range(16)] for r in range(2)]
    wkh = [c.cb(4 * CH) for _ in range(6)]
    cosc = [c.cf(CH) for _ in range(2)]
    sinc = [c.cf(CH) for _ in range(2)]
    epsr = [c.cb(CH)[0:1, :] for _ in range(2)]
    krot = [c.cf(CH) for _ in range(4)]
    krot_sw = [c.cf(CH) for _ in range(4)]
    p1 = [c.cf(CH) for _ in range(4)]
    p2 = [c.cf(CH) for _ in range(4)]
    rotb = [c.cb(CH) for _ in range(16)]
    rot2 = [c.cb(CH) for _ in range(2)]
    kn2 = [c.cb(CH) for _ in range(2)]
    stats = c.cf(5 * CH)[0:16, :]
    statb = c.cb(2 * CH)[0:16, :]
    rowsb = c.cf(8 * CH).bitcast(BF16)[0:1, :]
    knsb = [c.cb(CH) for _ in range(2)]
    tmpsb = [c.cb(CH) for _ in range(2)]
    vsb = [c.cb(CH) for _ in range(2)]
    maxcol = c.cf(2)[0:16, 0:1]
    c.maxk2 = sb("maxk2", [16, 1], F32)

    zt_done = [None, None]
    wk_done = [None] * 6
    cs_done = [None, None]
    eps_done = [None, None]
    row_done = [None, None]
    kn_done = [None, None]
    tmp_done = [None, None]
    v_done = [None, None]
    stats_done = None
    statb_done = None
    rot_done = [None] * 16
    rot2_done = [None, None]
    kn2_done = [None, None]
    tsq_pend = [None, None]
    krot_done = [None] * 4
    p12_done = [None] * 4

    tmk0 = p.op("vector", lambda e: e.memset(c.maxk2[:], 0.0), sig=c.s_dve)
    maxk_done = tmk0

    s1v = c.pstat[0:16, :]
    s2v = c.pb[4][0:16, :]
    knv = c.pb[5][0:16, :]

    wkc = 0
    for cc in range(nch):
        ri = cc % 2
        tzt = None
        for K in range(16):
            tzt = p.dmat("sync", zT[ri][K][:],
                         c.z_tok[cc * CH:(cc + 1) * CH, K * 128:(K + 1) * 128],
                         reads=[zt_done[ri], c.t_z_done], sig=r_zt[ri])
        tcs = p.dma("sync", cosc[ri][:], c.cosdr[:, cc * CH:(cc + 1) * CH],
                    reads=[cs_done[ri]], sig=r_cs[ri])
        tsn = p.dma("sync", sinc[ri][:], c.sindr[:, cc * CH:(cc + 1) * CH],
                    reads=[cs_done[ri]], sig=r_cs[ri])
        tep = p.dma("sync", epsr[ri][:],
                    c.epst.rearrange("a b -> (a b)")[cc * CH:(cc + 1) * CH].unsqueeze(0),
                    reads=[eps_done[ri], c.t_eps], sig=r_eps[ri])

        stat_last = [None, None]
        for hg in range(4):
            mmg = [None] * 4
            for q in range(4):
                wi = wkc % 6
                wkc += 1
                tw = p.dma("sync", wkh[wi][:], c.wkvP[hg * 4 + q],
                           reads=[wk_done[wi], c.t_wprep_end], sig=r_wk[wi])
                for Ki in range(4):
                    K = q * 4 + Ki
                    for hi in range(4):
                        mmg[hi] = p.op("tensor", lambda e, K=K, Ki=Ki, wi=wi, hi=hi, ri=ri: e.matmul(
                            out=c.pb[hi][:],
                            lhsT=wkh[wi][:, Ki * CH + hi * 128:Ki * CH + (hi + 1) * 128],
                            rhs=zT[ri][K][:], start=(K == 0), stop=False),
                            reads=[tw, tzt] + (c.pb_use[hi] if K == 0 else []), sig=c.s_pe)
                wk_done[wi] = mmg[3]
            for hi in range(4):
                h = hg * 4 + hi
                pbi = h % 4
                c.pb_use[hi] = []
                mmb = p.op("tensor", lambda e, h=h, hi=hi: e.matmul(
                    out=c.pb[hi][:], lhsT=c.browkv_bf[0:1, h * 128:(h + 1) * 128],
                    rhs=c.ones_row[:], start=False, stop=True),
                    reads=[mmg[hi], c.t_brow, c.t_const], sig=c.s_pe)
                tev = p.op("scalar", lambda e, hi=hi, ki=pbi: e.activation(
                    out=krot[ki][:], in_=c.pb[hi][:], func=AF.Copy),
                    reads=[mmb, krot_done[pbi]], sig=c.s_act)
                c.pb_use[hi].append(tev)
                tw1 = p.dma("gpsimd", krot_sw[pbi][0:64, :], krot[pbi][64:128, :],
                            reads=[tev, krot_done[pbi]], sig=r_sw[pbi])
                tw2 = p.dma("gpsimd", krot_sw[pbi][64:128, :], krot[pbi][0:64, :],
                            sig=r_sw[pbi])
                tp1 = p.op("vector", lambda e, ri=ri, ki=pbi: e.tensor_mul(
                    out=p1[ki][:], in0=krot[ki][:], in1=cosc[ri][:]),
                    reads=[tev, tsn, p12_done[pbi]], sig=c.s_dve)
                tp2 = p.op("gpsimd", lambda e, ri=ri, ki=pbi: e.tensor_mul(
                    out=p2[ki][:], in0=krot_sw[ki][:], in1=sinc[ri][:]),
                    reads=[tw2, tsn], sig=c.s_pool)
                krot_done[pbi] = [tp1, tp2]
                tcb = p.op("vector", lambda e, h=h, ki=pbi: e.tensor_add(
                    out=rotb[h][:], in0=p1[ki][:], in1=p2[ki][:]),
                    reads=[tp1, tp2, rot_done[h]], sig=c.s_dve)
                p12_done[pbi] = tcb
                rot_done[h] = [tcb]
        for h in range(16):
            ki = h % 2
            tsq = p.op("scalar", lambda e, h=h, ki=ki: e.activation(
                out=rot2[ki][:], in_=rotb[h][:], func=AF.Square),
                reads=[rot_done[h], rot2_done[ki]], sig=c.s_act)
            ts1 = p.op("tensor", lambda e, h=h: e.matmul(
                out=s1v, lhsT=c.ehs[:, h * 16:(h + 1) * 16], rhs=rotb[h][:],
                start=(h == 0), stop=(h == 15)),
                reads=[rot_done[h], c.t_const] + (c.pstat_use if h == 0 else []),
                sig=c.s_pe)
            ts2 = p.op("tensor", lambda e, h=h, ki=ki: e.matmul(
                out=s2v, lhsT=c.ehs[:, h * 16:(h + 1) * 16], rhs=rot2[ki][:],
                start=(h == 0), stop=False),
                reads=[tsq] + (c.pb_use[4] if h == 0 else []), sig=c.s_pe)
            rot2_done[ki] = ts2
            stat_last = [ts1, ts2]
        tse = p.op("tensor", lambda e, ri=ri: e.matmul(
            out=s2v, lhsT=c.ones_row[0:1, 0:16], rhs=epsr[ri][:],
            start=False, stop=True), reads=[tep, stat_last[1], c.t_const], sig=c.s_pe)
        eps_done[ri] = tse

        def smk(e):
            e.tensor_scalar(out=stats[:, 0:CH], in0=s1v, scalar1=1.0 / 128,
                            scalar2=None, op0=ALU.mult)
            e.tensor_scalar(out=stats[:, CH:2 * CH], in0=s2v, scalar1=1.0 / 128,
                            scalar2=None, op0=ALU.mult)
            e.drain()
            e.scalar_tensor_tensor(out=stats[:, 2 * CH:3 * CH], in0=stats[:, 0:CH],
                                   scalar=-1.0, in1=stats[:, 0:CH],
                                   op0=ALU.mult, op1=ALU.mult)
            e.drain()
            return e.tensor_tensor(out=stats[:, 2 * CH:3 * CH],
                                   in0=stats[:, CH:2 * CH],
                                   in1=stats[:, 2 * CH:3 * CH], op=ALU.add)
        tsk = p.op("vector", smk, reads=[tse, stat_last[0], stats_done], sig=c.s_dve)
        tl1 = p.op("scalar", lambda e: e.activation(
            out=stats[:, 3 * CH:4 * CH], in_=stats[:, 2 * CH:3 * CH], func=AF.Ln),
            reads=[tsk], sig=c.s_act)
        p.drain("scalar")
        tl2 = p.op("scalar", lambda e: e.activation(
            out=stats[:, 3 * CH:4 * CH], in_=stats[:, 3 * CH:4 * CH], func=AF.Exp,
            scale=-0.5), sig=c.s_act)

        def smk2(e):
            e.scalar_tensor_tensor(out=stats[:, 4 * CH:5 * CH], in0=stats[:, 0:CH],
                                   scalar=-1.0, in1=stats[:, 3 * CH:4 * CH],
                                   op0=ALU.mult, op1=ALU.mult)
            e.tensor_copy(out=statb[:, 0:CH], in_=stats[:, 3 * CH:4 * CH])
            e.drain()
            return e.tensor_copy(out=statb[:, CH:2 * CH], in_=stats[:, 4 * CH:5 * CH])
        tsk2 = p.op("vector", smk2, reads=[tl2, statb_done], sig=c.s_dve)
        stats_done = tsk2
        c.pstat_use = [tsk]
        c.pb_use[4] = [tsk]

        trb = p.dma("gpsimd", c.rowk, statb[:],
                    reads=[tsk2, getattr(c, "_rowk_read", [])], sig=s_rb)
        statb_done = trb

        for vd in range(4):
            vps = [c.pb[t] for t in range(4)]
            mm_last = [None] * 4
            for q in range(4):
                wi = wkc % 6
                wkc += 1
                tw = p.dma("sync", wkh[wi][:], c.wkvP[16 + vd * 4 + q],
                           reads=[wk_done[wi], c.t_wprep_end], sig=r_wk[wi])
                for Ki in range(4):
                    K = q * 4 + Ki
                    for t4 in range(4):
                        mm_last[t4] = p.op("tensor", lambda e, K=K, Ki=Ki, wi=wi, t4=t4, ri=ri, vv=vps[t4]: e.matmul(
                            out=vv[:], lhsT=zT[ri][K][:, t4 * 128:(t4 + 1) * 128],
                            rhs=wkh[wi][:, Ki * CH:(Ki + 1) * CH],
                            start=(K == 0), stop=False),
                            reads=[tw, tzt] + (c.pb_use[t4] if K == 0 else []), sig=c.s_pe)
                wk_done[wi] = mm_last[3]
            for t4 in range(4):
                c.pb_use[t4] = []
                mmb = p.op("tensor", lambda e, vd=vd, vv=vps[t4]: e.matmul(
                    out=vv[:], lhsT=c.ones_row[0:1, 0:128],
                    rhs=c.browkv_bf[0:1, 2048 + vd * CH:2048 + (vd + 1) * CH],
                    start=False, stop=True), reads=[mm_last[t4], c.t_brow], sig=c.s_pe)
                si = t4 % 2
                tev = p.op("scalar", lambda e, si=si, vv=vps[t4]: e.activation(
                    out=vsb[si][:], in_=vv[:], func=AF.Copy),
                    reads=[mmb, v_done[si]], sig=c.s_act)
                c.pb_use[t4].append(tev)
                kt = cc * 4 + t4
                tvs = p.dma("gpsimd",
                            c.vtok[kt, vd * 4:(vd + 1) * 4].rearrange("h t d -> t h d"),
                            vsb[si][:].rearrange("t (h d) -> t h d", h=4),
                            reads=[tev], sig=r_vst[si])
                v_done[si] = tvs
        rowk_reads = []
        row_users = []
        for h in range(16):
            si = h % 2
            pc = c.pb[2 * si]
            pd = c.pb[2 * si + 1]
            if h % 8 == 0:
                rflat = c.rowk.rearrange("a b -> (a b)")
                trh = p.dma("sync", rowsb[:],
                            rflat[h * 2 * CH:(h + 8) * 2 * CH].unsqueeze(0),
                            reads=[trb, row_done[0]], sig=r_row[0])
                rowk_reads.append(trh)
                row_done[0] = []
            ho = (h % 8) * 2 * CH
            tmc = p.op("tensor", lambda e, ho=ho, pc=pc: e.matmul(
                out=pc[:], lhsT=c.lrow[0:1, 2 * HD:3 * HD],
                rhs=rowsb[0:1, ho:ho + CH], start=True, stop=True),
                reads=[trh, c.t_lrow] + c.pb_use[2 * si], sig=c.s_pe)
            tmd1 = p.op("tensor", lambda e, ho=ho, pd=pd: e.matmul(
                out=pd[:], lhsT=c.lrow[0:1, 2 * HD:3 * HD],
                rhs=rowsb[0:1, ho + CH:ho + 2 * CH], start=True, stop=False),
                reads=[trh] + c.pb_use[2 * si + 1], sig=c.s_pe)
            tmd2 = p.op("tensor", lambda e, pd=pd: e.matmul(
                out=pd[:], lhsT=c.lrow[0:1, 3 * HD:4 * HD], rhs=c.ones_row[:],
                start=False, stop=True), sig=c.s_pe)
            row_users += [tmc, tmd1]
            row_done[0] = row_done[0] + [tmc, tmd1] if isinstance(row_done[0], list) else [tmc, tmd1]
            c.pb_use[2 * si] = []
            c.pb_use[2 * si + 1] = []
            tt1 = p.op("vector", lambda e, h=h, si=si, pc=pc: e.tensor_tensor(
                out=tmpsb[si][:], in0=rotb[h][:], in1=pc[:], op=ALU.mult),
                reads=[tmc, tmp_done[si]], sig=c.s_dve)
            tt2 = p.op("vector", lambda e, si=si, pd=pd: e.tensor_tensor(
                out=knsb[si][:], in0=tmpsb[si][:], in1=pd[:], op=ALU.add),
                reads=[tmd2, kn_done[si]], sig=c.s_dve)
            rot_done[h] = [tt1]
            c.pb_use[2 * si].append(tt1)
            c.pb_use[2 * si + 1].append(tt2)
            tmp_done[si] = tt2
            if h >= 2:
                hd2 = h - 2
                tkn = p.op("tensor", lambda e, hd2=hd2, si=si: e.matmul(
                    out=knv, lhsT=c.ehs[:, hd2 * 16:(hd2 + 1) * 16], rhs=kn2[si][:],
                    start=(hd2 == 0), stop=False),
                    reads=[tsq_pend[si]] + (c.pb_use[5] if hd2 == 0 else []),
                    sig=c.s_pe)
                kn2_done[si] = tkn
            tsq2 = p.op("scalar", lambda e, si=si: e.activation(
                out=kn2[si][:], in_=knsb[si][:], func=AF.Square),
                reads=[tt2, kn2_done[si]], sig=c.s_act)
            tsq_pend[si] = tsq2
            tst = p.dma("gpsimd", c.knT[h * 128:(h + 1) * 128, cc * CH:(cc + 1) * CH],
                        knsb[si][:], reads=[tt2], sig=r_kst[si])
            kn_done[si] = [tst, tsq2]
        for hd2 in (14, 15):
            si = hd2 % 2
            tkn = p.op("tensor", lambda e, hd2=hd2, si=si: e.matmul(
                out=knv, lhsT=c.ehs[:, hd2 * 16:(hd2 + 1) * 16], rhs=kn2[si][:],
                start=False, stop=(hd2 == 15)),
                reads=[tsq_pend[si]], sig=c.s_pe)
            kn2_done[si] = tkn
        c._rowk_read = rowk_reads
        tmx = p.op("vector", lambda e: e.reduce_max(out=maxcol[:], in_=knv, axis=AX.X),
                   reads=[kn2_done[0], kn2_done[1], maxk_done], sig=c.s_dve)
        p.drain("vector")
        tmx2 = p.op("vector", lambda e: e.tensor_max(
            out=c.maxk2[:], in0=c.maxk2[:], in1=maxcol[:]), sig=c.s_dve)
        maxk_done = tmx2
        c.pb_use[5] = [tmx]

        zt_done[ri] = [mm_last[t] for t in range(4)]
        cs_done[ri] = krot_done[:]

    c.t_kv_end = [Tok(s, s.count, "_kv_", True)
                  for s in (r_kst[0], r_kst[1], r_vst[0], r_vst[1])]
    c.t_maxk = maxk_done

    if "d_kn" in c.dbg:
        p.dma("gpsimd", c.dbg["d_kn"][:, 0:nch * CH], c.knT[:, 0:nch * CH],
              reads=[c.t_kv_end], sig=c.s_store)
        p.dma("gpsimd", c.dbg["d_mk"].rearrange("o s -> s o"), c.maxk2[:],
              reads=[maxk_done], sig=c.s_store)


def phase_dequant(c):
    p = c.p
    r_nb = p.ring("rnb", 2)
    r_sc = p.ring("rsc", 2)
    r_wo = p.ring_slots("rwo", 2)
    nib = [c.cu(CH) for _ in range(2)]
    t2b = [c.cu(CH) for _ in range(2)]
    bits = [c.cu(CH) for _ in range(2)]
    scf = [c.cf(CH) for _ in range(2)]
    wout = [c.cb(CH) for _ in range(2)]
    sc16 = [c.sb(f"sc16_{i}", [128, CH], F16) for i in range(2)]

    nib_done = [None, None]
    t2_done = [None, None]
    bits_done = [None, None]
    sc_done = [None, None]
    scf_done = [None, None]
    w_done = [None, None]

    jobs = []
    for g in range(16):
        for j in range(2 * I_FFN // CH):
            jobs.append(("gu", g, j))
    for g in range(44):
        for j in range(D // CH):
            jobs.append(("dn", g, j))
    if getattr(c, "DQN", None):
        jobs = jobs[:c.DQN]

    for idx, (kind, g, j) in enumerate(jobs):
        si = idx % 2
        if kind == "gu":
            nsrc = c.gunib[g * 128:(g + 1) * 128, j * CH:(j + 1) * CH]
            ssrc = c.gusc[g, :, j * CH:(j + 1) * CH]
            wdst = c.wguT[g * 128:(g + 1) * 128, j * CH:(j + 1) * CH]
        else:
            nsrc = c.dnnib[g * 128:(g + 1) * 128, j * CH:(j + 1) * CH]
            ssrc = c.dnsc[g, :, j * CH:(j + 1) * CH]
            wdst = c.wdT[g * 128:(g + 1) * 128, j * CH:(j + 1) * CH]
        tn = p.dma("sync", nib[si][:], nsrc, reads=[nib_done[si]], sig=r_nb[si])
        tsc = p.dma("sync", sc16[si][:], ssrc, reads=[sc_done[si]], sig=r_sc[si])
        tt2 = p.op("vector", lambda e, si=si: e.tensor_scalar(
            out=t2b[si][:], in0=nib[si][:], scalar1=8, scalar2=32256,
            op0=ALU.is_ge, op1=ALU.mult), reads=[tn, t2_done[si]], sig=c.s_dve)
        tbi = p.op("vector", lambda e, si=si: e.scalar_tensor_tensor(
            out=bits[si][:], in0=nib[si][:], scalar=64.0, in1=t2b[si][:],
            op0=ALU.mult, op1=ALU.add), reads=[tt2, bits_done[si]], sig=c.s_dve)
        nib_done[si] = tbi
        t2_done[si] = tbi
        tsf = p.op("vector", lambda e, si=si: e.tensor_scalar(
            out=scf[si][:], in0=sc16[si][:], scalar1=float(2.0 ** 126), scalar2=None,
            op0=ALU.mult), reads=[tsc, scf_done[si]], sig=c.s_dve)
        sc_done[si] = tsf
        tw = p.op("vector", lambda e, si=si: e.tensor_tensor(
            out=wout[si][:], in0=bits[si][:].bitcast(BF16)[:, 0:CH], in1=scf[si][:],
            op=ALU.mult), reads=[tbi, tsf, w_done[si]], sig=c.s_dve)
        bits_done[si] = tw
        scf_done[si] = tw
        tst = p.dma("gpsimd", wdst, wout[si][:], reads=[tw], sig=r_wo[si])
        w_done[si] = tst
    c.t_deq_end = [Tok(s, s.count, "_dq_", True) for s in (r_wo[0], r_wo[1])]
    if "d_wgu" in c.dbg:
        p.dma("gpsimd", c.dbg["d_wgu"], c.wguT, reads=[c.t_deq_end], sig=c.s_store)
        p.dma("gpsimd", c.dbg["d_wd"], c.wdT, reads=[c.t_deq_end], sig=c.s_store)


def phase_ffn(c):
    p = c.p
    noc = min(4, getattr(c, "FFNN", 4))

    r_xt = p.ring("rxt", 2)
    r_wg = c.rw12
    r_wd = p.ring("rwd", 8)
    r_ys = p.ring_slots("rys", 2)

    xnT = [c.cb(CH) for _ in range(16)]
    act = [c.cb(CH) for _ in range(44)]
    sg = [c.cb(CH) for _ in range(4)]
    upb = [c.cb(CH) for _ in range(4)]
    wg = [c.cb(CH) for _ in range(12)]
    wd = [c.cb(CH) for _ in range(8)]
    ysb = [c.cf(CH) for _ in range(2)]
    sqjunk = c.cf(CH)
    sqc = c.cf(16)
    rcol = c.cf(4)
    c.ffn_rstd = c.sb("ffn_rstd", [128, 16], F32)

    xnT_done = [None] * 16
    wg_done = [None] * 12
    wd_done = [None] * 8
    sg_done = [None] * 4
    up_done = [None] * 4
    act_done = [None] * 44
    y_done = [None, None]
    sq_done = None

    wgc = 0
    wdc = 0
    for oc in range(noc):
        txt = None
        for K in range(16):
            txt = p.dmat("sync", xnT[K][:],
                         c.xn_tok[oc * CH:(oc + 1) * CH, K * 128:(K + 1) * 128],
                         reads=[xnT_done[K], c.t_xn_done], sig=r_xt[oc % 2])
        for j4 in range(11):
            for half in range(2):
                dst = sg if half == 0 else upb
                dst_done = sg_done if half == 0 else up_done
                base = half * I_FFN + j4 * 4 * 128
                mmlast = [None] * 4
                for K in range(16):
                    wi = wgc % 12
                    wgc += 1
                    twl = p.dma("sync", wg[wi][:],
                                c.wguT[K * 128:(K + 1) * 128, base:base + CH],
                                reads=[wg_done[wi], c.t_deq_end], sig=r_wg[wi])
                    for o4 in range(4):
                        mmlast[o4] = p.op("tensor", lambda e, K=K, wi=wi, o4=o4: e.matmul(
                            out=c.pb[o4][:], lhsT=wg[wi][:, o4 * 128:(o4 + 1) * 128],
                            rhs=xnT[K][:], start=(K == 0), stop=(K == 15)),
                            reads=[twl, txt] + (c.pb_use[o4] if K == 0 else []),
                            sig=c.s_pe)
                    wg_done[wi] = mmlast[3]
                for o4 in range(4):
                    c.pb_use[o4] = []
                    if half == 0:
                        tev = p.op("scalar", lambda e, o4=o4: e.activation(
                            out=sg[o4][:], in_=c.pb[o4][:], func=AF.Silu),
                            reads=[mmlast[o4], sg_done[o4]], sig=c.s_act)
                        sg_done[o4] = tev
                    else:
                        tev = p.op("scalar", lambda e, o4=o4: e.activation(
                            out=upb[o4][:], in_=c.pb[o4][:], func=AF.Copy),
                            reads=[mmlast[o4], up_done[o4]], sig=c.s_act)
                        up_done[o4] = tev
                    c.pb_use[o4].append(tev)
            for o4 in range(4):
                jj = j4 * 4 + o4
                tmu = p.op("vector", lambda e, o4=o4, jj=jj: e.tensor_mul(
                    out=act[jj][:], in0=sg[o4][:], in1=upb[o4][:]),
                    reads=[sg_done[o4], up_done[o4], act_done[jj]], sig=c.s_dve)
                sg_done[o4] = [sg_done[o4], tmu]
                up_done[o4] = [up_done[o4], tmu]
                act_done[jj] = tmu
        for K in range(16):
            xnT_done[K] = wg_done[5]
        sq_toks = []
        for hj in range(4):
            mmlast = [None] * 4
            for i in range(44):
                wi = wdc % 8
                wdc += 1
                twl = p.dma("sync", wd[wi][:],
                            c.wdT[i * 128:(i + 1) * 128, hj * CH:(hj + 1) * CH],
                            reads=[wd_done[wi], c.t_deq_end], sig=r_wd[wi])
                for ts in range(4):
                    mmlast[ts] = p.op("tensor", lambda e, i=i, wi=wi, ts=ts: e.matmul(
                        out=c.pb[ts][:], lhsT=act[i][:, ts * 128:(ts + 1) * 128],
                        rhs=wd[wi][:], start=(i == 0), stop=(i == 43)),
                        reads=[twl, act_done[i]] + (c.pb_use[ts] if i == 0 else []),
                        sig=c.s_pe)
                wd_done[wi] = mmlast[3]
            for ts in range(4):
                c.pb_use[ts] = []
                si = ts % 2
                tyc = p.op("vector", lambda e, ts=ts, si=si: e.tensor_copy(
                    out=ysb[si][:], in_=c.pb[ts][:]), reads=[mmlast[ts], y_done[si]],
                    sig=c.s_dve)
                c.pb_use[ts].append(tyc)
                tsq = p.op("scalar", lambda e, ts=ts, hj=hj, si=si: e.activation(
                    out=sqjunk[:], in_=ysb[si][:], func=AF.Square,
                    accum_out=sqc[:, ts * 4 + hj:ts * 4 + hj + 1]),
                    reads=[tyc, sq_done], sig=c.s_act)
                tys = p.dma("gpsimd",
                            c.ffny[oc * CH + ts * 128: oc * CH + (ts + 1) * 128,
                                   hj * CH:(hj + 1) * CH],
                            ysb[si][:], reads=[tyc], sig=r_ys[si])
                y_done[si] = [tys, tsq]
                sq_toks.append(tsq)
        for i in range(44):
            act_done[i] = wd_done[5]
        for ts in range(4):
            def rms_ts(e, ts=ts, oc=oc):
                e.tensor_tensor(out=rcol[:, 0:1], in0=sqc[:, ts * 4:ts * 4 + 1],
                                in1=sqc[:, ts * 4 + 1:ts * 4 + 2], op=ALU.add)
                e.tensor_tensor(out=rcol[:, 1:2], in0=sqc[:, ts * 4 + 2:ts * 4 + 3],
                                in1=sqc[:, ts * 4 + 3:ts * 4 + 4], op=ALU.add)
                e.drain()
                return e.tensor_scalar(out=rcol[:, 2:3], in0=rcol[:, 0:1],
                                       scalar1=rcol[:, 1:2], scalar2=1.0 / D,
                                       op0=ALU.add, op1=ALU.mult)
            trm = p.op("vector", rms_ts, reads=sq_toks + [getattr(c, "_rms_prev", None)],
                       sig=c.s_dve)
            tl = p.op("scalar", lambda e: e.activation(
                out=rcol[:, 3:4], in_=rcol[:, 2:3], func=AF.Ln, bias=c.rmseps[:]),
                reads=[trm, c.t_const], sig=c.s_act)
            c._rms_prev = tl
            p.drain("scalar")
            tex = p.op("scalar", lambda e, oc=oc, ts=ts: e.activation(
                out=c.ffn_rstd[:, oc * 4 + ts:oc * 4 + ts + 1], in_=rcol[:, 3:4],
                func=AF.Exp, scale=-0.5), sig=c.s_act)
            sq_done = tex
            c.t_ffn_rstd = tex
    c.t_ffn_end = [Tok(s, s.count, "_ff_", True) for s in (r_ys[0], r_ys[1])]
    if "d_ffny" in c.dbg:
        p.dma("gpsimd", c.dbg["d_ffny"][0:noc * CH, :], c.ffny[0:noc * CH, :],
              reads=[c.t_ffn_end], sig=c.s_store)


def phase_attn(c):
    p, sb = c.p, c.sb
    nqc = min(4, getattr(c, "ATN", 4))
    SEG = 8

    r_zt = p.ring("azt", 1)
    r_wq = c.rw12
    r_cs = p.ring("acs", 2)
    r_eps = p.ring("aeps", 2)
    r_row = p.ring("arow", 2)
    r_kn = p.ring("akn", 3)
    r_vt = p.ring("avt", 3)
    r_wo = p.ring("awo", 6)
    r_ff = p.ring("aff", 2)
    r_xo = p.ring("axo", 2)
    r_mk = p.ring("amk", 1)
    r_sw = p.ring("asw2", 2)
    s_rb = p.sem("a_rowb")
    s_out = p.ring_slots("sout", 2)
    c.s_out = s_out[0]

    zT = [c.cb(CH) for _ in range(16)]
    qrotb = [c.cb(CH) for _ in range(16)]
    qn = zT
    aT = qrotb
    masks_sb = c.cb(16 * CH)
    wq4 = [c.cb(CH) for _ in range(12)]
    wot = [c.cb(CH) for _ in range(6)]
    knseg = [c.cb(SEG * 128) for _ in range(3)]
    vseg = [c.cb(SEG * 128) for _ in range(3)]
    Pb = [c.cb(CH) for _ in range(2)]
    rbc = c.cb(CH)
    statb = c.cb(2 * CH)[0:16, :]
    rowsb = c.cf(8 * CH).bitcast(BF16)[0:1, :]
    epsr = [c.cb(CH)[0:1, :] for _ in range(2)]
    q2b = [c.cb(CH) for _ in range(2)]
    lrb = c.cb(CH)[0:1, :]
    mx16 = c.cb(16)[0:16, 0:1]
    mxr = c.cb(16)[0:1, :]

    cosc = [c.cf(CH) for _ in range(2)]
    sinc = [c.cf(CH) for _ in range(2)]
    krot = [c.cf(CH) for _ in range(2)]
    krot_sw = [c.cf(CH) for _ in range(2)]
    p1 = [c.cf(CH) for _ in range(2)]
    p2 = [c.cf(CH) for _ in range(2)]
    stats = c.cf(5 * CH)[0:16, :]
    fft = [c.cf(CH) for _ in range(2)]
    xot = [c.cf(CH) for _ in range(2)]
    outb = [c.cf(CH) for _ in range(2)]
    tmpf = [c.cf(CH) for _ in range(2)]
    lrow = c.cf(CH)[0:1, :]
    qmaxc = c.cf(4)[0:16, 0:1]
    mrow = c.cf(4)[0:1, :]
    negMb = c.cf(1)

    zt_done = None
    wq_done = [None] * 12
    wo_done = [None] * 6
    cs_done = [None, None]
    eps_done = [None, None]
    row_done = [None, None]
    ksg_done = [None, None, None]
    vsg_done = [None, None, None]
    ff_done = [None, None]
    xo_done = [None, None]
    out_done = [None, None]
    tmp_done = [None, None]
    P_done = [None, None]
    q2_done = [None, None]
    qsq_pend = [None, None]
    qrot_done = [None] * 16
    qn_done = [None] * 16
    aT_done = [None] * 16
    krot_done = [None, None]
    p12_done = [None, None]
    stats_done = None
    statb_done = None
    lr_done = None
    rbc_done = None
    negM_done = None
    mx_done = None
    mk_done = None

    s1v = c.pstat[0:16, :]
    s2v = c.pb[4][0:16, :]
    qnv = c.pb[5][0:16, :]

    wqc = 0
    woc = 0
    for qi in range(nqc):
        tzt = None
        for K in range(16):
            tzt = p.dmat("sync", zT[K][:],
                         c.zown[qi * CH:(qi + 1) * CH, K * 128:(K + 1) * 128],
                         reads=[zt_done, c.t_z_done], sig=r_zt[0])
        ri = qi % 2
        tcs = p.dma("sync", cosc[ri][:], c.cosdo[:, qi * CH:(qi + 1) * CH],
                    reads=[cs_done[ri]], sig=r_cs[ri])
        tsn = p.dma("sync", sinc[ri][:], c.sindo[:, qi * CH:(qi + 1) * CH],
                    reads=[cs_done[ri]], sig=r_cs[ri])
        tep = p.dma("sync", epsr[ri][:],
                    c.epso.rearrange("a b -> (a b)")[qi * CH:(qi + 1) * CH].unsqueeze(0),
                    reads=[eps_done[ri], c.t_epso], sig=r_eps[ri])
        tmk = p.dma("sync", masks_sb[:].rearrange("p (m q) -> p m q", m=16),
                    c.maskv[qi].rearrange("m p q -> p m q"),
                    reads=[mk_done], sig=r_mk[0])

        stat_last = [None, None]
        for hg in range(4):
            mmg = [None] * 4
            for K in range(16):
                wi = wqc % 12
                wqc += 1
                tw = p.dma("sync", wq4[wi][:],
                           c.wqP[K * 128:(K + 1) * 128, hg * CH:(hg + 1) * CH],
                           reads=[wq_done[wi], c.t_wprep_end], sig=r_wq[wi])
                for hi in range(4):
                    mmg[hi] = p.op("tensor", lambda e, K=K, wi=wi, hi=hi: e.matmul(
                        out=c.pb[hi][:], lhsT=wq4[wi][:, hi * 128:(hi + 1) * 128],
                        rhs=zT[K][:], start=(K == 0), stop=False),
                        reads=[tw, tzt] + (c.pb_use[hi] if K == 0 else []), sig=c.s_pe)
                wq_done[wi] = mmg[3]
            for hi in range(4):
                h = hg * 4 + hi
                pbi = h % 2
                c.pb_use[hi] = []
                mmb = p.op("tensor", lambda e, h=h, hi=hi: e.matmul(
                    out=c.pb[hi][:], lhsT=c.browq_bf[0:1, h * 128:(h + 1) * 128],
                    rhs=c.ones_row[:], start=False, stop=True),
                    reads=[mmg[hi], c.t_brow, c.t_const], sig=c.s_pe)
                tev = p.op("scalar", lambda e, hi=hi, ki=pbi: e.activation(
                    out=krot[ki][:], in_=c.pb[hi][:], func=AF.Copy),
                    reads=[mmb, krot_done[pbi]], sig=c.s_act)
                c.pb_use[hi].append(tev)
                tw1 = p.dma("gpsimd", krot_sw[pbi][0:64, :], krot[pbi][64:128, :],
                            reads=[tev, krot_done[pbi]], sig=r_sw[pbi])
                tw2 = p.dma("gpsimd", krot_sw[pbi][64:128, :], krot[pbi][0:64, :],
                            sig=r_sw[pbi])
                tp1 = p.op("vector", lambda e, ri=ri, ki=pbi: e.tensor_mul(
                    out=p1[ki][:], in0=krot[ki][:], in1=cosc[ri][:]),
                    reads=[tev, tsn, p12_done[pbi]], sig=c.s_dve)
                tp2 = p.op("gpsimd", lambda e, ri=ri, ki=pbi: e.tensor_mul(
                    out=p2[ki][:], in0=krot_sw[ki][:], in1=sinc[ri][:]),
                    reads=[tw2, tsn], sig=c.s_pool)
                krot_done[pbi] = [tp1, tp2]
                tcb = p.op("vector", lambda e, h=h, ki=pbi: e.tensor_add(
                    out=qrotb[h][:], in0=p1[ki][:], in1=p2[ki][:]),
                    reads=[tp1, tp2, qrot_done[h]], sig=c.s_dve)
                p12_done[pbi] = tcb
                qrot_done[h] = [tcb]
        for h in range(16):
            ki = h % 2
            tsq = p.op("scalar", lambda e, h=h, ki=ki: e.activation(
                out=q2b[ki][:], in_=qrotb[h][:], func=AF.Square),
                reads=[qrot_done[h], q2_done[ki]], sig=c.s_act)
            ts1 = p.op("tensor", lambda e, h=h: e.matmul(
                out=s1v, lhsT=c.ehs[:, h * 16:(h + 1) * 16], rhs=qrotb[h][:],
                start=(h == 0), stop=(h == 15)),
                reads=[qrot_done[h], c.t_const] + (c.pstat_use if h == 0 else []),
                sig=c.s_pe)
            ts2 = p.op("tensor", lambda e, h=h, ki=ki: e.matmul(
                out=s2v, lhsT=c.ehs[:, h * 16:(h + 1) * 16], rhs=q2b[ki][:],
                start=(h == 0), stop=False),
                reads=[tsq] + (c.pb_use[4] if h == 0 else []), sig=c.s_pe)
            q2_done[ki] = ts2
            stat_last = [ts1, ts2]
        tse = p.op("tensor", lambda e, ri=ri: e.matmul(
            out=s2v, lhsT=c.ones_row[0:1, 0:16], rhs=epsr[ri][:],
            start=False, stop=True), reads=[tep, stat_last[1], c.t_const], sig=c.s_pe)
        eps_done[ri] = tse

        def smk(e):
            e.tensor_scalar(out=stats[:, 0:CH], in0=s1v, scalar1=1.0 / 128,
                            scalar2=None, op0=ALU.mult)
            e.tensor_scalar(out=stats[:, CH:2 * CH], in0=s2v, scalar1=1.0 / 128,
                            scalar2=None, op0=ALU.mult)
            e.drain()
            e.scalar_tensor_tensor(out=stats[:, 2 * CH:3 * CH], in0=stats[:, 0:CH],
                                   scalar=-1.0, in1=stats[:, 0:CH],
                                   op0=ALU.mult, op1=ALU.mult)
            e.drain()
            return e.tensor_tensor(out=stats[:, 2 * CH:3 * CH],
                                   in0=stats[:, CH:2 * CH],
                                   in1=stats[:, 2 * CH:3 * CH], op=ALU.add)
        tsk = p.op("vector", smk, reads=[tse, stat_last[0], stats_done], sig=c.s_dve)
        tl1 = p.op("scalar", lambda e: e.activation(
            out=stats[:, 3 * CH:4 * CH], in_=stats[:, 2 * CH:3 * CH], func=AF.Ln),
            reads=[tsk], sig=c.s_act)
        p.drain("scalar")
        tl2 = p.op("scalar", lambda e: e.activation(
            out=stats[:, 3 * CH:4 * CH], in_=stats[:, 3 * CH:4 * CH], func=AF.Exp,
            scale=-0.5), sig=c.s_act)

        def smk2(e):
            e.scalar_tensor_tensor(out=stats[:, 4 * CH:5 * CH], in0=stats[:, 0:CH],
                                   scalar=-1.0, in1=stats[:, 3 * CH:4 * CH],
                                   op0=ALU.mult, op1=ALU.mult)
            e.tensor_copy(out=statb[:, 0:CH], in_=stats[:, 3 * CH:4 * CH])
            e.drain()
            return e.tensor_copy(out=statb[:, CH:2 * CH], in_=stats[:, 4 * CH:5 * CH])
        tsk2 = p.op("vector", smk2, reads=[tl2, statb_done], sig=c.s_dve)
        stats_done = tsk2
        c.pstat_use = [tsk]
        c.pb_use[4] = [tsk]

        trb = p.dma("gpsimd", c.rowq, statb[:],
                    reads=[tsk2, getattr(c, "_rowq_read", [])], sig=s_rb)
        statb_done = trb

        rowq_reads = []
        row_users = []
        for h in range(16):
            si = h % 2
            pc = c.pb[2 * si]
            pd = c.pb[2 * si + 1]
            if h % 8 == 0:
                rflat = c.rowq.rearrange("a b -> (a b)")
                trh = p.dma("sync", rowsb[:],
                            rflat[h * 2 * CH:(h + 8) * 2 * CH].unsqueeze(0),
                            reads=[trb, row_done[0]], sig=r_row[0])
                rowq_reads.append(trh)
                row_done[0] = []
            ho = (h % 8) * 2 * CH
            tmc = p.op("tensor", lambda e, ho=ho, pc=pc: e.matmul(
                out=pc[:], lhsT=c.lrow[0:1, 0:HD],
                rhs=rowsb[0:1, ho:ho + CH],
                start=True, stop=True),
                reads=[trh, c.t_lrow] + c.pb_use[2 * si], sig=c.s_pe)
            tmd1 = p.op("tensor", lambda e, ho=ho, pd=pd: e.matmul(
                out=pd[:], lhsT=c.lrow[0:1, 0:HD],
                rhs=rowsb[0:1, ho + CH:ho + 2 * CH],
                start=True, stop=False),
                reads=[trh] + c.pb_use[2 * si + 1], sig=c.s_pe)
            tmd2 = p.op("tensor", lambda e, pd=pd: e.matmul(
                out=pd[:], lhsT=c.lrow[0:1, HD:2 * HD], rhs=c.ones_row[:],
                start=False, stop=True), sig=c.s_pe)
            row_users += [tmc, tmd1]
            row_done[0] = row_done[0] + [tmc, tmd1]
            c.pb_use[2 * si] = []
            c.pb_use[2 * si + 1] = []
            tt1 = p.op("vector", lambda e, h=h, si=si, pc=pc: e.tensor_tensor(
                out=Pb[si][:], in0=qrotb[h][:], in1=pc[:], op=ALU.mult),
                reads=[tmc, P_done[si]], sig=c.s_dve)
            tt2 = p.op("vector", lambda e, h=h, si=si, pd=pd: e.tensor_tensor(
                out=qn[h][:], in0=Pb[si][:], in1=pd[:], op=ALU.add),
                reads=[tmd2, qn_done[h]], sig=c.s_dve)
            qrot_done[h] = [tt1]
            P_done[si] = tt2
            c.pb_use[2 * si].append(tt1)
            c.pb_use[2 * si + 1].append(tt2)
            if h >= 2:
                hd2 = h - 2
                tqn = p.op("tensor", lambda e, hd2=hd2, si=si: e.matmul(
                    out=qnv, lhsT=c.ehs[:, hd2 * 16:(hd2 + 1) * 16], rhs=q2b[si][:],
                    start=(hd2 == 0), stop=False),
                    reads=[qsq_pend[si]] + (c.pb_use[5] if hd2 == 0 else []),
                    sig=c.s_pe)
                q2_done[si] = tqn
            tsq2 = p.op("scalar", lambda e, h=h, si=si: e.activation(
                out=q2b[si][:], in_=qn[h][:], func=AF.Square),
                reads=[tt2, q2_done[si]], sig=c.s_act)
            qsq_pend[si] = tsq2
            qn_done[h] = [tsq2]
        for hd2 in (14, 15):
            si = hd2 % 2
            tqn = p.op("tensor", lambda e, hd2=hd2, si=si: e.matmul(
                out=qnv, lhsT=c.ehs[:, hd2 * 16:(hd2 + 1) * 16], rhs=q2b[si][:],
                start=False, stop=(hd2 == 15)),
                reads=[qsq_pend[si]], sig=c.s_pe)
            q2_done[si] = tqn
        c._rowq_read = rowq_reads

        def mx1(e):
            e.reduce_max(out=qmaxc[:, 0:1], in_=qnv, axis=AX.X)
            e.drain()
            e.tensor_tensor(out=qmaxc[:, 0:1], in0=qmaxc[:, 0:1], in1=c.maxk2[:],
                            op=ALU.mult)
            e.drain()
            return e.tensor_copy(out=mx16[:, 0:1], in_=qmaxc[:, 0:1])
        tmx = p.op("vector", mx1, reads=[q2_done[0], q2_done[1], c.t_maxk, mx_done],
                   sig=c.s_dve)
        c.pb_use[5] = [tmx]
        ttr = p.op("tensor", lambda e: e.transpose(
            out=c.ptr[0:1, 0:16], in_=mx16[0:16, 0:1], identity=c.identb[0:16, 0:16]),
            reads=[tmx, c.t_const] + c.ptr_use, sig=c.s_pe)
        c.ptr_use = []
        tmr = p.op("vector", lambda e: e.tensor_copy(out=mxr[0:1, 0:16],
                                                     in_=c.ptr[0:1, 0:16]),
                   reads=[ttr], sig=c.s_dve)
        c.ptr_use.append(tmr)
        p.drain("vector")
        tmm = p.op("vector", lambda e: e.reduce_max(out=mrow[0:1, 0:1],
                                                    in_=mxr[0:1, 0:16], axis=AX.X),
                   sig=c.s_dve)
        tsf = p.op("scalar", lambda e: e.activation(
            out=mrow[0:1, 1:2], in_=mrow[0:1, 0:1], func=AF.Sqrt),
            reads=[tmm], sig=c.s_act)
        p.drain("scalar")
        tng = p.op("scalar", lambda e: e.activation(
            out=mrow[0:1, 2:3], in_=mrow[0:1, 1:2], func=AF.Copy, scale=-1.02),
            sig=c.s_act)
        tngb = p.op("vector", lambda e: e.tensor_copy(out=mxr[0:1, 0:1],
                                                      in_=mrow[0:1, 2:3]),
                    reads=[tng, tmr], sig=c.s_dve)
        tmb = p.op("tensor", lambda e: e.matmul(
            out=c.pb[0][:, 0:1], lhsT=c.ones_row[0:1, 0:128], rhs=mxr[0:1, 0:1],
            start=True, stop=True), reads=[tngb, c.t_const] + c.pb_use[0], sig=c.s_pe)
        tnb = p.op("vector", lambda e: e.tensor_copy(out=negMb[:, 0:1],
                                                     in_=c.pb[0][:, 0:1]),
                   reads=[tmb], sig=c.s_dve)
        c.pb_use[0] = [tnb]
        negM_done = tnb
        mx_done = [tmm, tngb]

        nkt = SLOT_NKT[qi]
        mk_users = []
        for h in range(16):
            acc = c.pb[4]
            lps = c.pb[5][0:1, :]
            for kt in range(nkt):
                sgi = (kt // SEG) % 3
                if kt % SEG == 0:
                    tkl = p.dma("sync", knseg[sgi][:],
                                c.knT[h * 128:(h + 1) * 128,
                                      kt * 128:(kt + SEG) * 128],
                                reads=[ksg_done[sgi], c.t_kv_end], sig=r_kn[sgi])
                    tvl = p.dma("sync", vseg[sgi][:].rearrange("t (k d) -> t k d", k=SEG),
                                c.vtok[kt:kt + SEG, h].rearrange("k t d -> t k d"),
                                reads=[vsg_done[sgi], c.t_kv_end], sig=r_vt[sgi])
                    ksg_done[sgi] = []
                    vsg_done[sgi] = []
                ko = (kt % SEG) * 128
                sb_ = kt % 2
                spb = c.pb[2 + (kt % 2)]
                diag = kt >= nkt - 16
                tmmS = p.op("tensor", lambda e, h=h, spb=spb, diag=diag, sgi=sgi, ko=ko: e.matmul(
                    out=spb[:], lhsT=knseg[sgi][:, ko:ko + 128], rhs=qn[h][:],
                    start=True, stop=not diag),
                    reads=[tkl, qn_done[h]] + c.pb_use[2 + (kt % 2)], sig=c.s_pe)
                ksg_done[sgi].append(tmmS)
                if diag:
                    midx = kt - (nkt - 16)
                    tmmS = p.op("tensor", lambda e, midx=midx, spb=spb: e.matmul(
                        out=spb[:], lhsT=c.identb[:],
                        rhs=masks_sb[:, midx * CH:(midx + 1) * CH],
                        start=False, stop=True), reads=[c.t_const, tmk], sig=c.s_pe)
                    mk_users.append(tmmS)
                c.pb_use[2 + (kt % 2)] = []
                texp = p.op("scalar", lambda e, sb_=sb_, spb=spb: e.activation(
                    out=Pb[sb_][:], in_=spb[:], func=AF.Exp, bias=negMb[:, 0:1],
                    scale=1.0), reads=[tmmS, negM_done, P_done[sb_]], sig=c.s_act)
                c.pb_use[2 + (kt % 2)].append(texp)
                st = (kt == 0)
                sp = (kt == nkt - 1)
                tacc = p.op("tensor", lambda e, sb_=sb_, acc=acc, st=st, sp=sp,
                            sgi=sgi, ko=ko: e.matmul(
                    out=acc[:], lhsT=vseg[sgi][:, ko:ko + 128], rhs=Pb[sb_][:],
                    start=st, stop=sp),
                    reads=[tvl, texp] + (c.pb_use[4] if st else []), sig=c.s_pe)
                tl_ = p.op("tensor", lambda e, sb_=sb_, lps=lps, st=st, sp=sp: e.matmul(
                    out=lps, lhsT=c.onescol_bf[:], rhs=Pb[sb_][:],
                    start=st, stop=sp),
                    reads=[texp, c.t_const] + (c.pb_use[5] if st else []), sig=c.s_pe)
                vsg_done[sgi].append(tacc)
                P_done[sb_] = [tacc, tl_]
            c.pb_use[4] = []
            c.pb_use[5] = []
            def lrec(e):
                e.tensor_scalar(out=lrow[0:1, :], in0=lps, scalar1=1e-35,
                                scalar2=None, op0=ALU.max)
                e.drain()
                e.reciprocal(out=lrow[0:1, :], in_=lrow[0:1, :])
                e.drain()
                return e.tensor_copy(out=lrb[0:1, :], in_=lrow[0:1, :])
            tlr = p.op("vector", lrec, reads=[tl_, lr_done], sig=c.s_dve)
            lr_done = tlr
            c.pb_use[5].append(tlr)
            trc = p.op("tensor", lambda e: e.matmul(
                out=c.pb[0][:], lhsT=c.ones_row[0:1, 0:128], rhs=lrb[0:1, :],
                start=True, stop=True), reads=[tlr, c.t_const] + c.pb_use[0],
                sig=c.s_pe)
            c.pb_use[0] = []
            tre = p.op("scalar", lambda e: e.activation(
                out=rbc[:], in_=c.pb[0][:], func=AF.Copy),
                reads=[trc, rbc_done], sig=c.s_act)
            c.pb_use[0].append(tre)
            tdv = p.op("vector", lambda e, h=h: e.tensor_tensor(
                out=aT[h][:], in0=acc[:], in1=rbc[:], op=ALU.mult),
                reads=[tre, aT_done[h], qrot_done[h]], sig=c.s_dve)
            rbc_done = tdv
            c.pb_use[4].append(tdv)
            aT_done[h] = tdv
            qn_done[h] = [qn_done[h], tmmS]

        for do in range(4):
            mml = [None] * 4
            for h in range(16):
                wi = woc % 6
                woc += 1
                twl = p.dma("sync", wot[wi][:],
                            c.woP[h * 128:(h + 1) * 128, do * CH:(do + 1) * CH],
                            reads=[wo_done[wi], c.t_wprep_end], sig=r_wo[wi])
                for ts in range(4):
                    mml[ts] = p.op("tensor", lambda e, h=h, wi=wi, ts=ts: e.matmul(
                        out=c.pb[ts][:], lhsT=aT[h][:, ts * 128:(ts + 1) * 128],
                        rhs=wot[wi][:], start=(h == 0), stop=False),
                        reads=[twl, aT_done[h]] + (c.pb_use[ts] if h == 0 else []),
                        sig=c.s_pe)
                wo_done[wi] = mml[3]
            for ts in range(4):
                c.pb_use[ts] = []
                mmb = p.op("tensor", lambda e, do=do, ts=ts: e.matmul(
                    out=c.pb[ts][:], lhsT=c.ones_row[0:1, 0:128],
                    rhs=c.bo_bf[0:1, do * CH:(do + 1) * CH],
                    start=False, stop=True), reads=[mml[ts], c.t_bo], sig=c.s_pe)
                si = ts % 2
                tff = p.dma("sync", fft[si][:],
                            c.ffny[qi * CH + ts * 128: qi * CH + (ts + 1) * 128,
                                   do * CH:(do + 1) * CH],
                            reads=[ff_done[si], c.t_ffn_end], sig=r_ff[si])
                txo = p.dma("sync", xot[si][:],
                            c.xown[qi * CH + ts * 128: qi * CH + (ts + 1) * 128,
                                   do * CH:(do + 1) * CH],
                            reads=[xo_done[si]], sig=r_xo[si])
                tst1 = p.op("vector", lambda e, si=si, ts=ts, qi=qi: e.scalar_tensor_tensor(
                    out=tmpf[si][:], in0=fft[si][:],
                    scalar=c.ffn_rstd[:, qi * 4 + ts:qi * 4 + ts + 1],
                    in1=c.pb[ts][:], op0=ALU.mult, op1=ALU.add),
                    reads=[tff, mmb, c.t_ffn_rstd, tmp_done[si]], sig=c.s_dve)
                c.pb_use[ts].append(tst1)
                tst2 = p.op("vector", lambda e, si=si: e.tensor_tensor(
                    out=outb[si][:], in0=tmpf[si][:], in1=xot[si][:], op=ALU.add),
                    reads=[txo, out_done[si]], sig=c.s_dve)
                ff_done[si] = tst1
                xo_done[si] = tst2
                tmp_done[si] = tst2
                tso = p.dma("gpsimd", c.y[qi * CH + ts * 128: qi * CH + (ts + 1) * 128,
                                          do * CH:(do + 1) * CH],
                            outb[si][:], reads=[tst2], sig=s_out[si])
                out_done[si] = tso
        for h in range(16):
            aT_done[h] = [aT_done[h], mml[3]]
        cs_done[ri] = krot_done[:]
        zt_done = mml[:]
        mk_done = mk_users




_NC_CACHE = {}


def kernel(**inputs):
    from concourse.bass_utils import run_bass_kernel_spmd
    in_maps = host_prepare(inputs)
    if "nc" not in _NC_CACHE:
        _NC_CACHE["nc"] = Core(0).build()
    nc = _NC_CACHE["nc"]
    res = run_bass_kernel_spmd(nc, in_maps, core_ids=list(range(8)))
    return host_gather(res.results)


# version 12 (final)
# speedup vs baseline: 1.4571x; 1.3190x over previous
import math
import numpy as np
import ml_dtypes
import concourse.bass as bass
import concourse.mybir as mybir
from concourse.bass_utils import run_bass_kernel_spmd

F32 = mybir.dt.float32
BF16 = mybir.dt.bfloat16
F16 = mybir.dt.float16
I32 = mybir.dt.int32
U16 = mybir.dt.uint16
AF = mybir.ActivationFunctionType
ALU = mybir.AluOpType
AX = mybir.AxisListType

B, S, D, H, HD = 2, 8192, 2048, 16, 128
I_FFN = 5632
CH = 512
NCH = S // CH
KT = 128
ROLE_CHUNKS = [[r, 7 - r, 8 + r, 15 - r] for r in range(4)]
SLOT_NKT = [16, 32, 48, 64]
LN_EPS = 1e-5
RMS_EPS = 1e-6
THETA = 30000.0
SCALE2 = HD ** -0.5
NEG_MASK = -30000.0

HDPERM = np.concatenate([np.arange(0, HD, 2), np.arange(1, HD, 2)])
DPERM = np.concatenate([h * HD + HDPERM for h in range(H)])


def host_prepare(inputs):
    x = np.asarray(inputs["L0_attn_x"])
    Wq = np.asarray(inputs["L0_attn_Wq"])
    Wkv = np.asarray(inputs["L0_attn_Wkv"])
    Wo = np.asarray(inputs["L0_attn_Wo"])
    bq = np.asarray(inputs["L0_attn_bq"])
    bkv = np.asarray(inputs["L0_attn_bkv"])
    bo = np.asarray(inputs["L0_attn_bo"])
    qn_g = np.asarray(inputs["L0_attn_qn_g"])
    qn_b = np.asarray(inputs["L0_attn_qn_b"])
    kvn_g = np.asarray(inputs["L0_attn_kvn_g"])
    kvn_b = np.asarray(inputs["L0_attn_kvn_b"])
    ln_g = np.asarray(inputs["L0_attn_ln_g"])
    ln_b = np.asarray(inputs["L0_attn_ln_b"])
    gup = np.asarray(inputs["L0_ffn_gate_up_packed"])
    gus = np.asarray(inputs["L0_ffn_gate_up_scales"])
    dnp = np.asarray(inputs["L0_ffn_down_packed"])
    dns = np.asarray(inputs["L0_ffn_down_scales"])

    WqP = np.ascontiguousarray(Wq[:, DPERM])
    WkvP = np.concatenate([Wkv[:, :D][:, DPERM], Wkv[:, D:]], axis=1)
    bqP = bq[DPERM].copy()
    bkvP = np.concatenate([bkv[:D][DPERM], bkv[D:]])
    ln_gP = ln_g[HDPERM].copy()
    ln_bP = ln_b[HDPERM].copy()

    def unpack_T(packed, n_in):
        w = packed.view(np.uint32)
        nib = (w[:, :, None] >> (4 * np.arange(8, dtype=np.uint32))) & np.uint32(15)
        nib = nib.reshape(packed.shape[0], n_in)
        return np.ascontiguousarray(nib.T.astype(np.uint16))

    gu_nib = unpack_T(gup, D)
    dn_nib = unpack_T(dnp, I_FFN)

    gu_sc = np.ascontiguousarray(np.broadcast_to(gus[:, None, :], (16, 128, 11264)))
    dn_sc = np.ascontiguousarray(np.broadcast_to(dns[:, None, :], (44, 128, 2048)))

    pos = np.arange(S, dtype=np.float64)
    freqs = (np.arange(0, HD, 2, dtype=np.float64) / HD) * math.log(THETA)
    ang = pos[None, :] * freqs[:, None]
    cosd = np.tile(np.cos(ang), (2, 1)).astype(np.float32)
    sind = np.concatenate([-np.sin(ang), np.sin(ang)]).astype(np.float32)

    qq = np.arange(CH)[None, :]
    kk = np.arange(KT)[:, None]

    ident = np.eye(128, dtype=np.float32).astype(ml_dtypes.bfloat16)

    ehs = np.zeros((H, 128, H), dtype=np.float32)
    for h in range(H):
        ehs[h, :, h] = 1.0
    ehs = ehs.astype(ml_dtypes.bfloat16)

    in_maps = []
    for c in range(8):
        b = c // 4
        role = c % 4
        chunks = ROLE_CHUNKS[role]
        xb = x[b]
        xown = np.concatenate([xb[cc * CH:(cc + 1) * CH] for cc in chunks], axis=0)
        ownpos = np.concatenate([np.arange(cc * CH, (cc + 1) * CH) for cc in chunks])
        mv = np.zeros((4, 16, KT, CH), dtype=np.float32)
        for qi, cc in enumerate(chunks):
            E = SLOT_NKT[qi]
            for p128 in range(16):
                kt = E - 16 + p128
                if kt < 4 * cc:
                    pass
                elif kt <= 4 * cc + 3:
                    d = (kt - 4 * cc) * 128
                    mv[qi, p128] = np.where(qq < kk + d, NEG_MASK, 0.0)
                else:
                    mv[qi, p128] = NEG_MASK
        m = dict(
            xpre=np.ascontiguousarray(xb),
            xown=np.ascontiguousarray(xown),
            wq=WqP, wkv=WkvP, wo=Wo,
            bq=bqP[None, :], bkv=bkvP[None, :], bo=bo[None, :],
            qng=np.ascontiguousarray(qn_g.reshape(16, 128).T),
            qnb=np.ascontiguousarray(qn_b.reshape(16, 128).T),
            kvg=np.ascontiguousarray(kvn_g.reshape(16, 128).T),
            kvb=np.ascontiguousarray(kvn_b.reshape(16, 128).T),
            lng=ln_gP[None, :], lnb=ln_bP[None, :],
            gunib=gu_nib, dnnib=dn_nib,
            gusc=gu_sc, dnsc=dn_sc,
            cosd=np.ascontiguousarray(cosd),
            sind=np.ascontiguousarray(sind),
            cosdo=np.ascontiguousarray(cosd[:, ownpos]),
            sindo=np.ascontiguousarray(sind[:, ownpos]),
            maskv=mv.astype(ml_dtypes.bfloat16),
            ident=ident, ehs=ehs,
        )
        in_maps.append(m)
    return in_maps


def host_gather(results):
    out = np.empty((B, S, D), dtype=np.float32)
    for c in range(8):
        b = c // 4
        chunks = ROLE_CHUNKS[c % 4]
        y = results[c]["y"]
        for j, cc in enumerate(chunks):
            out[b, cc * CH:(cc + 1) * CH] = y[j * CH:(j + 1) * CH]
    return out


class Sem:
    __slots__ = ("h", "name", "count")

    def __init__(self, handle, name):
        self.h = handle
        self.name = name
        self.count = 0


class Tok:
    __slots__ = ("sem", "val", "eng", "dma")

    def __init__(self, sem, val, eng, dma):
        self.sem, self.val, self.eng, self.dma = sem, val, eng, dma


class Prog:

    ENGS = ("sync", "tensor", "vector", "scalar", "gpsimd")

    def __init__(self, nc, stack):
        self.nc = nc
        self.ops = []
        self.sems = []
        self._stack = stack

    def sem(self, name):
        s = Sem(self._stack.enter_context(self.nc.semaphore(name)), name)
        self.sems.append(s)
        return s

    def ring(self, name, n):
        return [self.sem(f"{name}{i}") for i in range(n)]

    def ring_slots(self, name, n):
        return [self.sem(f"{name}{i}") for i in range(n)]

    @staticmethod
    def _flat(reads):
        out = []
        for t in reads:
            if t is None:
                continue
            if isinstance(t, (list, tuple)):
                out.extend(Prog._flat(t))
            else:
                out.append(t)
        return out

    def op(self, eng, fn, reads=(), sig=None, amt=1, dma=False):
        waits = []
        for t in self._flat(reads):
            if (not t.dma) and t.eng == eng:
                continue
            waits.append(t)
        tok = None
        if sig is not None:
            sig.count += amt
            tok = Tok(sig, sig.count, eng, dma)
        self.ops.append((eng, fn, waits, sig, amt))
        return tok

    def drain(self, eng):
        self.ops.append((eng, lambda e: e.drain(), [], None, 0))

    def dma(self, eng, out, in_, reads=(), sig=None, **kw):
        assert sig is not None
        return self.op(eng, lambda e: e.dma_start(out=out, in_=in_, **kw),
                       reads=reads, sig=sig, amt=16, dma=True)

    def dmat(self, eng, out, in_, reads=(), sig=None):
        assert sig is not None
        return self.op(eng, lambda e: e.dma_start_transpose(out, in_),
                       reads=reads, sig=sig, amt=16, dma=True)

    def gate_all(self, toks):
        for eng in self.ENGS:
            self.ops.append((eng, lambda e: e.nop(), Prog._flat(toks), None, 0))

    def final_barrier(self, eng):
        toks = [Tok(s, s.count, "_barrier_", True) for s in self.sems if s.count]
        self.ops.append((eng, lambda e: e.nop(), toks, None, 0))

    def emit(self, block):
        ops = self.ops if Prog.KEEP is None else self.ops[:Prog.KEEP]
        per = {e: [] for e in self.ENGS}
        for rec in ops:
            per[rec[0]].append(rec)

        def make(eng_name):
            eops = per[eng_name]

            def body(eng):
                waited = {}
                for (_e, fn, waits, sig, amt) in eops:
                    for t in waits:
                        if waited.get(t.sem.name, -1) >= t.val:
                            continue
                        eng.wait_ge(t.sem.h, t.val)
                        waited[t.sem.name] = t.val
                    inst = fn(eng)
                    if sig is not None:
                        assert inst is not None, "fn with sig must return its last instruction"
                        inst.then_inc(sig.h, amt)
            return body

        for name in self.ENGS:
            if not per[name]:
                continue
            getattr(block, name)(make(name))

    KEEP = None


class Core:
    def __init__(self, role=0, dev=()):
        self.role = role
        self.dev = set(dev)
        self.chunks = ROLE_CHUNKS[role]
        self.npre = NCH
        self.pre = S
        self.nt = S // 128
        self.nc = bass.Bass("TRN2", target_bir_lowering=False, detect_race_conditions=False)
        self.dbg = {}

    def ap(self, name, shape, dt, kind="ExternalInput"):
        return self.nc.dram_tensor(name, list(shape), dt, kind=kind).ap()

    def carve_reset(self):
        self._cf = self._cb = self._cu = 0

    def cf(self, n):
        v = self.PF[:, self._cf:self._cf + n]
        self._cf += n
        assert self._cf <= 20480, f"PF overflow {self._cf}"
        return v

    def cb(self, n):
        v = self.PB[:, self._cb:self._cb + n]
        self._cb += n
        assert self._cb <= 46080, f"PB overflow {self._cb}"
        return v

    def cu(self, n):
        v = self.PU[:, self._cu:self._cu + n]
        self._cu += n
        assert self._cu <= 4096, f"PU overflow {self._cu}"
        return v

    def group_barrier(self):
        return [Tok(s, s.count, "_g_", True) for s in self.p.sems if s.count]

    def build(self):
        import contextlib
        nc = self.nc
        c = self
        pre = c.pre
        c.xpre = c.ap("xpre", [pre, D], F32)
        c.xown = c.ap("xown", [4 * CH, D], F32)
        c.wq = c.ap("wq", [D, D], F32)
        c.wkv = c.ap("wkv", [D, 2 * D], F32)
        c.wo = c.ap("wo", [D, D], F32)
        c.bqr = c.ap("bq", [1, D], F32)
        c.bkvr = c.ap("bkv", [1, 2 * D], F32)
        c.bor = c.ap("bo", [1, D], F32)
        c.qng = c.ap("qng", [128, 16], F32)
        c.qnbr = c.ap("qnb", [128, 16], F32)
        c.kvg = c.ap("kvg", [128, 16], F32)
        c.kvbr = c.ap("kvb", [128, 16], F32)
        c.lngr = c.ap("lng", [1, HD], F32)
        c.lnbr = c.ap("lnb", [1, HD], F32)
        c.gunib = c.ap("gunib", [D, 2 * I_FFN], U16)
        c.dnnib = c.ap("dnnib", [I_FFN, D], U16)
        c.gusc = c.ap("gusc", [16, 128, 2 * I_FFN], F16)
        c.dnsc = c.ap("dnsc", [44, 128, D], F16)
        c.cosdr = c.ap("cosd", [128, pre], F32)
        c.sindr = c.ap("sind", [128, pre], F32)
        c.cosdo = c.ap("cosdo", [128, 4 * CH], F32)
        c.sindo = c.ap("sindo", [128, 4 * CH], F32)
        c.maskv = c.ap("maskv", [4, 16, KT, CH], BF16)
        c.identin = c.ap("ident", [128, 128], BF16)
        c.ehsin = c.ap("ehs", [H, 128, H], BF16)
        c.y = c.ap("y", [4 * CH, D], F32, kind="ExternalOutput")
        c.z_tok = c.ap("z_tok", [pre, D], BF16, kind="Internal")
        c.zown = c.ap("zown", [4 * CH, D], BF16, kind="Internal")
        c.xn_tok = c.ap("xn_tok", [4 * CH, D], BF16, kind="Internal")
        c.epst = c.ap("epst", [64, 128], BF16, kind="Internal")
        c.epso = c.ap("epso", [16, 128], BF16, kind="Internal")
        c.rowq = c.ap("rowq", [16, 2 * CH], BF16, kind="Internal")
        c.rowk = c.ap("rowk", [16, 2 * CH], BF16, kind="Internal")
        c.wqP = c.ap("wqP", [D, D], BF16, kind="Internal")
        c.wkvP = c.ap("wkvP", [32, 128, 4 * CH], BF16, kind="Internal")
        c.woP = c.ap("woP", [D, D], BF16, kind="Internal")
        c.knT = c.ap("knT", [D, pre], BF16, kind="Internal")
        c.vtok = c.ap("vtok", [NCH * 4, H, 128, HD], BF16, kind="Internal")
        c.wguT = c.ap("wguT", [D, 2 * I_FFN], BF16, kind="Internal")
        c.wdT = c.ap("wdT", [I_FFN, D], BF16, kind="Internal")
        c.ffny = c.ap("ffny", [4 * CH, D], F32, kind="Internal")

        if "s1" in c.dev:
            c.dbg["d_xn"] = c.ap("d_xn", [4 * CH, D], BF16, kind="ExternalOutput")
            c.dbg["d_z"] = c.ap("d_z", [pre, D], BF16, kind="ExternalOutput")
            c.dbg["d_eps"] = c.ap("d_eps", [64, 128], BF16, kind="ExternalOutput")
        if "kv" in c.dev:
            c.dbg["d_kn"] = c.ap("d_kn", [D, pre], BF16, kind="ExternalOutput")
            c.dbg["d_v"] = c.ap("d_v", [pre, D], BF16, kind="ExternalOutput")
            c.dbg["d_mk"] = c.ap("d_mk", [1, 16], F32, kind="ExternalOutput")
        if "ffn" in c.dev:
            c.dbg["d_wgu"] = c.ap("d_wgu", [D, 2 * I_FFN], BF16, kind="ExternalOutput")
            c.dbg["d_wd"] = c.ap("d_wd", [I_FFN, D], BF16, kind="ExternalOutput")
            c.dbg["d_ffny"] = c.ap("d_ffny", [4 * CH, D], F32, kind="ExternalOutput")
        if "qn" in c.dev:
            c.dbg["d_qn"] = c.ap("d_qn", [D, 4 * CH], BF16, kind="ExternalOutput")

        with contextlib.ExitStack() as st:
            self._st = st
            E = st.enter_context
            c.sb = lambda name, shape, dt: E(nc.sbuf_tensor(name, list(shape), dt))
            c.ps = lambda name, shape, dt=F32: E(nc.psum_tensor(name, list(shape), dt))
            p = c.p = Prog(nc, st)
            c.s_const = p.sem("s_const")
            c.s_store = p.sem("s_store")
            c.s_act = p.sem("s_act")
            c.s_dve = p.sem("s_dve")
            c.s_pool = p.sem("s_pool")
            c.s_pe = p.sem("s_pe")

            c.identb = c.sb("identb", [128, 128], BF16)
            p.dma("sync", c.identb[:], c.identin, sig=c.s_const)
            c.ehs = c.sb("ehs_sb", [128, H * H], BF16)
            p.dma("sync", c.ehs[:].rearrange("p (h e) -> p h e", h=H),
                  c.ehsin.rearrange("h p e -> p h e"), sig=c.s_const)

            c.lneps = c.sb("lneps", [128, 1], F32)
            tm1 = p.op("vector", lambda e: e.memset(c.lneps[:], LN_EPS), sig=c.s_dve)
            c.rmseps = c.sb("rmseps", [128, 1], F32)
            tm2 = p.op("vector", lambda e: e.memset(c.rmseps[:], RMS_EPS), sig=c.s_dve)
            c.ones_row = c.sb("ones_row", [1, CH], BF16)
            tm3 = p.op("vector", lambda e: e.memset(c.ones_row[:], 1.0), sig=c.s_dve)
            c.onescol_bf = c.sb("onescol_bf", [128, 1], BF16)
            tm4 = p.op("vector", lambda e: e.memset(c.onescol_bf[:], 1.0), sig=c.s_dve)
            c.t_const = [Tok(c.s_const, c.s_const.count, "_c_", True), tm1, tm2, tm3, tm4]

            c.PF = c.sb("poolF", [128, 20480], F32)
            c.PB = c.sb("poolB", [128, 46080], BF16)
            c.PU = c.sb("poolU", [128, 4096], U16)
            c._cf = c._cb = c._cu = 0

            c.pb = [c.ps(f"pb{i}", [128, CH], F32) for i in range(6)]
            c.pstat = c.ps("pstat", [128, CH], F32)
            c.ptr = c.ps("ptr", [128, 128], BF16)
            c.pb_use = [[] for _ in range(6)]
            c.pstat_use = []
            c.ptr_use = []

            c.carve_reset()
            phase_s1(c)
            need_kv = not c.dev or "kv" in c.dev or "attn" in c.dev or "full" in c.dev
            need_ffn = not c.dev or "ffn" in c.dev or "full" in c.dev
            need_attn = not c.dev or "attn" in c.dev or "full" in c.dev
            if need_kv:
                phase_wprep(c)
            if need_ffn:
                phase_dequant(c)
            p.gate_all(c.group_barrier())
            c.carve_reset()
            if need_kv:
                phase_kv(c)
                p.gate_all(c.group_barrier())
                c.carve_reset()
            if need_ffn:
                phase_ffn(c)
                p.gate_all(c.group_barrier())
                c.carve_reset()
            if need_attn:
                phase_attn(c)

            p.final_barrier("sync")
            blk = E(nc.Block())
            p.emit(blk)
        return self.nc


def phase_s1(c):
    p, sb = c.p, c.sb
    nt = min(c.nt, getattr(c, "S1N", 10 ** 9))
    C_EPS128 = 128.0 * LN_EPS / (SCALE2 * D)

    r_x = p.ring("rx", 3)
    r_z = p.ring_slots("rz", 2)
    r_xn = p.ring_slots("rxn", 2)
    s_eps = p.sem("s_eps")

    xb = [c.cf(D) for _ in range(3)]
    junk = c.cf(D)
    xnb = [c.cb(D) for _ in range(2)]
    zb = [c.cb(D) for _ in range(2)]
    cols = [c.cf(16) for _ in range(2)]
    epscols = c.cb(64)
    epscols2 = c.cb(16)
    eps_ps = c.ptr
    eps_sb = c.cb(128)

    st = dict(last_use=[None, None, None], xn_done=[None, None], z_done=[None, None],
              col_done=[None, None], t_ec=None)

    def tile(i, src_rows, zdst, ecol, ecol_i, xndst):
        bi = i % 3
        ci = i % 2
        zi = i % 2
        cc_ = cols[ci]
        xt = xb[bi]
        tload = p.dma("sync", xt[:], src_rows, reads=[st["last_use"][bi]], sig=r_x[bi])
        t1 = p.op("scalar", lambda e, xt=xt, cc=cc_: e.activation(
            out=junk[:], in_=xt[:], func=AF.Copy, accum_out=cc[:, 0:1]),
            reads=[tload, st["col_done"][ci]], sig=c.s_act)
        t2 = p.op("scalar", lambda e, xt=xt, cc=cc_: e.activation(
            out=junk[:], in_=xt[:], func=AF.Square, accum_out=cc[:, 1:2]),
            reads=[tload], sig=c.s_act)

        def sm(e, cc=cc_):
            e.tensor_scalar(out=cc[:, 2:3], in0=cc[:, 0:1], scalar1=1.0 / D,
                            scalar2=None, op0=ALU.mult)
            e.tensor_scalar(out=cc[:, 3:4], in0=cc[:, 1:2], scalar1=1.0 / D,
                            scalar2=None, op0=ALU.mult)
            e.drain()
            e.scalar_tensor_tensor(out=cc[:, 4:5], in0=cc[:, 2:3], scalar=-1.0,
                                   in1=cc[:, 2:3], op0=ALU.mult, op1=ALU.mult)
            e.drain()
            return e.tensor_tensor(out=cc[:, 4:5], in0=cc[:, 3:4], in1=cc[:, 4:5],
                                   op=ALU.add)
        tsm = p.op("vector", sm, reads=[t1, t2, st["col_done"][ci]], sig=c.s_dve)
        tln = p.op("scalar", lambda e, cc=cc_: e.activation(
            out=cc[:, 5:6], in_=cc[:, 4:5], func=AF.Ln, bias=c.lneps[:]),
            reads=[tsm, c.t_const], sig=c.s_act)
        p.drain("scalar")
        trs = p.op("scalar", lambda e, cc=cc_: e.activation(
            out=cc[:, 5:6], in_=cc[:, 5:6], func=AF.Exp, scale=-0.5), sig=c.s_act)

        def sm2(e, cc=cc_):
            e.tensor_tensor(out=cc[:, 6:7], in0=cc[:, 5:6], in1=cc[:, 5:6], op=ALU.mult)
            e.drain()
            return e.tensor_tensor(out=cc[:, 6:7], in0=cc[:, 6:7], in1=cc[:, 4:5],
                                   op=ALU.mult)
        tsm2 = p.op("vector", sm2, reads=[trs], sig=c.s_dve)
        tln2 = p.op("scalar", lambda e, cc=cc_: e.activation(
            out=cc[:, 7:8], in_=cc[:, 6:7], func=AF.Ln, bias=c.lneps[:]),
            reads=[tsm2], sig=c.s_act)
        p.drain("scalar")
        tr2 = p.op("scalar", lambda e, cc=cc_: e.activation(
            out=cc[:, 7:8], in_=cc[:, 7:8], func=AF.Exp, scale=-0.5), sig=c.s_act)

        def sm3(e, cc=cc_, ecol=ecol, i_=ecol_i):
            e.tensor_tensor(out=cc[:, 8:9], in0=cc[:, 5:6], in1=cc[:, 7:8], op=ALU.mult)
            e.scalar_tensor_tensor(out=cc[:, 9:10], in0=cc[:, 2:3], scalar=-1.0,
                                   in1=cc[:, 5:6], op0=ALU.mult, op1=ALU.mult)
            e.reciprocal(out=cc[:, 11:12], in_=cc[:, 6:7])
            e.drain()
            e.scalar_tensor_tensor(out=cc[:, 10:11], in0=cc[:, 2:3], scalar=-1.0,
                                   in1=cc[:, 8:9], op0=ALU.mult, op1=ALU.mult)
            return e.tensor_scalar(out=ecol[:, i_:i_ + 1], in0=cc[:, 11:12],
                                   scalar1=C_EPS128, scalar2=None, op0=ALU.mult)
        tsm3 = p.op("vector", sm3, reads=[tr2], sig=c.s_dve)
        st["t_ec"] = tsm3

        tz = p.op("gpsimd", lambda e, xt=xt, cc=cc_, zo=zb[zi]: e.tensor_scalar(
            out=zo[:], in0=xt[:], scalar1=cc[:, 8:9], scalar2=cc[:, 10:11],
            op0=ALU.mult, op1=ALU.add), reads=[tsm3, st["z_done"][zi], tload],
            sig=c.s_pool)
        tzs = p.dma("gpsimd", zdst, zb[zi][:], reads=[tz], sig=r_z[zi])
        st["z_done"][zi] = tzs
        uses = [tz]
        if xndst is not None:
            p.drain("vector")
            txn = p.op("vector", lambda e, xt=xt, cc=cc_, xo=xnb[zi]: e.tensor_scalar(
                out=xo[:], in0=xt[:], scalar1=cc[:, 5:6], scalar2=cc[:, 9:10],
                op0=ALU.mult, op1=ALU.add), reads=[tsm3, st["xn_done"][zi], tload],
                sig=c.s_dve)
            txs = p.dma("gpsimd", xndst, xnb[zi][:], reads=[txn], sig=r_xn[zi])
            st["xn_done"][zi] = txs
            uses.append(txn)
        st["last_use"][bi] = uses
        st["col_done"][ci] = uses

    for i in range(nt):
        tile(i, c.xpre[i * 128:(i + 1) * 128, :],
             c.z_tok[i * 128:(i + 1) * 128, :], epscols, i, None)
    t_ec_pre = st["t_ec"]
    nt2 = 16
    for i in range(nt2):
        tile(i, c.xown[i * 128:(i + 1) * 128, :],
             c.zown[i * 128:(i + 1) * 128, :], epscols2, i,
             c.xn_tok[i * 128:(i + 1) * 128, :])

    tec = p.op("tensor", lambda e: e.transpose(
        out=eps_ps[0:nt, 0:128], in_=epscols[:, 0:nt], identity=c.identb[:]),
        reads=[c.t_const, t_ec_pre, st["t_ec"]], sig=c.s_pe)
    tev = p.op("vector", lambda e: e.tensor_copy(out=eps_sb[0:nt, :],
                                                 in_=eps_ps[0:nt, 0:128]),
               reads=[tec], sig=c.s_dve)
    tes = p.dma("gpsimd", c.epst[0:nt, :], eps_sb[0:nt, :], reads=[tev], sig=s_eps)
    tec2 = p.op("tensor", lambda e: e.transpose(
        out=eps_ps[0:nt2, 0:128], in_=epscols2[:, 0:nt2], identity=c.identb[:]),
        reads=[st["t_ec"], tev], sig=c.s_pe)
    tev2 = p.op("vector", lambda e: e.tensor_copy(out=eps_sb[0:nt2, :],
                                                  in_=eps_ps[0:nt2, 0:128]),
               reads=[tec2, tes], sig=c.s_dve)
    tes2 = p.dma("gpsimd", c.epso[0:nt2, :], eps_sb[0:nt2, :], reads=[tev2], sig=s_eps)
    c.ptr_use = [tev2]
    c.t_eps = Tok(s_eps, s_eps.count, "_e_", True)
    c.t_epso = c.t_eps
    c.t_z_done = [t for t in st["z_done"] if t is not None]
    c.t_xn_done = [t for t in st["xn_done"] if t is not None]

    if "d_xn" in c.dbg:
        p.dma("gpsimd", c.dbg["d_xn"], c.xn_tok, reads=c.t_xn_done, sig=c.s_store)
        p.dma("gpsimd", c.dbg["d_z"], c.z_tok, reads=c.t_z_done, sig=c.s_store)
        p.dma("gpsimd", c.dbg["d_eps"][0:nt, :], c.epst[0:nt, :], reads=[tes],
              sig=c.s_store)


def phase_wprep(c):
    p, sb = c.p, c.sb

    s_v = p.sem("s_wv")
    gq = c.cf(16)
    gkv = c.cf(16)
    bqn = c.cf(16)
    bkvn = c.cf(16)
    t1 = p.dma("sync", gq[:], c.qng, sig=s_v)
    t2 = p.dma("sync", gkv[:], c.kvg, sig=s_v)
    t3 = p.dma("sync", bqn[:], c.qnbr, sig=s_v)
    t4 = p.dma("sync", bkvn[:], c.kvbr, sig=s_v)
    s_r = p.sem("s_wr")
    rows_fv = c.cf(4 * D)
    rows_f = rows_fv[0:1, :]
    t5 = p.dma("sync", rows_f[:, 0:D], c.bqr, sig=s_r)
    t6 = p.dma("sync", rows_f[:, D:3 * D], c.bkvr, sig=s_r)
    t7 = p.dma("sync", rows_f[:, 3 * D:4 * D], c.bor, sig=s_r)
    lrv = c.cf(2 * HD)
    lr = lrv[0:1, :]
    t8 = p.dma("sync", lr[:, 0:HD], c.lngr, sig=s_r)
    t9 = p.dma("sync", lr[:, HD:2 * HD], c.lnbr, sig=s_r)
    tv = [t1, t2, t3, t4]
    tr = [t5, t6, t7, t8, t9]

    bqn_bf = c.cb(16)
    bkvn_bf = c.cb(16)
    tb1 = p.op("vector", lambda e: e.tensor_copy(out=bqn_bf[:], in_=bqn[:]),
               reads=[tv], sig=c.s_dve)
    tb2 = p.op("vector", lambda e: e.tensor_copy(out=bkvn_bf[:], in_=bkvn[:]),
               reads=[tv], sig=c.s_dve)
    c.lrow = sb("lrow", [1, 4 * HD], BF16)
    def mkrows(e):
        e.tensor_copy(out=c.lrow[:, 0:2 * HD], in_=lr[:, 0:2 * HD])
        return e.tensor_scalar(out=c.lrow[:, 2 * HD:4 * HD], in0=lr[:, 0:2 * HD],
                               scalar1=float(SCALE2), scalar2=None, op0=ALU.mult)
    tlr = p.op("vector", mkrows, reads=[tr], sig=c.s_dve)
    c.t_lrow = tlr

    c.bo_bf = sb("bo_bf", [1, D], BF16)
    tbo = p.op("vector", lambda e: e.tensor_copy(out=c.bo_bf[:], in_=rows_f[:, 3 * D:4 * D]),
               reads=[tr], sig=c.s_dve)
    c.t_bo = tbo

    r_w = p.ring("rw", 4)
    wbuf = [c.cf(CH) for _ in range(4)]
    wub = [c.cb(CH) for _ in range(2)]
    wsb = [c.cb(CH) for _ in range(2)]
    r_ws = p.ring_slots("rws", 2)

    c.browq_bf = sb("browq_bf", [1, D], BF16)
    c.browkv_bf = sb("browkv_bf", [1, 2 * D], BF16)

    brow_ps = c.pstat
    specs = [
        ("wq", c.wq, c.wqP, 4, gq, bqn_bf, c.browq_bf, 0),
        ("wkv", c.wkv, c.wkvP, 8, gkv, bkvn_bf, c.browkv_bf, D),
        ("wo", c.wo, c.woP, 4, None, None, None, 3 * D),
    ]
    wu_done = [None, None]
    ws_done = [None, None]
    wl_done = [None, None, None, None]
    t_brow_es = []
    for (name, wsrc, wdst, jmax, gcol, bcol, browdst, boff) in specs:
        for j in range(jmax):
            mm_toks = []
            for K in range(16):
                i4 = (j * 16 + K) % 4
                i2 = (j * 16 + K) % 2
                tl = p.dma("sync", wbuf[i4][:], wsrc[K * 128:(K + 1) * 128, j * CH:(j + 1) * CH],
                           reads=[wl_done[i4]], sig=r_w[i4])
                tc = p.op("vector", lambda e, a=wub[i2], b=wbuf[i4]: e.tensor_copy(
                    out=a[:], in_=b[:]), reads=[tl, wu_done[i2]], sig=c.s_dve)
                uses = [tc]
                if gcol is not None:
                    tmm = p.op("tensor", lambda e, K=K, i2=i2, b=bcol: e.matmul(
                        out=brow_ps[0:1, 0:CH], lhsT=b[:, K:K + 1], rhs=wub[i2][:],
                        start=(K == 0), stop=(K == 15)),
                        reads=[tc, tb1, tb2] + (c.pstat_use if K == 0 else []),
                        sig=c.s_pe)
                    mm_toks.append(tmm)
                    tsc = p.op("vector", lambda e, K=K, i2=i2, g=gcol: e.tensor_scalar(
                        out=wsb[i2][:], in0=wub[i2][:], scalar1=g[:, K:K + 1],
                        scalar2=None, op0=ALU.mult), reads=[tv, ws_done[i2]], sig=c.s_dve)
                    if name == "wkv":
                        wdap = wdst[j * 4 + K // 4, :, (K % 4) * CH:(K % 4 + 1) * CH]
                    else:
                        wdap = wdst[K * 128:(K + 1) * 128, j * CH:(j + 1) * CH]
                    tst = p.dma("gpsimd", wdap, wsb[i2][:], reads=[tsc], sig=r_ws[i2])
                    ws_done[i2] = tst
                    uses.append(tsc)
                    wu_done[i2] = uses
                else:
                    tst = p.dma("gpsimd", wdst[K * 128:(K + 1) * 128, j * CH:(j + 1) * CH],
                                wub[i2][:], reads=[tc], sig=r_ws[i2])
                    wu_done[i2] = [tst]
                wl_done[i4] = uses if gcol is None else uses
                if gcol is None:
                    wl_done[i4] = [tc]
            if gcol is not None:
                c.pstat_use = []
                tadd = p.op("vector", lambda e, j=j, boff=boff, browdst=browdst: e.tensor_tensor(
                    out=browdst[0:1, j * CH:(j + 1) * CH], in0=brow_ps[0:1, 0:CH],
                    in1=rows_f[0:1, boff + j * CH:boff + (j + 1) * CH], op=ALU.add),
                    reads=[mm_toks[-1], tr], sig=c.s_dve)
                c.pstat_use.append(tadd)
                t_brow_es.append(tadd)
    c.t_brow = t_brow_es
    c.t_wqP = ws_done + wu_done
    c.t_wprep_end = [Tok(s, s.count, "_w_", True) for s in (r_ws[0], r_ws[1])]


def phase_kv(c):
    p, sb = c.p, c.sb
    nch = min(c.npre, getattr(c, "KVN", 10 ** 9))

    r_zt = p.ring("rzt", 2)
    c.rw12 = p.ring("rw12", 12)
    r_wk = c.rw12[0:5]
    r_wv = c.rw12[5:9]
    r_cs = p.ring("rcs", 2)
    r_eps = p.ring("reps", 2)
    r_row = p.ring("rrow", 2)
    r_kst = p.ring_slots("rkst", 2)
    r_vst = p.ring_slots("rvst", 2)
    r_sw = p.ring("rsw", 4)
    s_rb = p.sem("s_rowb")

    zT = [[c.cb(CH) for K in range(16)] for r in range(2)]
    wkh = [c.cb(4 * CH) for _ in range(5)]
    wvh = [c.cb(2 * CH) for _ in range(4)]
    cosc = [c.cf(CH) for _ in range(2)]
    sinc = [c.cf(CH) for _ in range(2)]
    epsr = [c.cb(CH)[0:1, :] for _ in range(2)]
    krot = [c.cf(CH) for _ in range(4)]
    krot_sw = [c.cf(CH) for _ in range(4)]
    p1 = [c.cf(CH) for _ in range(4)]
    p2 = [c.cf(CH) for _ in range(4)]
    rotb = [c.cb(CH) for _ in range(16)]
    rot2 = [c.cb(CH) for _ in range(2)]
    kn2 = [c.cb(CH) for _ in range(2)]
    stats = c.cf(5 * CH)[0:16, :]
    statb = c.cb(2 * CH)[0:16, :]
    rowsb = c.cf(8 * CH).bitcast(BF16)[0:1, :]
    knsb = [c.cb(CH) for _ in range(2)]
    tmpsb = [c.cb(CH) for _ in range(2)]
    vsb = [c.cb(CH) for _ in range(2)]
    maxcol = c.cf(2)[0:16, 0:1]
    c.maxk2 = sb("maxk2", [16, 1], F32)

    zt_done = [None, None]
    wk_done = [None] * 5
    wv_done = [None] * 4
    cs_done = [None, None]
    eps_done = [None, None]
    row_done = [None, None]
    kn_done = [None, None]
    tmp_done = [None, None]
    v_done = [None, None]
    stats_done = None
    statb_done = None
    rot_done = [None] * 16
    rot2_done = [None, None]
    kn2_done = [None, None]
    tsq_pend = [None, None]
    krot_done = [None] * 4
    p12_done = [None] * 4

    tmk0 = p.op("vector", lambda e: e.memset(c.maxk2[:], 0.0), sig=c.s_dve)
    maxk_done = tmk0

    s1v = c.pstat[0:16, :]
    s2v = c.pb[4][0:16, :]
    knv = c.pb[5][0:16, :]

    wkc = 0
    wvc = 0
    for cc in range(nch):
        ri = cc % 2
        tzt = None
        for K in range(16):
            tzt = p.dmat("sync", zT[ri][K][:],
                         c.z_tok[cc * CH:(cc + 1) * CH, K * 128:(K + 1) * 128],
                         reads=[zt_done[ri], c.t_z_done], sig=r_zt[ri])
        tcs = p.dma("sync", cosc[ri][:], c.cosdr[:, cc * CH:(cc + 1) * CH],
                    reads=[cs_done[ri]], sig=r_cs[ri])
        tsn = p.dma("sync", sinc[ri][:], c.sindr[:, cc * CH:(cc + 1) * CH],
                    reads=[cs_done[ri]], sig=r_cs[ri])
        tep = p.dma("sync", epsr[ri][:],
                    c.epst.rearrange("a b -> (a b)")[cc * CH:(cc + 1) * CH].unsqueeze(0),
                    reads=[eps_done[ri], c.t_eps], sig=r_eps[ri])

        stat_last = [None, None]
        for hg in range(4):
            mmg = [None] * 4
            for q in range(4):
                wi = wkc % 5
                wkc += 1
                tw = p.dma("sync", wkh[wi][:], c.wkvP[hg * 4 + q],
                           reads=[wk_done[wi], c.t_wprep_end], sig=r_wk[wi])
                for Ki in range(4):
                    K = q * 4 + Ki
                    for hi in range(4):
                        mmg[hi] = p.op("tensor", lambda e, K=K, Ki=Ki, wi=wi, hi=hi, ri=ri: e.matmul(
                            out=c.pb[hi][:],
                            lhsT=wkh[wi][:, Ki * CH + hi * 128:Ki * CH + (hi + 1) * 128],
                            rhs=zT[ri][K][:], start=(K == 0), stop=False),
                            reads=[tw, tzt] + (c.pb_use[hi] if K == 0 else []), sig=c.s_pe)
                wk_done[wi] = mmg[3]
            for hi in range(4):
                h = hg * 4 + hi
                pbi = h % 4
                c.pb_use[hi] = []
                mmb = p.op("tensor", lambda e, h=h, hi=hi: e.matmul(
                    out=c.pb[hi][:], lhsT=c.browkv_bf[0:1, h * 128:(h + 1) * 128],
                    rhs=c.ones_row[:], start=False, stop=True),
                    reads=[mmg[hi], c.t_brow, c.t_const], sig=c.s_pe)
                tev = p.op("scalar", lambda e, hi=hi, ki=pbi: e.activation(
                    out=krot[ki][:], in_=c.pb[hi][:], func=AF.Copy),
                    reads=[mmb, krot_done[pbi]], sig=c.s_act)
                c.pb_use[hi].append(tev)
                tw1 = p.dma("gpsimd", krot_sw[pbi][0:64, :], krot[pbi][64:128, :],
                            reads=[tev, krot_done[pbi]], sig=r_sw[pbi])
                tw2 = p.dma("gpsimd", krot_sw[pbi][64:128, :], krot[pbi][0:64, :],
                            sig=r_sw[pbi])
                tp1 = p.op("vector", lambda e, ri=ri, ki=pbi: e.tensor_mul(
                    out=p1[ki][:], in0=krot[ki][:], in1=cosc[ri][:]),
                    reads=[tev, tsn, p12_done[pbi]], sig=c.s_dve)
                tp2 = p.op("gpsimd", lambda e, ri=ri, ki=pbi: e.tensor_mul(
                    out=p2[ki][:], in0=krot_sw[ki][:], in1=sinc[ri][:]),
                    reads=[tw2, tsn], sig=c.s_pool)
                krot_done[pbi] = [tp1, tp2]
                tcb = p.op("vector", lambda e, h=h, ki=pbi: e.tensor_add(
                    out=rotb[h][:], in0=p1[ki][:], in1=p2[ki][:]),
                    reads=[tp1, tp2, rot_done[h]], sig=c.s_dve)
                p12_done[pbi] = tcb
                rot_done[h] = [tcb]
        for h in range(16):
            ki = h % 2
            tsq = p.op("scalar", lambda e, h=h, ki=ki: e.activation(
                out=rot2[ki][:], in_=rotb[h][:], func=AF.Square),
                reads=[rot_done[h], rot2_done[ki]], sig=c.s_act)
            ts1 = p.op("tensor", lambda e, h=h: e.matmul(
                out=s1v, lhsT=c.ehs[:, h * 16:(h + 1) * 16], rhs=rotb[h][:],
                start=(h == 0), stop=(h == 15)),
                reads=[rot_done[h], c.t_const] + (c.pstat_use if h == 0 else []),
                sig=c.s_pe)
            ts2 = p.op("tensor", lambda e, h=h, ki=ki: e.matmul(
                out=s2v, lhsT=c.ehs[:, h * 16:(h + 1) * 16], rhs=rot2[ki][:],
                start=(h == 0), stop=False),
                reads=[tsq] + (c.pb_use[4] if h == 0 else []), sig=c.s_pe)
            rot2_done[ki] = ts2
            stat_last = [ts1, ts2]
        tse = p.op("tensor", lambda e, ri=ri: e.matmul(
            out=s2v, lhsT=c.ones_row[0:1, 0:16], rhs=epsr[ri][:],
            start=False, stop=True), reads=[tep, stat_last[1], c.t_const], sig=c.s_pe)
        eps_done[ri] = tse

        def smk(e):
            e.tensor_scalar(out=stats[:, 0:CH], in0=s1v, scalar1=1.0 / 128,
                            scalar2=None, op0=ALU.mult)
            e.tensor_scalar(out=stats[:, CH:2 * CH], in0=s2v, scalar1=1.0 / 128,
                            scalar2=None, op0=ALU.mult)
            e.drain()
            e.scalar_tensor_tensor(out=stats[:, 2 * CH:3 * CH], in0=stats[:, 0:CH],
                                   scalar=-1.0, in1=stats[:, 0:CH],
                                   op0=ALU.mult, op1=ALU.mult)
            e.drain()
            return e.tensor_tensor(out=stats[:, 2 * CH:3 * CH],
                                   in0=stats[:, CH:2 * CH],
                                   in1=stats[:, 2 * CH:3 * CH], op=ALU.add)
        tsk = p.op("vector", smk, reads=[tse, stat_last[0], stats_done], sig=c.s_dve)
        tl1 = p.op("scalar", lambda e: e.activation(
            out=stats[:, 3 * CH:4 * CH], in_=stats[:, 2 * CH:3 * CH], func=AF.Ln),
            reads=[tsk], sig=c.s_act)
        p.drain("scalar")
        tl2 = p.op("scalar", lambda e: e.activation(
            out=stats[:, 3 * CH:4 * CH], in_=stats[:, 3 * CH:4 * CH], func=AF.Exp,
            scale=-0.5), sig=c.s_act)

        def smk2(e):
            e.scalar_tensor_tensor(out=stats[:, 4 * CH:5 * CH], in0=stats[:, 0:CH],
                                   scalar=-1.0, in1=stats[:, 3 * CH:4 * CH],
                                   op0=ALU.mult, op1=ALU.mult)
            e.tensor_copy(out=statb[:, 0:CH], in_=stats[:, 3 * CH:4 * CH])
            e.drain()
            return e.tensor_copy(out=statb[:, CH:2 * CH], in_=stats[:, 4 * CH:5 * CH])
        tsk2 = p.op("vector", smk2, reads=[tl2, statb_done], sig=c.s_dve)
        stats_done = tsk2
        c.pstat_use = [tsk]
        c.pb_use[4] = [tsk]

        trb = p.dma("gpsimd", c.rowk, statb[:],
                    reads=[tsk2, getattr(c, "_rowk_read", [])], sig=s_rb)
        statb_done = trb

        for vd in range(4):
            vps = [c.pb[t] for t in range(4)]
            mm_last = [None] * 4
            for q in range(4):
                for hf in range(2):
                    wi = wvc % 4
                    wvc += 1
                    tw = p.dma("sync", wvh[wi][:],
                               c.wkvP[16 + vd * 4 + q, :, hf * 2 * CH:(hf + 1) * 2 * CH],
                               reads=[wv_done[wi], c.t_wprep_end], sig=r_wv[wi])
                    for Ki in range(2):
                        K = q * 4 + hf * 2 + Ki
                        for t4 in range(4):
                            mm_last[t4] = p.op("tensor", lambda e, K=K, Ki=Ki, wi=wi, t4=t4, ri=ri, vv=vps[t4]: e.matmul(
                                out=vv[:], lhsT=zT[ri][K][:, t4 * 128:(t4 + 1) * 128],
                                rhs=wvh[wi][:, Ki * CH:(Ki + 1) * CH],
                                start=(K == 0), stop=False),
                                reads=[tw, tzt] + (c.pb_use[t4] if K == 0 else []), sig=c.s_pe)
                    wv_done[wi] = mm_last[3]
            for t4 in range(4):
                c.pb_use[t4] = []
                mmb = p.op("tensor", lambda e, vd=vd, vv=vps[t4]: e.matmul(
                    out=vv[:], lhsT=c.ones_row[0:1, 0:128],
                    rhs=c.browkv_bf[0:1, 2048 + vd * CH:2048 + (vd + 1) * CH],
                    start=False, stop=True), reads=[mm_last[t4], c.t_brow], sig=c.s_pe)
                si = t4 % 2
                tev = p.op("scalar", lambda e, si=si, vv=vps[t4]: e.activation(
                    out=vsb[si][:], in_=vv[:], func=AF.Copy),
                    reads=[mmb, v_done[si]], sig=c.s_act)
                c.pb_use[t4].append(tev)
                kt = cc * 4 + t4
                tvs = p.dma("gpsimd",
                            c.vtok[kt, vd * 4:(vd + 1) * 4].rearrange("h t d -> t h d"),
                            vsb[si][:].rearrange("t (h d) -> t h d", h=4),
                            reads=[tev], sig=r_vst[si])
                v_done[si] = tvs
        rowk_reads = []
        row_users = []
        for h in range(16):
            si = h % 2
            pc = c.pb[2 * si]
            pd = c.pb[2 * si + 1]
            if h % 8 == 0:
                rflat = c.rowk.rearrange("a b -> (a b)")
                trh = p.dma("sync", rowsb[:],
                            rflat[h * 2 * CH:(h + 8) * 2 * CH].unsqueeze(0),
                            reads=[trb, row_done[0]], sig=r_row[0])
                rowk_reads.append(trh)
                row_done[0] = []
            ho = (h % 8) * 2 * CH
            tmc = p.op("tensor", lambda e, ho=ho, pc=pc: e.matmul(
                out=pc[:], lhsT=c.lrow[0:1, 2 * HD:3 * HD],
                rhs=rowsb[0:1, ho:ho + CH], start=True, stop=True),
                reads=[trh, c.t_lrow] + c.pb_use[2 * si], sig=c.s_pe)
            tmd1 = p.op("tensor", lambda e, ho=ho, pd=pd: e.matmul(
                out=pd[:], lhsT=c.lrow[0:1, 2 * HD:3 * HD],
                rhs=rowsb[0:1, ho + CH:ho + 2 * CH], start=True, stop=False),
                reads=[trh] + c.pb_use[2 * si + 1], sig=c.s_pe)
            tmd2 = p.op("tensor", lambda e, pd=pd: e.matmul(
                out=pd[:], lhsT=c.lrow[0:1, 3 * HD:4 * HD], rhs=c.ones_row[:],
                start=False, stop=True), sig=c.s_pe)
            row_users += [tmc, tmd1]
            row_done[0] = row_done[0] + [tmc, tmd1] if isinstance(row_done[0], list) else [tmc, tmd1]
            c.pb_use[2 * si] = []
            c.pb_use[2 * si + 1] = []
            tt1 = p.op("vector", lambda e, h=h, si=si, pc=pc: e.tensor_tensor(
                out=tmpsb[si][:], in0=rotb[h][:], in1=pc[:], op=ALU.mult),
                reads=[tmc, tmp_done[si]], sig=c.s_dve)
            tt2 = p.op("vector", lambda e, si=si, pd=pd: e.tensor_tensor(
                out=knsb[si][:], in0=tmpsb[si][:], in1=pd[:], op=ALU.add),
                reads=[tmd2, kn_done[si]], sig=c.s_dve)
            rot_done[h] = [tt1]
            c.pb_use[2 * si].append(tt1)
            c.pb_use[2 * si + 1].append(tt2)
            tmp_done[si] = tt2
            if h >= 2:
                hd2 = h - 2
                tkn = p.op("tensor", lambda e, hd2=hd2, si=si: e.matmul(
                    out=knv, lhsT=c.ehs[:, hd2 * 16:(hd2 + 1) * 16], rhs=kn2[si][:],
                    start=(hd2 == 0), stop=False),
                    reads=[tsq_pend[si]] + (c.pb_use[5] if hd2 == 0 else []),
                    sig=c.s_pe)
                kn2_done[si] = tkn
            tsq2 = p.op("scalar", lambda e, si=si: e.activation(
                out=kn2[si][:], in_=knsb[si][:], func=AF.Square),
                reads=[tt2, kn2_done[si]], sig=c.s_act)
            tsq_pend[si] = tsq2
            tst = p.dma("gpsimd", c.knT[h * 128:(h + 1) * 128, cc * CH:(cc + 1) * CH],
                        knsb[si][:], reads=[tt2], sig=r_kst[si])
            kn_done[si] = [tst, tsq2]
        for hd2 in (14, 15):
            si = hd2 % 2
            tkn = p.op("tensor", lambda e, hd2=hd2, si=si: e.matmul(
                out=knv, lhsT=c.ehs[:, hd2 * 16:(hd2 + 1) * 16], rhs=kn2[si][:],
                start=False, stop=(hd2 == 15)),
                reads=[tsq_pend[si]], sig=c.s_pe)
            kn2_done[si] = tkn
        c._rowk_read = rowk_reads
        tmx = p.op("vector", lambda e: e.reduce_max(out=maxcol[:], in_=knv, axis=AX.X),
                   reads=[kn2_done[0], kn2_done[1], maxk_done], sig=c.s_dve)
        p.drain("vector")
        tmx2 = p.op("vector", lambda e: e.tensor_max(
            out=c.maxk2[:], in0=c.maxk2[:], in1=maxcol[:]), sig=c.s_dve)
        maxk_done = tmx2
        c.pb_use[5] = [tmx]

        zt_done[ri] = [mm_last[t] for t in range(4)]
        cs_done[ri] = krot_done[:]

    c.t_kv_end = [Tok(s, s.count, "_kv_", True)
                  for s in (r_kst[0], r_kst[1], r_vst[0], r_vst[1])]
    c.t_maxk = maxk_done

    if "d_kn" in c.dbg:
        p.dma("gpsimd", c.dbg["d_kn"][:, 0:nch * CH], c.knT[:, 0:nch * CH],
              reads=[c.t_kv_end], sig=c.s_store)
        p.dma("gpsimd", c.dbg["d_mk"].rearrange("o s -> s o"), c.maxk2[:],
              reads=[maxk_done], sig=c.s_store)


def phase_dequant(c):
    p = c.p
    r_nb = p.ring("rnb", 2)
    r_sc = p.ring("rsc", 2)
    r_wo = p.ring_slots("rwo", 2)
    nib = [c.cu(CH) for _ in range(2)]
    t2b = [c.cu(CH) for _ in range(2)]
    bits = [c.cu(CH) for _ in range(2)]
    scf = [c.cf(CH) for _ in range(2)]
    wout = [c.cb(CH) for _ in range(2)]
    sc16 = [c.sb(f"sc16_{i}", [128, CH], F16) for i in range(2)]

    nib_done = [None, None]
    t2_done = [None, None]
    bits_done = [None, None]
    sc_done = [None, None]
    scf_done = [None, None]
    w_done = [None, None]

    jobs = []
    for g in range(16):
        for j in range(2 * I_FFN // CH):
            jobs.append(("gu", g, j))
    for g in range(44):
        for j in range(D // CH):
            jobs.append(("dn", g, j))
    if getattr(c, "DQN", None):
        jobs = jobs[:c.DQN]

    for idx, (kind, g, j) in enumerate(jobs):
        si = idx % 2
        if kind == "gu":
            nsrc = c.gunib[g * 128:(g + 1) * 128, j * CH:(j + 1) * CH]
            ssrc = c.gusc[g, :, j * CH:(j + 1) * CH]
            wdst = c.wguT[g * 128:(g + 1) * 128, j * CH:(j + 1) * CH]
        else:
            nsrc = c.dnnib[g * 128:(g + 1) * 128, j * CH:(j + 1) * CH]
            ssrc = c.dnsc[g, :, j * CH:(j + 1) * CH]
            wdst = c.wdT[g * 128:(g + 1) * 128, j * CH:(j + 1) * CH]
        tn = p.dma("sync", nib[si][:], nsrc, reads=[nib_done[si]], sig=r_nb[si])
        tsc = p.dma("sync", sc16[si][:], ssrc, reads=[sc_done[si]], sig=r_sc[si])
        tt2 = p.op("vector", lambda e, si=si: e.tensor_scalar(
            out=t2b[si][:], in0=nib[si][:], scalar1=8, scalar2=32256,
            op0=ALU.is_ge, op1=ALU.mult), reads=[tn, t2_done[si]], sig=c.s_dve)
        tbi = p.op("vector", lambda e, si=si: e.scalar_tensor_tensor(
            out=bits[si][:], in0=nib[si][:], scalar=64.0, in1=t2b[si][:],
            op0=ALU.mult, op1=ALU.add), reads=[tt2, bits_done[si]], sig=c.s_dve)
        nib_done[si] = tbi
        t2_done[si] = tbi
        tsf = p.op("vector", lambda e, si=si: e.tensor_scalar(
            out=scf[si][:], in0=sc16[si][:], scalar1=float(2.0 ** 126), scalar2=None,
            op0=ALU.mult), reads=[tsc, scf_done[si]], sig=c.s_dve)
        sc_done[si] = tsf
        tw = p.op("vector", lambda e, si=si: e.tensor_tensor(
            out=wout[si][:], in0=bits[si][:].bitcast(BF16)[:, 0:CH], in1=scf[si][:],
            op=ALU.mult), reads=[tbi, tsf, w_done[si]], sig=c.s_dve)
        bits_done[si] = tw
        scf_done[si] = tw
        tst = p.dma("gpsimd", wdst, wout[si][:], reads=[tw], sig=r_wo[si])
        w_done[si] = tst
    c.t_deq_end = [Tok(s, s.count, "_dq_", True) for s in (r_wo[0], r_wo[1])]
    if "d_wgu" in c.dbg:
        p.dma("gpsimd", c.dbg["d_wgu"], c.wguT, reads=[c.t_deq_end], sig=c.s_store)
        p.dma("gpsimd", c.dbg["d_wd"], c.wdT, reads=[c.t_deq_end], sig=c.s_store)


def phase_ffn(c):
    p = c.p
    noc = min(4, getattr(c, "FFNN", 4))

    r_xt = p.ring("rxt", 2)
    r_wg = c.rw12
    r_wd = p.ring("rwd", 8)
    r_ys = p.ring_slots("rys", 2)

    xnT = [c.cb(CH) for _ in range(16)]
    act = [c.cb(CH) for _ in range(44)]
    sg = [c.cb(CH) for _ in range(4)]
    upb = [c.cb(CH) for _ in range(4)]
    wg = [c.cb(CH) for _ in range(12)]
    wd = [c.cb(CH) for _ in range(8)]
    ysb = [c.cf(CH) for _ in range(2)]
    sqjunk = c.cf(CH)
    sqc = c.cf(16)
    rcol = c.cf(4)
    c.ffn_rstd = c.sb("ffn_rstd", [128, 16], F32)

    xnT_done = [None] * 16
    wg_done = [None] * 12
    wd_done = [None] * 8
    sg_done = [None] * 4
    up_done = [None] * 4
    act_done = [None] * 44
    y_done = [None, None]
    sq_done = None

    wgc = 0
    wdc = 0
    for oc in range(noc):
        txt = None
        for K in range(16):
            txt = p.dmat("sync", xnT[K][:],
                         c.xn_tok[oc * CH:(oc + 1) * CH, K * 128:(K + 1) * 128],
                         reads=[xnT_done[K], c.t_xn_done], sig=r_xt[oc % 2])
        for j4 in range(11):
            for half in range(2):
                dst = sg if half == 0 else upb
                dst_done = sg_done if half == 0 else up_done
                base = half * I_FFN + j4 * 4 * 128
                mmlast = [None] * 4
                for K in range(16):
                    wi = wgc % 12
                    wgc += 1
                    twl = p.dma("sync", wg[wi][:],
                                c.wguT[K * 128:(K + 1) * 128, base:base + CH],
                                reads=[wg_done[wi], c.t_deq_end], sig=r_wg[wi])
                    for o4 in range(4):
                        mmlast[o4] = p.op("tensor", lambda e, K=K, wi=wi, o4=o4: e.matmul(
                            out=c.pb[o4][:], lhsT=wg[wi][:, o4 * 128:(o4 + 1) * 128],
                            rhs=xnT[K][:], start=(K == 0), stop=(K == 15)),
                            reads=[twl, txt] + (c.pb_use[o4] if K == 0 else []),
                            sig=c.s_pe)
                    wg_done[wi] = mmlast[3]
                for o4 in range(4):
                    c.pb_use[o4] = []
                    if half == 0:
                        tev = p.op("scalar", lambda e, o4=o4: e.activation(
                            out=sg[o4][:], in_=c.pb[o4][:], func=AF.Silu),
                            reads=[mmlast[o4], sg_done[o4]], sig=c.s_act)
                        sg_done[o4] = tev
                    else:
                        tev = p.op("scalar", lambda e, o4=o4: e.activation(
                            out=upb[o4][:], in_=c.pb[o4][:], func=AF.Copy),
                            reads=[mmlast[o4], up_done[o4]], sig=c.s_act)
                        up_done[o4] = tev
                    c.pb_use[o4].append(tev)
            for o4 in range(4):
                jj = j4 * 4 + o4
                tmu = p.op("vector", lambda e, o4=o4, jj=jj: e.tensor_mul(
                    out=act[jj][:], in0=sg[o4][:], in1=upb[o4][:]),
                    reads=[sg_done[o4], up_done[o4], act_done[jj]], sig=c.s_dve)
                sg_done[o4] = [sg_done[o4], tmu]
                up_done[o4] = [up_done[o4], tmu]
                act_done[jj] = tmu
        for K in range(16):
            xnT_done[K] = wg_done[5]
        sq_toks = []
        for hj in range(4):
            mmlast = [None] * 4
            for i in range(44):
                wi = wdc % 8
                wdc += 1
                twl = p.dma("sync", wd[wi][:],
                            c.wdT[i * 128:(i + 1) * 128, hj * CH:(hj + 1) * CH],
                            reads=[wd_done[wi], c.t_deq_end], sig=r_wd[wi])
                for ts in range(4):
                    mmlast[ts] = p.op("tensor", lambda e, i=i, wi=wi, ts=ts: e.matmul(
                        out=c.pb[ts][:], lhsT=act[i][:, ts * 128:(ts + 1) * 128],
                        rhs=wd[wi][:], start=(i == 0), stop=(i == 43)),
                        reads=[twl, act_done[i]] + (c.pb_use[ts] if i == 0 else []),
                        sig=c.s_pe)
                wd_done[wi] = mmlast[3]
            for ts in range(4):
                c.pb_use[ts] = []
                si = ts % 2
                tyc = p.op("vector", lambda e, ts=ts, si=si: e.tensor_copy(
                    out=ysb[si][:], in_=c.pb[ts][:]), reads=[mmlast[ts], y_done[si]],
                    sig=c.s_dve)
                c.pb_use[ts].append(tyc)
                tsq = p.op("scalar", lambda e, ts=ts, hj=hj, si=si: e.activation(
                    out=sqjunk[:], in_=ysb[si][:], func=AF.Square,
                    accum_out=sqc[:, ts * 4 + hj:ts * 4 + hj + 1]),
                    reads=[tyc, sq_done], sig=c.s_act)
                tys = p.dma("gpsimd",
                            c.ffny[oc * CH + ts * 128: oc * CH + (ts + 1) * 128,
                                   hj * CH:(hj + 1) * CH],
                            ysb[si][:], reads=[tyc], sig=r_ys[si])
                y_done[si] = [tys, tsq]
                sq_toks.append(tsq)
        for i in range(44):
            act_done[i] = wd_done[5]
        for ts in range(4):
            def rms_ts(e, ts=ts, oc=oc):
                e.tensor_tensor(out=rcol[:, 0:1], in0=sqc[:, ts * 4:ts * 4 + 1],
                                in1=sqc[:, ts * 4 + 1:ts * 4 + 2], op=ALU.add)
                e.tensor_tensor(out=rcol[:, 1:2], in0=sqc[:, ts * 4 + 2:ts * 4 + 3],
                                in1=sqc[:, ts * 4 + 3:ts * 4 + 4], op=ALU.add)
                e.drain()
                return e.tensor_scalar(out=rcol[:, 2:3], in0=rcol[:, 0:1],
                                       scalar1=rcol[:, 1:2], scalar2=1.0 / D,
                                       op0=ALU.add, op1=ALU.mult)
            trm = p.op("vector", rms_ts, reads=sq_toks + [getattr(c, "_rms_prev", None)],
                       sig=c.s_dve)
            tl = p.op("scalar", lambda e: e.activation(
                out=rcol[:, 3:4], in_=rcol[:, 2:3], func=AF.Ln, bias=c.rmseps[:]),
                reads=[trm, c.t_const], sig=c.s_act)
            c._rms_prev = tl
            p.drain("scalar")
            tex = p.op("scalar", lambda e, oc=oc, ts=ts: e.activation(
                out=c.ffn_rstd[:, oc * 4 + ts:oc * 4 + ts + 1], in_=rcol[:, 3:4],
                func=AF.Exp, scale=-0.5), sig=c.s_act)
            sq_done = tex
            c.t_ffn_rstd = tex
    c.t_ffn_end = [Tok(s, s.count, "_ff_", True) for s in (r_ys[0], r_ys[1])]
    if "d_ffny" in c.dbg:
        p.dma("gpsimd", c.dbg["d_ffny"][0:noc * CH, :], c.ffny[0:noc * CH, :],
              reads=[c.t_ffn_end], sig=c.s_store)


def phase_attn(c):
    p, sb = c.p, c.sb
    nqc = min(4, getattr(c, "ATN", 4))
    SEG = 8

    r_zt = p.ring("azt", 1)
    r_wq = c.rw12
    r_cs = p.ring("acs", 2)
    r_eps = p.ring("aeps", 2)
    r_row = p.ring("arow", 2)
    r_kn = p.ring("akn", 3)
    r_vt = p.ring("avt", 3)
    r_wo = p.ring("awo", 6)
    r_ff = p.ring("aff", 2)
    r_xo = p.ring("axo", 2)
    r_mk = p.ring("amk", 1)
    r_sw = p.ring("asw2", 2)
    s_rb = p.sem("a_rowb")
    s_out = p.ring_slots("sout", 2)
    c.s_out = s_out[0]

    zT = [c.cb(CH) for _ in range(16)]
    qrotb = [c.cb(CH) for _ in range(16)]
    qn = zT
    aT = qrotb
    masks_sb = c.cb(16 * CH)
    wq4 = [c.cb(CH) for _ in range(12)]
    wot = [c.cb(CH) for _ in range(6)]
    knseg = [c.cb(SEG * 128) for _ in range(3)]
    vseg = [c.cb(SEG * 128) for _ in range(3)]
    Pb = [c.cb(CH) for _ in range(2)]
    rbc = c.cb(CH)
    statb = c.cb(2 * CH)[0:16, :]
    rowsb = c.cf(8 * CH).bitcast(BF16)[0:1, :]
    epsr = [c.cb(CH)[0:1, :] for _ in range(2)]
    q2b = [c.cb(CH) for _ in range(2)]
    lrb = c.cb(CH)[0:1, :]
    mx16 = c.cb(16)[0:16, 0:1]
    mxr = c.cb(16)[0:1, :]

    cosc = [c.cf(CH) for _ in range(2)]
    sinc = [c.cf(CH) for _ in range(2)]
    krot = [c.cf(CH) for _ in range(2)]
    krot_sw = [c.cf(CH) for _ in range(2)]
    p1 = [c.cf(CH) for _ in range(2)]
    p2 = [c.cf(CH) for _ in range(2)]
    stats = c.cf(5 * CH)[0:16, :]
    fft = [c.cf(CH) for _ in range(2)]
    xot = [c.cf(CH) for _ in range(2)]
    outb = [c.cf(CH) for _ in range(2)]
    tmpf = [c.cf(CH) for _ in range(2)]
    lrow = c.cf(CH)[0:1, :]
    qmaxc = c.cf(4)[0:16, 0:1]
    mrow = c.cf(4)[0:1, :]
    negMb = c.cf(1)

    zt_done = None
    wq_done = [None] * 12
    wo_done = [None] * 6
    cs_done = [None, None]
    eps_done = [None, None]
    row_done = [None, None]
    ksg_done = [None, None, None]
    vsg_done = [None, None, None]
    ff_done = [None, None]
    xo_done = [None, None]
    out_done = [None, None]
    tmp_done = [None, None]
    P_done = [None, None]
    q2_done = [None, None]
    qsq_pend = [None, None]
    qrot_done = [None] * 16
    qn_done = [None] * 16
    aT_done = [None] * 16
    krot_done = [None, None]
    p12_done = [None, None]
    stats_done = None
    statb_done = None
    lr_done = None
    rbc_done = None
    negM_done = None
    mx_done = None
    mk_done = None

    s1v = c.pstat[0:16, :]
    s2v = c.pb[4][0:16, :]
    qnv = c.pb[5][0:16, :]

    wqc = 0
    woc = 0
    for qi in range(nqc):
        tzt = None
        for K in range(16):
            tzt = p.dmat("sync", zT[K][:],
                         c.zown[qi * CH:(qi + 1) * CH, K * 128:(K + 1) * 128],
                         reads=[zt_done, c.t_z_done], sig=r_zt[0])
        ri = qi % 2
        tcs = p.dma("sync", cosc[ri][:], c.cosdo[:, qi * CH:(qi + 1) * CH],
                    reads=[cs_done[ri]], sig=r_cs[ri])
        tsn = p.dma("sync", sinc[ri][:], c.sindo[:, qi * CH:(qi + 1) * CH],
                    reads=[cs_done[ri]], sig=r_cs[ri])
        tep = p.dma("sync", epsr[ri][:],
                    c.epso.rearrange("a b -> (a b)")[qi * CH:(qi + 1) * CH].unsqueeze(0),
                    reads=[eps_done[ri], c.t_epso], sig=r_eps[ri])
        tmk = p.dma("sync", masks_sb[:].rearrange("p (m q) -> p m q", m=16),
                    c.maskv[qi].rearrange("m p q -> p m q"),
                    reads=[mk_done], sig=r_mk[0])

        stat_last = [None, None]
        for hg in range(4):
            mmg = [None] * 4
            for K in range(16):
                wi = wqc % 12
                wqc += 1
                tw = p.dma("sync", wq4[wi][:],
                           c.wqP[K * 128:(K + 1) * 128, hg * CH:(hg + 1) * CH],
                           reads=[wq_done[wi], c.t_wprep_end], sig=r_wq[wi])
                for hi in range(4):
                    mmg[hi] = p.op("tensor", lambda e, K=K, wi=wi, hi=hi: e.matmul(
                        out=c.pb[hi][:], lhsT=wq4[wi][:, hi * 128:(hi + 1) * 128],
                        rhs=zT[K][:], start=(K == 0), stop=False),
                        reads=[tw, tzt] + (c.pb_use[hi] if K == 0 else []), sig=c.s_pe)
                wq_done[wi] = mmg[3]
            for hi in range(4):
                h = hg * 4 + hi
                pbi = h % 2
                c.pb_use[hi] = []
                mmb = p.op("tensor", lambda e, h=h, hi=hi: e.matmul(
                    out=c.pb[hi][:], lhsT=c.browq_bf[0:1, h * 128:(h + 1) * 128],
                    rhs=c.ones_row[:], start=False, stop=True),
                    reads=[mmg[hi], c.t_brow, c.t_const], sig=c.s_pe)
                tev = p.op("scalar", lambda e, hi=hi, ki=pbi: e.activation(
                    out=krot[ki][:], in_=c.pb[hi][:], func=AF.Copy),
                    reads=[mmb, krot_done[pbi]], sig=c.s_act)
                c.pb_use[hi].append(tev)
                tw1 = p.dma("gpsimd", krot_sw[pbi][0:64, :], krot[pbi][64:128, :],
                            reads=[tev, krot_done[pbi]], sig=r_sw[pbi])
                tw2 = p.dma("gpsimd", krot_sw[pbi][64:128, :], krot[pbi][0:64, :],
                            sig=r_sw[pbi])
                tp1 = p.op("vector", lambda e, ri=ri, ki=pbi: e.tensor_mul(
                    out=p1[ki][:], in0=krot[ki][:], in1=cosc[ri][:]),
                    reads=[tev, tsn, p12_done[pbi]], sig=c.s_dve)
                tp2 = p.op("gpsimd", lambda e, ri=ri, ki=pbi: e.tensor_mul(
                    out=p2[ki][:], in0=krot_sw[ki][:], in1=sinc[ri][:]),
                    reads=[tw2, tsn], sig=c.s_pool)
                krot_done[pbi] = [tp1, tp2]
                tcb = p.op("vector", lambda e, h=h, ki=pbi: e.tensor_add(
                    out=qrotb[h][:], in0=p1[ki][:], in1=p2[ki][:]),
                    reads=[tp1, tp2, qrot_done[h]], sig=c.s_dve)
                p12_done[pbi] = tcb
                qrot_done[h] = [tcb]
        for h in range(16):
            ki = h % 2
            tsq = p.op("scalar", lambda e, h=h, ki=ki: e.activation(
                out=q2b[ki][:], in_=qrotb[h][:], func=AF.Square),
                reads=[qrot_done[h], q2_done[ki]], sig=c.s_act)
            ts1 = p.op("tensor", lambda e, h=h: e.matmul(
                out=s1v, lhsT=c.ehs[:, h * 16:(h + 1) * 16], rhs=qrotb[h][:],
                start=(h == 0), stop=(h == 15)),
                reads=[qrot_done[h], c.t_const] + (c.pstat_use if h == 0 else []),
                sig=c.s_pe)
            ts2 = p.op("tensor", lambda e, h=h, ki=ki: e.matmul(
                out=s2v, lhsT=c.ehs[:, h * 16:(h + 1) * 16], rhs=q2b[ki][:],
                start=(h == 0), stop=False),
                reads=[tsq] + (c.pb_use[4] if h == 0 else []), sig=c.s_pe)
            q2_done[ki] = ts2
            stat_last = [ts1, ts2]
        tse = p.op("tensor", lambda e, ri=ri: e.matmul(
            out=s2v, lhsT=c.ones_row[0:1, 0:16], rhs=epsr[ri][:],
            start=False, stop=True), reads=[tep, stat_last[1], c.t_const], sig=c.s_pe)
        eps_done[ri] = tse

        def smk(e):
            e.tensor_scalar(out=stats[:, 0:CH], in0=s1v, scalar1=1.0 / 128,
                            scalar2=None, op0=ALU.mult)
            e.tensor_scalar(out=stats[:, CH:2 * CH], in0=s2v, scalar1=1.0 / 128,
                            scalar2=None, op0=ALU.mult)
            e.drain()
            e.scalar_tensor_tensor(out=stats[:, 2 * CH:3 * CH], in0=stats[:, 0:CH],
                                   scalar=-1.0, in1=stats[:, 0:CH],
                                   op0=ALU.mult, op1=ALU.mult)
            e.drain()
            return e.tensor_tensor(out=stats[:, 2 * CH:3 * CH],
                                   in0=stats[:, CH:2 * CH],
                                   in1=stats[:, 2 * CH:3 * CH], op=ALU.add)
        tsk = p.op("vector", smk, reads=[tse, stat_last[0], stats_done], sig=c.s_dve)
        tl1 = p.op("scalar", lambda e: e.activation(
            out=stats[:, 3 * CH:4 * CH], in_=stats[:, 2 * CH:3 * CH], func=AF.Ln),
            reads=[tsk], sig=c.s_act)
        p.drain("scalar")
        tl2 = p.op("scalar", lambda e: e.activation(
            out=stats[:, 3 * CH:4 * CH], in_=stats[:, 3 * CH:4 * CH], func=AF.Exp,
            scale=-0.5), sig=c.s_act)

        def smk2(e):
            e.scalar_tensor_tensor(out=stats[:, 4 * CH:5 * CH], in0=stats[:, 0:CH],
                                   scalar=-1.0, in1=stats[:, 3 * CH:4 * CH],
                                   op0=ALU.mult, op1=ALU.mult)
            e.tensor_copy(out=statb[:, 0:CH], in_=stats[:, 3 * CH:4 * CH])
            e.drain()
            return e.tensor_copy(out=statb[:, CH:2 * CH], in_=stats[:, 4 * CH:5 * CH])
        tsk2 = p.op("vector", smk2, reads=[tl2, statb_done], sig=c.s_dve)
        stats_done = tsk2
        c.pstat_use = [tsk]
        c.pb_use[4] = [tsk]

        trb = p.dma("gpsimd", c.rowq, statb[:],
                    reads=[tsk2, getattr(c, "_rowq_read", [])], sig=s_rb)
        statb_done = trb

        rowq_reads = []
        row_users = []
        for h in range(16):
            si = h % 2
            pc = c.pb[2 * si]
            pd = c.pb[2 * si + 1]
            if h % 8 == 0:
                rflat = c.rowq.rearrange("a b -> (a b)")
                trh = p.dma("sync", rowsb[:],
                            rflat[h * 2 * CH:(h + 8) * 2 * CH].unsqueeze(0),
                            reads=[trb, row_done[0]], sig=r_row[0])
                rowq_reads.append(trh)
                row_done[0] = []
            ho = (h % 8) * 2 * CH
            tmc = p.op("tensor", lambda e, ho=ho, pc=pc: e.matmul(
                out=pc[:], lhsT=c.lrow[0:1, 0:HD],
                rhs=rowsb[0:1, ho:ho + CH],
                start=True, stop=True),
                reads=[trh, c.t_lrow] + c.pb_use[2 * si], sig=c.s_pe)
            tmd1 = p.op("tensor", lambda e, ho=ho, pd=pd: e.matmul(
                out=pd[:], lhsT=c.lrow[0:1, 0:HD],
                rhs=rowsb[0:1, ho + CH:ho + 2 * CH],
                start=True, stop=False),
                reads=[trh] + c.pb_use[2 * si + 1], sig=c.s_pe)
            tmd2 = p.op("tensor", lambda e, pd=pd: e.matmul(
                out=pd[:], lhsT=c.lrow[0:1, HD:2 * HD], rhs=c.ones_row[:],
                start=False, stop=True), sig=c.s_pe)
            row_users += [tmc, tmd1]
            row_done[0] = row_done[0] + [tmc, tmd1]
            c.pb_use[2 * si] = []
            c.pb_use[2 * si + 1] = []
            tt1 = p.op("vector", lambda e, h=h, si=si, pc=pc: e.tensor_tensor(
                out=Pb[si][:], in0=qrotb[h][:], in1=pc[:], op=ALU.mult),
                reads=[tmc, P_done[si]], sig=c.s_dve)
            tt2 = p.op("vector", lambda e, h=h, si=si, pd=pd: e.tensor_tensor(
                out=qn[h][:], in0=Pb[si][:], in1=pd[:], op=ALU.add),
                reads=[tmd2, qn_done[h]], sig=c.s_dve)
            qrot_done[h] = [tt1]
            P_done[si] = tt2
            c.pb_use[2 * si].append(tt1)
            c.pb_use[2 * si + 1].append(tt2)
            if h >= 2:
                hd2 = h - 2
                tqn = p.op("tensor", lambda e, hd2=hd2, si=si: e.matmul(
                    out=qnv, lhsT=c.ehs[:, hd2 * 16:(hd2 + 1) * 16], rhs=q2b[si][:],
                    start=(hd2 == 0), stop=False),
                    reads=[qsq_pend[si]] + (c.pb_use[5] if hd2 == 0 else []),
                    sig=c.s_pe)
                q2_done[si] = tqn
            tsq2 = p.op("scalar", lambda e, h=h, si=si: e.activation(
                out=q2b[si][:], in_=qn[h][:], func=AF.Square),
                reads=[tt2, q2_done[si]], sig=c.s_act)
            qsq_pend[si] = tsq2
            qn_done[h] = [tsq2]
        for hd2 in (14, 15):
            si = hd2 % 2
            tqn = p.op("tensor", lambda e, hd2=hd2, si=si: e.matmul(
                out=qnv, lhsT=c.ehs[:, hd2 * 16:(hd2 + 1) * 16], rhs=q2b[si][:],
                start=False, stop=(hd2 == 15)),
                reads=[qsq_pend[si]], sig=c.s_pe)
            q2_done[si] = tqn
        c._rowq_read = rowq_reads

        def mx1(e):
            e.reduce_max(out=qmaxc[:, 0:1], in_=qnv, axis=AX.X)
            e.drain()
            e.tensor_tensor(out=qmaxc[:, 0:1], in0=qmaxc[:, 0:1], in1=c.maxk2[:],
                            op=ALU.mult)
            e.drain()
            return e.tensor_copy(out=mx16[:, 0:1], in_=qmaxc[:, 0:1])
        tmx = p.op("vector", mx1, reads=[q2_done[0], q2_done[1], c.t_maxk, mx_done],
                   sig=c.s_dve)
        c.pb_use[5] = [tmx]
        ttr = p.op("tensor", lambda e: e.transpose(
            out=c.ptr[0:1, 0:16], in_=mx16[0:16, 0:1], identity=c.identb[0:16, 0:16]),
            reads=[tmx, c.t_const] + c.ptr_use, sig=c.s_pe)
        c.ptr_use = []
        tmr = p.op("vector", lambda e: e.tensor_copy(out=mxr[0:1, 0:16],
                                                     in_=c.ptr[0:1, 0:16]),
                   reads=[ttr], sig=c.s_dve)
        c.ptr_use.append(tmr)
        p.drain("vector")
        tmm = p.op("vector", lambda e: e.reduce_max(out=mrow[0:1, 0:1],
                                                    in_=mxr[0:1, 0:16], axis=AX.X),
                   sig=c.s_dve)
        tsf = p.op("scalar", lambda e: e.activation(
            out=mrow[0:1, 1:2], in_=mrow[0:1, 0:1], func=AF.Sqrt),
            reads=[tmm], sig=c.s_act)
        p.drain("scalar")
        tng = p.op("scalar", lambda e: e.activation(
            out=mrow[0:1, 2:3], in_=mrow[0:1, 1:2], func=AF.Copy, scale=-1.02),
            sig=c.s_act)
        tngb = p.op("vector", lambda e: e.tensor_copy(out=mxr[0:1, 0:1],
                                                      in_=mrow[0:1, 2:3]),
                    reads=[tng, tmr], sig=c.s_dve)
        tmb = p.op("tensor", lambda e: e.matmul(
            out=c.pb[0][:, 0:1], lhsT=c.ones_row[0:1, 0:128], rhs=mxr[0:1, 0:1],
            start=True, stop=True), reads=[tngb, c.t_const] + c.pb_use[0], sig=c.s_pe)
        tnb = p.op("vector", lambda e: e.tensor_copy(out=negMb[:, 0:1],
                                                     in_=c.pb[0][:, 0:1]),
                   reads=[tmb], sig=c.s_dve)
        c.pb_use[0] = [tnb]
        negM_done = tnb
        mx_done = [tmm, tngb]

        nkt = SLOT_NKT[qi]
        mk_users = []

        def emit_accl(pd, nkt=nkt):
            kt0, sb0, sg0, ko0, tvl0, texp0 = pd
            st = (kt0 == 0)
            sp = (kt0 == nkt - 1)
            tacc = p.op("tensor", lambda e, sb0=sb0, st=st, sp=sp,
                        sg0=sg0, ko0=ko0: e.matmul(
                out=c.pb[4][:], lhsT=vseg[sg0][:, ko0:ko0 + 128], rhs=Pb[sb0][:],
                start=st, stop=sp),
                reads=[tvl0, texp0] + (c.pb_use[4] if st else []), sig=c.s_pe)
            tl0 = p.op("tensor", lambda e, sb0=sb0, st=st, sp=sp: e.matmul(
                out=c.pb[5][0:1, :], lhsT=c.onescol_bf[:], rhs=Pb[sb0][:],
                start=st, stop=sp),
                reads=[texp0, c.t_const] + (c.pb_use[5] if st else []), sig=c.s_pe)
            vsg_done[sg0].append(tacc)
            P_done[sb0] = [tacc, tl0]
            return tl0

        for h in range(16):
            acc = c.pb[4]
            lps = c.pb[5][0:1, :]
            pend = None
            for kt in range(nkt):
                sgi = (kt // SEG) % 3
                if kt % SEG == 0:
                    tkl = p.dma("sync", knseg[sgi][:],
                                c.knT[h * 128:(h + 1) * 128,
                                      kt * 128:(kt + SEG) * 128],
                                reads=[ksg_done[sgi], c.t_kv_end], sig=r_kn[sgi])
                    tvl = p.dma("sync", vseg[sgi][:].rearrange("t (k d) -> t k d", k=SEG),
                                c.vtok[kt:kt + SEG, h].rearrange("k t d -> t k d"),
                                reads=[vsg_done[sgi], c.t_kv_end], sig=r_vt[sgi])
                    ksg_done[sgi] = []
                    vsg_done[sgi] = []
                ko = (kt % SEG) * 128
                sb_ = kt % 2
                spb = c.pb[2 + (kt % 2)]
                diag = kt >= nkt - 16
                tmmS = p.op("tensor", lambda e, h=h, spb=spb, diag=diag, sgi=sgi, ko=ko: e.matmul(
                    out=spb[:], lhsT=knseg[sgi][:, ko:ko + 128], rhs=qn[h][:],
                    start=True, stop=not diag),
                    reads=[tkl, qn_done[h]] + c.pb_use[2 + (kt % 2)], sig=c.s_pe)
                ksg_done[sgi].append(tmmS)
                if diag:
                    midx = kt - (nkt - 16)
                    tmmS = p.op("tensor", lambda e, midx=midx, spb=spb: e.matmul(
                        out=spb[:], lhsT=c.identb[:],
                        rhs=masks_sb[:, midx * CH:(midx + 1) * CH],
                        start=False, stop=True), reads=[c.t_const, tmk], sig=c.s_pe)
                    mk_users.append(tmmS)
                c.pb_use[2 + (kt % 2)] = []
                texp = p.op("scalar", lambda e, sb_=sb_, spb=spb: e.activation(
                    out=Pb[sb_][:], in_=spb[:], func=AF.Exp, bias=negMb[:, 0:1],
                    scale=1.0), reads=[tmmS, negM_done, P_done[sb_]], sig=c.s_act)
                c.pb_use[2 + (kt % 2)].append(texp)
                if pend is not None:
                    tl_ = emit_accl(pend)
                pend = (kt, sb_, sgi, ko, tvl, texp)
            tl_ = emit_accl(pend)
            pend = None
            c.pb_use[4] = []
            c.pb_use[5] = []
            def lrec(e):
                e.tensor_scalar(out=lrow[0:1, :], in0=lps, scalar1=1e-35,
                                scalar2=None, op0=ALU.max)
                e.drain()
                e.reciprocal(out=lrow[0:1, :], in_=lrow[0:1, :])
                e.drain()
                return e.tensor_copy(out=lrb[0:1, :], in_=lrow[0:1, :])
            tlr = p.op("vector", lrec, reads=[tl_, lr_done], sig=c.s_dve)
            lr_done = tlr
            c.pb_use[5].append(tlr)
            trc = p.op("tensor", lambda e: e.matmul(
                out=c.pb[0][:], lhsT=c.ones_row[0:1, 0:128], rhs=lrb[0:1, :],
                start=True, stop=True), reads=[tlr, c.t_const] + c.pb_use[0],
                sig=c.s_pe)
            c.pb_use[0] = []
            tre = p.op("scalar", lambda e: e.activation(
                out=rbc[:], in_=c.pb[0][:], func=AF.Copy),
                reads=[trc, rbc_done], sig=c.s_act)
            c.pb_use[0].append(tre)
            tdv = p.op("vector", lambda e, h=h: e.tensor_tensor(
                out=aT[h][:], in0=acc[:], in1=rbc[:], op=ALU.mult),
                reads=[tre, aT_done[h], qrot_done[h]], sig=c.s_dve)
            rbc_done = tdv
            c.pb_use[4].append(tdv)
            aT_done[h] = tdv
            qn_done[h] = [qn_done[h], tmmS]

        for do in range(4):
            mml = [None] * 4
            for h in range(16):
                wi = woc % 6
                woc += 1
                twl = p.dma("sync", wot[wi][:],
                            c.woP[h * 128:(h + 1) * 128, do * CH:(do + 1) * CH],
                            reads=[wo_done[wi], c.t_wprep_end], sig=r_wo[wi])
                for ts in range(4):
                    mml[ts] = p.op("tensor", lambda e, h=h, wi=wi, ts=ts: e.matmul(
                        out=c.pb[ts][:], lhsT=aT[h][:, ts * 128:(ts + 1) * 128],
                        rhs=wot[wi][:], start=(h == 0), stop=False),
                        reads=[twl, aT_done[h]] + (c.pb_use[ts] if h == 0 else []),
                        sig=c.s_pe)
                wo_done[wi] = mml[3]
            for ts in range(4):
                c.pb_use[ts] = []
                mmb = p.op("tensor", lambda e, do=do, ts=ts: e.matmul(
                    out=c.pb[ts][:], lhsT=c.ones_row[0:1, 0:128],
                    rhs=c.bo_bf[0:1, do * CH:(do + 1) * CH],
                    start=False, stop=True), reads=[mml[ts], c.t_bo], sig=c.s_pe)
                si = ts % 2
                tff = p.dma("sync", fft[si][:],
                            c.ffny[qi * CH + ts * 128: qi * CH + (ts + 1) * 128,
                                   do * CH:(do + 1) * CH],
                            reads=[ff_done[si], c.t_ffn_end], sig=r_ff[si])
                txo = p.dma("sync", xot[si][:],
                            c.xown[qi * CH + ts * 128: qi * CH + (ts + 1) * 128,
                                   do * CH:(do + 1) * CH],
                            reads=[xo_done[si]], sig=r_xo[si])
                tst1 = p.op("vector", lambda e, si=si, ts=ts, qi=qi: e.scalar_tensor_tensor(
                    out=tmpf[si][:], in0=fft[si][:],
                    scalar=c.ffn_rstd[:, qi * 4 + ts:qi * 4 + ts + 1],
                    in1=c.pb[ts][:], op0=ALU.mult, op1=ALU.add),
                    reads=[tff, mmb, c.t_ffn_rstd, tmp_done[si]], sig=c.s_dve)
                c.pb_use[ts].append(tst1)
                tst2 = p.op("vector", lambda e, si=si: e.tensor_tensor(
                    out=outb[si][:], in0=tmpf[si][:], in1=xot[si][:], op=ALU.add),
                    reads=[txo, out_done[si]], sig=c.s_dve)
                ff_done[si] = tst1
                xo_done[si] = tst2
                tmp_done[si] = tst2
                tso = p.dma("gpsimd", c.y[qi * CH + ts * 128: qi * CH + (ts + 1) * 128,
                                          do * CH:(do + 1) * CH],
                            outb[si][:], reads=[tst2], sig=s_out[si])
                out_done[si] = tso
        for h in range(16):
            aT_done[h] = [aT_done[h], mml[3]]
        cs_done[ri] = krot_done[:]
        zt_done = mml[:]
        mk_done = mk_users




_NC_CACHE = {}


def kernel(**inputs):
    from concourse.bass_utils import run_bass_kernel_spmd
    in_maps = host_prepare(inputs)
    if "nc" not in _NC_CACHE:
        _NC_CACHE["nc"] = Core(0).build()
    nc = _NC_CACHE["nc"]
    res = run_bass_kernel_spmd(nc, in_maps, core_ids=list(range(8)))
    return host_gather(res.results)
